# version 05
# baseline (speedup 1.0000x reference)
import numpy as np
import concourse.bass as bass
import concourse.mybir as mybir

F32 = mybir.dt.float32
BF16 = mybir.dt.bfloat16
I32 = mybir.dt.int32
AF = mybir.ActivationFunctionType
ALU = mybir.AluOpType
AX = mybir.AxisListType

T = 4096
D = 1024
NCH = D // 128
NTT = T // 512


class Sem:
    def __init__(self, h):
        self.h = h
        self.n = 0


class Prog:
    ENGS = "qgsvt"

    def __init__(self, nc):
        self.nc = nc
        self.ops = {e: [] for e in self.ENGS}
        self.n_inst = 0

    def wait(self, eng, sem, val=None):
        v = sem.n if val is None else val
        if v <= 0:
            return
        self.ops[eng].append(lambda E, s=sem.h, vv=v: E.wait_ge(s, vv))

    def op(self, eng, name, *args, inc=None, **kwargs):
        amt = 0
        if inc is not None:
            amt = 16 if "dma" in name else 1
            inc.n += amt

        def fn(E, name=name, args=args, kwargs=kwargs, inc=inc, amt=amt):
            ins = getattr(E, name)(*args, **kwargs)
            if inc is not None:
                ins.then_inc(inc.h, amt)
        self.ops[eng].append(fn)
        self.n_inst += 1

    def replay(self, block):
        mapping = {"q": block.sync, "g": block.gpsimd, "s": block.scalar,
                   "v": block.vector, "t": block.tensor}
        for e in self.ENGS:
            ops = self.ops[e]

            def body(E, ops=ops):
                for fn in ops:
                    fn(E)
            mapping[e](body)


class Ctx:
    def __init__(self, nc, es):
        self.nc = nc
        self.es = es
        self.p = Prog(nc)
        self._semgen = 0
        self.new_sems()
        self.psa = es.enter_context(nc.psum_tensor("psarena", [128, 4096], F32))
        self._uid = 0
        self.sC = Sem(es.enter_context(nc.semaphore("sC")))
        self._consts = {}
        self._const_t = es.enter_context(nc.sbuf_tensor("constp", [128, 32], F32))
        self._const_n = 0

    def sb(self, name, shape, dt=F32, side=None):
        return self.es.enter_context(
            self.nc.sbuf_tensor(name, shape, dt, side=side))

    def bank(self, b, rows=128):
        return self.psa[0:rows, b * 512:(b + 1) * 512]

    def banksl(self, b, sl, rows=128):
        return self.psa[0:rows, b * 512 + sl.start: b * 512 + sl.stop]

    def uid(self):
        self._uid += 1
        return self._uid

    def new_sems(self):
        g = self._semgen
        self._semgen += 1
        for nm in ("ld", "ldg", "stq", "sT", "sV", "sS", "sG"):
            h = self.es.enter_context(self.nc.semaphore(f"{nm}{g}"))
            setattr(self, nm, Sem(h))

    def phase_boundary(self):
        self.barrier()
        self.new_sems()

    def scope(self):
        from contextlib import ExitStack
        return ExitStack()

    def barrier(self):
        sems = [self.ld, self.ldg, self.stq, self.sT, self.sV, self.sS,
                self.sG]
        for e in "qgsvt":
            for s in sems:
                self.p.wait(e, s, s.n)

    def sbs(self, es, name, shape, dt=F32, side=None):
        return es.enter_context(
            self.nc.sbuf_tensor(name, shape, dt, side=side))

    def constf(self, val):
        val = float(val)
        if val not in self._consts and val not in (0.0, 1.0, -1.0):
            i = self._const_n
            assert i < 32
            self._const_n += 1
            ap = self._const_t[:, i:i + 1]
            self.nc.const_aps.register(F32, val, ap)
            self._consts[val] = ap
        return val

    def finalize(self):
        pre = []
        items = list(self._consts.items())
        for i, (val, ap) in enumerate(items):
            def fn(E, ap=ap, val=val, last=(i == len(items) - 1)):
                ins = E.memset(ap, val)
                if last:
                    ins.then_inc(self.sC.h, 1)
            pre.append(fn)
        if not items:
            def fn0(E):
                E.memset(self._const_t[:, 0:1], 0.0).then_inc(self.sC.h, 1)
            pre.append(fn0)
        self.p.ops["g"] = pre + self.p.ops["g"]
        for e in "qsvt":
            def w(E):
                E.wait_ge(self.sC.h, 1)
            self.p.ops[e] = [w] + self.p.ops[e]


def build_consts(cx):
    p = cx.p
    cx.ones_b = cx.sb("ones_b", [128, 1], BF16)
    cx.ones_f = cx.sb("ones_f", [128, 1], F32)
    cx.ones_fT = cx.sb("ones_fT", [1, 128], F32)
    cx.ones_bT = cx.sb("ones_bT", [1, 128], BF16)
    cx.ident = cx.sb("ident", [128, 128], BF16)
    p.op("g", "memset", cx.ones_b[:], 1.0)
    p.op("g", "memset", cx.ones_f[:], 1.0)
    p.op("g", "memset", cx.ones_fT[:], 1.0)
    p.op("g", "memset", cx.ones_bT[:], 1.0)
    p.op("g", "memset", cx.ident[:], 0.0)
    p.op("g", "affine_select", out=cx.ident[:], in_=cx.ident[:],
         compare_op=ALU.not_equal, fill=1.0, base=0,
         pattern=[[-1, 128]], channel_multiplier=1, inc=cx.sG)


def dram_to_bf16(cx, src_ap, dst_ap, stage_sb):
    p = cx.p
    R, C = src_ap.shape
    n = R // 128
    src_v = src_ap.rearrange("(n p) f -> n p f", p=128)
    dst_v = dst_ap.rearrange("(n p) f -> n p f", p=128)
    for i in range(n):
        p.wait("g", cx.stq, cx.stq.n)
        p.op("g", "dma_start", out=stage_sb, in_=src_v[i], inc=cx.ldg)
        p.wait("q", cx.ldg, cx.ldg.n)
        p.op("q", "dma_start", out=dst_v[i], in_=stage_sb, inc=cx.stq)
    p.wait("q", cx.stq, cx.stq.n)


def emit_ln_tile(cx, Rt, sq, mu_sb, var_sb, mu2_sb, eps=1e-5):
    p = cx.p
    sT, sV, sS = cx.sT, cx.sV, cx.sS
    sq_done = []
    for c in range(NCH):
        csl = slice(c * 512, (c + 1) * 512)
        p.op("t", "matmul", cx.bank(0, 1), cx.ones_f[:], Rt[:, csl],
             start=(c == 0), stop=(c == NCH - 1))
        if c >= 2:
            p.wait("s", sT, sq_done[c - 2])
        p.op("s", "activation", out=sq[c % 2][:], in_=Rt[:, csl],
             func=AF.Square, inc=sS)
        p.wait("t", sS)
        p.op("t", "matmul", cx.bank(1, 1), cx.ones_b[:], sq[c % 2][:],
             start=(c == 0), stop=(c == NCH - 1), inc=sT)
        sq_done.append(sT.n)
    p.wait("v", sT)
    p.op("v", "tensor_scalar_mul", mu_sb, cx.bank(0, 1), 1.0 / D)
    p.op("v", "tensor_scalar_mul", var_sb, cx.bank(1, 1), 1.0 / D, inc=sV)
    p.wait("s", sV)
    p.op("s", "activation", out=mu2_sb, in_=mu_sb, func=AF.Square, inc=sS)
    p.wait("v", sS)
    p.op("v", "tensor_sub", var_sb, var_sb, mu2_sb, inc=sV)
    p.wait("s", sV)
    p.op("s", "activation", out=var_sb, in_=var_sb, func=AF.Ln, bias=cx.constf(eps))
    p.op("s", "activation", out=var_sb, in_=var_sb, func=AF.Exp, scale=cx.constf(-0.5),
         inc=sS)
    p.wait("t", sS)
    p.op("t", "matmul", cx.bank(2), cx.ones_fT[:], mu_sb, start=True,
         stop=True)
    p.op("t", "matmul", cx.bank(3), cx.ones_fT[:], var_sb, start=True,
         stop=True, inc=sT)
    p.wait("v", sT)


def emit_xn_tile(cx, Rt, xn, scrf, dst_fn=None):
    p = cx.p
    for c in range(NCH):
        csl = slice(c * 512, (c + 1) * 512)
        dst = xn[:, csl] if dst_fn is None else dst_fn(c)
        p.op("v", "tensor_sub", scrf[:], Rt[:, csl], cx.bank(2))
        p.op("v", "tensor_mul", dst, scrf[:], cx.bank(3),
             inc=(cx.sV if c == NCH - 1 else None))


FF = 3072
NFC = FF // 128
SCALE = 1.0 / 127.0


def emit_ffn0(cx, r_in, r_out, wgT, wuT, wdT):
    p = cx.p
    u = cx.uid()
    cx.barrier()
    esF = cx.scope()
    cx.sb = lambda name, shape, dt=F32, side=None, _sbs=cx.sbs, _es=esF: _sbs(_es, name, shape, dt, side)
    Rt = cx.sb(f"f{u}_Rt", [128, NCH * 512], F32)
    Ro = cx.sb(f"f{u}_Ro", [128, NCH * 512], F32)
    xn = cx.sb(f"f{u}_xn", [128, NCH * 512], BF16)
    sq = [cx.sb(f"f{u}_sq{i}", [128, 512], BF16) for i in range(2)]
    scrf = cx.sb(f"f{u}_scrf", [128, 512], F32)
    mu_sb = cx.sb(f"f{u}_mu", [1, 512], F32)
    var_sb = cx.sb(f"f{u}_var", [1, 512], F32)
    mu2_sb = cx.sb(f"f{u}_mu2", [1, 512], F32)
    gu = cx.sb(f"f{u}_gu", [128, NFC * 512], BF16)
    gsil = [cx.sb(f"f{u}_gsil{i}", [128, 512], BF16) for i in range(2)]
    usil = [cx.sb(f"f{u}_usil{i}", [128, 512], BF16) for i in range(2)]
    wg_sb = [cx.sb(f"f{u}_wg{i}", [128, NCH * 512], BF16) for i in range(2)]
    wu_sb = [cx.sb(f"f{u}_wu{i}", [128, NCH * 512], BF16) for i in range(2)]
    wd_sb = [cx.sb(f"f{u}_wd{i}", [128, 24 * 128], BF16) for i in range(2)]

    r_in_v = r_in.rearrange("(c p) t -> p c t", p=128)
    r_out_v = r_out.rearrange("(c p) t -> p c t", p=128)
    wgT_v = wgT.rearrange("(c p) f -> p c f", p=128)
    wuT_v = wuT.rearrange("(c p) f -> p c f", p=128)
    wdT_v = wdT.rearrange("(k p) o -> p k o", p=128)

    rt_free = 0
    ro_free = 0
    wbuf_free = [0, 0]
    wdbuf_free = [0, 0]
    gu_evac = {}
    NGRP = 6

    for tt in range(NTT):
        tsl = slice(tt * 512, (tt + 1) * 512)
        p.wait("q", cx.sV, rt_free)
        p.op("q", "dma_start",
             out=Rt[:].rearrange("p (c t) -> p c t", c=NCH),
             in_=r_in_v[:, :, tsl], inc=cx.ld)
        ld_rt = cx.ld.n
        p.wait("t", cx.ld, ld_rt)
        p.wait("s", cx.ld, ld_rt)
        emit_ln_tile(cx, Rt[:], sq, mu_sb[:], var_sb[:], mu2_sb[:])
        emit_xn_tile(cx, Rt[:], xn[:], scrf)
        xn_done = cx.sV.n
        for grp in range(NGRP):
            gb = grp % 2
            fsl = slice(grp * 512, (grp + 1) * 512)
            p.wait("g", cx.sT, wbuf_free[gb])
            p.op("g", "dma_start",
                 out=wg_sb[gb][:].rearrange("p (c f) -> p c f", c=NCH),
                 in_=wgT_v[:, :, fsl], inc=cx.ldg)
            p.op("g", "dma_start",
                 out=wu_sb[gb][:].rearrange("p (c f) -> p c f", c=NCH),
                 in_=wuT_v[:, :, fsl], inc=cx.ldg)
            ld_w = cx.ldg.n
            p.wait("t", cx.ldg, ld_w)
            p.wait("t", cx.sV, xn_done)
            for fl in range(4):
                fc = grp * 4 + fl
                bk_g = 4 + (fc % 2) * 2
                bk_u = 5 + (fc % 2) * 2
                prev_fc = fc - 2 if fc >= 2 else 22 + fc
                p.wait("t", cx.sV, gu_evac.get(prev_fc, 0))
                for c in range(NCH):
                    lhs_g = wg_sb[gb][:, c * 512 + fl * 128:
                                      c * 512 + (fl + 1) * 128]
                    lhs_u = wu_sb[gb][:, c * 512 + fl * 128:
                                      c * 512 + (fl + 1) * 128]
                    rhs = xn[:, c * 512:(c + 1) * 512]
                    p.op("t", "matmul", cx.bank(bk_g), lhs_g, rhs,
                         start=(c == 0), stop=(c == NCH - 1))
                    p.op("t", "matmul", cx.bank(bk_u), lhs_u, rhs,
                         start=(c == 0), stop=(c == NCH - 1),
                         inc=(cx.sT if c == NCH - 1 else None))
                mm_done = cx.sT.n
                if fl == 3:
                    wbuf_free[gb] = mm_done
                p.wait("s", cx.sT, mm_done)
                p.wait("s", cx.sV, gu_evac.get(prev_fc + 1 if False else (fc - 2 if fc >= 2 else 22 + fc), 0))
                p.op("s", "activation", out=gsil[fc % 2][:], in_=cx.bank(bk_g),
                     func=AF.Silu, scale=cx.constf(SCALE))
                p.op("s", "activation", out=usil[fc % 2][:], in_=cx.bank(bk_u),
                     func=AF.Copy, scale=cx.constf(SCALE), inc=cx.sS)
                p.wait("v", cx.sS)
                p.op("v", "tensor_mul", gu[:, fc * 512:(fc + 1) * 512],
                     gsil[fc % 2][:], usil[fc % 2][:], inc=cx.sV)
                gu_evac[fc] = cx.sV.n
        all_gu = cx.sV.n
        down_evac = {}
        for oc in range(NCH):
            db = oc % 2
            p.wait("g", cx.sT, wdbuf_free[db])
            p.op("g", "dma_start",
                 out=wd_sb[db][:].rearrange("p (k o) -> p k o", k=24),
                 in_=wdT_v[:, :, oc * 128:(oc + 1) * 128], inc=cx.ldg)
            ld_wd = cx.ldg.n
            p.wait("t", cx.ldg, ld_wd)
            p.wait("t", cx.sV, all_gu)
            bk = oc % 2
            if oc >= 2:
                p.wait("t", cx.sV, down_evac[oc - 2])
            for kf in range(24):
                p.op("t", "matmul", cx.bank(bk),
                     wd_sb[db][:, kf * 128:(kf + 1) * 128],
                     gu[:, kf * 512:(kf + 1) * 512],
                     start=(kf == 0), stop=(kf == 23),
                     inc=(cx.sT if kf == 23 else None))
            wdbuf_free[db] = cx.sT.n
            p.wait("v", cx.sT)
            if oc == 0:
                p.wait("v", cx.stq, ro_free)
            csl = slice(oc * 512, (oc + 1) * 512)
            p.op("v", "scalar_tensor_tensor", out=Ro[:, csl],
                 in0=cx.bank(bk), scalar=0.5 * SCALE, in1=Rt[:, csl],
                 op0=ALU.mult, op1=ALU.add,
                 inc=cx.sV)
            down_evac[oc] = cx.sV.n
        rt_free = cx.sV.n
        p.wait("q", cx.sV, cx.sV.n)
        p.op("q", "dma_start", out=r_out_v[:, :, tsl],
             in_=Ro[:].rearrange("p (c t) -> p c t", c=NCH), inc=cx.stq)
        ro_free = cx.stq.n
    p.wait("q", cx.stq, cx.stq.n)
    del cx.sb
    esF.close()
    cx.barrier()


def prep_ffn0_weights(cx, wgT_i, wuT_i, wdT_i, wgT, wuT, wdT):
    with cx.scope() as esP:
        stage = cx.sbs(esP, "w_stage", [128, 3072], BF16, side="right")
        dram_to_bf16(cx, wgT_i, wgT, stage[:])
        dram_to_bf16(cx, wuT_i, wuT, stage[:])
        dram_to_bf16(cx, wdT_i, wdT, stage[:, 0:1024])


DI = 2048
NJC = DI // 128
DS = 16
DTR = 64


def emit_mamba(cx, r_in, r_out, W, scr):
    p = cx.p
    u = cx.uid()
    cx.barrier()
    r_in_v = r_in.rearrange("(c p) t -> p c t", p=128)
    r_out_v = r_out.rearrange("(c p) t -> p c t", p=128)

    fc1b = cx.sb(f"m{u}_fc1b", [128, 8], F32)
    fc2b = cx.sb(f"m{u}_fc2b", [128, 8], F32)
    es0 = cx.scope()
    stage = cx.sbs(es0, f"m{u}_stage", [128, 4096], BF16, side="right")
    dram_to_bf16(cx, W["inpT"], scr["inpT_bf"], stage[:])
    dram_to_bf16(cx, W["opT"], scr["opT_bf"], stage[:, 0:1024])

    esW = cx.scope()
    xpw = cx.sbs(esW, f"m{u}_xpw", [128, NJC * 96], BF16)
    dtpw = cx.sbs(esW, f"m{u}_dtpw", [64, DI], BF16)
    A_neg = cx.sbs(esW, f"m{u}_Aneg", [128, NJC * DS], F32)
    convw = cx.sbs(esW, f"m{u}_convw", [128, NJC * 4], F32)
    convb = cx.sbs(esW, f"m{u}_convb", [128, NJC], F32)
    dtpb = cx.sbs(esW, f"m{u}_dtpb", [128, NJC], F32)
    D_sb = cx.sbs(esW, f"m{u}_D", [128, NJC], F32)
    sel = [cx.sbs(esW, f"m{u}_sel{i}", [32, 128], BF16) for i in range(32)]
    ridx32 = cx.sbs(esW, f"m{u}_ridx32", [32, 1], F32)
    dtlow_bf = cx.sbs(esW, f"m{u}_dtlow", [64, T], BF16)
    BC_bf = cx.sbs(esW, f"m{u}_BC", [32, T], BF16)

    es_h = cx.scope()
    h_sb = cx.sbs(es_h, f"m{u}_h", [128, 8 * T], BF16)
    es1w = cx.scope()
    fc1w = cx.sbs(es1w, f"m{u}_fc1w", [128, 8 * 1024], BF16)
    with cx.scope() as esw0:
        wf = cx.sbs(esw0, f"m{u}_wf", [128, 8 * 1024], F32, side="right")
        p.op("q", "dma_start", out=wf[:].rearrange("p (c o) -> p c o", c=8),
             in_=W["fc1_wT"].rearrange("(c p) o -> p c o", p=128), inc=cx.ld)
        p.wait("v", cx.ld)
        p.op("v", "tensor_copy", fc1w[:], wf[:], inc=cx.sV)
        p.wait("q", cx.sV)
        p.op("q", "dma_start",
             out=wf[:, 0:NJC * 96].rearrange("p (c o) -> p c o", c=NJC),
             in_=W["xpT"].rearrange("(c p) o -> p c o", p=128), inc=cx.ld)
        p.wait("v", cx.ld)
        p.op("v", "tensor_copy", xpw[:], wf[:, 0:NJC * 96], inc=cx.sV)
        p.wait("q", cx.sV)
        p.op("q", "dma_start", out=wf[0:64, 0:DI], in_=W["dtpT"], inc=cx.ld)
        p.wait("v", cx.ld)
        p.op("v", "tensor_copy", dtpw[:], wf[0:64, 0:DI], inc=cx.sV)
        p.wait("q", cx.sV)
        p.op("q", "dma_start", out=wf[:, 0:NJC * DS], in_=W["A_log"],
             inc=cx.ld)
        p.wait("s", cx.ld)
        p.op("s", "activation", out=A_neg[:], in_=wf[:, 0:NJC * DS],
             func=AF.Exp, inc=cx.sS)
        p.wait("v", cx.sS)
        p.op("v", "tensor_scalar_mul", A_neg[:], A_neg[:], -1.0, inc=cx.sV)
        p.op("q", "dma_start", out=fc1b[:], in_=W["fc1_b"], inc=cx.ld)
        p.op("q", "dma_start", out=fc2b[:], in_=W["fc2_b"], inc=cx.ld)
        p.op("q", "dma_start", out=convw[:], in_=W["conv_w"], inc=cx.ld)
        p.op("q", "dma_start", out=convb[:], in_=W["conv_b"], inc=cx.ld)
        p.op("q", "dma_start", out=dtpb[:], in_=W["dtp_b"], inc=cx.ld)
        p.op("q", "dma_start", out=D_sb[:], in_=W["Dv"], inc=cx.ld)
        p.op("g", "iota", ridx32[:], pattern=[[1, 1]], base=0,
             channel_multiplier=1, allow_small_or_imprecise_dtypes=True,
             inc=cx.sG)
        p.wait("g", cx.sG)
        for i in range(32):
            p.op("g", "tensor_scalar", sel[i][:],
                 ridx32[:].broadcast_to([32, 128]), float(i), None,
                 ALU.is_equal, inc=(cx.sG if i == 31 else None))
        prep_v = cx.sV.n
        prep_ld = cx.ld.n
        prep_g = cx.sG.n
    es0.close()
    cx.barrier()

    es12 = cx.scope()
    xn_sb = cx.sbs(es12, f"m{u}_xn", [128, 8 * T], BF16)
    es1 = cx.scope()
    Rt = cx.sbs(es1, f"m{u}_Rt", [128, NCH * 512], F32)
    sq = [cx.sbs(es1, f"m{u}_sq{i}", [128, 512], BF16) for i in range(2)]
    scrf = cx.sbs(es1, f"m{u}_scrf", [128, 512], F32)
    mu_sb = cx.sbs(es1, f"m{u}_mu", [1, 512], F32)
    var_sb = cx.sbs(es1, f"m{u}_var", [1, 512], F32)
    mu2_sb = cx.sbs(es1, f"m{u}_mu2", [1, 512], F32)
    for tt in range(NTT):
        tsl = slice(tt * 512, (tt + 1) * 512)
        p.wait("q", cx.sV)
        p.op("q", "dma_start",
             out=Rt[:].rearrange("p (c t) -> p c t", c=NCH),
             in_=r_in_v[:, :, tsl], inc=cx.ld)
        p.wait("t", cx.ld)
        p.wait("s", cx.ld)
        emit_ln_tile(cx, Rt[:], sq, mu_sb[:], var_sb[:], mu2_sb[:])
        emit_xn_tile(cx, Rt[:], None, scrf,
                     dst_fn=lambda c, tsl=tsl: xn_sb[:, c * T + tsl.start:
                                                     c * T + tsl.stop])
    es1.close()

    p.wait("t", cx.sV)
    evac_done = {}
    for oc in range(NCH):
        for tt in range(NTT):
            tsl = slice(tt * 512, (tt + 1) * 512)
            bk = (oc * NTT + tt) % 2
            key = oc * NTT + tt
            if key >= 2:
                p.wait("t", cx.sV, evac_done[key - 2][1])
            for c in range(NCH):
                p.op("t", "matmul", cx.bank(bk),
                     fc1w[:, c * 1024 + oc * 128:c * 1024 + (oc + 1) * 128],
                     xn_sb[:, c * T + tsl.start:c * T + tsl.stop],
                     start=(c == 0), stop=(c == NCH - 1),
                     inc=(cx.sT if c == NCH - 1 else None))
            p.wait("v", cx.sT)
            p.op("v", "tensor_scalar",
                 h_sb[:, oc * T + tsl.start:oc * T + tsl.stop],
                 cx.bank(bk), fc1b[:, oc:oc + 1], None, ALU.add, inc=cx.sV)
            evac_done[key] = ("v", cx.sV.n)
    p2_evac = cx.sV.n
    es12.close()
    es1w.close()
    cx.barrier()

    es3 = cx.scope()
    wsb = [cx.sbs(es3, f"m{u}_w3{i}", [128, 8 * 128], BF16) for i in range(2)]
    xcz = cx.sbs(es3, f"m{u}_xcz", [128, 4099], F32)
    conv_acc = cx.sbs(es3, f"m{u}_cacc", [128, T], F32)
    u_f = cx.sbs(es3, f"m{u}_uf", [128, T], F32)
    u_bf = cx.sbs(es3, f"m{u}_ubf", [128, T], BF16)
    xdbl = cx.sbs(es3, f"m{u}_xdbl", [96, T], F32)
    inpT_v = scr["inpT_bf"].rearrange("(c p) o -> p c o", p=128)
    u_d_v = scr["u_d"].rearrange("(c p) t -> p c t", p=128)
    z_d_v = scr["z_d"].rearrange("(c p) t -> p c t", p=128)

    p.op("g", "memset", xcz[:, 0:3], 0.0, inc=cx.sG)
    xcz_zero = cx.sG.n
    wbuf_free = [0, 0]
    xcz_free = 0
    xcz_free_eng = "none"
    ubf_free = 0
    uf_free = 0
    uf_free_st = 0
    conv_free = 0
    for j in range(2 * NJC):
        jb = j % 2
        p.wait("g", cx.sT, wbuf_free[jb])
        p.op("g", "dma_start",
             out=wsb[jb][:].rearrange("p (c o) -> p c o", c=8),
             in_=inpT_v[:, :, j * 128:(j + 1) * 128], inc=cx.ldg)
        p.wait("t", cx.ldg)
        for tt in range(NTT):
            tsl = slice(tt * 512, (tt + 1) * 512)
            bk = tt % 2
            if tt >= 2:
                p.wait("t", cx.sS, evac_done[tt - 2])
            elif j > 0:
                p.wait("t", cx.sS, evac_done[6 + tt])
            else:
                p.wait("t", cx.sV, p2_evac)
            for c in range(NCH):
                p.op("t", "matmul", cx.bank(bk),
                     wsb[jb][:, c * 128:(c + 1) * 128],
                     h_sb[:, c * T + tsl.start:c * T + tsl.stop],
                     start=(c == 0), stop=(c == NCH - 1),
                     inc=(cx.sT if c == NCH - 1 else None))
            if tt == NTT - 1:
                wbuf_free[jb] = cx.sT.n
            p.wait("s", cx.sT)
            if tt == 0:
                if xcz_free_eng == "v":
                    p.wait("s", cx.sV, xcz_free)
                elif xcz_free_eng == "q":
                    p.wait("s", cx.stq, xcz_free)
            p.op("s", "activation", out=xcz[:, 3 + tsl.start:3 + tsl.stop],
                 in_=cx.bank(bk), func=AF.Copy, inc=cx.sS)
            evac_done[tt] = cx.sS.n
        if j < NJC:
            p.wait("v", cx.sS)
            p.wait("v", cx.sG, xcz_zero)
            p.wait("v", cx.ld, prep_ld)
            p.wait("v", cx.sS, conv_free)
            p.op("v", "tensor_scalar",
                 conv_acc[:], xcz[:, 0:T], convw[:, 4 * j:4 * j + 1], None,
                 ALU.mult)
            for w in range(1, 4):
                p.op("v", "scalar_tensor_tensor", out=conv_acc[:],
                     in0=xcz[:, w:w + T],
                     scalar=convw[:, 4 * j + w:4 * j + w + 1],
                     in1=conv_acc[:], op0=ALU.mult, op1=ALU.add,
                     inc=(cx.sV if w == 3 else None))
            xcz_free, xcz_free_eng = cx.sV.n, "v"
            p.wait("s", cx.sV)
            p.wait("s", cx.stq, uf_free_st)
            p.wait("s", cx.sV, uf_free)
            p.op("s", "activation", out=u_f[:], in_=conv_acc[:], func=AF.Silu,
                 bias=convb[:, j:j + 1], inc=cx.sS)
            conv_free = cx.sS.n
            p.wait("q", cx.sS)
            p.op("q", "dma_start", out=u_d_v[:, j, :], in_=u_f[:], inc=cx.stq)
            uf_free_st = cx.stq.n
            p.wait("v", cx.sS)
            p.wait("v", cx.sT, ubf_free)
            p.op("v", "tensor_copy", u_bf[:], u_f[:], inc=cx.sV)
            uf_free = cx.sV.n
            p.wait("t", cx.sV)
            for tt in range(NTT):
                tsl = slice(tt * 512, (tt + 1) * 512)
                p.op("t", "matmul", cx.bank(2, 96),
                     xpw[:, j * 96:(j + 1) * 96], u_bf[:, tsl],
                     start=True, stop=True, inc=cx.sT)
                if j == 0:
                    p.wait("s", cx.sT)
                    p.op("s", "activation", out=xdbl[:, tsl],
                         in_=cx.bank(2, 96), func=AF.Copy, inc=cx.sS)
                    p.wait("t", cx.sS)
                else:
                    p.wait("v", cx.sT)
                    p.op("v", "tensor_add", xdbl[:, tsl], xdbl[:, tsl],
                         cx.bank(2, 96), inc=cx.sV)
                    p.wait("t", cx.sV)
            ubf_free = cx.sT.n
        else:
            p.wait("q", cx.sS)
            p.op("q", "dma_start", out=z_d_v[:, j - NJC, :],
                 in_=xcz[:, 3:3 + T], inc=cx.stq)
            xcz_free, xcz_free_eng = cx.stq.n, "q"
    p.wait("v", cx.sV)
    p.op("v", "tensor_copy", dtlow_bf[:], xdbl[0:64, :])
    p.op("v", "tensor_copy", BC_bf[:], xdbl[64:96, :], inc=cx.sV)
    es3.close()
    es_h.close()
    cx.barrier()

    TH = T // 4
    NTH = TH // 512
    es4 = cx.scope()
    u2 = [cx.sbs(es4, f"m{u}_u2{i}", [128, TH], F32) for i in range(2)]
    dt2 = [cx.sbs(es4, f"m{u}_dt2{i}", [128, TH], F32) for i in range(2)]
    w2 = [cx.sbs(es4, f"m{u}_w2{i}", [128, TH], F32) for i in range(2)]
    y2 = [cx.sbs(es4, f"m{u}_y2{i}", [128, TH], F32) for i in range(2)]
    dA2 = [[cx.sbs(es4, f"m{u}_dA{i}_{j}", [128, TH], F32) for j in range(2)]
           for i in range(2)]
    b2_ = [[cx.sbs(es4, f"m{u}_b2{i}_{j}", [128, TH], F32) for j in range(2)]
           for i in range(2)]
    H2 = [cx.sbs(es4, f"m{u}_H{i}", [128, TH], F32) for i in range(2)]
    CH2 = [cx.sbs(es4, f"m{u}_CH{i}", [128, TH], F32) for i in range(2)]
    Brep = [cx.sbs(es4, f"m{u}_Brep{j}", [128, TH], F32) for j in range(2)]
    Crep = [cx.sbs(es4, f"m{u}_Crep{j}", [128, TH], F32) for j in range(2)]
    dtraw = cx.sbs(es4, f"m{u}_dtraw", [128, TH], F32)
    carry = cx.sbs(es4, f"m{u}_carry", [128, NJC * DS], F32)
    y_d_v = scr["y_d"].rearrange("(c p) t -> p c t", p=128)

    p.op("g", "memset", carry[:], 0.0, inc=cx.sG)
    carry_zero = cx.sG.n
    scan_dA = [[0, 0], [0, 0]]
    scan_b = [[0, 0], [0, 0]]
    rep_b_user = [0, 0]
    rep_c_user_g = [0, 0]
    rep_c_user_v = [0, 0]
    ch_done = [0, 0]
    yadd_done = [("g", 0), ("v", 0)]
    dt_done = [0, 0]
    b_used = [0, 0]
    ydma_done = [0, 0]
    for half in range(4):
        hsl = slice(half * TH, (half + 1) * TH)
        for g in range(NJC // 2):
            for b2 in range(2):
                j = 2 * g + b2
                ye, yv = yadd_done[b2]
                p.wait("q", cx.sG if ye == "g" else cx.sV, yv)
                p.op("q", "dma_start", out=u2[b2][:],
                     in_=u_d_v[:, j, hsl], inc=cx.ld)
                p.wait("s", cx.sV, max(dt_done))
                for tt in range(NTH):
                    tsl = slice(half * TH + tt * 512, half * TH + (tt + 1) * 512)
                    p.wait("t", cx.sS)
                    p.wait("t", cx.sV)
                    p.op("t", "matmul", cx.bank(3),
                         dtpw[:, j * 128:(j + 1) * 128],
                         dtlow_bf[:, tsl], start=True, stop=True, inc=cx.sT)
                    p.wait("v", cx.sT)
                    p.op("v", "tensor_scalar",
                         dtraw[:, tt * 512:(tt + 1) * 512],
                         cx.bank(3), dtpb[:, j:j + 1], None, ALU.add,
                         inc=cx.sV)
                scr_sp = dA2[b2][0]
                p.wait("s", cx.sV, scan_dA[b2][0])
                p.wait("s", cx.sV)
                p.op("s", "activation", out=scr_sp[:], in_=dtraw[:],
                     func=AF.Abs)
                p.op("s", "activation", out=scr_sp[:], in_=scr_sp[:],
                     func=AF.Exp, scale=-1.0)
                p.op("s", "activation", out=scr_sp[:], in_=scr_sp[:],
                     func=AF.Ln, bias=1.0, inc=cx.sS)
                p.wait("v", cx.sS)
                p.wait("v", cx.sV, scan_dA[b2][0])
                p.op("v", "tensor_scalar_max", dt2[b2][:], dtraw[:], 0.0)
                p.op("v", "tensor_add", dt2[b2][:], dt2[b2][:], scr_sp[:],
                     inc=cx.sV)
                p.wait("v", cx.ld)
                p.wait("v", cx.sG, b_used[b2])
                p.op("v", "tensor_mul", w2[b2][:], dt2[b2][:], u2[b2][:],
                     inc=cx.sV)
                dt_done[b2] = cx.sV.n
                ye, yv = yadd_done[b2]
                if ye == "v":
                    p.wait("g", cx.sV, yv)
                p.wait("g", cx.stq, ydma_done[b2])
                p.op("g", "memset", y2[b2][:], 0.0, inc=cx.sG)
            grp_ready_v = cx.sV.n
            grp_ready_g = cx.sG.n
            for n in range(DS):
                pn = n % 2
                p.wait("t", cx.sV, grp_ready_v)
                p.wait("t", cx.sG, rep_b_user[pn])
                p.wait("t", cx.sG, rep_c_user_g[pn])
                p.wait("t", cx.sV, rep_c_user_v[pn])
                for tt in range(NTH):
                    tsl = slice(half * TH + tt * 512,
                                half * TH + (tt + 1) * 512)
                    p.op("t", "matmul", cx.bank(4 * pn + tt), sel[n][:],
                         BC_bf[:, tsl], start=True, stop=True)
                    p.op("t", "matmul", cx.bank(4 * pn + 2 + tt),
                         sel[16 + n][:], BC_bf[:, tsl], start=True, stop=True,
                         inc=(cx.sT if tt == NTH - 1 else None))
                p.wait("s", cx.sT)
                p.op("s", "activation", out=Brep[pn][:],
                     in_=cx.psa[:, 4 * pn * 512:(4 * pn + 2) * 512],
                     func=AF.Copy)
                p.op("s", "activation", out=Crep[pn][:],
                     in_=cx.psa[:, (4 * pn + 2) * 512:(4 * pn + 4) * 512],
                     func=AF.Copy, inc=cx.sS)
                rep_done = cx.sS.n
                for b2 in range(2):
                    j = 2 * g + b2
                    cidx = j * DS + n
                    p.wait("s", cx.sV, scan_dA[b2][pn])
                    p.op("s", "activation", out=dA2[b2][pn][:],
                         in_=dt2[b2][:], func=AF.Exp,
                         scale=A_neg[:, cidx:cidx + 1], inc=cx.sS)
                    dA_done = cx.sS.n
                    p.wait("g", cx.sS, rep_done)
                    p.wait("g", cx.sV, scan_b[b2][pn])
                    p.op("g", "tensor_mul", b2_[b2][pn][:], w2[b2][:],
                         Brep[pn][:], inc=cx.sG)
                    rep_b_user[pn] = cx.sG.n
                    b_used[b2] = cx.sG.n
                    b_ready = cx.sG.n
                    p.wait("v", cx.sS, dA_done)
                    p.wait("v", cx.sG, b_ready)
                    if b2 == 0:
                        p.wait("v", cx.sG, ch_done[0])
                    else:
                        pass
                    p.wait("v", cx.sG, carry_zero)
                    p.op("v", "tensor_tensor_scan", H2[b2][:],
                         dA2[b2][pn][:], b2_[b2][pn][:],
                         carry[:, cidx:cidx + 1], ALU.mult, ALU.add,
                         inc=cx.sV)
                    scan_dA[b2][pn] = cx.sV.n
                    scan_b[b2][pn] = cx.sV.n
                    p.wait("v", cx.sV)
                    p.op("v", "tensor_copy", carry[:, cidx:cidx + 1],
                         H2[b2][:, TH - 1:TH], inc=cx.sV)
                    scan_done_v = cx.sV.n
                    if b2 == 0:
                        p.wait("g", cx.sS, rep_done)
                        p.wait("g", cx.sV, scan_done_v)
                        p.op("g", "tensor_mul", CH2[0][:], H2[0][:],
                             Crep[pn][:], inc=cx.sG)
                        ch_done[0] = cx.sG.n
                        rep_c_user_g[pn] = cx.sG.n
                        p.op("g", "tensor_add", y2[0][:], y2[0][:],
                             CH2[0][:], inc=cx.sG)
                        yadd_done[0] = ("g", cx.sG.n)
                    else:
                        p.wait("v", cx.sS, rep_done)
                        p.op("v", "tensor_mul", CH2[1][:], H2[1][:],
                             Crep[pn][:], inc=cx.sV)
                        rep_c_user_v[pn] = cx.sV.n
                        p.op("v", "tensor_add", y2[1][:], y2[1][:],
                             CH2[1][:], inc=cx.sV)
                        yadd_done[1] = ("v", cx.sV.n)
                    if n == 0:
                        p.wait("v", cx.sG, grp_ready_g)
            for b2 in range(2):
                j = 2 * g + b2
                p.wait("v", cx.stq, ydma_done[b2])
                if b2 == 0:
                    p.wait("v", cx.sG, ch_done[0])
                    p.wait("v", cx.sG)
                p.op("v", "scalar_tensor_tensor", out=y2[b2][:],
                     in0=u2[b2][:], scalar=D_sb[:, j:j + 1], in1=y2[b2][:],
                     op0=ALU.mult, op1=ALU.add, inc=cx.sV)
                yadd_done[b2] = ("v", cx.sV.n)
                p.wait("q", cx.sV, cx.sV.n)
                p.op("q", "dma_start", out=y_d_v[:, j, hsl], in_=y2[b2][:],
                     inc=cx.stq)
                ydma_done[b2] = cx.stq.n
    p.wait("q", cx.stq, cx.stq.n)
    es4.close()
    esW.close()
    cx.barrier()

    es5 = cx.scope()
    opw = cx.sbs(es5, f"m{u}_opw", [128, NJC * 1024], BF16)
    fc2w = cx.sbs(es5, f"m{u}_fc2w", [128, 8 * 1024], BF16)
    p.op("q", "dma_start",
         out=opw[:].rearrange("p (c o) -> p c o", c=NJC),
         in_=scr["opT_bf"].rearrange("(c p) o -> p c o", p=128), inc=cx.ld)
    with cx.scope() as esw1:
        wf2 = cx.sbs(esw1, f"m{u}_wf2", [128, 4 * 1024], F32, side="right")
        wv = W["fc2_wT"].rearrange("(c p) o -> p c o", p=128)
        for hf in range(2):
            p.wait("q", cx.sV)
            p.op("q", "dma_start",
                 out=wf2[:].rearrange("p (c o) -> p c o", c=4),
                 in_=wv[:, hf * 4:(hf + 1) * 4, :], inc=cx.ld)
            p.wait("v", cx.ld)
            p.op("v", "tensor_copy", fc2w[:, hf * 4096:(hf + 1) * 4096],
                 wf2[:], inc=cx.sV)
    opw_ld = cx.ld.n
    ybf_t = cx.sbs(es5, f"m{u}_ybft", [128, NJC * 512], BF16)
    zf = [cx.sbs(es5, f"m{u}_zf{i}", [128, 512], F32) for i in range(2)]
    yf = [cx.sbs(es5, f"m{u}_yf{i}", [128, 512], F32) for i in range(2)]
    zs = [cx.sbs(es5, f"m{u}_zs{i}", [128, 512], F32) for i in range(2)]
    o1t = cx.sbs(es5, f"m{u}_o1t", [128, 8 * 512], BF16)
    Rt5 = cx.sbs(es5, f"m{u}_Rt5", [128, 8 * 512], F32)
    Ro5 = cx.sbs(es5, f"m{u}_Ro5", [128, 8 * 512], F32)
    rt5_free = 0
    ro5_st = 0
    zy_free = [0, 0]
    for tt in range(NTT):
        tsl = slice(tt * 512, (tt + 1) * 512)
        p.wait("q", cx.sV, rt5_free)
        p.op("q", "dma_start",
             out=Rt5[:].rearrange("p (c t) -> p c t", c=NCH),
             in_=r_in_v[:, :, tsl], inc=cx.ld)
        rt5_ld = cx.ld.n
        ybf_mm = cx.sT.n
        for j in range(NJC):
            jb = j % 2
            p.wait("q", cx.sV, zy_free[jb])
            p.op("q", "dma_start", out=zf[jb][:], in_=z_d_v[:, j, tsl],
                 inc=cx.ld)
            p.op("q", "dma_start", out=yf[jb][:], in_=y_d_v[:, j, tsl],
                 inc=cx.ld)
            p.wait("s", cx.ld)
            p.wait("s", cx.sV, zy_free[jb])
            p.op("s", "activation", out=zs[jb][:], in_=zf[jb][:],
                 func=AF.Silu, inc=cx.sS)
            p.wait("v", cx.sS)
            if j < 2:
                p.wait("v", cx.sT, ybf_mm)
            p.op("v", "tensor_mul", ybf_t[:, j * 512:(j + 1) * 512],
                 yf[jb][:], zs[jb][:], inc=cx.sV)
            zy_free[jb] = cx.sV.n
        ybf_done = cx.sV.n
        p.wait("t", cx.sV, ybf_done)
        p.wait("t", cx.ld, opw_ld)
        if tt > 0:
            p.wait("t", cx.sS)
            p.wait("t", cx.sV)
        for oc in range(NCH):
            for c in range(NJC):
                p.op("t", "matmul", cx.bank(oc),
                     opw[:, c * 1024 + oc * 128:c * 1024 + (oc + 1) * 128],
                     ybf_t[:, c * 512:(c + 1) * 512],
                     start=(c == 0), stop=(c == NJC - 1),
                     inc=(cx.sT if c == NJC - 1 else None))
        p.wait("s", cx.sT)
        for oc in range(NCH):
            p.op("s", "activation", out=o1t[:, oc * 512:(oc + 1) * 512],
                 in_=cx.bank(oc), func=AF.Copy,
                 inc=(cx.sS if oc == NCH - 1 else None))
        if "o1_d" in scr:
            p.wait("q", cx.sS)
            p.op("q", "dma_start",
                 out=scr["o1_d"].rearrange("(c p) t -> p c t", p=128)[:, :, tsl],
                 in_=o1t[:].rearrange("p (c t) -> p c t", c=NCH), inc=cx.stq)
        p.wait("t", cx.sS)
        fc2_evac = {}
        for oc2 in range(NCH):
            bk = oc2 % 2
            if oc2 >= 2:
                p.wait("t", cx.sV, fc2_evac[oc2 - 2])
            for c in range(NCH):
                p.op("t", "matmul", cx.bank(bk),
                     fc2w[:, c * 1024 + oc2 * 128:c * 1024 + (oc2 + 1) * 128],
                     o1t[:, c * 512:(c + 1) * 512],
                     start=(c == 0), stop=(c == NCH - 1),
                     inc=(cx.sT if c == NCH - 1 else None))
            p.wait("v", cx.sT)
            if oc2 == 0:
                p.wait("v", cx.ld, rt5_ld)
                p.wait("v", cx.stq, ro5_st)
            csl = slice(oc2 * 512, (oc2 + 1) * 512)
            p.op("v", "scalar_tensor_tensor", out=Ro5[:, csl],
                 in0=cx.bank(bk), scalar=fc2b[:, oc2:oc2 + 1],
                 in1=Rt5[:, csl], op0=ALU.add, op1=ALU.add, inc=cx.sV)
            fc2_evac[oc2] = cx.sV.n
        rt5_free = cx.sV.n
        p.wait("q", cx.sV, rt5_free)
        p.op("q", "dma_start", out=r_out_v[:, :, tsl],
             in_=Ro5[:].rearrange("p (c t) -> p c t", c=NCH), inc=cx.stq)
        ro5_st = cx.stq.n
    p.wait("q", cx.stq, cx.stq.n)
    es5.close()


def to_pcol(v):
    import numpy as np
    v = np.asarray(v)
    if v.ndim == 1:
        c = v.shape[0] // 128
        return np.ascontiguousarray(v.reshape(c, 128).T)
    c = v.shape[0] // 128
    k = v.shape[1]
    return np.ascontiguousarray(
        v.reshape(c, 128, k).transpose(1, 0, 2).reshape(128, c * k))


NEXP = 8
DFF = 4096
NFC2 = DFF // 128


def emit_moe(cx, r_in, r_out, W, scr):
    p = cx.p
    u = cx.uid()
    DBG = scr.get("dbg_stage", 5)
    cx.barrier()
    r_in_v = r_in.rearrange("(c p) t -> p c t", p=128)
    r_out_v = r_out.rearrange("(c p) t -> p c t", p=128)

    if not scr.get("moe_prepped"):
        with cx.scope() as esP:
            stage = cx.sbs(esP, f"o{u}_stage", [128, 4096], BF16, side="right")
            for e in range(NEXP):
                dram_to_bf16(cx, W["w1"][e], scr["w1_bf"][e], stage[:])
                dram_to_bf16(cx, W["w2"][e], scr["w2_bf"][e],
                             stage[:, 0:1024])
        scr["moe_prepped"] = True

    esM = cx.scope()
    if "cpack" not in scr:
        gw = cx.sb(f"o{u}_gw", [128, 8 * 8], F32)
        gb = cx.sb(f"o{u}_gb", [8, 2], F32)
        gbias = cx.sb(f"o{u}_gbias", [8, 1], F32)
        b1_sb = cx.sb(f"o{u}_b1", [128, NEXP * 32], F32)
        b2_sb = cx.sb(f"o{u}_b2", [128, NEXP * 8], F32)
        sel8 = [cx.sb(f"o{u}_sel8_{e}", [8, 128], BF16) for e in range(NEXP)]
        ident_f = cx.sb(f"o{u}_identf", [128, 128], F32)
        revidx = cx.sb(f"o{u}_rev", [128, 8], F32)
        ridx8 = cx.sb(f"o{u}_ridx8", [8, 1], F32)
        p.op("q", "dma_start",
             out=gw[:].rearrange("p (c e) -> p c e", c=NCH),
             in_=W["gate_wT"].rearrange("(c p) e -> p c e", p=128), inc=cx.ld)
        p.op("q", "dma_start", out=gb[:], in_=W["gbias"], inc=cx.ld)
        p.op("q", "dma_start", out=b1_sb[:], in_=W["b1"], inc=cx.ld)
        p.op("q", "dma_start", out=b2_sb[:], in_=W["b2"], inc=cx.ld)
        p.wait("v", cx.ld)
        p.op("v", "tensor_add", gbias[:], gb[:, 0:1], gb[:, 1:2], inc=cx.sV)
        p.op("g", "iota", ridx8[:], pattern=[[1, 1]], base=0,
             channel_multiplier=1, allow_small_or_imprecise_dtypes=True,
             inc=cx.sG)
        p.wait("g", cx.sG)
        for e in range(NEXP):
            p.op("g", "tensor_scalar", sel8[e][:],
                 ridx8[:].broadcast_to([8, 128]), float(e), None,
                 ALU.is_equal)
        p.op("g", "memset", ident_f[:], 0.0)
        p.op("g", "affine_select", out=ident_f[:], in_=ident_f[:],
             compare_op=ALU.not_equal, fill=1.0, base=0,
             pattern=[[-1, 128]], channel_multiplier=1)
        p.op("g", "iota", revidx[:], pattern=[[1, 8]], base=0,
             channel_multiplier=0, allow_small_or_imprecise_dtypes=True,
             inc=cx.sG)
        p.wait("v", cx.sG)
        p.op("v", "tensor_scalar", revidx[:], revidx[:], -1.0, 8.0, ALU.mult,
             ALU.add, inc=cx.sV)
        scr["cpack"] = dict(gw=gw, gbias=gbias, b1_sb=b1_sb, b2_sb=b2_sb,
                            sel8=sel8, ident_f=ident_f, revidx=revidx)
    cp = scr["cpack"]
    gw = cp["gw"]; gbias = cp["gbias"]; b1_sb = cp["b1_sb"]
    b2_sb = cp["b2_sb"]; sel8 = cp["sel8"]; ident_f = cp["ident_f"]
    revidx = cp["revidx"]
    prep_g = cx.sG.n
    prep_v = cx.sV.n

    Rt = cx.sbs(esM, f"o{u}_Rt", [128, NCH * 512], F32)
    Ro = cx.sbs(esM, f"o{u}_Ro", [128, NCH * 512], F32)
    xnb = cx.sbs(esM, f"o{u}_xnb", [128, NCH * 512], BF16)
    xnf = cx.sbs(esM, f"o{u}_xnf", [128, NCH * 512], F32)
    sq = [cx.sbs(esM, f"o{u}_sq{i}", [128, 512], BF16) for i in range(2)]
    scrf = cx.sbs(esM, f"o{u}_scrf", [128, 512], F32)
    mu_sb = cx.sbs(esM, f"o{u}_mu", [1, 512], F32)
    var_sb = cx.sbs(esM, f"o{u}_var", [1, 512], F32)
    mu2_sb = cx.sbs(esM, f"o{u}_mu2", [1, 512], F32)
    g_sb = cx.sbs(esM, f"o{u}_gsb", [8, 512], F32)
    gts = cx.sbs(esM, f"o{u}_gts", [128, 4 * 8], F32)
    mx = cx.sbs(esM, f"o{u}_mx", [128, 1], F32)
    eq = cx.sbs(esM, f"o{u}_eq", [128, 8], F32)
    msk = cx.sbs(esM, f"o{u}_msk", [128, 4 * 8], BF16)
    mskT = cx.sbs(esM, f"o{u}_mskT", [8, 512], BF16)
    mrep = cx.sbs(esM, f"o{u}_mrep", [128, NEXP * 512], F32)
    he = cx.sbs(esM, f"o{u}_he", [128, NFC2 * 512], BF16)
    acc = cx.sbs(esM, f"o{u}_acc", [128, NCH * 512], F32)
    w1s = [cx.sbs(esM, f"o{u}_w1s{i}", [128, 8 * 512], BF16) for i in range(2)]
    w2s = [cx.sbs(esM, f"o{u}_w2s{i}", [128, 4 * 1024], BF16) for i in range(2)]

    w1v = [scr["w1_bf"][e].rearrange("(c p) f -> p c f", p=128)
           for e in range(NEXP)]
    w2v = [scr["w2_bf"][e].rearrange("(g p) d -> p g d", p=128)
           for e in range(NEXP)]

    rt_free = 0
    ro_st = 0
    w1_free = [0, 0]
    w2_free = [0, 0]
    for tt in range(NTT):
        tsl = slice(tt * 512, (tt + 1) * 512)
        p.wait("q", cx.sV, rt_free)
        p.op("q", "dma_start",
             out=Rt[:].rearrange("p (c t) -> p c t", c=NCH),
             in_=r_in_v[:, :, tsl], inc=cx.ld)
        p.wait("t", cx.ld)
        p.wait("s", cx.ld)
        emit_ln_tile(cx, Rt[:], sq, mu_sb[:], var_sb[:], mu2_sb[:])
        emit_xn_tile(cx, Rt[:], xnf[:], scrf)
        p.wait("v", cx.sV)
        p.op("v", "tensor_copy", xnb[:], xnf[:], inc=cx.sV)
        xn_done = cx.sV.n
        if DBG >= 1:
            p.wait("t", cx.sV, xn_done)
            for c in range(NCH):
                p.op("t", "matmul", cx.bank(4, 8), gw[:, c * 8:(c + 1) * 8],
                     xnf[:, c * 512:(c + 1) * 512], start=(c == 0),
                     stop=(c == NCH - 1), inc=(cx.sT if c == NCH - 1 else None))
            p.wait("v", cx.sT)
            p.wait("v", cx.sV, prep_v)
            p.op("v", "tensor_scalar", g_sb[:], cx.bank(4, 8), gbias[0:8, 0:1],
                 None, ALU.add, inc=cx.sV)
            p.wait("t", cx.sV)
            p.wait("t", cx.sG, prep_g)
            for st in range(4):
                p.op("t", "transpose",
                     cx.psa[0:128, 5 * 512 + st * 8:5 * 512 + (st + 1) * 8],
                     g_sb[:, st * 128:(st + 1) * 128], ident_f[0:8, 0:8],
                     inc=(cx.sT if st == 3 else None))
            p.wait("v", cx.sT)
            for st in range(4):
                p.op("v", "tensor_copy", gts[:, st * 8:(st + 1) * 8],
                     cx.psa[0:128, 5 * 512 + st * 8:5 * 512 + (st + 1) * 8],
                     inc=(cx.sV if st == 3 else None))
            gts_done = cx.sV.n
            if DBG >= 2:
                for st in range(4):
                    ss = slice(st * 8, (st + 1) * 8)
                    p.op("v", "reduce_max", mx[:], gts[:, ss], axis=AX.X, inc=cx.sV)
                    p.wait("v", cx.sV)
                    p.op("v", "tensor_scalar", eq[:], gts[:, ss], mx[:, 0:1], None,
                         ALU.is_ge, inc=cx.sV)
                    p.wait("v", cx.sV)
                    p.op("v", "tensor_mul", eq[:], eq[:], revidx[:], inc=cx.sV)
                    p.wait("v", cx.sV)
                    p.op("v", "reduce_max", mx[:], eq[:], axis=AX.X, inc=cx.sV)
                    p.wait("v", cx.sV)
                    p.op("v", "tensor_scalar", msk[:, ss], eq[:], mx[:, 0:1], None,
                         ALU.is_ge, inc=cx.sV)
                    p.wait("v", cx.sV)
                msk_done = cx.sV.n
            if DBG >= 3:
                p.wait("t", cx.sV, msk_done)
                for st in range(4):
                    p.op("t", "transpose",
                         cx.psa[0:8, 5 * 512:6 * 512].bitcast(BF16)[:, st * 128:
                                                                    (st + 1) * 128],
                         msk[:, st * 8:(st + 1) * 8], cx.ident[:],
                         inc=(cx.sT if st == 3 else None))
                p.wait("s", cx.sT)
                p.op("s", "activation", out=mskT[:],
                     in_=cx.psa[0:8, 5 * 512:6 * 512].bitcast(BF16)[:, 0:512],
                     func=AF.Copy, inc=cx.sS)
                mrep_mm = {}
                for e in range(NEXP):
                    bk = 6 + e % 2
                    p.wait("t", cx.sS)
                    if e >= 2:
                        p.wait("t", cx.sS, mrep_mm[e - 2])
                    p.op("t", "matmul", cx.bank(bk), sel8[e][:], mskT[:],
                         start=True, stop=True, inc=cx.sT)
                    p.wait("s", cx.sT)
                    p.op("s", "activation", out=mrep[:, e * 512:(e + 1) * 512],
                         in_=cx.bank(bk), func=AF.Copy, inc=cx.sS)
                    mrep_mm[e] = cx.sS.n
                mrep_done = cx.sS.n
            if DBG >= 4:
                acc_done = 0
                for e in range(NEXP):
                    he_evac = {}
                    for fg in range(NFC2 // 4):
                        wb = fg % 2
                        p.wait("g", cx.sT, w1_free[wb])
                        p.op("g", "dma_start",
                             out=w1s[wb][:].rearrange("p (c f) -> p c f", c=NCH),
                             in_=w1v[e][:, :, fg * 512:(fg + 1) * 512], inc=cx.ldg)
                        p.wait("t", cx.ldg)
                        for fl in range(4):
                            fc = fg * 4 + fl
                            bk = fc % 2
                            if fc >= 2:
                                p.wait("t", cx.sS, he_evac[fc - 2])
                            else:
                                p.wait("t", cx.sV, acc_done)
                                if not (e == 0 and tt == 0):
                                    p.wait("t", cx.sS)
                            for c in range(NCH):
                                p.op("t", "matmul", cx.bank(bk),
                                     w1s[wb][:, c * 512 + fl * 128:
                                             c * 512 + (fl + 1) * 128],
                                     xnb[:, c * 512:(c + 1) * 512],
                                     start=(c == 0), stop=(c == NCH - 1),
                                     inc=(cx.sT if c == NCH - 1 else None))
                            if fl == 3:
                                w1_free[wb] = cx.sT.n
                            p.wait("s", cx.sT)
                            p.op("s", "activation",
                                 out=he[:, fc * 512:(fc + 1) * 512],
                                 in_=cx.bank(bk), func=AF.Relu,
                                 bias=b1_sb[:, e * 32 + fc:e * 32 + fc + 1],
                                 inc=cx.sS)
                            he_evac[fc] = cx.sS.n
                    all_he = cx.sS.n
                    p.wait("t", cx.sS, all_he)
                    if e > 0:
                        p.wait("t", cx.sV, acc_done)
                    else:
                        p.wait("t", cx.sS, mrep_done)
                    for kg in range(8):
                        wb = kg % 2
                        p.wait("g", cx.sT, w2_free[wb])
                        p.op("g", "dma_start",
                             out=w2s[wb][:].rearrange("p (g o) -> p g o", g=4),
                             in_=w2v[e][:, kg * 4:(kg + 1) * 4, :], inc=cx.ldg)
                        p.wait("t", cx.ldg)
                        for kl in range(4):
                            kf = kg * 4 + kl
                            for oc in range(NCH):
                                p.op("t", "matmul", cx.bank(oc),
                                     w2s[wb][:, kl * 1024 + oc * 128:
                                             kl * 1024 + (oc + 1) * 128],
                                     he[:, kf * 512:(kf + 1) * 512],
                                     start=(kf == 0), stop=(kf == 31),
                                     inc=(cx.sT if kl == 3 and oc == NCH - 1
                                          else None))
                        w2_free[wb] = cx.sT.n
                    p.wait("v", cx.sT)
                    if e == 0:
                        p.wait("v", cx.sS, mrep_done)
                    for oc in range(NCH):
                        csl = slice(oc * 512, (oc + 1) * 512)
                        p.op("v", "scalar_tensor_tensor", out=scrf[:],
                             in0=cx.bank(oc), scalar=b2_sb[:, e * 8 + oc:e * 8 + oc + 1],
                             in1=mrep[:, e * 512:(e + 1) * 512], op0=ALU.add,
                             op1=ALU.mult)
                        if e == 0:
                            p.op("v", "tensor_copy", acc[:, csl], scrf[:],
                                 inc=(cx.sV if oc == NCH - 1 else None))
                        else:
                            p.op("v", "tensor_add", acc[:, csl], acc[:, csl],
                                 scrf[:], inc=(cx.sV if oc == NCH - 1 else None))
                    acc_done = cx.sV.n
        p.wait("v", cx.stq, ro_st)
        if DBG >= 4:
            for oc in range(NCH):
                csl = slice(oc * 512, (oc + 1) * 512)
                p.op("v", "scalar_tensor_tensor", out=Ro[:, csl],
                     in0=acc[:, csl], scalar=0.5, in1=Rt[:, csl],
                     op0=ALU.mult, op1=ALU.add,
                     inc=(cx.sV if oc == NCH - 1 else None))
        else:
            p.op("v", "tensor_copy", Ro[:], Rt[:], inc=cx.sV)
        rt_free = cx.sV.n
        p.wait("q", cx.sV, rt_free)
        p.op("q", "dma_start", out=r_out_v[:, :, tsl],
             in_=Ro[:].rearrange("p (c t) -> p c t", c=NCH), inc=cx.stq)
        ro_st = cx.stq.n
    p.wait("q", cx.stq, cx.stq.n)
    es_close = esM.close()
    cx.barrier()


NH = 16
HD = 64


def emit_attn(cx, r_in, r_out, W, scr):
    p = cx.p
    u = cx.uid()
    cx.barrier()
    r_in_v = r_in.rearrange("(c p) t -> p c t", p=128)
    r_out_v = r_out.rearrange("(c p) t -> p c t", p=128)
    xn_d_v = scr["xn_d"].rearrange("(c p) t -> p c t", p=128)
    q_d_v = scr["q_d"].rearrange("(c p) t -> p c t", p=128)
    k_d_v = scr["k_d"].rearrange("(c p) t -> p c t", p=128)
    st_d_v = scr["st_d"].rearrange("(c p) t -> p c t", p=128)
    st2_d_v = scr["st2_d"].rearrange("(c p) t -> p c t", p=128)
    v_d_v = scr["v_d"].rearrange("(kt p) (h d) -> p kt h d", p=128, h=NH)

    esA = cx.scope()
    wq = cx.sbs(esA, f"a{u}_wq", [128, 8 * 1024], BF16)
    wk = cx.sbs(esA, f"a{u}_wk", [128, 8 * 1024], BF16)
    wv = cx.sbs(esA, f"a{u}_wv", [128, 8 * 1024], BF16)
    wo = cx.sbs(esA, f"a{u}_wo", [128, 8 * 1024], BF16)
    bq = cx.sbs(esA, f"a{u}_bq", [128, 8], F32)
    bk = cx.sbs(esA, f"a{u}_bk", [128, 8], F32)
    bo = cx.sbs(esA, f"a{u}_bo", [128, 8], F32)
    lnw = cx.sbs(esA, f"a{u}_lnw", [128, 8], F32)
    lnb = cx.sbs(esA, f"a{u}_lnb", [128, 8], F32)
    bv_rep = cx.sbs(esA, f"a{u}_bvrep", [128, 1024], F32)
    bv_sb = cx.sbs(esA, f"a{u}_bv", [1, 1024], F32)
    hsel = cx.sbs(esA, f"a{u}_hsel", [128, 2], BF16)
    hrep_f = cx.sbs(esA, f"a{u}_hrepf", [2, 128], F32)
    idf2 = cx.sbs(esA, f"a{u}_idf2", [2, 2], F32)
    for nm, dst in (("bq", bq), ("bk", bk), ("bo", bo), ("lnw", lnw),
                    ("lnb", lnb)):
        p.op("q", "dma_start", out=dst[:], in_=W[nm], inc=cx.ld)
    p.op("q", "dma_start", out=bv_sb[:], in_=W["bv"], inc=cx.ld)
    with cx.scope() as esw:
        wf = cx.sbs(esw, f"a{u}_wf", [128, 8 * 1024], F32, side="right")
        for nm, dst in (("Wq", wq), ("Wk", wk), ("Wv", wv), ("Wo", wo)):
            p.wait("q", cx.sV)
            p.op("q", "dma_start",
                 out=wf[:].rearrange("p (c o) -> p c o", c=8),
                 in_=W[nm].rearrange("(c p) o -> p c o", p=128), inc=cx.ld)
            p.wait("v", cx.ld)
            p.op("v", "tensor_copy", dst[:], wf[:], inc=cx.sV)
    scr_pidx = cx.sbs(esA, f"a{u}_pidx", [128, 2], F32)
    scr_h2 = cx.sbs(esA, f"a{u}_h2", [2, 3 * 128], F32)
    hselF = cx.sbs(esA, f"a{u}_hselF", [128, 2], F32)
    p.op("g", "iota", scr_pidx[:, 0:1], pattern=[[1, 1]], base=0,
         channel_multiplier=1, allow_small_or_imprecise_dtypes=True,
         inc=cx.sG)
    p.wait("g", cx.sG)
    p.op("g", "tensor_scalar", hselF[:, 0:1], scr_pidx[:, 0:1], 64.0, None,
         ALU.is_lt)
    p.op("g", "tensor_scalar", hselF[:, 1:2], scr_pidx[:, 0:1], 64.0, None,
         ALU.is_ge, inc=cx.sG)
    p.wait("g", cx.sG)
    p.op("g", "tensor_copy", hsel[:], hselF[:], inc=cx.sG)
    p.op("g", "iota", scr_h2[:, 0:128], pattern=[[1, 128]], base=0,
         channel_multiplier=0, allow_small_or_imprecise_dtypes=True)
    p.op("g", "iota", scr_h2[:, 128:129], pattern=[[1, 1]], base=0,
         channel_multiplier=64, allow_small_or_imprecise_dtypes=True,
         inc=cx.sG)
    p.wait("g", cx.sG)
    p.op("g", "tensor_scalar", scr_h2[:, 0:128], scr_h2[:, 0:128],
         scr_h2[:, 128:129], None, ALU.subtract, inc=cx.sG)
    p.wait("g", cx.sG)
    p.op("g", "tensor_scalar", scr_h2[:, 128:256], scr_h2[:, 0:128], 0.0,
         None, ALU.is_ge)
    p.op("g", "tensor_scalar", scr_h2[:, 256:384], scr_h2[:, 0:128], 64.0,
         None, ALU.is_lt, inc=cx.sG)
    p.wait("g", cx.sG)
    p.op("g", "tensor_mul", hrep_f[:], scr_h2[:, 128:256],
         scr_h2[:, 256:384], inc=cx.sG)
    p.op("g", "memset", idf2[:], 0.0)
    p.op("g", "affine_select", out=idf2[:], in_=idf2[:],
         compare_op=ALU.not_equal, fill=1.0, base=0, pattern=[[-1, 2]],
         channel_multiplier=1, inc=cx.sG)
    prep_g = cx.sG.n
    p.wait("t", cx.ld)
    p.wait("t", cx.sV)
    for nt in range(2):
        p.op("t", "matmul", cx.bank(nt), cx.ones_fT[:],
             bv_sb[:, nt * 512:(nt + 1) * 512], start=True, stop=True,
             inc=(cx.sT if nt == 1 else None))
    p.wait("s", cx.sT)
    p.op("s", "activation", out=bv_rep[:], in_=cx.psa[:, 0:1024],
         func=AF.Copy, inc=cx.sS)
    bv_done = cx.sS.n

    esQ = cx.scope()
    Rt = cx.sbs(esQ, f"a{u}_Rt", [128, NCH * 512], F32)
    xn = cx.sbs(esQ, f"a{u}_xn", [128, NCH * 512], F32)
    zq = cx.sbs(esQ, f"a{u}_zq", [128, NCH * 512], BF16)
    sq = [cx.sbs(esQ, f"a{u}_sq{i}", [128, 512], BF16) for i in range(2)]
    scrf = cx.sbs(esQ, f"a{u}_scrf", [128, 512], F32)
    scrf2 = cx.sbs(esQ, f"a{u}_scrf2", [128, 512], F32)
    mu_sb = cx.sbs(esQ, f"a{u}_mu", [1, 512], F32)
    var_sb = cx.sbs(esQ, f"a{u}_var", [1, 512], F32)
    mu2_sb = cx.sbs(esQ, f"a{u}_mu2", [1, 512], F32)
    qkob = cx.sbs(esQ, f"a{u}_qkob", [128, 512], BF16)
    s2n = cx.sbs(esQ, f"a{u}_s2n", [2, 512], F32)
    qn_bf = cx.sbs(esQ, f"a{u}_qnbf", [128, 512], BF16)
    vout = cx.sbs(esQ, f"a{u}_vout", [128, 1024], BF16)

    for tt in range(NTT):
        tsl = slice(tt * 512, (tt + 1) * 512)
        p.wait("q", cx.sV)
        p.wait("q", cx.stq, cx.stq.n)
        p.op("q", "dma_start",
             out=Rt[:].rearrange("p (c t) -> p c t", c=NCH),
             in_=r_in_v[:, :, tsl], inc=cx.ld)
        p.wait("t", cx.ld)
        p.wait("s", cx.ld)
        emit_ln_tile(cx, Rt[:], sq, mu_sb[:], var_sb[:], mu2_sb[:])
        emit_xn_tile(cx, Rt[:], xn[:], scrf)
        p.wait("q", cx.sV)
        p.op("q", "dma_start", out=xn_d_v[:, :, tsl],
             in_=xn[:].rearrange("p (c t) -> p c t", c=NCH), inc=cx.stq)
        p.wait("t", cx.sV)
        emit_ln_tile(cx, xn[:], sq, mu_sb[:], var_sb[:], mu2_sb[:])
        p.wait("v", cx.ld)
        for c in range(NCH):
            csl = slice(c * 512, (c + 1) * 512)
            p.op("v", "tensor_sub", scrf[:], xn[:, csl], cx.bank(2))
            p.op("v", "tensor_mul", scrf2[:], scrf[:], cx.bank(3))
            p.op("v", "tensor_scalar", zq[:, csl], scrf2[:],
                 lnw[:, c:c + 1], lnb[:, c:c + 1], ALU.mult, ALU.add,
                 inc=(cx.sV if c == NCH - 1 else None))
        zq_done = cx.sV.n
        for which, wmat, bias, dst_v in (("q", wq, bq, q_d_v),
                                         ("k", wk, bk, k_d_v)):
            for oc in range(NCH):
                p.wait("t", cx.sV, zq_done)
                p.wait("t", cx.stq, cx.stq.n)
                for c in range(NCH):
                    p.op("t", "matmul", cx.bank(4),
                         wmat[:, c * 1024 + oc * 128:c * 1024 + (oc + 1) * 128],
                         zq[:, c * 512:(c + 1) * 512], start=(c == 0),
                         stop=(c == NCH - 1),
                         inc=(cx.sT if c == NCH - 1 else None))
                p.wait("v", cx.sT)
                p.op("v", "tensor_scalar", scrf[:], cx.bank(4),
                     bias[:, oc:oc + 1], None, ALU.add, inc=cx.sV)
                p.wait("s", cx.sV)
                p.op("s", "activation", out=qkob[:], in_=scrf[:],
                     func=AF.Square, inc=cx.sS)
                p.wait("t", cx.sS)
                p.op("t", "matmul", cx.banksl(5, slice(0, 512), 2),
                     hsel[:], qkob[:], start=True, stop=True, inc=cx.sT)
                p.wait("v", cx.sT)
                p.op("v", "tensor_copy", s2n[:], cx.banksl(5, slice(0, 512), 2),
                     inc=cx.sV)
                p.wait("s", cx.sV)
                p.op("s", "activation", out=s2n[:], in_=s2n[:], func=AF.Ln)
                p.op("s", "activation", out=s2n[:], in_=s2n[:], func=AF.Exp,
                     scale=cx.constf(0.5), inc=cx.sS)
                p.wait("v", cx.sS)
                p.op("v", "tensor_scalar_max", s2n[:], s2n[:], 1e-12,
                     inc=cx.sV)
                p.wait("v", cx.sV)
                p.op("v", "reciprocal", s2n[:], s2n[:], inc=cx.sV)
                p.wait("t", cx.sV)
                p.op("t", "matmul", cx.banksl(5, slice(0, 512), 128),
                     hrep_f[:], s2n[:], start=True, stop=True, inc=cx.sT)
                p.wait("v", cx.sT)
                p.op("v", "tensor_mul", qn_bf[:], scrf[:],
                     cx.banksl(5, slice(0, 512), 128), inc=cx.sV)
                p.wait("q", cx.sV)
                p.op("q", "dma_start", out=dst_v[:, oc, tsl], in_=qn_bf[:],
                     inc=cx.stq)
        p.wait("t", cx.sS, bv_done)
        for stile in range(4):
            p.wait("t", cx.stq, cx.stq.n)
            p.wait("t", cx.sV)
            for half in range(2):
                for c in range(NCH):
                    p.op("t", "matmul", cx.bank(6 + half),
                         zq[:, c * 512 + stile * 128:
                            c * 512 + (stile + 1) * 128],
                         wv[:, c * 1024 + half * 512:
                            c * 1024 + (half + 1) * 512],
                         start=(c == 0), stop=(c == NCH - 1),
                         inc=(cx.sT if c == NCH - 1 else None))
            p.wait("v", cx.sT)
            for half in range(2):
                p.op("v", "tensor_add", vout[:, half * 512:(half + 1) * 512],
                     cx.bank(6 + half), bv_rep[:, half * 512:(half + 1) * 512],
                     inc=(cx.sV if half == 1 else None))
            p.wait("q", cx.sV)
            p.op("q", "dma_start",
                 out=scr["v_d"][(tt * 4 + stile) * 128:
                                (tt * 4 + stile + 1) * 128, :],
                 in_=vout[:], inc=cx.stq)
    p.wait("q", cx.stq, cx.stq.n)
    esQ.close()
    cx.barrier()

    esB = cx.scope()
    kT = cx.sbs(esB, f"a{u}_kT", [128, T], BF16)
    stT = cx.sbs(esB, f"a{u}_stT", [128, T], BF16)
    stN = cx.sbs(esB, f"a{u}_stN", [128, T], BF16)
    vwk = [cx.sbs(esB, f"a{u}_vwk{i}", [128, 32 * 65], BF16) for i in range(2)]
    pT = [cx.sbs(esB, f"a{u}_pT0", [128, 32 * 1024], BF16)]
    sqc = cx.sbs(esB, f"a{u}_sqc", [128, T], BF16)
    cbs = [cx.sbs(esB, f"a{u}_cbs{i}", [128, 1], F32) for i in range(2)]
    negc = cx.sbs(esB, f"a{u}_negc", [2, 1], F32)
    negcT = cx.sbs(esB, f"a{u}_negcT", [1, 2], F32)
    rq = cx.sbs(esB, f"a{u}_rq", [1, 512], F32)
    avsb = [cx.sbs(esB, f"a{u}_avsb{i}", [65, 512], F32) for i in range(2)]
    p.op("g", "memset", vwk[0][:].rearrange("p (kt d) -> p kt d", kt=32)[:, :, 64:65], 1.0)
    p.op("g", "memset", vwk[1][:].rearrange("p (kt d) -> p kt d", kt=32)[:, :, 64:65], 1.0, inc=cx.sG)
    vones = cx.sG.n

    for s_ in range(2):
        src_v = q_d_v if s_ == 0 else st_d_v
        out_v = st_d_v if s_ == 0 else st2_d_v
        for hp in range(NH // 2):
            p.wait("q", cx.sT)
            p.wait("q", cx.stq, cx.stq.n)
            p.op("q", "dma_start", out=kT[:],
                 in_=k_d_v[:, hp, :], inc=cx.ld)
            p.op("q", "dma_start", out=stT[:],
                 in_=src_v[:, hp, :], inc=cx.ld)
            p.wait("g", cx.sT)
            p.wait("g", cx.sG, vones)
            for b in range(2):
                p.op("g", "dma_start",
                     out=vwk[b][:].rearrange("p (kt d) -> p kt d",
                                             kt=32)[:, :, 0:64],
                     in_=v_d_v[:, :, 2 * hp + b, :], inc=cx.ldg)
            ld_hp = cx.ld.n
            ldg_hp = cx.ldg.n
            p.wait("s", cx.ld, ld_hp)
            p.op("s", "activation", out=sqc[:], in_=stT[:], func=AF.Square,
                 inc=cx.sS)
            p.wait("t", cx.sS)
            p.wait("t", cx.sV)
            for nt in range(NTT):
                p.op("t", "matmul", cx.psa[0:2, nt * 512:(nt + 1) * 512],
                     hsel[:], sqc[:, nt * 512:(nt + 1) * 512],
                     start=True, stop=True,
                     inc=(cx.sT if nt == NTT - 1 else None))
            p.wait("v", cx.sT)
            p.op("v", "reduce_max", negc[:], cx.psa[0:2, 0:4096], axis=AX.X,
                 inc=cx.sV)
            p.wait("t", cx.sV)
            p.op("t", "matmul", cx.banksl(1, slice(0, 2), 1), negc[:],
                 idf2[:], start=True, stop=True, inc=cx.sT)
            p.wait("v", cx.sT)
            p.op("v", "tensor_copy", negcT[:], cx.banksl(1, slice(0, 2), 1),
                 inc=cx.sV)
            p.wait("v", cx.sV)
            p.op("v", "reduce_max", negcT[:, 0:1], negcT[:], axis=AX.X,
                 inc=cx.sV)
            p.wait("s", cx.sV)
            p.op("s", "activation", out=negcT[:, 0:1], in_=negcT[:, 0:1],
                 func=AF.Ln)
            p.op("s", "activation", out=negcT[:, 0:1], in_=negcT[:, 0:1],
                 func=AF.Exp, scale=cx.constf(0.5), inc=cx.sS)
            p.wait("v", cx.sS)
            p.op("v", "tensor_scalar_mul", negcT[:, 0:1], negcT[:, 0:1],
                 -1.0, inc=cx.sV)
            p.wait("t", cx.sV)
            p.op("t", "matmul", cx.banksl(1, slice(0, 1), 128),
                 cx.ones_fT[:], negcT[:, 0:1], start=True, stop=True,
                 inc=cx.sT)
            p.wait("s", cx.sT)
            p.op("s", "activation", out=cbs[0][:],
                 in_=cx.banksl(1, slice(0, 1), 128), func=AF.Copy,
                 inc=cx.sS)
            cb_done = cx.sS.n
            exp_done = {}
            avb_free = {0: 0, 1: 0}
            avb_free_s = {0: 0, 1: 0}
            for qc in range(NTT):
                qsl = slice(qc * 512, (qc + 1) * 512)
                av0 = 4 if (qc % 2 == 0) else 6
                for kt in range(32):
                    bb = 2 * (kt % 2)
                    if kt >= 2:
                        p.wait("t", cx.sS, exp_done[kt - 2])
                    elif qc == 0:
                        p.wait("t", cx.sS, cb_done)
                    else:
                        p.wait("t", cx.sS, prev_exp[kt - 2])
                    if kt == 0:
                        p.wait("t", cx.sV, avb_free[qc % 2])
                        p.wait("t", cx.sS, avb_free_s[qc % 2])
                    for b in range(2):
                        p.op("t", "matmul", cx.bank(bb + b),
                             kT[b * 64:(b + 1) * 64,
                                kt * 128:(kt + 1) * 128],
                             stT[b * 64:(b + 1) * 64, qsl],
                             start=True, stop=True,
                             tile_position=(64 * b, 0),
                             inc=(cx.sT if b == 1 else None))
                    p.wait("s", cx.sT)
                    p.op("s", "activation",
                         out=pT[0][:, kt * 1024:(kt + 1) * 1024],
                         in_=cx.psa[:, bb * 512:(bb + 2) * 512],
                         func=AF.Exp, bias=cbs[0][:, 0:1], inc=cx.sS)
                    exp_done[kt] = cx.sS.n
                    p.wait("t", cx.sS, exp_done[kt])
                    for b in range(2):
                        p.op("t", "matmul", cx.bank(av0 + b, 65),
                             vwk[b][:, kt * 65:(kt + 1) * 65],
                             pT[0][:, kt * 1024 + b * 512:
                                    kt * 1024 + (b + 1) * 512],
                             start=(kt == 0), stop=(kt == 31),
                             inc=(cx.sT if kt == 31 and b == 1 else None))
                prev_exp = {-2: exp_done.get(30, 0), -1: exp_done.get(31, 0)}
                for b in range(2):
                    p.wait("s", cx.sT)
                    p.op("s", "activation", out=avsb[b][:],
                         in_=cx.psa[0:65, (av0 + b) * 512:(av0 + b) * 512 + 512],
                         func=AF.Copy, inc=cx.sS)
                    p.wait("v", cx.sS)
                    p.op("v", "reciprocal", rq[:], avsb[b][64:65, :],
                         inc=cx.sV)
                    p.wait("t", cx.sV)
                    p.op("t", "matmul",
                         cx.banksl(av0 + b, slice(0, 512), 64),
                         cx.ones_fT[:, 0:64], rq[:], start=True, stop=True,
                         inc=cx.sT)
                    p.wait("v", cx.sT)
                    if qc <= 1:
                        p.wait("v", cx.stq, cx.stq.n)
                    p.op("v", "tensor_mul", stN[b * 64:(b + 1) * 64, qsl],
                         avsb[b][0:64, :],
                         cx.banksl(av0 + b, slice(0, 512), 64), inc=cx.sV)
                avb_free[qc % 2] = cx.sV.n
                avb_free_s[qc % 2] = cx.sS.n
            p.wait("q", cx.sV, max(avb_free.values()))
            p.op("q", "dma_start", out=out_v[:, hp, :], in_=stN[:],
                 inc=cx.stq)
    p.wait("q", cx.stq, cx.stq.n)
    esB.close()
    cx.barrier()

    esC = cx.scope()
    Rt2 = cx.sbs(esC, f"a{u}_Rt2", [128, NCH * 512], F32)
    xn2 = cx.sbs(esC, f"a{u}_xn2", [128, NCH * 512], F32)
    st2 = cx.sbs(esC, f"a{u}_st2", [128, NCH * 512], BF16)
    Ro = cx.sbs(esC, f"a{u}_Ro", [128, NCH * 512], F32)
    scrg = cx.sbs(esC, f"a{u}_scrg", [128, 512], F32)
    ro_st = 0
    for tt in range(NTT):
        tsl = slice(tt * 512, (tt + 1) * 512)
        p.wait("q", cx.sV)
        p.op("q", "dma_start",
             out=Rt2[:].rearrange("p (c t) -> p c t", c=NCH),
             in_=r_in_v[:, :, tsl], inc=cx.ld)
        p.op("q", "dma_start",
             out=xn2[:].rearrange("p (c t) -> p c t", c=NCH),
             in_=xn_d_v[:, :, tsl], inc=cx.ld)
        p.op("q", "dma_start",
             out=st2[:].rearrange("p (c t) -> p c t", c=NCH),
             in_=st2_d_v[:, :, tsl], inc=cx.ld)
        ld_t = cx.ld.n
        p.wait("t", cx.ld, ld_t)
        fc_evac = {}
        for oc in range(NCH):
            bk = oc % 2
            if oc >= 2:
                p.wait("t", cx.sV, fc_evac[oc - 2])
            elif tt > 0:
                p.wait("t", cx.sV, fc_evac_prev[NCH - 2 + oc])
            for c in range(NCH):
                p.op("t", "matmul", cx.bank(bk),
                     wo[:, c * 1024 + oc * 128:c * 1024 + (oc + 1) * 128],
                     st2[:, c * 512:(c + 1) * 512],
                     start=(c == 0), stop=(c == NCH - 1),
                     inc=(cx.sT if c == NCH - 1 else None))
            p.wait("v", cx.sT)
            if oc == 0:
                p.wait("v", cx.stq, ro_st)
            csl = slice(oc * 512, (oc + 1) * 512)
            p.op("v", "scalar_tensor_tensor", out=scrg[:], in0=cx.bank(bk),
                 scalar=bo[:, oc:oc + 1], in1=xn2[:, csl], op0=ALU.add,
                 op1=ALU.add, inc=cx.sV)
            p.wait("v", cx.sV)
            p.op("v", "tensor_add", Ro[:, csl], scrg[:], Rt2[:, csl],
                 inc=cx.sV)
            fc_evac[oc] = cx.sV.n
        fc_evac_prev = fc_evac
        p.wait("q", cx.sV)
        p.op("q", "dma_start", out=r_out_v[:, :, tsl],
             in_=Ro[:].rearrange("p (c t) -> p c t", c=NCH), inc=cx.stq)
        ro_st = cx.stq.n
    p.wait("q", cx.stq, cx.stq.n)
    esC.close()
    esA.close()
    cx.barrier()


from contextlib import ExitStack
from concourse.bass_utils import run_bass_kernel_spmd

NCORES = 8


def _build_ffn0():
    nc = bass.Bass("TRN2", target_bir_lowering=False)
    r_in = nc.dram_tensor("r_in", [1024, 4096], F32, kind="ExternalInput").ap()
    wgT_i = nc.dram_tensor("wgT_i", [1024, 3072], I32, kind="ExternalInput").ap()
    wuT_i = nc.dram_tensor("wuT_i", [1024, 3072], I32, kind="ExternalInput").ap()
    wdT_i = nc.dram_tensor("wdT_i", [3072, 1024], I32, kind="ExternalInput").ap()
    r_out = nc.dram_tensor("r_out", [1024, 4096], F32, kind="ExternalOutput").ap()
    wgT = nc.dram_tensor("wgT", [1024, 3072], BF16, kind="Internal").ap()
    wuT = nc.dram_tensor("wuT", [1024, 3072], BF16, kind="Internal").ap()
    wdT = nc.dram_tensor("wdT", [3072, 1024], BF16, kind="Internal").ap()
    es = ExitStack()
    cx = Ctx(nc, es)
    build_consts(cx)
    prep_ffn0_weights(cx, wgT_i, wuT_i, wdT_i, wgT, wuT, wdT)
    emit_ffn0(cx, r_in, r_out, wgT, wuT, wdT)
    cx.finalize()
    with nc.Block() as block:
        cx.p.replay(block)
    es.close()
    return nc


def _build_mamba():
    nc = bass.Bass("TRN2", target_bir_lowering=False)
    r_in = nc.dram_tensor("r_in", [1024, 4096], F32, kind="ExternalInput").ap()
    r_out = nc.dram_tensor("r_out", [1024, 4096], F32, kind="ExternalOutput").ap()
    Wsh = dict(fc1_wT=[1024, 1024], fc1_b=[128, 8], inpT=[1024, 4096],
               conv_w=[128, 64], conv_b=[128, 16], xpT=[2048, 96],
               dtpT=[64, 2048], dtp_b=[128, 16], A_log=[128, 256],
               Dv=[128, 16], opT=[2048, 1024], fc2_wT=[1024, 1024],
               fc2_b=[128, 8])
    W = {k: nc.dram_tensor(k, s, F32, kind="ExternalInput").ap()
         for k, s in Wsh.items()}
    scr = {
        "inpT_bf": nc.dram_tensor("inpT_bf", [1024, 4096], BF16, kind="Internal").ap(),
        "opT_bf": nc.dram_tensor("opT_bf", [2048, 1024], BF16, kind="Internal").ap(),
        "u_d": nc.dram_tensor("u_d", [2048, 4096], F32, kind="Internal").ap(),
        "z_d": nc.dram_tensor("z_d", [2048, 4096], F32, kind="Internal").ap(),
        "y_d": nc.dram_tensor("y_d", [2048, 4096], F32, kind="Internal").ap(),
    }
    es = ExitStack()
    cx = Ctx(nc, es)
    build_consts(cx)
    emit_mamba(cx, r_in, r_out, W, scr)
    cx.finalize()
    with nc.Block() as block:
        cx.p.replay(block)
    es.close()
    return nc


def _build_moe():
    nc = bass.Bass("TRN2", target_bir_lowering=False)
    r_in = nc.dram_tensor("r_in", [1024, 4096], F32, kind="ExternalInput").ap()
    r_out = nc.dram_tensor("r_out", [1024, 4096], F32, kind="ExternalOutput").ap()
    W = {
        "gate_wT": nc.dram_tensor("gate_wT", [1024, 8], F32, kind="ExternalInput").ap(),
        "gbias": nc.dram_tensor("gbias", [8, 2], F32, kind="ExternalInput").ap(),
        "b1": nc.dram_tensor("b1", [128, 256], F32, kind="ExternalInput").ap(),
        "b2": nc.dram_tensor("b2", [128, 64], F32, kind="ExternalInput").ap(),
        "w1": nc.dram_tensor("w1", [8, 1024, 4096], F32, kind="ExternalInput").ap(),
        "w2": nc.dram_tensor("w2", [8, 4096, 1024], F32, kind="ExternalInput").ap(),
    }
    scr = {
        "w1_bf": nc.dram_tensor("w1_bf", [8, 1024, 4096], BF16, kind="Internal").ap(),
        "w2_bf": nc.dram_tensor("w2_bf", [8, 4096, 1024], BF16, kind="Internal").ap(),
    }
    es = ExitStack()
    cx = Ctx(nc, es)
    build_consts(cx)
    emit_moe(cx, r_in, r_out, W, scr)
    cx.finalize()
    with nc.Block() as block:
        cx.p.replay(block)
    es.close()
    return nc


def _build_attn():
    nc = bass.Bass("TRN2", target_bir_lowering=False)
    r_in = nc.dram_tensor("r_in", [1024, 4096], F32, kind="ExternalInput").ap()
    r_out = nc.dram_tensor("r_out", [1024, 4096], F32, kind="ExternalOutput").ap()
    W = {}
    for nm in ("Wq", "Wk", "Wv", "Wo"):
        W[nm] = nc.dram_tensor(nm, [1024, 1024], F32, kind="ExternalInput").ap()
    for nm in ("bq", "bk", "bo", "lnw", "lnb"):
        W[nm] = nc.dram_tensor(nm, [128, 8], F32, kind="ExternalInput").ap()
    W["bv"] = nc.dram_tensor("bv", [1, 1024], F32, kind="ExternalInput").ap()
    scr = {
        "xn_d": nc.dram_tensor("xn_d", [1024, 4096], F32, kind="Internal").ap(),
        "q_d": nc.dram_tensor("q_d", [1024, 4096], BF16, kind="Internal").ap(),
        "k_d": nc.dram_tensor("k_d", [1024, 4096], BF16, kind="Internal").ap(),
        "st_d": nc.dram_tensor("st_d", [1024, 4096], BF16, kind="Internal").ap(),
        "st2_d": nc.dram_tensor("st2_d", [1024, 4096], BF16, kind="Internal").ap(),
        "v_d": nc.dram_tensor("v_d", [4096, 1024], BF16, kind="Internal").ap(),
    }
    es = ExitStack()
    cx = Ctx(nc, es)
    build_consts(cx)
    emit_attn(cx, r_in, r_out, W, scr)
    cx.finalize()
    with nc.Block() as block:
        cx.p.replay(block)
    es.close()
    return nc


def kernel_unfused(**inputs):
    g = inputs
    ct = np.ascontiguousarray
    r = [ct(g["L0_attn_x"][i].T) for i in range(NCORES)]

    ffn0_w = {
        "wgT_i": ct(g["L0_ffn_w_gate"].T),
        "wuT_i": ct(g["L0_ffn_w_up"].T),
        "wdT_i": ct(g["L0_ffn_w_down"].T),
    }
    mamba_w = {
        "fc1_wT": ct(np.asarray(g["L0_attn_fc1_w"]).T),
        "fc1_b": to_pcol(g["L0_attn_fc1_b"]),
        "inpT": ct(np.asarray(g["L0_attn_in_proj_w"]).T),
        "conv_w": to_pcol(np.asarray(g["L0_attn_conv_w"])),
        "conv_b": to_pcol(g["L0_attn_conv_b"]),
        "xpT": ct(np.asarray(g["L0_attn_x_proj_w"]).T),
        "dtpT": ct(np.asarray(g["L0_attn_dt_proj_w"]).T),
        "dtp_b": to_pcol(g["L0_attn_dt_proj_b"]),
        "A_log": to_pcol(np.asarray(g["L0_attn_A_log"])),
        "Dv": to_pcol(g["L0_attn_D"]),
        "opT": ct(np.asarray(g["L0_attn_out_proj_w"]).T),
        "fc2_wT": ct(np.asarray(g["L0_attn_fc2_w"]).T),
        "fc2_b": to_pcol(g["L0_attn_fc2_b"]),
    }
    moe_w = {
        "gate_wT": ct(np.asarray(g["L1_ffn_gate_w"]).T),
        "gbias": ct(np.stack([np.asarray(g["L1_ffn_gate_b"]),
                              np.asarray(g["L1_ffn_expert_bias"])], axis=1)),
        "b1": np.concatenate([to_pcol(np.asarray(g["L1_ffn_b1"])[e])
                              for e in range(8)], axis=1),
        "b2": np.concatenate([to_pcol(np.asarray(g["L1_ffn_b2"])[e])
                              for e in range(8)], axis=1),
        "w1": ct(np.asarray(g["L1_ffn_w1"])),
        "w2": ct(np.asarray(g["L1_ffn_w2"])),
    }
    attn_w = {
        "Wq": ct(np.asarray(g["L1_attn_Wq"])),
        "Wk": ct(np.asarray(g["L1_attn_Wk"])),
        "Wv": ct(np.asarray(g["L1_attn_Wv"])),
        "Wo": ct(np.asarray(g["L1_attn_Wo"])),
        "bq": to_pcol(g["L1_attn_bq"]),
        "bk": to_pcol(g["L1_attn_bk"]),
        "bo": to_pcol(g["L1_attn_bo"]),
        "lnw": to_pcol(g["L1_attn_ln_w"]),
        "lnb": to_pcol(g["L1_attn_ln_b"]),
        "bv": ct(np.asarray(g["L1_attn_bv"]).reshape(1, 1024)),
    }

    nc_ffn = _build_ffn0()
    nc_mamba = _build_mamba()
    nc_moe = _build_moe()
    nc_attn = _build_attn()

    def launch(nc, wdict):
        nonlocal r
        in_maps = [dict(wdict, r_in=r[i]) for i in range(NCORES)]
        res = run_bass_kernel_spmd(nc, in_maps, core_ids=list(range(NCORES)))
        r = [res.results[i]["r_out"] for i in range(NCORES)]

    launch(nc_ffn, ffn0_w)
    launch(nc_mamba, mamba_w)
    launch(nc_ffn, ffn0_w)
    launch(nc_moe, moe_w)
    launch(nc_attn, attn_w)
    launch(nc_moe, moe_w)

    out = np.stack([r[i].T for i in range(NCORES)], axis=0)
    return out.astype(np.float32)


def _build_fused():
    nc = bass.Bass("TRN2", target_bir_lowering=False)
    x_in = nc.dram_tensor("x_in", [1024, 4096], F32, kind="ExternalInput").ap()
    r_fin = nc.dram_tensor("r_fin", [1024, 4096], F32,
                           kind="ExternalOutput").ap()
    rbuf = [nc.dram_tensor(f"rb{i}", [1024, 4096], F32, kind="Internal").ap()
            for i in range(5)]
    wgT_i = nc.dram_tensor("wgT_i", [1024, 3072], I32, kind="ExternalInput").ap()
    wuT_i = nc.dram_tensor("wuT_i", [1024, 3072], I32, kind="ExternalInput").ap()
    wdT_i = nc.dram_tensor("wdT_i", [3072, 1024], I32, kind="ExternalInput").ap()
    wgT = nc.dram_tensor("wgT", [1024, 3072], BF16, kind="Internal").ap()
    wuT = nc.dram_tensor("wuT", [1024, 3072], BF16, kind="Internal").ap()
    wdT = nc.dram_tensor("wdT", [3072, 1024], BF16, kind="Internal").ap()
    Wm_sh = dict(fc1_wT=[1024, 1024], fc1_b=[128, 8], inpT=[1024, 4096],
                 conv_w=[128, 64], conv_b=[128, 16], xpT=[2048, 96],
                 dtpT=[64, 2048], dtp_b=[128, 16], A_log=[128, 256],
                 Dv=[128, 16], opT=[2048, 1024], fc2_wT=[1024, 1024],
                 fc2_b=[128, 8])
    Wm = {k: nc.dram_tensor("m_" + k, s, F32, kind="ExternalInput").ap()
          for k, s in Wm_sh.items()}
    scrm = {
        "inpT_bf": nc.dram_tensor("inpT_bf", [1024, 4096], BF16, kind="Internal").ap(),
        "opT_bf": nc.dram_tensor("opT_bf", [2048, 1024], BF16, kind="Internal").ap(),
        "u_d": nc.dram_tensor("u_d", [2048, 4096], F32, kind="Internal").ap(),
        "z_d": nc.dram_tensor("z_d", [2048, 4096], F32, kind="Internal").ap(),
        "y_d": nc.dram_tensor("y_d", [2048, 4096], F32, kind="Internal").ap(),
    }
    Wo_ = {
        "gate_wT": nc.dram_tensor("gate_wT", [1024, 8], F32, kind="ExternalInput").ap(),
        "gbias": nc.dram_tensor("gbias", [8, 2], F32, kind="ExternalInput").ap(),
        "b1": nc.dram_tensor("b1", [128, 256], F32, kind="ExternalInput").ap(),
        "b2": nc.dram_tensor("b2", [128, 64], F32, kind="ExternalInput").ap(),
        "w1": nc.dram_tensor("w1", [8, 1024, 4096], F32, kind="ExternalInput").ap(),
        "w2": nc.dram_tensor("w2", [8, 4096, 1024], F32, kind="ExternalInput").ap(),
    }
    scro = {
        "w1_bf": nc.dram_tensor("w1_bf", [8, 1024, 4096], BF16, kind="Internal").ap(),
        "w2_bf": nc.dram_tensor("w2_bf", [8, 4096, 1024], BF16, kind="Internal").ap(),
    }
    Wa = {}
    for nm in ("Wq", "Wk", "Wv", "Wo"):
        Wa[nm] = nc.dram_tensor("a_" + nm, [1024, 1024], F32, kind="ExternalInput").ap()
    for nm in ("bq", "bk", "bo", "lnw", "lnb"):
        Wa[nm] = nc.dram_tensor("a_" + nm, [128, 8], F32, kind="ExternalInput").ap()
    Wa["bv"] = nc.dram_tensor("a_bv", [1, 1024], F32, kind="ExternalInput").ap()
    scra = {
        "xn_d": nc.dram_tensor("xn_d", [1024, 4096], F32, kind="Internal").ap(),
        "q_d": nc.dram_tensor("q_d", [1024, 4096], BF16, kind="Internal").ap(),
        "k_d": nc.dram_tensor("k_d", [1024, 4096], BF16, kind="Internal").ap(),
        "st_d": nc.dram_tensor("st_d", [1024, 4096], BF16, kind="Internal").ap(),
        "st2_d": nc.dram_tensor("st2_d", [1024, 4096], BF16, kind="Internal").ap(),
        "v_d": nc.dram_tensor("v_d", [4096, 1024], BF16, kind="Internal").ap(),
    }

    es = ExitStack()
    cx = Ctx(nc, es)
    build_consts(cx)
    prep_ffn0_weights(cx, wgT_i, wuT_i, wdT_i, wgT, wuT, wdT)
    cx.phase_boundary()
    emit_ffn0(cx, x_in, rbuf[0], wgT, wuT, wdT)
    cx.phase_boundary()
    emit_mamba(cx, rbuf[0], rbuf[1], Wm, scrm)
    cx.phase_boundary()
    emit_ffn0(cx, rbuf[1], rbuf[2], wgT, wuT, wdT)
    cx.phase_boundary()
    emit_moe(cx, rbuf[2], rbuf[3], Wo_, scro)
    cx.phase_boundary()
    emit_attn(cx, rbuf[3], rbuf[4], Wa, scra)
    cx.phase_boundary()
    emit_moe(cx, rbuf[4], r_fin, Wo_, scro)
    cx.finalize()
    with nc.Block() as block:
        cx.p.replay(block)
    es.close()
    return nc, cx.p.n_inst


def kernel(**inputs):
    g = inputs
    ct = np.ascontiguousarray
    wmap = {
        "wgT_i": ct(np.asarray(g["L0_ffn_w_gate"]).T),
        "wuT_i": ct(np.asarray(g["L0_ffn_w_up"]).T),
        "wdT_i": ct(np.asarray(g["L0_ffn_w_down"]).T),
        "m_fc1_wT": ct(np.asarray(g["L0_attn_fc1_w"]).T),
        "m_fc1_b": to_pcol(g["L0_attn_fc1_b"]),
        "m_inpT": ct(np.asarray(g["L0_attn_in_proj_w"]).T),
        "m_conv_w": to_pcol(np.asarray(g["L0_attn_conv_w"])),
        "m_conv_b": to_pcol(g["L0_attn_conv_b"]),
        "m_xpT": ct(np.asarray(g["L0_attn_x_proj_w"]).T),
        "m_dtpT": ct(np.asarray(g["L0_attn_dt_proj_w"]).T),
        "m_dtp_b": to_pcol(g["L0_attn_dt_proj_b"]),
        "m_A_log": to_pcol(np.asarray(g["L0_attn_A_log"])),
        "m_Dv": to_pcol(g["L0_attn_D"]),
        "m_opT": ct(np.asarray(g["L0_attn_out_proj_w"]).T),
        "m_fc2_wT": ct(np.asarray(g["L0_attn_fc2_w"]).T),
        "m_fc2_b": to_pcol(g["L0_attn_fc2_b"]),
        "gate_wT": ct(np.asarray(g["L1_ffn_gate_w"]).T),
        "gbias": ct(np.stack([np.asarray(g["L1_ffn_gate_b"]),
                              np.asarray(g["L1_ffn_expert_bias"])], axis=1)),
        "b1": np.concatenate([to_pcol(np.asarray(g["L1_ffn_b1"])[e])
                              for e in range(8)], axis=1),
        "b2": np.concatenate([to_pcol(np.asarray(g["L1_ffn_b2"])[e])
                              for e in range(8)], axis=1),
        "w1": ct(np.asarray(g["L1_ffn_w1"])),
        "w2": ct(np.asarray(g["L1_ffn_w2"])),
        "a_Wq": ct(np.asarray(g["L1_attn_Wq"])),
        "a_Wk": ct(np.asarray(g["L1_attn_Wk"])),
        "a_Wv": ct(np.asarray(g["L1_attn_Wv"])),
        "a_Wo": ct(np.asarray(g["L1_attn_Wo"])),
        "a_bq": to_pcol(g["L1_attn_bq"]),
        "a_bk": to_pcol(g["L1_attn_bk"]),
        "a_bo": to_pcol(g["L1_attn_bo"]),
        "a_lnw": to_pcol(g["L1_attn_ln_w"]),
        "a_lnb": to_pcol(g["L1_attn_ln_b"]),
        "a_bv": ct(np.asarray(g["L1_attn_bv"]).reshape(1, 1024)),
    }
    nc, _ = _build_fused()
    in_maps = [dict(wmap, x_in=ct(np.asarray(g["L0_attn_x"])[i].T))
               for i in range(NCORES)]
    res = run_bass_kernel_spmd(nc, in_maps, core_ids=list(range(NCORES)))
    out = np.stack([res.results[i]["r_fin"].T for i in range(NCORES)], axis=0)
    return out.astype(np.float32)


# version 07
# speedup vs baseline: 1.1180x; 1.1180x over previous
import numpy as np
import concourse.bass as bass
import concourse.mybir as mybir

F32 = mybir.dt.float32
BF16 = mybir.dt.bfloat16
I32 = mybir.dt.int32
AF = mybir.ActivationFunctionType
ALU = mybir.AluOpType
AX = mybir.AxisListType

T = 4096
D = 1024
NCH = D // 128
NTT = T // 512


class Sem:
    def __init__(self, h):
        self.h = h
        self.n = 0


class Prog:
    ENGS = "qgsvt"

    def __init__(self, nc):
        self.nc = nc
        self.ops = {e: [] for e in self.ENGS}
        self.n_inst = 0

    def wait(self, eng, sem, val=None):
        v = sem.n if val is None else val
        if v <= 0:
            return
        self.ops[eng].append(lambda E, s=sem.h, vv=v: E.wait_ge(s, vv))

    def op(self, eng, name, *args, inc=None, **kwargs):
        amt = 0
        if inc is not None:
            amt = 16 if "dma" in name else 1
            inc.n += amt

        def fn(E, name=name, args=args, kwargs=kwargs, inc=inc, amt=amt):
            ins = getattr(E, name)(*args, **kwargs)
            if inc is not None:
                ins.then_inc(inc.h, amt)
        self.ops[eng].append(fn)
        self.n_inst += 1

    def replay(self, block):
        mapping = {"q": block.sync, "g": block.gpsimd, "s": block.scalar,
                   "v": block.vector, "t": block.tensor}
        for e in self.ENGS:
            ops = self.ops[e]

            def body(E, ops=ops):
                for fn in ops:
                    fn(E)
            mapping[e](body)


class Ctx:
    def __init__(self, nc, es):
        self.nc = nc
        self.es = es
        self.p = Prog(nc)
        self._semgen = 0
        self.new_sems()
        self.psa = es.enter_context(nc.psum_tensor("psarena", [128, 4096], F32))
        self._uid = 0
        self.sC = Sem(es.enter_context(nc.semaphore("sC")))
        self._consts = {}
        self._const_t = es.enter_context(nc.sbuf_tensor("constp", [128, 32], F32))
        self._const_n = 0

    def sb(self, name, shape, dt=F32, side=None):
        return self.es.enter_context(
            self.nc.sbuf_tensor(name, shape, dt, side=side))

    def bank(self, b, rows=128):
        return self.psa[0:rows, b * 512:(b + 1) * 512]

    def banksl(self, b, sl, rows=128):
        return self.psa[0:rows, b * 512 + sl.start: b * 512 + sl.stop]

    def uid(self):
        self._uid += 1
        return self._uid

    def new_sems(self):
        g = self._semgen
        self._semgen += 1
        for nm in ("ld", "ldg", "stq", "sT", "sV", "sS", "sG"):
            h = self.es.enter_context(self.nc.semaphore(f"{nm}{g}"))
            setattr(self, nm, Sem(h))

    def phase_boundary(self):
        self.barrier()
        self.new_sems()

    def scope(self):
        from contextlib import ExitStack
        return ExitStack()

    def barrier(self):
        sems = [self.ld, self.ldg, self.stq, self.sT, self.sV, self.sS,
                self.sG]
        for e in "qgsvt":
            for s in sems:
                self.p.wait(e, s, s.n)

    def sbs(self, es, name, shape, dt=F32, side=None):
        return es.enter_context(
            self.nc.sbuf_tensor(name, shape, dt, side=side))

    def constf(self, val):
        val = float(val)
        if val not in self._consts and val not in (0.0, 1.0, -1.0):
            i = self._const_n
            assert i < 32
            self._const_n += 1
            ap = self._const_t[:, i:i + 1]
            self.nc.const_aps.register(F32, val, ap)
            self._consts[val] = ap
        return val

    def finalize(self):
        pre = []
        items = list(self._consts.items())
        for i, (val, ap) in enumerate(items):
            def fn(E, ap=ap, val=val, last=(i == len(items) - 1)):
                ins = E.memset(ap, val)
                if last:
                    ins.then_inc(self.sC.h, 1)
            pre.append(fn)
        if not items:
            def fn0(E):
                E.memset(self._const_t[:, 0:1], 0.0).then_inc(self.sC.h, 1)
            pre.append(fn0)
        self.p.ops["g"] = pre + self.p.ops["g"]
        for e in "qsvt":
            def w(E):
                E.wait_ge(self.sC.h, 1)
            self.p.ops[e] = [w] + self.p.ops[e]


def build_consts(cx):
    p = cx.p
    cx.ones_b = cx.sb("ones_b", [128, 1], BF16)
    cx.ones_f = cx.sb("ones_f", [128, 1], F32)
    cx.ones_fT = cx.sb("ones_fT", [1, 128], F32)
    cx.ones_bT = cx.sb("ones_bT", [1, 128], BF16)
    cx.ident = cx.sb("ident", [128, 128], BF16)
    p.op("g", "memset", cx.ones_b[:], 1.0)
    p.op("g", "memset", cx.ones_f[:], 1.0)
    p.op("g", "memset", cx.ones_fT[:], 1.0)
    p.op("g", "memset", cx.ones_bT[:], 1.0)
    p.op("g", "memset", cx.ident[:], 0.0)
    p.op("g", "affine_select", out=cx.ident[:], in_=cx.ident[:],
         compare_op=ALU.not_equal, fill=1.0, base=0,
         pattern=[[-1, 128]], channel_multiplier=1, inc=cx.sG)


def dram_to_bf16(cx, src_ap, dst_ap, stage_sb):
    p = cx.p
    R, C = src_ap.shape
    n = R // 128
    src_v = src_ap.rearrange("(n p) f -> n p f", p=128)
    dst_v = dst_ap.rearrange("(n p) f -> n p f", p=128)
    for i in range(n):
        p.wait("g", cx.stq, cx.stq.n)
        p.op("g", "dma_start", out=stage_sb, in_=src_v[i], inc=cx.ldg)
        p.wait("q", cx.ldg, cx.ldg.n)
        p.op("q", "dma_start", out=dst_v[i], in_=stage_sb, inc=cx.stq)
    p.wait("q", cx.stq, cx.stq.n)


def emit_ln_tile(cx, Rt, sq, mu_sb, var_sb, mu2_sb, eps=1e-5):
    p = cx.p
    sT, sV, sS = cx.sT, cx.sV, cx.sS
    sq_done = []
    for c in range(NCH):
        csl = slice(c * 512, (c + 1) * 512)
        p.op("t", "matmul", cx.bank(0, 1), cx.ones_f[:], Rt[:, csl],
             start=(c == 0), stop=(c == NCH - 1))
        if c >= 2:
            p.wait("s", sT, sq_done[c - 2])
        p.op("s", "activation", out=sq[c % 2][:], in_=Rt[:, csl],
             func=AF.Square, inc=sS)
        p.wait("t", sS)
        p.op("t", "matmul", cx.bank(1, 1), cx.ones_b[:], sq[c % 2][:],
             start=(c == 0), stop=(c == NCH - 1), inc=sT)
        sq_done.append(sT.n)
    p.wait("v", sT)
    p.op("v", "tensor_scalar_mul", mu_sb, cx.bank(0, 1), 1.0 / D)
    p.op("v", "tensor_scalar_mul", var_sb, cx.bank(1, 1), 1.0 / D, inc=sV)
    p.wait("s", sV)
    p.op("s", "activation", out=mu2_sb, in_=mu_sb, func=AF.Square, inc=sS)
    p.wait("v", sS)
    p.op("v", "tensor_sub", var_sb, var_sb, mu2_sb, inc=sV)
    p.wait("s", sV)
    p.op("s", "activation", out=var_sb, in_=var_sb, func=AF.Ln, bias=cx.constf(eps))
    p.op("s", "activation", out=var_sb, in_=var_sb, func=AF.Exp, scale=cx.constf(-0.5),
         inc=sS)
    p.wait("t", sS)
    p.op("t", "matmul", cx.bank(2), cx.ones_fT[:], mu_sb, start=True,
         stop=True)
    p.op("t", "matmul", cx.bank(3), cx.ones_fT[:], var_sb, start=True,
         stop=True, inc=sT)
    p.wait("v", sT)


def emit_xn_tile(cx, Rt, xn, scrf, dst_fn=None):
    p = cx.p
    for c in range(NCH):
        csl = slice(c * 512, (c + 1) * 512)
        dst = xn[:, csl] if dst_fn is None else dst_fn(c)
        p.op("v", "tensor_sub", scrf[:], Rt[:, csl], cx.bank(2))
        p.op("v", "tensor_mul", dst, scrf[:], cx.bank(3),
             inc=(cx.sV if c == NCH - 1 else None))


FF = 3072
NFC = FF // 128
SCALE = 1.0 / 127.0


def emit_ffn0(cx, r_in, r_out, wgT, wuT, wdT):
    p = cx.p
    u = cx.uid()
    cx.barrier()
    esF = cx.scope()
    cx.sb = lambda name, shape, dt=F32, side=None, _sbs=cx.sbs, _es=esF: _sbs(_es, name, shape, dt, side)
    Rt = cx.sb(f"f{u}_Rt", [128, NCH * 512], F32)
    Ro = cx.sb(f"f{u}_Ro", [128, NCH * 512], F32)
    xn = cx.sb(f"f{u}_xn", [128, NCH * 512], BF16)
    sq = [cx.sb(f"f{u}_sq{i}", [128, 512], BF16) for i in range(2)]
    scrf = cx.sb(f"f{u}_scrf", [128, 512], F32)
    mu_sb = cx.sb(f"f{u}_mu", [1, 512], F32)
    var_sb = cx.sb(f"f{u}_var", [1, 512], F32)
    mu2_sb = cx.sb(f"f{u}_mu2", [1, 512], F32)
    gu = cx.sb(f"f{u}_gu", [128, NFC * 512], BF16)
    gsil = [cx.sb(f"f{u}_gsil{i}", [128, 512], BF16) for i in range(2)]
    usil = [cx.sb(f"f{u}_usil{i}", [128, 512], BF16) for i in range(2)]
    wg_sb = [cx.sb(f"f{u}_wg{i}", [128, NCH * 512], BF16) for i in range(2)]
    wu_sb = [cx.sb(f"f{u}_wu{i}", [128, NCH * 512], BF16) for i in range(2)]
    wd_sb = [cx.sb(f"f{u}_wd{i}", [128, 24 * 128], BF16) for i in range(2)]

    r_in_v = r_in.rearrange("(c p) t -> p c t", p=128)
    r_out_v = r_out.rearrange("(c p) t -> p c t", p=128)
    wgT_v = wgT.rearrange("(c p) f -> p c f", p=128)
    wuT_v = wuT.rearrange("(c p) f -> p c f", p=128)
    wdT_v = wdT.rearrange("(k p) o -> p k o", p=128)

    rt_free = 0
    ro_free = 0
    wbuf_free = [0, 0]
    wdbuf_free = [0, 0]
    gu_evac = {}
    NGRP = 6

    for tt in range(NTT):
        tsl = slice(tt * 512, (tt + 1) * 512)
        p.wait("q", cx.sV, rt_free)
        p.op("q", "dma_start",
             out=Rt[:].rearrange("p (c t) -> p c t", c=NCH),
             in_=r_in_v[:, :, tsl], inc=cx.ld)
        ld_rt = cx.ld.n
        p.wait("t", cx.ld, ld_rt)
        p.wait("s", cx.ld, ld_rt)
        emit_ln_tile(cx, Rt[:], sq, mu_sb[:], var_sb[:], mu2_sb[:])
        emit_xn_tile(cx, Rt[:], xn[:], scrf)
        xn_done = cx.sV.n
        for grp in range(NGRP):
            gb = grp % 2
            fsl = slice(grp * 512, (grp + 1) * 512)
            p.wait("g", cx.sT, wbuf_free[gb])
            p.op("g", "dma_start",
                 out=wg_sb[gb][:].rearrange("p (c f) -> p c f", c=NCH),
                 in_=wgT_v[:, :, fsl], inc=cx.ldg)
            p.op("g", "dma_start",
                 out=wu_sb[gb][:].rearrange("p (c f) -> p c f", c=NCH),
                 in_=wuT_v[:, :, fsl], inc=cx.ldg)
            ld_w = cx.ldg.n
            p.wait("t", cx.ldg, ld_w)
            p.wait("t", cx.sV, xn_done)
            for fl in range(4):
                fc = grp * 4 + fl
                bk_g = 4 + (fc % 2) * 2
                bk_u = 5 + (fc % 2) * 2
                prev_fc = fc - 2 if fc >= 2 else 22 + fc
                p.wait("t", cx.sV, gu_evac.get(prev_fc, 0))
                for c in range(NCH):
                    lhs_g = wg_sb[gb][:, c * 512 + fl * 128:
                                      c * 512 + (fl + 1) * 128]
                    lhs_u = wu_sb[gb][:, c * 512 + fl * 128:
                                      c * 512 + (fl + 1) * 128]
                    rhs = xn[:, c * 512:(c + 1) * 512]
                    p.op("t", "matmul", cx.bank(bk_g), lhs_g, rhs,
                         start=(c == 0), stop=(c == NCH - 1))
                    p.op("t", "matmul", cx.bank(bk_u), lhs_u, rhs,
                         start=(c == 0), stop=(c == NCH - 1),
                         inc=(cx.sT if c == NCH - 1 else None))
                mm_done = cx.sT.n
                if fl == 3:
                    wbuf_free[gb] = mm_done
                p.wait("s", cx.sT, mm_done)
                p.wait("s", cx.sV, gu_evac.get(prev_fc + 1 if False else (fc - 2 if fc >= 2 else 22 + fc), 0))
                p.op("s", "activation", out=gsil[fc % 2][:], in_=cx.bank(bk_g),
                     func=AF.Silu, scale=cx.constf(SCALE))
                p.op("s", "activation", out=usil[fc % 2][:], in_=cx.bank(bk_u),
                     func=AF.Copy, scale=cx.constf(SCALE), inc=cx.sS)
                p.wait("v", cx.sS)
                p.op("v", "tensor_mul", gu[:, fc * 512:(fc + 1) * 512],
                     gsil[fc % 2][:], usil[fc % 2][:], inc=cx.sV)
                gu_evac[fc] = cx.sV.n
        all_gu = cx.sV.n
        down_evac = {}
        for oc in range(NCH):
            db = oc % 2
            p.wait("g", cx.sT, wdbuf_free[db])
            p.op("g", "dma_start",
                 out=wd_sb[db][:].rearrange("p (k o) -> p k o", k=24),
                 in_=wdT_v[:, :, oc * 128:(oc + 1) * 128], inc=cx.ldg)
            ld_wd = cx.ldg.n
            p.wait("t", cx.ldg, ld_wd)
            p.wait("t", cx.sV, all_gu)
            bk = oc % 2
            if oc >= 2:
                p.wait("t", cx.sV, down_evac[oc - 2])
            for kf in range(24):
                p.op("t", "matmul", cx.bank(bk),
                     wd_sb[db][:, kf * 128:(kf + 1) * 128],
                     gu[:, kf * 512:(kf + 1) * 512],
                     start=(kf == 0), stop=(kf == 23),
                     inc=(cx.sT if kf == 23 else None))
            wdbuf_free[db] = cx.sT.n
            p.wait("v", cx.sT)
            if oc == 0:
                p.wait("v", cx.stq, ro_free)
            csl = slice(oc * 512, (oc + 1) * 512)
            p.op("v", "scalar_tensor_tensor", out=Ro[:, csl],
                 in0=cx.bank(bk), scalar=0.5 * SCALE, in1=Rt[:, csl],
                 op0=ALU.mult, op1=ALU.add,
                 inc=cx.sV)
            down_evac[oc] = cx.sV.n
        rt_free = cx.sV.n
        p.wait("q", cx.sV, cx.sV.n)
        p.op("q", "dma_start", out=r_out_v[:, :, tsl],
             in_=Ro[:].rearrange("p (c t) -> p c t", c=NCH), inc=cx.stq)
        ro_free = cx.stq.n
    p.wait("q", cx.stq, cx.stq.n)
    del cx.sb
    esF.close()
    cx.barrier()


def prep_ffn0_weights(cx, wgT_i, wuT_i, wdT_i, wgT, wuT, wdT):
    with cx.scope() as esP:
        stage = cx.sbs(esP, "w_stage", [128, 3072], BF16, side="right")
        dram_to_bf16(cx, wgT_i, wgT, stage[:])
        dram_to_bf16(cx, wuT_i, wuT, stage[:])
        dram_to_bf16(cx, wdT_i, wdT, stage[:, 0:1024])


DI = 2048
NJC = DI // 128
DS = 16
DTR = 64


def emit_mamba(cx, r_in, r_out, W, scr):
    p = cx.p
    u = cx.uid()
    cx.barrier()
    r_in_v = r_in.rearrange("(c p) t -> p c t", p=128)
    r_out_v = r_out.rearrange("(c p) t -> p c t", p=128)

    fc1b = cx.sb(f"m{u}_fc1b", [128, 8], F32)
    fc2b = cx.sb(f"m{u}_fc2b", [128, 8], F32)
    es0 = cx.scope()
    stage = cx.sbs(es0, f"m{u}_stage", [128, 4096], BF16, side="right")
    dram_to_bf16(cx, W["inpT"], scr["inpT_bf"], stage[:])
    dram_to_bf16(cx, W["opT"], scr["opT_bf"], stage[:, 0:1024])

    esW = cx.scope()
    xpw = cx.sbs(esW, f"m{u}_xpw", [128, NJC * 96], BF16)
    dtpw = cx.sbs(esW, f"m{u}_dtpw", [64, DI], BF16)
    A_neg = cx.sbs(esW, f"m{u}_Aneg", [128, NJC * DS], F32)
    convw = cx.sbs(esW, f"m{u}_convw", [128, NJC * 4], F32)
    convb = cx.sbs(esW, f"m{u}_convb", [128, NJC], F32)
    dtpb = cx.sbs(esW, f"m{u}_dtpb", [128, NJC], F32)
    D_sb = cx.sbs(esW, f"m{u}_D", [128, NJC], F32)
    sel = [cx.sbs(esW, f"m{u}_sel{i}", [32, 128], BF16) for i in range(32)]
    ridx32 = cx.sbs(esW, f"m{u}_ridx32", [32, 1], F32)
    dtlow_bf = cx.sbs(esW, f"m{u}_dtlow", [64, T], BF16)
    BC_bf = cx.sbs(esW, f"m{u}_BC", [32, T], BF16)

    es_h = cx.scope()
    h_sb = cx.sbs(es_h, f"m{u}_h", [128, 8 * T], BF16)
    es1w = cx.scope()
    fc1w = cx.sbs(es1w, f"m{u}_fc1w", [128, 8 * 1024], BF16)
    with cx.scope() as esw0:
        wf = cx.sbs(esw0, f"m{u}_wf", [128, 8 * 1024], F32, side="right")
        p.op("q", "dma_start", out=wf[:].rearrange("p (c o) -> p c o", c=8),
             in_=W["fc1_wT"].rearrange("(c p) o -> p c o", p=128), inc=cx.ld)
        p.wait("v", cx.ld)
        p.op("v", "tensor_copy", fc1w[:], wf[:], inc=cx.sV)
        p.wait("q", cx.sV)
        p.op("q", "dma_start",
             out=wf[:, 0:NJC * 96].rearrange("p (c o) -> p c o", c=NJC),
             in_=W["xpT"].rearrange("(c p) o -> p c o", p=128), inc=cx.ld)
        p.wait("v", cx.ld)
        p.op("v", "tensor_copy", xpw[:], wf[:, 0:NJC * 96], inc=cx.sV)
        p.wait("q", cx.sV)
        p.op("q", "dma_start", out=wf[0:64, 0:DI], in_=W["dtpT"], inc=cx.ld)
        p.wait("v", cx.ld)
        p.op("v", "tensor_copy", dtpw[:], wf[0:64, 0:DI], inc=cx.sV)
        p.wait("q", cx.sV)
        p.op("q", "dma_start", out=wf[:, 0:NJC * DS], in_=W["A_log"],
             inc=cx.ld)
        p.wait("s", cx.ld)
        p.op("s", "activation", out=A_neg[:], in_=wf[:, 0:NJC * DS],
             func=AF.Exp, inc=cx.sS)
        p.wait("v", cx.sS)
        p.op("v", "tensor_scalar_mul", A_neg[:], A_neg[:], -1.0, inc=cx.sV)
        p.op("q", "dma_start", out=fc1b[:], in_=W["fc1_b"], inc=cx.ld)
        p.op("q", "dma_start", out=fc2b[:], in_=W["fc2_b"], inc=cx.ld)
        p.op("q", "dma_start", out=convw[:], in_=W["conv_w"], inc=cx.ld)
        p.op("q", "dma_start", out=convb[:], in_=W["conv_b"], inc=cx.ld)
        p.op("q", "dma_start", out=dtpb[:], in_=W["dtp_b"], inc=cx.ld)
        p.op("q", "dma_start", out=D_sb[:], in_=W["Dv"], inc=cx.ld)
        p.op("g", "iota", ridx32[:], pattern=[[1, 1]], base=0,
             channel_multiplier=1, allow_small_or_imprecise_dtypes=True,
             inc=cx.sG)
        p.wait("g", cx.sG)
        for i in range(32):
            p.op("g", "tensor_scalar", sel[i][:],
                 ridx32[:].broadcast_to([32, 128]), float(i), None,
                 ALU.is_equal, inc=(cx.sG if i == 31 else None))
        prep_v = cx.sV.n
        prep_ld = cx.ld.n
        prep_g = cx.sG.n
    es0.close()
    cx.barrier()

    es12 = cx.scope()
    xn_sb = cx.sbs(es12, f"m{u}_xn", [128, 8 * T], BF16)
    es1 = cx.scope()
    Rt = cx.sbs(es1, f"m{u}_Rt", [128, NCH * 512], F32)
    sq = [cx.sbs(es1, f"m{u}_sq{i}", [128, 512], BF16) for i in range(2)]
    scrf = cx.sbs(es1, f"m{u}_scrf", [128, 512], F32)
    mu_sb = cx.sbs(es1, f"m{u}_mu", [1, 512], F32)
    var_sb = cx.sbs(es1, f"m{u}_var", [1, 512], F32)
    mu2_sb = cx.sbs(es1, f"m{u}_mu2", [1, 512], F32)
    for tt in range(NTT):
        tsl = slice(tt * 512, (tt + 1) * 512)
        p.wait("q", cx.sV)
        p.op("q", "dma_start",
             out=Rt[:].rearrange("p (c t) -> p c t", c=NCH),
             in_=r_in_v[:, :, tsl], inc=cx.ld)
        p.wait("t", cx.ld)
        p.wait("s", cx.ld)
        emit_ln_tile(cx, Rt[:], sq, mu_sb[:], var_sb[:], mu2_sb[:])
        emit_xn_tile(cx, Rt[:], None, scrf,
                     dst_fn=lambda c, tsl=tsl: xn_sb[:, c * T + tsl.start:
                                                     c * T + tsl.stop])
    es1.close()

    p.wait("t", cx.sV)
    evac_done = {}
    for oc in range(NCH):
        for tt in range(NTT):
            tsl = slice(tt * 512, (tt + 1) * 512)
            bk = (oc * NTT + tt) % 2
            key = oc * NTT + tt
            if key >= 2:
                p.wait("t", cx.sV, evac_done[key - 2][1])
            for c in range(NCH):
                p.op("t", "matmul", cx.bank(bk),
                     fc1w[:, c * 1024 + oc * 128:c * 1024 + (oc + 1) * 128],
                     xn_sb[:, c * T + tsl.start:c * T + tsl.stop],
                     start=(c == 0), stop=(c == NCH - 1),
                     inc=(cx.sT if c == NCH - 1 else None))
            p.wait("v", cx.sT)
            p.op("v", "tensor_scalar",
                 h_sb[:, oc * T + tsl.start:oc * T + tsl.stop],
                 cx.bank(bk), fc1b[:, oc:oc + 1], None, ALU.add, inc=cx.sV)
            evac_done[key] = ("v", cx.sV.n)
    p2_evac = cx.sV.n
    es12.close()
    es1w.close()
    cx.barrier()

    es3 = cx.scope()
    wsb = [cx.sbs(es3, f"m{u}_w3{i}", [128, 8 * 128], BF16) for i in range(2)]
    xcz = cx.sbs(es3, f"m{u}_xcz", [128, 4099], F32)
    conv_acc = cx.sbs(es3, f"m{u}_cacc", [128, T], F32)
    u_f = cx.sbs(es3, f"m{u}_uf", [128, T], F32)
    u_bf = cx.sbs(es3, f"m{u}_ubf", [128, T], BF16)
    xdbl = cx.sbs(es3, f"m{u}_xdbl", [96, T], F32)
    inpT_v = scr["inpT_bf"].rearrange("(c p) o -> p c o", p=128)
    u_d_v = scr["u_d"].rearrange("(c p) t -> p c t", p=128)
    z_d_v = scr["z_d"].rearrange("(c p) t -> p c t", p=128)

    p.op("g", "memset", xcz[:, 0:3], 0.0, inc=cx.sG)
    xcz_zero = cx.sG.n
    wbuf_free = [0, 0]
    xcz_free = 0
    xcz_free_eng = "none"
    ubf_free = 0
    uf_free = 0
    uf_free_st = 0
    conv_free = 0
    for j in range(2 * NJC):
        jb = j % 2
        p.wait("g", cx.sT, wbuf_free[jb])
        p.op("g", "dma_start",
             out=wsb[jb][:].rearrange("p (c o) -> p c o", c=8),
             in_=inpT_v[:, :, j * 128:(j + 1) * 128], inc=cx.ldg)
        p.wait("t", cx.ldg)
        for tt in range(NTT):
            tsl = slice(tt * 512, (tt + 1) * 512)
            bk = tt % 2
            if tt >= 2:
                p.wait("t", cx.sS, evac_done[tt - 2])
            elif j > 0:
                p.wait("t", cx.sS, evac_done[6 + tt])
            else:
                p.wait("t", cx.sV, p2_evac)
            for c in range(NCH):
                p.op("t", "matmul", cx.bank(bk),
                     wsb[jb][:, c * 128:(c + 1) * 128],
                     h_sb[:, c * T + tsl.start:c * T + tsl.stop],
                     start=(c == 0), stop=(c == NCH - 1),
                     inc=(cx.sT if c == NCH - 1 else None))
            if tt == NTT - 1:
                wbuf_free[jb] = cx.sT.n
            p.wait("s", cx.sT)
            if tt == 0:
                if xcz_free_eng == "v":
                    p.wait("s", cx.sV, xcz_free)
                elif xcz_free_eng == "q":
                    p.wait("s", cx.stq, xcz_free)
            p.op("s", "activation", out=xcz[:, 3 + tsl.start:3 + tsl.stop],
                 in_=cx.bank(bk), func=AF.Copy, inc=cx.sS)
            evac_done[tt] = cx.sS.n
        if j < NJC:
            p.wait("v", cx.sS)
            p.wait("v", cx.sG, xcz_zero)
            p.wait("v", cx.ld, prep_ld)
            p.wait("v", cx.sS, conv_free)
            p.op("v", "tensor_scalar",
                 conv_acc[:], xcz[:, 0:T], convw[:, 4 * j:4 * j + 1], None,
                 ALU.mult)
            for w in range(1, 4):
                p.op("v", "scalar_tensor_tensor", out=conv_acc[:],
                     in0=xcz[:, w:w + T],
                     scalar=convw[:, 4 * j + w:4 * j + w + 1],
                     in1=conv_acc[:], op0=ALU.mult, op1=ALU.add,
                     inc=(cx.sV if w == 3 else None))
            xcz_free, xcz_free_eng = cx.sV.n, "v"
            p.wait("s", cx.sV)
            p.wait("s", cx.stq, uf_free_st)
            p.wait("s", cx.sV, uf_free)
            p.op("s", "activation", out=u_f[:], in_=conv_acc[:], func=AF.Silu,
                 bias=convb[:, j:j + 1], inc=cx.sS)
            conv_free = cx.sS.n
            p.wait("q", cx.sS)
            p.op("q", "dma_start", out=u_d_v[:, j, :], in_=u_f[:], inc=cx.stq)
            uf_free_st = cx.stq.n
            p.wait("v", cx.sS)
            p.wait("v", cx.sT, ubf_free)
            p.op("v", "tensor_copy", u_bf[:], u_f[:], inc=cx.sV)
            uf_free = cx.sV.n
            p.wait("t", cx.sV)
            for tt in range(NTT):
                tsl = slice(tt * 512, (tt + 1) * 512)
                p.op("t", "matmul", cx.bank(2, 96),
                     xpw[:, j * 96:(j + 1) * 96], u_bf[:, tsl],
                     start=True, stop=True, inc=cx.sT)
                if j == 0:
                    p.wait("s", cx.sT)
                    p.op("s", "activation", out=xdbl[:, tsl],
                         in_=cx.bank(2, 96), func=AF.Copy, inc=cx.sS)
                    p.wait("t", cx.sS)
                else:
                    p.wait("v", cx.sT)
                    p.op("v", "tensor_add", xdbl[:, tsl], xdbl[:, tsl],
                         cx.bank(2, 96), inc=cx.sV)
                    p.wait("t", cx.sV)
            ubf_free = cx.sT.n
        else:
            p.wait("q", cx.sS)
            p.op("q", "dma_start", out=z_d_v[:, j - NJC, :],
                 in_=xcz[:, 3:3 + T], inc=cx.stq)
            xcz_free, xcz_free_eng = cx.stq.n, "q"
    p.wait("v", cx.sV)
    p.op("v", "tensor_copy", dtlow_bf[:], xdbl[0:64, :])
    p.op("v", "tensor_copy", BC_bf[:], xdbl[64:96, :], inc=cx.sV)
    es3.close()
    es_h.close()
    cx.barrier()

    TH = T // 4
    NTH = TH // 512
    es4 = cx.scope()
    u2 = [cx.sbs(es4, f"m{u}_u2{i}", [128, TH], F32) for i in range(2)]
    dt2 = [cx.sbs(es4, f"m{u}_dt2{i}", [128, TH], F32) for i in range(2)]
    w2 = [cx.sbs(es4, f"m{u}_w2{i}", [128, TH], F32) for i in range(2)]
    y2 = [cx.sbs(es4, f"m{u}_y2{i}", [128, TH], F32) for i in range(2)]
    dA2 = [[cx.sbs(es4, f"m{u}_dA{i}_{j}", [128, TH], F32) for j in range(2)]
           for i in range(2)]
    b2_ = [[cx.sbs(es4, f"m{u}_b2{i}_{j}", [128, TH], F32) for j in range(2)]
           for i in range(2)]
    H2 = [cx.sbs(es4, f"m{u}_H{i}", [128, TH], F32) for i in range(2)]
    CH2 = [cx.sbs(es4, f"m{u}_CH{i}", [128, TH], F32) for i in range(2)]
    Brep = [cx.sbs(es4, f"m{u}_Brep{j}", [128, TH], F32) for j in range(3)]
    Crep = [cx.sbs(es4, f"m{u}_Crep{j}", [128, TH], F32) for j in range(3)]
    dtraw = cx.sbs(es4, f"m{u}_dtraw", [128, TH], F32)
    carry = cx.sbs(es4, f"m{u}_carry", [128, NJC * DS], F32)
    y_d_v = scr["y_d"].rearrange("(c p) t -> p c t", p=128)

    p.op("g", "memset", carry[:], 0.0, inc=cx.sG)
    carry_zero = cx.sG.n
    scan_dA = [[0, 0], [0, 0]]
    scan_b = [[0, 0], [0, 0]]
    rep_b_user = [0, 0, 0]
    rep_c_user_g = [0, 0, 0]
    rep_c_user_v = [0, 0, 0]
    ch_done = [0, 0]
    yadd_done = [("g", 0), ("v", 0)]
    dt_done = [0, 0]
    b_used = [0, 0]
    ydma_done = [0, 0]
    for half in range(4):
        hsl = slice(half * TH, (half + 1) * TH)
        for g in range(NJC // 2):
            for b2 in range(2):
                j = 2 * g + b2
                ye, yv = yadd_done[b2]
                p.wait("q", cx.sG if ye == "g" else cx.sV, yv)
                p.op("q", "dma_start", out=u2[b2][:],
                     in_=u_d_v[:, j, hsl], inc=cx.ld)
                p.wait("s", cx.sV, max(dt_done))
                for tt in range(NTH):
                    tsl = slice(half * TH + tt * 512, half * TH + (tt + 1) * 512)
                    p.wait("t", cx.sS)
                    p.wait("t", cx.sV)
                    p.op("t", "matmul", cx.bank(3),
                         dtpw[:, j * 128:(j + 1) * 128],
                         dtlow_bf[:, tsl], start=True, stop=True, inc=cx.sT)
                    p.wait("v", cx.sT)
                    p.op("v", "tensor_scalar",
                         dtraw[:, tt * 512:(tt + 1) * 512],
                         cx.bank(3), dtpb[:, j:j + 1], None, ALU.add,
                         inc=cx.sV)
                scr_sp = dA2[b2][0]
                p.wait("s", cx.sV, scan_dA[b2][0])
                p.wait("s", cx.sV)
                p.op("s", "activation", out=scr_sp[:], in_=dtraw[:],
                     func=AF.Abs)
                p.op("s", "activation", out=scr_sp[:], in_=scr_sp[:],
                     func=AF.Exp, scale=-1.0)
                p.op("s", "activation", out=scr_sp[:], in_=scr_sp[:],
                     func=AF.Ln, bias=1.0, inc=cx.sS)
                p.wait("v", cx.sS)
                p.wait("v", cx.sV, scan_dA[b2][0])
                p.op("v", "tensor_scalar_max", dt2[b2][:], dtraw[:], 0.0)
                p.op("v", "tensor_add", dt2[b2][:], dt2[b2][:], scr_sp[:],
                     inc=cx.sV)
                p.wait("v", cx.ld)
                p.wait("v", cx.sG, b_used[b2])
                p.op("v", "tensor_mul", w2[b2][:], dt2[b2][:], u2[b2][:],
                     inc=cx.sV)
                dt_done[b2] = cx.sV.n
                ye, yv = yadd_done[b2]
                if ye == "v":
                    p.wait("g", cx.sV, yv)
                p.wait("g", cx.stq, ydma_done[b2])
                p.op("g", "memset", y2[b2][:], 0.0, inc=cx.sG)
            grp_ready_v = cx.sV.n
            grp_ready_g = cx.sG.n
            for n in range(DS):
                pn = n % 2
                rn = n % 3
                p.wait("t", cx.sV, grp_ready_v)
                p.wait("t", cx.sG, rep_b_user[rn])
                p.wait("t", cx.sG, rep_c_user_g[rn])
                p.wait("t", cx.sV, rep_c_user_v[rn])
                for tt in range(NTH):
                    tsl = slice(half * TH + tt * 512,
                                half * TH + (tt + 1) * 512)
                    p.op("t", "matmul", cx.bank(4 * pn + tt), sel[n][:],
                         BC_bf[:, tsl], start=True, stop=True)
                    p.op("t", "matmul", cx.bank(4 * pn + 2 + tt),
                         sel[16 + n][:], BC_bf[:, tsl], start=True, stop=True,
                         inc=(cx.sT if tt == NTH - 1 else None))
                p.wait("s", cx.sT)
                p.op("s", "activation", out=Brep[rn][:],
                     in_=cx.psa[:, 4 * pn * 512:(4 * pn + 2) * 512],
                     func=AF.Copy)
                p.op("s", "activation", out=Crep[rn][:],
                     in_=cx.psa[:, (4 * pn + 2) * 512:(4 * pn + 4) * 512],
                     func=AF.Copy, inc=cx.sS)
                rep_done = cx.sS.n
                scan_v = [0, 0]
                for b2 in range(2):
                    j = 2 * g + b2
                    cidx = j * DS + n
                    p.wait("s", cx.sV, scan_dA[b2][pn])
                    p.op("s", "activation", out=dA2[b2][pn][:],
                         in_=dt2[b2][:], func=AF.Exp,
                         scale=A_neg[:, cidx:cidx + 1], inc=cx.sS)
                    dA_done = cx.sS.n
                    p.wait("g", cx.sS, rep_done)
                    p.wait("g", cx.sV, scan_b[b2][pn])
                    p.op("g", "tensor_mul", b2_[b2][pn][:], w2[b2][:],
                         Brep[rn][:], inc=cx.sG)
                    rep_b_user[rn] = cx.sG.n
                    b_used[b2] = cx.sG.n
                    b_ready = cx.sG.n
                    p.wait("v", cx.sS, dA_done)
                    p.wait("v", cx.sG, b_ready)
                    if b2 == 0:
                        p.wait("v", cx.sG, ch_done[0])
                    p.wait("v", cx.sG, carry_zero)
                    p.op("v", "tensor_tensor_scan", H2[b2][:],
                         dA2[b2][pn][:], b2_[b2][pn][:],
                         carry[:, cidx:cidx + 1], ALU.mult, ALU.add,
                         inc=cx.sV)
                    scan_dA[b2][pn] = cx.sV.n
                    scan_b[b2][pn] = cx.sV.n
                    scan_v[b2] = cx.sV.n
                p.wait("g", cx.sS, rep_done)
                p.wait("g", cx.sV, scan_v[0])
                p.op("g", "tensor_mul", CH2[0][:], H2[0][:],
                     Crep[rn][:], inc=cx.sG)
                ch_done[0] = cx.sG.n
                rep_c_user_g[rn] = cx.sG.n
                p.op("g", "tensor_add", y2[0][:], y2[0][:],
                     CH2[0][:], inc=cx.sG)
                yadd_done[0] = ("g", cx.sG.n)
                p.op("v", "tensor_mul", CH2[1][:], H2[1][:],
                     Crep[rn][:], inc=cx.sV)
                rep_c_user_v[rn] = cx.sV.n
                p.op("v", "tensor_add", y2[1][:], y2[1][:],
                     CH2[1][:], inc=cx.sV)
                yadd_done[1] = ("v", cx.sV.n)
                for b2 in range(2):
                    j = 2 * g + b2
                    cidx = j * DS + n
                    p.wait("v", cx.sV, scan_v[b2])
                    p.op("v", "tensor_copy", carry[:, cidx:cidx + 1],
                         H2[b2][:, TH - 1:TH], inc=cx.sV)
                if n == 0:
                    p.wait("v", cx.sG, grp_ready_g)
            for b2 in range(2):
                j = 2 * g + b2
                p.wait("v", cx.stq, ydma_done[b2])
                if b2 == 0:
                    p.wait("v", cx.sG, ch_done[0])
                    p.wait("v", cx.sG)
                p.op("v", "scalar_tensor_tensor", out=y2[b2][:],
                     in0=u2[b2][:], scalar=D_sb[:, j:j + 1], in1=y2[b2][:],
                     op0=ALU.mult, op1=ALU.add, inc=cx.sV)
                yadd_done[b2] = ("v", cx.sV.n)
                p.wait("q", cx.sV, cx.sV.n)
                p.op("q", "dma_start", out=y_d_v[:, j, hsl], in_=y2[b2][:],
                     inc=cx.stq)
                ydma_done[b2] = cx.stq.n
    p.wait("q", cx.stq, cx.stq.n)
    es4.close()
    esW.close()
    cx.barrier()

    es5 = cx.scope()
    opw = cx.sbs(es5, f"m{u}_opw", [128, NJC * 1024], BF16)
    fc2w = cx.sbs(es5, f"m{u}_fc2w", [128, 8 * 1024], BF16)
    p.op("q", "dma_start",
         out=opw[:].rearrange("p (c o) -> p c o", c=NJC),
         in_=scr["opT_bf"].rearrange("(c p) o -> p c o", p=128), inc=cx.ld)
    with cx.scope() as esw1:
        wf2 = cx.sbs(esw1, f"m{u}_wf2", [128, 4 * 1024], F32, side="right")
        wv = W["fc2_wT"].rearrange("(c p) o -> p c o", p=128)
        for hf in range(2):
            p.wait("q", cx.sV)
            p.op("q", "dma_start",
                 out=wf2[:].rearrange("p (c o) -> p c o", c=4),
                 in_=wv[:, hf * 4:(hf + 1) * 4, :], inc=cx.ld)
            p.wait("v", cx.ld)
            p.op("v", "tensor_copy", fc2w[:, hf * 4096:(hf + 1) * 4096],
                 wf2[:], inc=cx.sV)
    opw_ld = cx.ld.n
    ybf_t = cx.sbs(es5, f"m{u}_ybft", [128, NJC * 512], BF16)
    zf = [cx.sbs(es5, f"m{u}_zf{i}", [128, 512], F32) for i in range(2)]
    yf = [cx.sbs(es5, f"m{u}_yf{i}", [128, 512], F32) for i in range(2)]
    zs = [cx.sbs(es5, f"m{u}_zs{i}", [128, 512], F32) for i in range(2)]
    o1t = cx.sbs(es5, f"m{u}_o1t", [128, 8 * 512], BF16)
    Rt5 = cx.sbs(es5, f"m{u}_Rt5", [128, 8 * 512], F32)
    Ro5 = cx.sbs(es5, f"m{u}_Ro5", [128, 8 * 512], F32)
    rt5_free = 0
    ro5_st = 0
    zy_free = [0, 0]
    for tt in range(NTT):
        tsl = slice(tt * 512, (tt + 1) * 512)
        p.wait("q", cx.sV, rt5_free)
        p.op("q", "dma_start",
             out=Rt5[:].rearrange("p (c t) -> p c t", c=NCH),
             in_=r_in_v[:, :, tsl], inc=cx.ld)
        rt5_ld = cx.ld.n
        ybf_mm = cx.sT.n
        for j in range(NJC):
            jb = j % 2
            p.wait("q", cx.sV, zy_free[jb])
            p.op("q", "dma_start", out=zf[jb][:], in_=z_d_v[:, j, tsl],
                 inc=cx.ld)
            p.op("q", "dma_start", out=yf[jb][:], in_=y_d_v[:, j, tsl],
                 inc=cx.ld)
            p.wait("s", cx.ld)
            p.wait("s", cx.sV, zy_free[jb])
            p.op("s", "activation", out=zs[jb][:], in_=zf[jb][:],
                 func=AF.Silu, inc=cx.sS)
            p.wait("v", cx.sS)
            if j < 2:
                p.wait("v", cx.sT, ybf_mm)
            p.op("v", "tensor_mul", ybf_t[:, j * 512:(j + 1) * 512],
                 yf[jb][:], zs[jb][:], inc=cx.sV)
            zy_free[jb] = cx.sV.n
        ybf_done = cx.sV.n
        p.wait("t", cx.sV, ybf_done)
        p.wait("t", cx.ld, opw_ld)
        if tt > 0:
            p.wait("t", cx.sS)
            p.wait("t", cx.sV)
        for oc in range(NCH):
            for c in range(NJC):
                p.op("t", "matmul", cx.bank(oc),
                     opw[:, c * 1024 + oc * 128:c * 1024 + (oc + 1) * 128],
                     ybf_t[:, c * 512:(c + 1) * 512],
                     start=(c == 0), stop=(c == NJC - 1),
                     inc=(cx.sT if c == NJC - 1 else None))
        p.wait("s", cx.sT)
        for oc in range(NCH):
            p.op("s", "activation", out=o1t[:, oc * 512:(oc + 1) * 512],
                 in_=cx.bank(oc), func=AF.Copy,
                 inc=(cx.sS if oc == NCH - 1 else None))
        if "o1_d" in scr:
            p.wait("q", cx.sS)
            p.op("q", "dma_start",
                 out=scr["o1_d"].rearrange("(c p) t -> p c t", p=128)[:, :, tsl],
                 in_=o1t[:].rearrange("p (c t) -> p c t", c=NCH), inc=cx.stq)
        p.wait("t", cx.sS)
        fc2_evac = {}
        for oc2 in range(NCH):
            bk = oc2 % 2
            if oc2 >= 2:
                p.wait("t", cx.sV, fc2_evac[oc2 - 2])
            for c in range(NCH):
                p.op("t", "matmul", cx.bank(bk),
                     fc2w[:, c * 1024 + oc2 * 128:c * 1024 + (oc2 + 1) * 128],
                     o1t[:, c * 512:(c + 1) * 512],
                     start=(c == 0), stop=(c == NCH - 1),
                     inc=(cx.sT if c == NCH - 1 else None))
            p.wait("v", cx.sT)
            if oc2 == 0:
                p.wait("v", cx.ld, rt5_ld)
                p.wait("v", cx.stq, ro5_st)
            csl = slice(oc2 * 512, (oc2 + 1) * 512)
            p.op("v", "scalar_tensor_tensor", out=Ro5[:, csl],
                 in0=cx.bank(bk), scalar=fc2b[:, oc2:oc2 + 1],
                 in1=Rt5[:, csl], op0=ALU.add, op1=ALU.add, inc=cx.sV)
            fc2_evac[oc2] = cx.sV.n
        rt5_free = cx.sV.n
        p.wait("q", cx.sV, rt5_free)
        p.op("q", "dma_start", out=r_out_v[:, :, tsl],
             in_=Ro5[:].rearrange("p (c t) -> p c t", c=NCH), inc=cx.stq)
        ro5_st = cx.stq.n
    p.wait("q", cx.stq, cx.stq.n)
    es5.close()


def to_pcol(v):
    import numpy as np
    v = np.asarray(v)
    if v.ndim == 1:
        c = v.shape[0] // 128
        return np.ascontiguousarray(v.reshape(c, 128).T)
    c = v.shape[0] // 128
    k = v.shape[1]
    return np.ascontiguousarray(
        v.reshape(c, 128, k).transpose(1, 0, 2).reshape(128, c * k))


NEXP = 8
DFF = 4096
NFC2 = DFF // 128


def emit_moe(cx, r_in, r_out, W, scr):
    p = cx.p
    u = cx.uid()
    DBG = scr.get("dbg_stage", 5)
    cx.barrier()
    r_in_v = r_in.rearrange("(c p) t -> p c t", p=128)
    r_out_v = r_out.rearrange("(c p) t -> p c t", p=128)

    if not scr.get("moe_prepped"):
        with cx.scope() as esP:
            stage = cx.sbs(esP, f"o{u}_stage", [128, 4096], BF16, side="right")
            for e in range(NEXP):
                dram_to_bf16(cx, W["w1"][e], scr["w1_bf"][e], stage[:])
                dram_to_bf16(cx, W["w2"][e], scr["w2_bf"][e],
                             stage[:, 0:1024])
        scr["moe_prepped"] = True

    esM = cx.scope()
    if "cpack" not in scr:
        gw = cx.sb(f"o{u}_gw", [128, 8 * 8], F32)
        gb = cx.sb(f"o{u}_gb", [8, 2], F32)
        gbias = cx.sb(f"o{u}_gbias", [8, 1], F32)
        b1_sb = cx.sb(f"o{u}_b1", [128, NEXP * 32], F32)
        b2_sb = cx.sb(f"o{u}_b2", [128, NEXP * 8], F32)
        sel8 = [cx.sb(f"o{u}_sel8_{e}", [8, 128], BF16) for e in range(NEXP)]
        ident_f = cx.sb(f"o{u}_identf", [128, 128], F32)
        revidx = cx.sb(f"o{u}_rev", [128, 8], F32)
        ridx8 = cx.sb(f"o{u}_ridx8", [8, 1], F32)
        p.op("q", "dma_start",
             out=gw[:].rearrange("p (c e) -> p c e", c=NCH),
             in_=W["gate_wT"].rearrange("(c p) e -> p c e", p=128), inc=cx.ld)
        p.op("q", "dma_start", out=gb[:], in_=W["gbias"], inc=cx.ld)
        p.op("q", "dma_start", out=b1_sb[:], in_=W["b1"], inc=cx.ld)
        p.op("q", "dma_start", out=b2_sb[:], in_=W["b2"], inc=cx.ld)
        p.wait("v", cx.ld)
        p.op("v", "tensor_add", gbias[:], gb[:, 0:1], gb[:, 1:2], inc=cx.sV)
        p.op("g", "iota", ridx8[:], pattern=[[1, 1]], base=0,
             channel_multiplier=1, allow_small_or_imprecise_dtypes=True,
             inc=cx.sG)
        p.wait("g", cx.sG)
        for e in range(NEXP):
            p.op("g", "tensor_scalar", sel8[e][:],
                 ridx8[:].broadcast_to([8, 128]), float(e), None,
                 ALU.is_equal)
        p.op("g", "memset", ident_f[:], 0.0)
        p.op("g", "affine_select", out=ident_f[:], in_=ident_f[:],
             compare_op=ALU.not_equal, fill=1.0, base=0,
             pattern=[[-1, 128]], channel_multiplier=1)
        p.op("g", "iota", revidx[:], pattern=[[1, 8]], base=0,
             channel_multiplier=0, allow_small_or_imprecise_dtypes=True,
             inc=cx.sG)
        p.wait("v", cx.sG)
        p.op("v", "tensor_scalar", revidx[:], revidx[:], -1.0, 8.0, ALU.mult,
             ALU.add, inc=cx.sV)
        scr["cpack"] = dict(gw=gw, gbias=gbias, b1_sb=b1_sb, b2_sb=b2_sb,
                            sel8=sel8, ident_f=ident_f, revidx=revidx)
    cp = scr["cpack"]
    gw = cp["gw"]; gbias = cp["gbias"]; b1_sb = cp["b1_sb"]
    b2_sb = cp["b2_sb"]; sel8 = cp["sel8"]; ident_f = cp["ident_f"]
    revidx = cp["revidx"]
    prep_g = cx.sG.n
    prep_v = cx.sV.n

    Rt = cx.sbs(esM, f"o{u}_Rt", [128, NCH * 512], F32)
    Ro = cx.sbs(esM, f"o{u}_Ro", [128, NCH * 512], F32)
    xnb = cx.sbs(esM, f"o{u}_xnb", [128, NCH * 512], BF16)
    xnf = cx.sbs(esM, f"o{u}_xnf", [128, NCH * 512], F32)
    sq = [cx.sbs(esM, f"o{u}_sq{i}", [128, 512], BF16) for i in range(2)]
    scrf = cx.sbs(esM, f"o{u}_scrf", [128, 512], F32)
    mu_sb = cx.sbs(esM, f"o{u}_mu", [1, 512], F32)
    var_sb = cx.sbs(esM, f"o{u}_var", [1, 512], F32)
    mu2_sb = cx.sbs(esM, f"o{u}_mu2", [1, 512], F32)
    g_sb = cx.sbs(esM, f"o{u}_gsb", [8, 512], F32)
    gts = cx.sbs(esM, f"o{u}_gts", [128, 4 * 8], F32)
    mx = cx.sbs(esM, f"o{u}_mx", [128, 1], F32)
    eq = cx.sbs(esM, f"o{u}_eq", [128, 8], F32)
    msk = cx.sbs(esM, f"o{u}_msk", [128, 4 * 8], BF16)
    mskT = cx.sbs(esM, f"o{u}_mskT", [8, 512], BF16)
    mrep = cx.sbs(esM, f"o{u}_mrep", [128, NEXP * 512], F32)
    he = cx.sbs(esM, f"o{u}_he", [128, NFC2 * 512], BF16)
    acc = cx.sbs(esM, f"o{u}_acc", [128, NCH * 512], F32)
    w1s = [cx.sbs(esM, f"o{u}_w1s{i}", [128, 8 * 512], BF16) for i in range(2)]
    w2s = [cx.sbs(esM, f"o{u}_w2s{i}", [128, 4 * 1024], BF16) for i in range(2)]

    w1v = [scr["w1_bf"][e].rearrange("(c p) f -> p c f", p=128)
           for e in range(NEXP)]
    w2v = [scr["w2_bf"][e].rearrange("(g p) d -> p g d", p=128)
           for e in range(NEXP)]

    rt_free = 0
    ro_st = 0
    w1_free = [0, 0]
    w2_free = [0, 0]
    for tt in range(NTT):
        tsl = slice(tt * 512, (tt + 1) * 512)
        p.wait("q", cx.sV, rt_free)
        p.op("q", "dma_start",
             out=Rt[:].rearrange("p (c t) -> p c t", c=NCH),
             in_=r_in_v[:, :, tsl], inc=cx.ld)
        p.wait("t", cx.ld)
        p.wait("s", cx.ld)
        emit_ln_tile(cx, Rt[:], sq, mu_sb[:], var_sb[:], mu2_sb[:])
        emit_xn_tile(cx, Rt[:], xnf[:], scrf)
        p.wait("v", cx.sV)
        p.op("v", "tensor_copy", xnb[:], xnf[:], inc=cx.sV)
        xn_done = cx.sV.n
        if DBG >= 1:
            p.wait("t", cx.sV, xn_done)
            for c in range(NCH):
                p.op("t", "matmul", cx.bank(4, 8), gw[:, c * 8:(c + 1) * 8],
                     xnf[:, c * 512:(c + 1) * 512], start=(c == 0),
                     stop=(c == NCH - 1), inc=(cx.sT if c == NCH - 1 else None))
            p.wait("v", cx.sT)
            p.wait("v", cx.sV, prep_v)
            p.op("v", "tensor_scalar", g_sb[:], cx.bank(4, 8), gbias[0:8, 0:1],
                 None, ALU.add, inc=cx.sV)
            p.wait("t", cx.sV)
            p.wait("t", cx.sG, prep_g)
            for st in range(4):
                p.op("t", "transpose",
                     cx.psa[0:128, 5 * 512 + st * 8:5 * 512 + (st + 1) * 8],
                     g_sb[:, st * 128:(st + 1) * 128], ident_f[0:8, 0:8],
                     inc=(cx.sT if st == 3 else None))
            p.wait("v", cx.sT)
            for st in range(4):
                p.op("v", "tensor_copy", gts[:, st * 8:(st + 1) * 8],
                     cx.psa[0:128, 5 * 512 + st * 8:5 * 512 + (st + 1) * 8],
                     inc=(cx.sV if st == 3 else None))
            gts_done = cx.sV.n
            if DBG >= 2:
                for st in range(4):
                    ss = slice(st * 8, (st + 1) * 8)
                    p.op("v", "reduce_max", mx[:], gts[:, ss], axis=AX.X, inc=cx.sV)
                    p.wait("v", cx.sV)
                    p.op("v", "tensor_scalar", eq[:], gts[:, ss], mx[:, 0:1], None,
                         ALU.is_ge, inc=cx.sV)
                    p.wait("v", cx.sV)
                    p.op("v", "tensor_mul", eq[:], eq[:], revidx[:], inc=cx.sV)
                    p.wait("v", cx.sV)
                    p.op("v", "reduce_max", mx[:], eq[:], axis=AX.X, inc=cx.sV)
                    p.wait("v", cx.sV)
                    p.op("v", "tensor_scalar", msk[:, ss], eq[:], mx[:, 0:1], None,
                         ALU.is_ge, inc=cx.sV)
                    p.wait("v", cx.sV)
                msk_done = cx.sV.n
            if DBG >= 3:
                p.wait("t", cx.sV, msk_done)
                for st in range(4):
                    p.op("t", "transpose",
                         cx.psa[0:8, 5 * 512:6 * 512].bitcast(BF16)[:, st * 128:
                                                                    (st + 1) * 128],
                         msk[:, st * 8:(st + 1) * 8], cx.ident[:],
                         inc=(cx.sT if st == 3 else None))
                p.wait("s", cx.sT)
                p.op("s", "activation", out=mskT[:],
                     in_=cx.psa[0:8, 5 * 512:6 * 512].bitcast(BF16)[:, 0:512],
                     func=AF.Copy, inc=cx.sS)
                mrep_mm = {}
                for e in range(NEXP):
                    bk = 6 + e % 2
                    p.wait("t", cx.sS)
                    if e >= 2:
                        p.wait("t", cx.sS, mrep_mm[e - 2])
                    p.op("t", "matmul", cx.bank(bk), sel8[e][:], mskT[:],
                         start=True, stop=True, inc=cx.sT)
                    p.wait("s", cx.sT)
                    p.op("s", "activation", out=mrep[:, e * 512:(e + 1) * 512],
                         in_=cx.bank(bk), func=AF.Copy, inc=cx.sS)
                    mrep_mm[e] = cx.sS.n
                mrep_done = cx.sS.n
            if DBG >= 4:
                acc_done = 0
                for e in range(NEXP):
                    he_evac = {}
                    for fg in range(NFC2 // 4):
                        wb = fg % 2
                        p.wait("g", cx.sT, w1_free[wb])
                        p.op("g", "dma_start",
                             out=w1s[wb][:].rearrange("p (c f) -> p c f", c=NCH),
                             in_=w1v[e][:, :, fg * 512:(fg + 1) * 512], inc=cx.ldg)
                        p.wait("t", cx.ldg)
                        for fl in range(4):
                            fc = fg * 4 + fl
                            bk = fc % 2
                            if fc >= 2:
                                p.wait("t", cx.sS, he_evac[fc - 2])
                            else:
                                p.wait("t", cx.sV, acc_done)
                                if not (e == 0 and tt == 0):
                                    p.wait("t", cx.sS)
                            for c in range(NCH):
                                p.op("t", "matmul", cx.bank(bk),
                                     w1s[wb][:, c * 512 + fl * 128:
                                             c * 512 + (fl + 1) * 128],
                                     xnb[:, c * 512:(c + 1) * 512],
                                     start=(c == 0), stop=(c == NCH - 1),
                                     inc=(cx.sT if c == NCH - 1 else None))
                            if fl == 3:
                                w1_free[wb] = cx.sT.n
                            p.wait("s", cx.sT)
                            p.op("s", "activation",
                                 out=he[:, fc * 512:(fc + 1) * 512],
                                 in_=cx.bank(bk), func=AF.Relu,
                                 bias=b1_sb[:, e * 32 + fc:e * 32 + fc + 1],
                                 inc=cx.sS)
                            he_evac[fc] = cx.sS.n
                    all_he = cx.sS.n
                    p.wait("t", cx.sS, all_he)
                    if e > 0:
                        p.wait("t", cx.sV, acc_done)
                    else:
                        p.wait("t", cx.sS, mrep_done)
                    for kg in range(8):
                        wb = kg % 2
                        p.wait("g", cx.sT, w2_free[wb])
                        p.op("g", "dma_start",
                             out=w2s[wb][:].rearrange("p (g o) -> p g o", g=4),
                             in_=w2v[e][:, kg * 4:(kg + 1) * 4, :], inc=cx.ldg)
                        p.wait("t", cx.ldg)
                        for kl in range(4):
                            kf = kg * 4 + kl
                            for oc in range(NCH):
                                p.op("t", "matmul", cx.bank(oc),
                                     w2s[wb][:, kl * 1024 + oc * 128:
                                             kl * 1024 + (oc + 1) * 128],
                                     he[:, kf * 512:(kf + 1) * 512],
                                     start=(kf == 0), stop=(kf == 31),
                                     inc=(cx.sT if kl == 3 and oc == NCH - 1
                                          else None))
                        w2_free[wb] = cx.sT.n
                    p.wait("v", cx.sT)
                    if e == 0:
                        p.wait("v", cx.sS, mrep_done)
                    for oc in range(NCH):
                        csl = slice(oc * 512, (oc + 1) * 512)
                        p.op("v", "scalar_tensor_tensor", out=scrf[:],
                             in0=cx.bank(oc), scalar=b2_sb[:, e * 8 + oc:e * 8 + oc + 1],
                             in1=mrep[:, e * 512:(e + 1) * 512], op0=ALU.add,
                             op1=ALU.mult)
                        if e == 0:
                            p.op("v", "tensor_copy", acc[:, csl], scrf[:],
                                 inc=(cx.sV if oc == NCH - 1 else None))
                        else:
                            p.op("v", "tensor_add", acc[:, csl], acc[:, csl],
                                 scrf[:], inc=(cx.sV if oc == NCH - 1 else None))
                    acc_done = cx.sV.n
        p.wait("v", cx.stq, ro_st)
        if DBG >= 4:
            for oc in range(NCH):
                csl = slice(oc * 512, (oc + 1) * 512)
                p.op("v", "scalar_tensor_tensor", out=Ro[:, csl],
                     in0=acc[:, csl], scalar=0.5, in1=Rt[:, csl],
                     op0=ALU.mult, op1=ALU.add,
                     inc=(cx.sV if oc == NCH - 1 else None))
        else:
            p.op("v", "tensor_copy", Ro[:], Rt[:], inc=cx.sV)
        rt_free = cx.sV.n
        p.wait("q", cx.sV, rt_free)
        p.op("q", "dma_start", out=r_out_v[:, :, tsl],
             in_=Ro[:].rearrange("p (c t) -> p c t", c=NCH), inc=cx.stq)
        ro_st = cx.stq.n
    p.wait("q", cx.stq, cx.stq.n)
    es_close = esM.close()
    cx.barrier()


NH = 16
HD = 64


def emit_attn(cx, r_in, r_out, W, scr):
    p = cx.p
    u = cx.uid()
    cx.barrier()
    r_in_v = r_in.rearrange("(c p) t -> p c t", p=128)
    r_out_v = r_out.rearrange("(c p) t -> p c t", p=128)
    xn_d_v = scr["xn_d"].rearrange("(c p) t -> p c t", p=128)
    q_d_v = scr["q_d"].rearrange("(c p) t -> p c t", p=128)
    k_d_v = scr["k_d"].rearrange("(c p) t -> p c t", p=128)
    st_d_v = scr["st_d"].rearrange("(c p) t -> p c t", p=128)
    st2_d_v = scr["st2_d"].rearrange("(c p) t -> p c t", p=128)
    v_d_v = scr["v_d"].rearrange("(kt p) (h d) -> p kt h d", p=128, h=NH)

    esA = cx.scope()
    wq = cx.sbs(esA, f"a{u}_wq", [128, 8 * 1024], BF16)
    wk = cx.sbs(esA, f"a{u}_wk", [128, 8 * 1024], BF16)
    wv = cx.sbs(esA, f"a{u}_wv", [128, 8 * 1024], BF16)
    wo = cx.sbs(esA, f"a{u}_wo", [128, 8 * 1024], BF16)
    bq = cx.sbs(esA, f"a{u}_bq", [128, 8], F32)
    bk = cx.sbs(esA, f"a{u}_bk", [128, 8], F32)
    bo = cx.sbs(esA, f"a{u}_bo", [128, 8], F32)
    lnw = cx.sbs(esA, f"a{u}_lnw", [128, 8], F32)
    lnb = cx.sbs(esA, f"a{u}_lnb", [128, 8], F32)
    bv_rep = cx.sbs(esA, f"a{u}_bvrep", [128, 1024], F32)
    bv_sb = cx.sbs(esA, f"a{u}_bv", [1, 1024], F32)
    hsel = cx.sbs(esA, f"a{u}_hsel", [128, 2], BF16)
    hrep_f = cx.sbs(esA, f"a{u}_hrepf", [2, 128], F32)
    idf2 = cx.sbs(esA, f"a{u}_idf2", [2, 2], F32)
    for nm, dst in (("bq", bq), ("bk", bk), ("bo", bo), ("lnw", lnw),
                    ("lnb", lnb)):
        p.op("q", "dma_start", out=dst[:], in_=W[nm], inc=cx.ld)
    p.op("q", "dma_start", out=bv_sb[:], in_=W["bv"], inc=cx.ld)
    with cx.scope() as esw:
        wf = cx.sbs(esw, f"a{u}_wf", [128, 8 * 1024], F32, side="right")
        for nm, dst in (("Wq", wq), ("Wk", wk), ("Wv", wv), ("Wo", wo)):
            p.wait("q", cx.sV)
            p.op("q", "dma_start",
                 out=wf[:].rearrange("p (c o) -> p c o", c=8),
                 in_=W[nm].rearrange("(c p) o -> p c o", p=128), inc=cx.ld)
            p.wait("v", cx.ld)
            p.op("v", "tensor_copy", dst[:], wf[:], inc=cx.sV)
    scr_pidx = cx.sbs(esA, f"a{u}_pidx", [128, 2], F32)
    scr_h2 = cx.sbs(esA, f"a{u}_h2", [2, 3 * 128], F32)
    hselF = cx.sbs(esA, f"a{u}_hselF", [128, 2], F32)
    p.op("g", "iota", scr_pidx[:, 0:1], pattern=[[1, 1]], base=0,
         channel_multiplier=1, allow_small_or_imprecise_dtypes=True,
         inc=cx.sG)
    p.wait("g", cx.sG)
    p.op("g", "tensor_scalar", hselF[:, 0:1], scr_pidx[:, 0:1], 64.0, None,
         ALU.is_lt)
    p.op("g", "tensor_scalar", hselF[:, 1:2], scr_pidx[:, 0:1], 64.0, None,
         ALU.is_ge, inc=cx.sG)
    p.wait("g", cx.sG)
    p.op("g", "tensor_copy", hsel[:], hselF[:], inc=cx.sG)
    p.op("g", "iota", scr_h2[:, 0:128], pattern=[[1, 128]], base=0,
         channel_multiplier=0, allow_small_or_imprecise_dtypes=True)
    p.op("g", "iota", scr_h2[:, 128:129], pattern=[[1, 1]], base=0,
         channel_multiplier=64, allow_small_or_imprecise_dtypes=True,
         inc=cx.sG)
    p.wait("g", cx.sG)
    p.op("g", "tensor_scalar", scr_h2[:, 0:128], scr_h2[:, 0:128],
         scr_h2[:, 128:129], None, ALU.subtract, inc=cx.sG)
    p.wait("g", cx.sG)
    p.op("g", "tensor_scalar", scr_h2[:, 128:256], scr_h2[:, 0:128], 0.0,
         None, ALU.is_ge)
    p.op("g", "tensor_scalar", scr_h2[:, 256:384], scr_h2[:, 0:128], 64.0,
         None, ALU.is_lt, inc=cx.sG)
    p.wait("g", cx.sG)
    p.op("g", "tensor_mul", hrep_f[:], scr_h2[:, 128:256],
         scr_h2[:, 256:384], inc=cx.sG)
    p.op("g", "memset", idf2[:], 0.0)
    p.op("g", "affine_select", out=idf2[:], in_=idf2[:],
         compare_op=ALU.not_equal, fill=1.0, base=0, pattern=[[-1, 2]],
         channel_multiplier=1, inc=cx.sG)
    prep_g = cx.sG.n
    p.wait("t", cx.ld)
    p.wait("t", cx.sV)
    for nt in range(2):
        p.op("t", "matmul", cx.bank(nt), cx.ones_fT[:],
             bv_sb[:, nt * 512:(nt + 1) * 512], start=True, stop=True,
             inc=(cx.sT if nt == 1 else None))
    p.wait("s", cx.sT)
    p.op("s", "activation", out=bv_rep[:], in_=cx.psa[:, 0:1024],
         func=AF.Copy, inc=cx.sS)
    bv_done = cx.sS.n

    esQ = cx.scope()
    Rt = cx.sbs(esQ, f"a{u}_Rt", [128, NCH * 512], F32)
    xn = cx.sbs(esQ, f"a{u}_xn", [128, NCH * 512], F32)
    zq = cx.sbs(esQ, f"a{u}_zq", [128, NCH * 512], BF16)
    sq = [cx.sbs(esQ, f"a{u}_sq{i}", [128, 512], BF16) for i in range(2)]
    scrf = cx.sbs(esQ, f"a{u}_scrf", [128, 512], F32)
    scrf2 = cx.sbs(esQ, f"a{u}_scrf2", [128, 512], F32)
    mu_sb = cx.sbs(esQ, f"a{u}_mu", [1, 512], F32)
    var_sb = cx.sbs(esQ, f"a{u}_var", [1, 512], F32)
    mu2_sb = cx.sbs(esQ, f"a{u}_mu2", [1, 512], F32)
    qkob = cx.sbs(esQ, f"a{u}_qkob", [128, 512], BF16)
    s2n = cx.sbs(esQ, f"a{u}_s2n", [2, 512], F32)
    qn_bf = cx.sbs(esQ, f"a{u}_qnbf", [128, 512], BF16)
    vout = cx.sbs(esQ, f"a{u}_vout", [128, 1024], BF16)

    for tt in range(NTT):
        tsl = slice(tt * 512, (tt + 1) * 512)
        p.wait("q", cx.sV)
        p.wait("q", cx.stq, cx.stq.n)
        p.op("q", "dma_start",
             out=Rt[:].rearrange("p (c t) -> p c t", c=NCH),
             in_=r_in_v[:, :, tsl], inc=cx.ld)
        p.wait("t", cx.ld)
        p.wait("s", cx.ld)
        emit_ln_tile(cx, Rt[:], sq, mu_sb[:], var_sb[:], mu2_sb[:])
        emit_xn_tile(cx, Rt[:], xn[:], scrf)
        p.wait("q", cx.sV)
        p.op("q", "dma_start", out=xn_d_v[:, :, tsl],
             in_=xn[:].rearrange("p (c t) -> p c t", c=NCH), inc=cx.stq)
        p.wait("t", cx.sV)
        emit_ln_tile(cx, xn[:], sq, mu_sb[:], var_sb[:], mu2_sb[:])
        p.wait("v", cx.ld)
        for c in range(NCH):
            csl = slice(c * 512, (c + 1) * 512)
            p.op("v", "tensor_sub", scrf[:], xn[:, csl], cx.bank(2))
            p.op("v", "tensor_mul", scrf2[:], scrf[:], cx.bank(3))
            p.op("v", "tensor_scalar", zq[:, csl], scrf2[:],
                 lnw[:, c:c + 1], lnb[:, c:c + 1], ALU.mult, ALU.add,
                 inc=(cx.sV if c == NCH - 1 else None))
        zq_done = cx.sV.n
        for which, wmat, bias, dst_v in (("q", wq, bq, q_d_v),
                                         ("k", wk, bk, k_d_v)):
            for oc in range(NCH):
                p.wait("t", cx.sV, zq_done)
                p.wait("t", cx.stq, cx.stq.n)
                for c in range(NCH):
                    p.op("t", "matmul", cx.bank(4),
                         wmat[:, c * 1024 + oc * 128:c * 1024 + (oc + 1) * 128],
                         zq[:, c * 512:(c + 1) * 512], start=(c == 0),
                         stop=(c == NCH - 1),
                         inc=(cx.sT if c == NCH - 1 else None))
                p.wait("v", cx.sT)
                p.op("v", "tensor_scalar", scrf[:], cx.bank(4),
                     bias[:, oc:oc + 1], None, ALU.add, inc=cx.sV)
                p.wait("s", cx.sV)
                p.op("s", "activation", out=qkob[:], in_=scrf[:],
                     func=AF.Square, inc=cx.sS)
                p.wait("t", cx.sS)
                p.op("t", "matmul", cx.banksl(5, slice(0, 512), 2),
                     hsel[:], qkob[:], start=True, stop=True, inc=cx.sT)
                p.wait("v", cx.sT)
                p.op("v", "tensor_copy", s2n[:], cx.banksl(5, slice(0, 512), 2),
                     inc=cx.sV)
                p.wait("s", cx.sV)
                p.op("s", "activation", out=s2n[:], in_=s2n[:], func=AF.Ln)
                p.op("s", "activation", out=s2n[:], in_=s2n[:], func=AF.Exp,
                     scale=cx.constf(0.5), inc=cx.sS)
                p.wait("v", cx.sS)
                p.op("v", "tensor_scalar_max", s2n[:], s2n[:], 1e-12,
                     inc=cx.sV)
                p.wait("v", cx.sV)
                p.op("v", "reciprocal", s2n[:], s2n[:], inc=cx.sV)
                p.wait("t", cx.sV)
                p.op("t", "matmul", cx.banksl(5, slice(0, 512), 128),
                     hrep_f[:], s2n[:], start=True, stop=True, inc=cx.sT)
                p.wait("v", cx.sT)
                p.op("v", "tensor_mul", qn_bf[:], scrf[:],
                     cx.banksl(5, slice(0, 512), 128), inc=cx.sV)
                p.wait("q", cx.sV)
                p.op("q", "dma_start", out=dst_v[:, oc, tsl], in_=qn_bf[:],
                     inc=cx.stq)
        p.wait("t", cx.sS, bv_done)
        for stile in range(4):
            p.wait("t", cx.stq, cx.stq.n)
            p.wait("t", cx.sV)
            for half in range(2):
                for c in range(NCH):
                    p.op("t", "matmul", cx.bank(6 + half),
                         zq[:, c * 512 + stile * 128:
                            c * 512 + (stile + 1) * 128],
                         wv[:, c * 1024 + half * 512:
                            c * 1024 + (half + 1) * 512],
                         start=(c == 0), stop=(c == NCH - 1),
                         inc=(cx.sT if c == NCH - 1 else None))
            p.wait("v", cx.sT)
            for half in range(2):
                p.op("v", "tensor_add", vout[:, half * 512:(half + 1) * 512],
                     cx.bank(6 + half), bv_rep[:, half * 512:(half + 1) * 512],
                     inc=(cx.sV if half == 1 else None))
            p.wait("q", cx.sV)
            p.op("q", "dma_start",
                 out=scr["v_d"][(tt * 4 + stile) * 128:
                                (tt * 4 + stile + 1) * 128, :],
                 in_=vout[:], inc=cx.stq)
    p.wait("q", cx.stq, cx.stq.n)
    esQ.close()
    cx.barrier()

    esB = cx.scope()
    kT = cx.sbs(esB, f"a{u}_kT", [128, T], BF16)
    stT = cx.sbs(esB, f"a{u}_stT", [128, T], BF16)
    stN = cx.sbs(esB, f"a{u}_stN", [128, T], BF16)
    vwk = [cx.sbs(esB, f"a{u}_vwk{i}", [128, 32 * 65], BF16) for i in range(2)]
    pT = [cx.sbs(esB, f"a{u}_pT0", [128, 32 * 1024], BF16)]
    sqc = cx.sbs(esB, f"a{u}_sqc", [128, T], BF16)
    cbs = [cx.sbs(esB, f"a{u}_cbs{i}", [128, 1], F32) for i in range(2)]
    negc = cx.sbs(esB, f"a{u}_negc", [2, 1], F32)
    negcT = cx.sbs(esB, f"a{u}_negcT", [1, 2], F32)
    rq = cx.sbs(esB, f"a{u}_rq", [1, 512], F32)
    avsb = [cx.sbs(esB, f"a{u}_avsb{i}", [65, 512], F32) for i in range(2)]
    p.op("g", "memset", vwk[0][:].rearrange("p (kt d) -> p kt d", kt=32)[:, :, 64:65], 1.0)
    p.op("g", "memset", vwk[1][:].rearrange("p (kt d) -> p kt d", kt=32)[:, :, 64:65], 1.0, inc=cx.sG)
    vones = cx.sG.n

    for s_ in range(2):
        src_v = q_d_v if s_ == 0 else st_d_v
        out_v = st_d_v if s_ == 0 else st2_d_v
        for hp in range(NH // 2):
            p.wait("q", cx.sT)
            p.wait("q", cx.stq, cx.stq.n)
            p.op("q", "dma_start", out=kT[:],
                 in_=k_d_v[:, hp, :], inc=cx.ld)
            p.op("q", "dma_start", out=stT[:],
                 in_=src_v[:, hp, :], inc=cx.ld)
            p.wait("g", cx.sT)
            p.wait("g", cx.sG, vones)
            for b in range(2):
                p.op("g", "dma_start",
                     out=vwk[b][:].rearrange("p (kt d) -> p kt d",
                                             kt=32)[:, :, 0:64],
                     in_=v_d_v[:, :, 2 * hp + b, :], inc=cx.ldg)
            ld_hp = cx.ld.n
            ldg_hp = cx.ldg.n
            p.wait("s", cx.ld, ld_hp)
            p.op("s", "activation", out=sqc[:], in_=stT[:], func=AF.Square,
                 inc=cx.sS)
            p.wait("t", cx.sS)
            p.wait("t", cx.sV)
            for nt in range(NTT):
                p.op("t", "matmul", cx.psa[0:2, nt * 512:(nt + 1) * 512],
                     hsel[:], sqc[:, nt * 512:(nt + 1) * 512],
                     start=True, stop=True,
                     inc=(cx.sT if nt == NTT - 1 else None))
            p.wait("v", cx.sT)
            p.op("v", "reduce_max", negc[:], cx.psa[0:2, 0:4096], axis=AX.X,
                 inc=cx.sV)
            p.wait("t", cx.sV)
            p.op("t", "matmul", cx.banksl(1, slice(0, 2), 1), negc[:],
                 idf2[:], start=True, stop=True, inc=cx.sT)
            p.wait("v", cx.sT)
            p.op("v", "tensor_copy", negcT[:], cx.banksl(1, slice(0, 2), 1),
                 inc=cx.sV)
            p.wait("v", cx.sV)
            p.op("v", "reduce_max", negcT[:, 0:1], negcT[:], axis=AX.X,
                 inc=cx.sV)
            p.wait("s", cx.sV)
            p.op("s", "activation", out=negcT[:, 0:1], in_=negcT[:, 0:1],
                 func=AF.Ln)
            p.op("s", "activation", out=negcT[:, 0:1], in_=negcT[:, 0:1],
                 func=AF.Exp, scale=cx.constf(0.5), inc=cx.sS)
            p.wait("v", cx.sS)
            p.op("v", "tensor_scalar_mul", negcT[:, 0:1], negcT[:, 0:1],
                 -1.0, inc=cx.sV)
            p.wait("t", cx.sV)
            p.op("t", "matmul", cx.banksl(1, slice(0, 1), 128),
                 cx.ones_fT[:], negcT[:, 0:1], start=True, stop=True,
                 inc=cx.sT)
            p.wait("s", cx.sT)
            p.op("s", "activation", out=cbs[0][:],
                 in_=cx.banksl(1, slice(0, 1), 128), func=AF.Copy,
                 inc=cx.sS)
            cb_done = cx.sS.n
            exp_done = {}
            avb_free = {0: 0, 1: 0}
            avb_free_s = {0: 0, 1: 0}
            for qc in range(NTT):
                qsl = slice(qc * 512, (qc + 1) * 512)
                av0 = 4 if (qc % 2 == 0) else 6
                for kt in range(33):
                    if kt < 32:
                        bb = 2 * (kt % 2)
                        if kt >= 2:
                            p.wait("t", cx.sS, exp_done[kt - 2])
                        elif qc == 0:
                            p.wait("t", cx.sS, cb_done)
                        else:
                            p.wait("t", cx.sS, prev_exp[kt - 2])
                        if kt == 0:
                            p.wait("t", cx.sV, avb_free[qc % 2])
                            p.wait("t", cx.sS, avb_free_s[qc % 2])
                        for b in range(2):
                            p.op("t", "matmul", cx.bank(bb + b),
                                 kT[b * 64:(b + 1) * 64,
                                    kt * 128:(kt + 1) * 128],
                                 stT[b * 64:(b + 1) * 64, qsl],
                                 start=True, stop=True,
                                 tile_position=(64 * b, 0),
                                 inc=(cx.sT if b == 1 else None))
                        mmd = cx.sT.n
                        p.wait("s", cx.sT, mmd)
                        p.op("s", "activation",
                             out=pT[0][:, kt * 1024:(kt + 1) * 1024],
                             in_=cx.psa[:, bb * 512:(bb + 2) * 512],
                             func=AF.Exp, bias=cbs[0][:, 0:1], inc=cx.sS)
                        exp_done[kt] = cx.sS.n
                    if kt >= 1:
                        ka = kt - 1
                        p.wait("t", cx.sS, exp_done[ka])
                        for b in range(2):
                            p.op("t", "matmul", cx.bank(av0 + b, 65),
                                 vwk[b][:, ka * 65:(ka + 1) * 65],
                                 pT[0][:, ka * 1024 + b * 512:
                                        ka * 1024 + (b + 1) * 512],
                                 start=(ka == 0), stop=(ka == 31),
                                 inc=(cx.sT if ka == 31 and b == 1 else None))
                prev_exp = {-2: exp_done.get(30, 0), -1: exp_done.get(31, 0)}
                for b in range(2):
                    p.wait("s", cx.sT)
                    p.op("s", "activation", out=avsb[b][:],
                         in_=cx.psa[0:65, (av0 + b) * 512:(av0 + b) * 512 + 512],
                         func=AF.Copy, inc=cx.sS)
                    p.wait("v", cx.sS)
                    p.op("v", "reciprocal", rq[:], avsb[b][64:65, :],
                         inc=cx.sV)
                    p.wait("t", cx.sV)
                    p.op("t", "matmul",
                         cx.banksl(av0 + b, slice(0, 512), 64),
                         cx.ones_fT[:, 0:64], rq[:], start=True, stop=True,
                         inc=cx.sT)
                    p.wait("v", cx.sT)
                    if qc <= 1:
                        p.wait("v", cx.stq, cx.stq.n)
                    p.op("v", "tensor_mul", stN[b * 64:(b + 1) * 64, qsl],
                         avsb[b][0:64, :],
                         cx.banksl(av0 + b, slice(0, 512), 64), inc=cx.sV)
                avb_free[qc % 2] = cx.sV.n
                avb_free_s[qc % 2] = cx.sS.n
            p.wait("q", cx.sV, max(avb_free.values()))
            p.op("q", "dma_start", out=out_v[:, hp, :], in_=stN[:],
                 inc=cx.stq)
    p.wait("q", cx.stq, cx.stq.n)
    esB.close()
    cx.barrier()

    esC = cx.scope()
    Rt2 = cx.sbs(esC, f"a{u}_Rt2", [128, NCH * 512], F32)
    xn2 = cx.sbs(esC, f"a{u}_xn2", [128, NCH * 512], F32)
    st2 = cx.sbs(esC, f"a{u}_st2", [128, NCH * 512], BF16)
    Ro = cx.sbs(esC, f"a{u}_Ro", [128, NCH * 512], F32)
    scrg = cx.sbs(esC, f"a{u}_scrg", [128, 512], F32)
    ro_st = 0
    for tt in range(NTT):
        tsl = slice(tt * 512, (tt + 1) * 512)
        p.wait("q", cx.sV)
        p.op("q", "dma_start",
             out=Rt2[:].rearrange("p (c t) -> p c t", c=NCH),
             in_=r_in_v[:, :, tsl], inc=cx.ld)
        p.op("q", "dma_start",
             out=xn2[:].rearrange("p (c t) -> p c t", c=NCH),
             in_=xn_d_v[:, :, tsl], inc=cx.ld)
        p.op("q", "dma_start",
             out=st2[:].rearrange("p (c t) -> p c t", c=NCH),
             in_=st2_d_v[:, :, tsl], inc=cx.ld)
        ld_t = cx.ld.n
        p.wait("t", cx.ld, ld_t)
        fc_evac = {}
        for oc in range(NCH):
            bk = oc % 2
            if oc >= 2:
                p.wait("t", cx.sV, fc_evac[oc - 2])
            elif tt > 0:
                p.wait("t", cx.sV, fc_evac_prev[NCH - 2 + oc])
            for c in range(NCH):
                p.op("t", "matmul", cx.bank(bk),
                     wo[:, c * 1024 + oc * 128:c * 1024 + (oc + 1) * 128],
                     st2[:, c * 512:(c + 1) * 512],
                     start=(c == 0), stop=(c == NCH - 1),
                     inc=(cx.sT if c == NCH - 1 else None))
            p.wait("v", cx.sT)
            if oc == 0:
                p.wait("v", cx.stq, ro_st)
            csl = slice(oc * 512, (oc + 1) * 512)
            p.op("v", "scalar_tensor_tensor", out=scrg[:], in0=cx.bank(bk),
                 scalar=bo[:, oc:oc + 1], in1=xn2[:, csl], op0=ALU.add,
                 op1=ALU.add, inc=cx.sV)
            p.wait("v", cx.sV)
            p.op("v", "tensor_add", Ro[:, csl], scrg[:], Rt2[:, csl],
                 inc=cx.sV)
            fc_evac[oc] = cx.sV.n
        fc_evac_prev = fc_evac
        p.wait("q", cx.sV)
        p.op("q", "dma_start", out=r_out_v[:, :, tsl],
             in_=Ro[:].rearrange("p (c t) -> p c t", c=NCH), inc=cx.stq)
        ro_st = cx.stq.n
    p.wait("q", cx.stq, cx.stq.n)
    esC.close()
    esA.close()
    cx.barrier()


from contextlib import ExitStack
from concourse.bass_utils import run_bass_kernel_spmd

NCORES = 8


def _build_ffn0():
    nc = bass.Bass("TRN2", target_bir_lowering=False)
    r_in = nc.dram_tensor("r_in", [1024, 4096], F32, kind="ExternalInput").ap()
    wgT_i = nc.dram_tensor("wgT_i", [1024, 3072], I32, kind="ExternalInput").ap()
    wuT_i = nc.dram_tensor("wuT_i", [1024, 3072], I32, kind="ExternalInput").ap()
    wdT_i = nc.dram_tensor("wdT_i", [3072, 1024], I32, kind="ExternalInput").ap()
    r_out = nc.dram_tensor("r_out", [1024, 4096], F32, kind="ExternalOutput").ap()
    wgT = nc.dram_tensor("wgT", [1024, 3072], BF16, kind="Internal").ap()
    wuT = nc.dram_tensor("wuT", [1024, 3072], BF16, kind="Internal").ap()
    wdT = nc.dram_tensor("wdT", [3072, 1024], BF16, kind="Internal").ap()
    es = ExitStack()
    cx = Ctx(nc, es)
    build_consts(cx)
    prep_ffn0_weights(cx, wgT_i, wuT_i, wdT_i, wgT, wuT, wdT)
    emit_ffn0(cx, r_in, r_out, wgT, wuT, wdT)
    cx.finalize()
    with nc.Block() as block:
        cx.p.replay(block)
    es.close()
    return nc


def _build_mamba():
    nc = bass.Bass("TRN2", target_bir_lowering=False)
    r_in = nc.dram_tensor("r_in", [1024, 4096], F32, kind="ExternalInput").ap()
    r_out = nc.dram_tensor("r_out", [1024, 4096], F32, kind="ExternalOutput").ap()
    Wsh = dict(fc1_wT=[1024, 1024], fc1_b=[128, 8], inpT=[1024, 4096],
               conv_w=[128, 64], conv_b=[128, 16], xpT=[2048, 96],
               dtpT=[64, 2048], dtp_b=[128, 16], A_log=[128, 256],
               Dv=[128, 16], opT=[2048, 1024], fc2_wT=[1024, 1024],
               fc2_b=[128, 8])
    W = {k: nc.dram_tensor(k, s, F32, kind="ExternalInput").ap()
         for k, s in Wsh.items()}
    scr = {
        "inpT_bf": nc.dram_tensor("inpT_bf", [1024, 4096], BF16, kind="Internal").ap(),
        "opT_bf": nc.dram_tensor("opT_bf", [2048, 1024], BF16, kind="Internal").ap(),
        "u_d": nc.dram_tensor("u_d", [2048, 4096], F32, kind="Internal").ap(),
        "z_d": nc.dram_tensor("z_d", [2048, 4096], F32, kind="Internal").ap(),
        "y_d": nc.dram_tensor("y_d", [2048, 4096], F32, kind="Internal").ap(),
    }
    es = ExitStack()
    cx = Ctx(nc, es)
    build_consts(cx)
    emit_mamba(cx, r_in, r_out, W, scr)
    cx.finalize()
    with nc.Block() as block:
        cx.p.replay(block)
    es.close()
    return nc


def _build_moe():
    nc = bass.Bass("TRN2", target_bir_lowering=False)
    r_in = nc.dram_tensor("r_in", [1024, 4096], F32, kind="ExternalInput").ap()
    r_out = nc.dram_tensor("r_out", [1024, 4096], F32, kind="ExternalOutput").ap()
    W = {
        "gate_wT": nc.dram_tensor("gate_wT", [1024, 8], F32, kind="ExternalInput").ap(),
        "gbias": nc.dram_tensor("gbias", [8, 2], F32, kind="ExternalInput").ap(),
        "b1": nc.dram_tensor("b1", [128, 256], F32, kind="ExternalInput").ap(),
        "b2": nc.dram_tensor("b2", [128, 64], F32, kind="ExternalInput").ap(),
        "w1": nc.dram_tensor("w1", [8, 1024, 4096], F32, kind="ExternalInput").ap(),
        "w2": nc.dram_tensor("w2", [8, 4096, 1024], F32, kind="ExternalInput").ap(),
    }
    scr = {
        "w1_bf": nc.dram_tensor("w1_bf", [8, 1024, 4096], BF16, kind="Internal").ap(),
        "w2_bf": nc.dram_tensor("w2_bf", [8, 4096, 1024], BF16, kind="Internal").ap(),
    }
    es = ExitStack()
    cx = Ctx(nc, es)
    build_consts(cx)
    emit_moe(cx, r_in, r_out, W, scr)
    cx.finalize()
    with nc.Block() as block:
        cx.p.replay(block)
    es.close()
    return nc


def _build_attn():
    nc = bass.Bass("TRN2", target_bir_lowering=False)
    r_in = nc.dram_tensor("r_in", [1024, 4096], F32, kind="ExternalInput").ap()
    r_out = nc.dram_tensor("r_out", [1024, 4096], F32, kind="ExternalOutput").ap()
    W = {}
    for nm in ("Wq", "Wk", "Wv", "Wo"):
        W[nm] = nc.dram_tensor(nm, [1024, 1024], F32, kind="ExternalInput").ap()
    for nm in ("bq", "bk", "bo", "lnw", "lnb"):
        W[nm] = nc.dram_tensor(nm, [128, 8], F32, kind="ExternalInput").ap()
    W["bv"] = nc.dram_tensor("bv", [1, 1024], F32, kind="ExternalInput").ap()
    scr = {
        "xn_d": nc.dram_tensor("xn_d", [1024, 4096], F32, kind="Internal").ap(),
        "q_d": nc.dram_tensor("q_d", [1024, 4096], BF16, kind="Internal").ap(),
        "k_d": nc.dram_tensor("k_d", [1024, 4096], BF16, kind="Internal").ap(),
        "st_d": nc.dram_tensor("st_d", [1024, 4096], BF16, kind="Internal").ap(),
        "st2_d": nc.dram_tensor("st2_d", [1024, 4096], BF16, kind="Internal").ap(),
        "v_d": nc.dram_tensor("v_d", [4096, 1024], BF16, kind="Internal").ap(),
    }
    es = ExitStack()
    cx = Ctx(nc, es)
    build_consts(cx)
    emit_attn(cx, r_in, r_out, W, scr)
    cx.finalize()
    with nc.Block() as block:
        cx.p.replay(block)
    es.close()
    return nc


def kernel_unfused(**inputs):
    g = inputs
    ct = np.ascontiguousarray
    r = [ct(g["L0_attn_x"][i].T) for i in range(NCORES)]

    ffn0_w = {
        "wgT_i": ct(g["L0_ffn_w_gate"].T),
        "wuT_i": ct(g["L0_ffn_w_up"].T),
        "wdT_i": ct(g["L0_ffn_w_down"].T),
    }
    mamba_w = {
        "fc1_wT": ct(np.asarray(g["L0_attn_fc1_w"]).T),
        "fc1_b": to_pcol(g["L0_attn_fc1_b"]),
        "inpT": ct(np.asarray(g["L0_attn_in_proj_w"]).T),
        "conv_w": to_pcol(np.asarray(g["L0_attn_conv_w"])),
        "conv_b": to_pcol(g["L0_attn_conv_b"]),
        "xpT": ct(np.asarray(g["L0_attn_x_proj_w"]).T),
        "dtpT": ct(np.asarray(g["L0_attn_dt_proj_w"]).T),
        "dtp_b": to_pcol(g["L0_attn_dt_proj_b"]),
        "A_log": to_pcol(np.asarray(g["L0_attn_A_log"])),
        "Dv": to_pcol(g["L0_attn_D"]),
        "opT": ct(np.asarray(g["L0_attn_out_proj_w"]).T),
        "fc2_wT": ct(np.asarray(g["L0_attn_fc2_w"]).T),
        "fc2_b": to_pcol(g["L0_attn_fc2_b"]),
    }
    moe_w = {
        "gate_wT": ct(np.asarray(g["L1_ffn_gate_w"]).T),
        "gbias": ct(np.stack([np.asarray(g["L1_ffn_gate_b"]),
                              np.asarray(g["L1_ffn_expert_bias"])], axis=1)),
        "b1": np.concatenate([to_pcol(np.asarray(g["L1_ffn_b1"])[e])
                              for e in range(8)], axis=1),
        "b2": np.concatenate([to_pcol(np.asarray(g["L1_ffn_b2"])[e])
                              for e in range(8)], axis=1),
        "w1": ct(np.asarray(g["L1_ffn_w1"])),
        "w2": ct(np.asarray(g["L1_ffn_w2"])),
    }
    attn_w = {
        "Wq": ct(np.asarray(g["L1_attn_Wq"])),
        "Wk": ct(np.asarray(g["L1_attn_Wk"])),
        "Wv": ct(np.asarray(g["L1_attn_Wv"])),
        "Wo": ct(np.asarray(g["L1_attn_Wo"])),
        "bq": to_pcol(g["L1_attn_bq"]),
        "bk": to_pcol(g["L1_attn_bk"]),
        "bo": to_pcol(g["L1_attn_bo"]),
        "lnw": to_pcol(g["L1_attn_ln_w"]),
        "lnb": to_pcol(g["L1_attn_ln_b"]),
        "bv": ct(np.asarray(g["L1_attn_bv"]).reshape(1, 1024)),
    }

    nc_ffn = _build_ffn0()
    nc_mamba = _build_mamba()
    nc_moe = _build_moe()
    nc_attn = _build_attn()

    def launch(nc, wdict):
        nonlocal r
        in_maps = [dict(wdict, r_in=r[i]) for i in range(NCORES)]
        res = run_bass_kernel_spmd(nc, in_maps, core_ids=list(range(NCORES)))
        r = [res.results[i]["r_out"] for i in range(NCORES)]

    launch(nc_ffn, ffn0_w)
    launch(nc_mamba, mamba_w)
    launch(nc_ffn, ffn0_w)
    launch(nc_moe, moe_w)
    launch(nc_attn, attn_w)
    launch(nc_moe, moe_w)

    out = np.stack([r[i].T for i in range(NCORES)], axis=0)
    return out.astype(np.float32)


def _build_fused():
    nc = bass.Bass("TRN2", target_bir_lowering=False)
    x_in = nc.dram_tensor("x_in", [1024, 4096], F32, kind="ExternalInput").ap()
    r_fin = nc.dram_tensor("r_fin", [1024, 4096], F32,
                           kind="ExternalOutput").ap()
    rbuf = [nc.dram_tensor(f"rb{i}", [1024, 4096], F32, kind="Internal").ap()
            for i in range(5)]
    wgT_i = nc.dram_tensor("wgT_i", [1024, 3072], I32, kind="ExternalInput").ap()
    wuT_i = nc.dram_tensor("wuT_i", [1024, 3072], I32, kind="ExternalInput").ap()
    wdT_i = nc.dram_tensor("wdT_i", [3072, 1024], I32, kind="ExternalInput").ap()
    wgT = nc.dram_tensor("wgT", [1024, 3072], BF16, kind="Internal").ap()
    wuT = nc.dram_tensor("wuT", [1024, 3072], BF16, kind="Internal").ap()
    wdT = nc.dram_tensor("wdT", [3072, 1024], BF16, kind="Internal").ap()
    Wm_sh = dict(fc1_wT=[1024, 1024], fc1_b=[128, 8], inpT=[1024, 4096],
                 conv_w=[128, 64], conv_b=[128, 16], xpT=[2048, 96],
                 dtpT=[64, 2048], dtp_b=[128, 16], A_log=[128, 256],
                 Dv=[128, 16], opT=[2048, 1024], fc2_wT=[1024, 1024],
                 fc2_b=[128, 8])
    Wm = {k: nc.dram_tensor("m_" + k, s, F32, kind="ExternalInput").ap()
          for k, s in Wm_sh.items()}
    scrm = {
        "inpT_bf": nc.dram_tensor("inpT_bf", [1024, 4096], BF16, kind="Internal").ap(),
        "opT_bf": nc.dram_tensor("opT_bf", [2048, 1024], BF16, kind="Internal").ap(),
        "u_d": nc.dram_tensor("u_d", [2048, 4096], F32, kind="Internal").ap(),
        "z_d": nc.dram_tensor("z_d", [2048, 4096], F32, kind="Internal").ap(),
        "y_d": nc.dram_tensor("y_d", [2048, 4096], F32, kind="Internal").ap(),
    }
    Wo_ = {
        "gate_wT": nc.dram_tensor("gate_wT", [1024, 8], F32, kind="ExternalInput").ap(),
        "gbias": nc.dram_tensor("gbias", [8, 2], F32, kind="ExternalInput").ap(),
        "b1": nc.dram_tensor("b1", [128, 256], F32, kind="ExternalInput").ap(),
        "b2": nc.dram_tensor("b2", [128, 64], F32, kind="ExternalInput").ap(),
        "w1": nc.dram_tensor("w1", [8, 1024, 4096], F32, kind="ExternalInput").ap(),
        "w2": nc.dram_tensor("w2", [8, 4096, 1024], F32, kind="ExternalInput").ap(),
    }
    scro = {
        "w1_bf": nc.dram_tensor("w1_bf", [8, 1024, 4096], BF16, kind="Internal").ap(),
        "w2_bf": nc.dram_tensor("w2_bf", [8, 4096, 1024], BF16, kind="Internal").ap(),
    }
    Wa = {}
    for nm in ("Wq", "Wk", "Wv", "Wo"):
        Wa[nm] = nc.dram_tensor("a_" + nm, [1024, 1024], F32, kind="ExternalInput").ap()
    for nm in ("bq", "bk", "bo", "lnw", "lnb"):
        Wa[nm] = nc.dram_tensor("a_" + nm, [128, 8], F32, kind="ExternalInput").ap()
    Wa["bv"] = nc.dram_tensor("a_bv", [1, 1024], F32, kind="ExternalInput").ap()
    scra = {
        "xn_d": nc.dram_tensor("xn_d", [1024, 4096], F32, kind="Internal").ap(),
        "q_d": nc.dram_tensor("q_d", [1024, 4096], BF16, kind="Internal").ap(),
        "k_d": nc.dram_tensor("k_d", [1024, 4096], BF16, kind="Internal").ap(),
        "st_d": nc.dram_tensor("st_d", [1024, 4096], BF16, kind="Internal").ap(),
        "st2_d": nc.dram_tensor("st2_d", [1024, 4096], BF16, kind="Internal").ap(),
        "v_d": nc.dram_tensor("v_d", [4096, 1024], BF16, kind="Internal").ap(),
    }

    es = ExitStack()
    cx = Ctx(nc, es)
    build_consts(cx)
    prep_ffn0_weights(cx, wgT_i, wuT_i, wdT_i, wgT, wuT, wdT)
    cx.phase_boundary()
    emit_ffn0(cx, x_in, rbuf[0], wgT, wuT, wdT)
    cx.phase_boundary()
    emit_mamba(cx, rbuf[0], rbuf[1], Wm, scrm)
    cx.phase_boundary()
    emit_ffn0(cx, rbuf[1], rbuf[2], wgT, wuT, wdT)
    cx.phase_boundary()
    emit_moe(cx, rbuf[2], rbuf[3], Wo_, scro)
    cx.phase_boundary()
    emit_attn(cx, rbuf[3], rbuf[4], Wa, scra)
    cx.phase_boundary()
    emit_moe(cx, rbuf[4], r_fin, Wo_, scro)
    cx.finalize()
    with nc.Block() as block:
        cx.p.replay(block)
    es.close()
    return nc, cx.p.n_inst


def kernel(**inputs):
    g = inputs
    ct = np.ascontiguousarray
    wmap = {
        "wgT_i": ct(np.asarray(g["L0_ffn_w_gate"]).T),
        "wuT_i": ct(np.asarray(g["L0_ffn_w_up"]).T),
        "wdT_i": ct(np.asarray(g["L0_ffn_w_down"]).T),
        "m_fc1_wT": ct(np.asarray(g["L0_attn_fc1_w"]).T),
        "m_fc1_b": to_pcol(g["L0_attn_fc1_b"]),
        "m_inpT": ct(np.asarray(g["L0_attn_in_proj_w"]).T),
        "m_conv_w": to_pcol(np.asarray(g["L0_attn_conv_w"])),
        "m_conv_b": to_pcol(g["L0_attn_conv_b"]),
        "m_xpT": ct(np.asarray(g["L0_attn_x_proj_w"]).T),
        "m_dtpT": ct(np.asarray(g["L0_attn_dt_proj_w"]).T),
        "m_dtp_b": to_pcol(g["L0_attn_dt_proj_b"]),
        "m_A_log": to_pcol(np.asarray(g["L0_attn_A_log"])),
        "m_Dv": to_pcol(g["L0_attn_D"]),
        "m_opT": ct(np.asarray(g["L0_attn_out_proj_w"]).T),
        "m_fc2_wT": ct(np.asarray(g["L0_attn_fc2_w"]).T),
        "m_fc2_b": to_pcol(g["L0_attn_fc2_b"]),
        "gate_wT": ct(np.asarray(g["L1_ffn_gate_w"]).T),
        "gbias": ct(np.stack([np.asarray(g["L1_ffn_gate_b"]),
                              np.asarray(g["L1_ffn_expert_bias"])], axis=1)),
        "b1": np.concatenate([to_pcol(np.asarray(g["L1_ffn_b1"])[e])
                              for e in range(8)], axis=1),
        "b2": np.concatenate([to_pcol(np.asarray(g["L1_ffn_b2"])[e])
                              for e in range(8)], axis=1),
        "w1": ct(np.asarray(g["L1_ffn_w1"])),
        "w2": ct(np.asarray(g["L1_ffn_w2"])),
        "a_Wq": ct(np.asarray(g["L1_attn_Wq"])),
        "a_Wk": ct(np.asarray(g["L1_attn_Wk"])),
        "a_Wv": ct(np.asarray(g["L1_attn_Wv"])),
        "a_Wo": ct(np.asarray(g["L1_attn_Wo"])),
        "a_bq": to_pcol(g["L1_attn_bq"]),
        "a_bk": to_pcol(g["L1_attn_bk"]),
        "a_bo": to_pcol(g["L1_attn_bo"]),
        "a_lnw": to_pcol(g["L1_attn_ln_w"]),
        "a_lnb": to_pcol(g["L1_attn_ln_b"]),
        "a_bv": ct(np.asarray(g["L1_attn_bv"]).reshape(1, 1024)),
    }
    nc, _ = _build_fused()
    in_maps = [dict(wmap, x_in=ct(np.asarray(g["L0_attn_x"])[i].T))
               for i in range(NCORES)]
    res = run_bass_kernel_spmd(nc, in_maps, core_ids=list(range(NCORES)))
    out = np.stack([res.results[i]["r_fin"].T for i in range(NCORES)], axis=0)
    return out.astype(np.float32)


# version 08
# speedup vs baseline: 1.1592x; 1.0369x over previous
import numpy as np
import concourse.bass as bass
import concourse.mybir as mybir

F32 = mybir.dt.float32
BF16 = mybir.dt.bfloat16
I32 = mybir.dt.int32
AF = mybir.ActivationFunctionType
ALU = mybir.AluOpType
AX = mybir.AxisListType

T = 4096
D = 1024
NCH = D // 128
NTT = T // 512


class Sem:
    def __init__(self, h):
        self.h = h
        self.n = 0


class Prog:
    ENGS = "qgsvt"

    def __init__(self, nc):
        self.nc = nc
        self.ops = {e: [] for e in self.ENGS}
        self.n_inst = 0

    def wait(self, eng, sem, val=None):
        v = sem.n if val is None else val
        if v <= 0:
            return
        self.ops[eng].append(lambda E, s=sem.h, vv=v: E.wait_ge(s, vv))

    def op(self, eng, name, *args, inc=None, **kwargs):
        amt = 0
        if inc is not None:
            amt = 16 if "dma" in name else 1
            inc.n += amt

        def fn(E, name=name, args=args, kwargs=kwargs, inc=inc, amt=amt):
            ins = getattr(E, name)(*args, **kwargs)
            if inc is not None:
                ins.then_inc(inc.h, amt)
        self.ops[eng].append(fn)
        self.n_inst += 1

    def replay(self, block):
        mapping = {"q": block.sync, "g": block.gpsimd, "s": block.scalar,
                   "v": block.vector, "t": block.tensor}
        for e in self.ENGS:
            ops = self.ops[e]

            def body(E, ops=ops):
                for fn in ops:
                    fn(E)
            mapping[e](body)


class Ctx:
    def __init__(self, nc, es):
        self.nc = nc
        self.es = es
        self.p = Prog(nc)
        self._semgen = 0
        self.new_sems()
        self.psa = es.enter_context(nc.psum_tensor("psarena", [128, 4096], F32))
        self._uid = 0
        self.sC = Sem(es.enter_context(nc.semaphore("sC")))
        self._consts = {}
        self._const_t = es.enter_context(nc.sbuf_tensor("constp", [128, 32], F32))
        self._const_n = 0

    def sb(self, name, shape, dt=F32, side=None):
        return self.es.enter_context(
            self.nc.sbuf_tensor(name, shape, dt, side=side))

    def bank(self, b, rows=128):
        return self.psa[0:rows, b * 512:(b + 1) * 512]

    def banksl(self, b, sl, rows=128):
        return self.psa[0:rows, b * 512 + sl.start: b * 512 + sl.stop]

    def uid(self):
        self._uid += 1
        return self._uid

    def new_sems(self):
        g = self._semgen
        self._semgen += 1
        for nm in ("ld", "ldg", "stq", "sT", "sV", "sS", "sG"):
            h = self.es.enter_context(self.nc.semaphore(f"{nm}{g}"))
            setattr(self, nm, Sem(h))

    def phase_boundary(self):
        self.barrier()
        self.new_sems()

    def scope(self):
        from contextlib import ExitStack
        return ExitStack()

    def barrier(self):
        sems = [self.ld, self.ldg, self.stq, self.sT, self.sV, self.sS,
                self.sG]
        for e in "qgsvt":
            for s in sems:
                self.p.wait(e, s, s.n)

    def sbs(self, es, name, shape, dt=F32, side=None):
        return es.enter_context(
            self.nc.sbuf_tensor(name, shape, dt, side=side))

    def constf(self, val):
        val = float(val)
        if val not in self._consts and val not in (0.0, 1.0, -1.0):
            i = self._const_n
            assert i < 32
            self._const_n += 1
            ap = self._const_t[:, i:i + 1]
            self.nc.const_aps.register(F32, val, ap)
            self._consts[val] = ap
        return val

    def finalize(self):
        pre = []
        items = list(self._consts.items())
        for i, (val, ap) in enumerate(items):
            def fn(E, ap=ap, val=val, last=(i == len(items) - 1)):
                ins = E.memset(ap, val)
                if last:
                    ins.then_inc(self.sC.h, 1)
            pre.append(fn)
        if not items:
            def fn0(E):
                E.memset(self._const_t[:, 0:1], 0.0).then_inc(self.sC.h, 1)
            pre.append(fn0)
        self.p.ops["g"] = pre + self.p.ops["g"]
        for e in "qsvt":
            def w(E):
                E.wait_ge(self.sC.h, 1)
            self.p.ops[e] = [w] + self.p.ops[e]


def build_consts(cx):
    p = cx.p
    cx.ones_b = cx.sb("ones_b", [128, 1], BF16)
    cx.ones_f = cx.sb("ones_f", [128, 1], F32)
    cx.ones_fT = cx.sb("ones_fT", [1, 128], F32)
    cx.ones_bT = cx.sb("ones_bT", [1, 128], BF16)
    cx.ident = cx.sb("ident", [128, 128], BF16)
    p.op("g", "memset", cx.ones_b[:], 1.0)
    p.op("g", "memset", cx.ones_f[:], 1.0)
    p.op("g", "memset", cx.ones_fT[:], 1.0)
    p.op("g", "memset", cx.ones_bT[:], 1.0)
    p.op("g", "memset", cx.ident[:], 0.0)
    p.op("g", "affine_select", out=cx.ident[:], in_=cx.ident[:],
         compare_op=ALU.not_equal, fill=1.0, base=0,
         pattern=[[-1, 128]], channel_multiplier=1, inc=cx.sG)


def dram_to_bf16(cx, src_ap, dst_ap, stage_sb):
    p = cx.p
    R, C = src_ap.shape
    n = R // 128
    src_v = src_ap.rearrange("(n p) f -> n p f", p=128)
    dst_v = dst_ap.rearrange("(n p) f -> n p f", p=128)
    for i in range(n):
        p.wait("g", cx.stq, cx.stq.n)
        p.op("g", "dma_start", out=stage_sb, in_=src_v[i], inc=cx.ldg)
        p.wait("q", cx.ldg, cx.ldg.n)
        p.op("q", "dma_start", out=dst_v[i], in_=stage_sb, inc=cx.stq)
    p.wait("q", cx.stq, cx.stq.n)


def emit_ln_tile(cx, Rt, sq, mu_sb, var_sb, mu2_sb, eps=1e-5):
    p = cx.p
    sT, sV, sS = cx.sT, cx.sV, cx.sS
    sq_done = []
    for c in range(NCH):
        csl = slice(c * 512, (c + 1) * 512)
        p.op("t", "matmul", cx.bank(0, 1), cx.ones_f[:], Rt[:, csl],
             start=(c == 0), stop=(c == NCH - 1))
        if c >= 2:
            p.wait("s", sT, sq_done[c - 2])
        p.op("s", "activation", out=sq[c % 2][:], in_=Rt[:, csl],
             func=AF.Square, inc=sS)
        p.wait("t", sS)
        p.op("t", "matmul", cx.bank(1, 1), cx.ones_b[:], sq[c % 2][:],
             start=(c == 0), stop=(c == NCH - 1), inc=sT)
        sq_done.append(sT.n)
    p.wait("v", sT)
    p.op("v", "tensor_scalar_mul", mu_sb, cx.bank(0, 1), 1.0 / D)
    p.op("v", "tensor_scalar_mul", var_sb, cx.bank(1, 1), 1.0 / D, inc=sV)
    p.wait("s", sV)
    p.op("s", "activation", out=mu2_sb, in_=mu_sb, func=AF.Square, inc=sS)
    p.wait("v", sS)
    p.op("v", "tensor_sub", var_sb, var_sb, mu2_sb, inc=sV)
    p.wait("s", sV)
    p.op("s", "activation", out=var_sb, in_=var_sb, func=AF.Ln, bias=cx.constf(eps))
    p.op("s", "activation", out=var_sb, in_=var_sb, func=AF.Exp, scale=cx.constf(-0.5),
         inc=sS)
    p.wait("t", sS)
    p.op("t", "matmul", cx.bank(2), cx.ones_fT[:], mu_sb, start=True,
         stop=True)
    p.op("t", "matmul", cx.bank(3), cx.ones_fT[:], var_sb, start=True,
         stop=True, inc=sT)
    p.wait("v", sT)


def emit_xn_tile(cx, Rt, xn, scrf, dst_fn=None):
    p = cx.p
    for c in range(NCH):
        csl = slice(c * 512, (c + 1) * 512)
        dst = xn[:, csl] if dst_fn is None else dst_fn(c)
        p.op("v", "tensor_sub", scrf[:], Rt[:, csl], cx.bank(2))
        p.op("v", "tensor_mul", dst, scrf[:], cx.bank(3),
             inc=(cx.sV if c == NCH - 1 else None))


FF = 3072
NFC = FF // 128
SCALE = 1.0 / 127.0


def emit_ffn0(cx, r_in, r_out, wgT, wuT, wdT):
    p = cx.p
    u = cx.uid()
    cx.barrier()
    esF = cx.scope()
    cx.sb = lambda name, shape, dt=F32, side=None, _sbs=cx.sbs, _es=esF: _sbs(_es, name, shape, dt, side)
    Rt = cx.sb(f"f{u}_Rt", [128, NCH * 512], F32)
    Ro = cx.sb(f"f{u}_Ro", [128, NCH * 512], F32)
    xn = cx.sb(f"f{u}_xn", [128, NCH * 512], BF16)
    sq = [cx.sb(f"f{u}_sq{i}", [128, 512], BF16) for i in range(2)]
    scrf = cx.sb(f"f{u}_scrf", [128, 512], F32)
    mu_sb = cx.sb(f"f{u}_mu", [1, 512], F32)
    var_sb = cx.sb(f"f{u}_var", [1, 512], F32)
    mu2_sb = cx.sb(f"f{u}_mu2", [1, 512], F32)
    gu = cx.sb(f"f{u}_gu", [128, NFC * 512], BF16)
    gsil = [cx.sb(f"f{u}_gsil{i}", [128, 512], BF16) for i in range(2)]
    usil = [cx.sb(f"f{u}_usil{i}", [128, 512], BF16) for i in range(2)]
    wg_sb = [cx.sb(f"f{u}_wg{i}", [128, NCH * 512], BF16) for i in range(2)]
    wu_sb = [cx.sb(f"f{u}_wu{i}", [128, NCH * 512], BF16) for i in range(2)]
    wd_sb = [cx.sb(f"f{u}_wd{i}", [128, 24 * 128], BF16) for i in range(2)]

    r_in_v = r_in.rearrange("(c p) t -> p c t", p=128)
    r_out_v = r_out.rearrange("(c p) t -> p c t", p=128)
    wgT_v = wgT.rearrange("(c p) f -> p c f", p=128)
    wuT_v = wuT.rearrange("(c p) f -> p c f", p=128)
    wdT_v = wdT.rearrange("(k p) o -> p k o", p=128)

    rt_free = 0
    ro_free = 0
    wbuf_free = [0, 0]
    wdbuf_free = [0, 0]
    gu_evac = {}
    NGRP = 6

    for tt in range(NTT):
        tsl = slice(tt * 512, (tt + 1) * 512)
        p.wait("q", cx.sV, rt_free)
        p.op("q", "dma_start",
             out=Rt[:].rearrange("p (c t) -> p c t", c=NCH),
             in_=r_in_v[:, :, tsl], inc=cx.ld)
        ld_rt = cx.ld.n
        p.wait("t", cx.ld, ld_rt)
        p.wait("s", cx.ld, ld_rt)
        emit_ln_tile(cx, Rt[:], sq, mu_sb[:], var_sb[:], mu2_sb[:])
        emit_xn_tile(cx, Rt[:], xn[:], scrf)
        xn_done = cx.sV.n
        for grp in range(NGRP):
            gb = grp % 2
            fsl = slice(grp * 512, (grp + 1) * 512)
            p.wait("g", cx.sT, wbuf_free[gb])
            p.op("g", "dma_start",
                 out=wg_sb[gb][:].rearrange("p (c f) -> p c f", c=NCH),
                 in_=wgT_v[:, :, fsl], inc=cx.ldg)
            p.op("g", "dma_start",
                 out=wu_sb[gb][:].rearrange("p (c f) -> p c f", c=NCH),
                 in_=wuT_v[:, :, fsl], inc=cx.ldg)
            ld_w = cx.ldg.n
            p.wait("t", cx.ldg, ld_w)
            p.wait("t", cx.sV, xn_done)
            for fl in range(4):
                fc = grp * 4 + fl
                bk_g = 4 + (fc % 2) * 2
                bk_u = 5 + (fc % 2) * 2
                prev_fc = fc - 2 if fc >= 2 else 22 + fc
                p.wait("t", cx.sV, gu_evac.get(prev_fc, 0))
                for c in range(NCH):
                    lhs_g = wg_sb[gb][:, c * 512 + fl * 128:
                                      c * 512 + (fl + 1) * 128]
                    lhs_u = wu_sb[gb][:, c * 512 + fl * 128:
                                      c * 512 + (fl + 1) * 128]
                    rhs = xn[:, c * 512:(c + 1) * 512]
                    p.op("t", "matmul", cx.bank(bk_g), lhs_g, rhs,
                         start=(c == 0), stop=(c == NCH - 1))
                    p.op("t", "matmul", cx.bank(bk_u), lhs_u, rhs,
                         start=(c == 0), stop=(c == NCH - 1),
                         inc=(cx.sT if c == NCH - 1 else None))
                mm_done = cx.sT.n
                if fl == 3:
                    wbuf_free[gb] = mm_done
                p.wait("s", cx.sT, mm_done)
                p.wait("s", cx.sV, gu_evac.get(prev_fc + 1 if False else (fc - 2 if fc >= 2 else 22 + fc), 0))
                p.op("s", "activation", out=gsil[fc % 2][:], in_=cx.bank(bk_g),
                     func=AF.Silu, scale=cx.constf(SCALE))
                p.op("s", "activation", out=usil[fc % 2][:], in_=cx.bank(bk_u),
                     func=AF.Copy, scale=cx.constf(SCALE), inc=cx.sS)
                p.wait("v", cx.sS)
                p.op("v", "tensor_mul", gu[:, fc * 512:(fc + 1) * 512],
                     gsil[fc % 2][:], usil[fc % 2][:], inc=cx.sV)
                gu_evac[fc] = cx.sV.n
        all_gu = cx.sV.n
        down_evac = {}
        for oc in range(NCH):
            db = oc % 2
            p.wait("g", cx.sT, wdbuf_free[db])
            p.op("g", "dma_start",
                 out=wd_sb[db][:].rearrange("p (k o) -> p k o", k=24),
                 in_=wdT_v[:, :, oc * 128:(oc + 1) * 128], inc=cx.ldg)
            ld_wd = cx.ldg.n
            p.wait("t", cx.ldg, ld_wd)
            p.wait("t", cx.sV, all_gu)
            bk = oc % 2
            if oc >= 2:
                p.wait("t", cx.sV, down_evac[oc - 2])
            for kf in range(24):
                p.op("t", "matmul", cx.bank(bk),
                     wd_sb[db][:, kf * 128:(kf + 1) * 128],
                     gu[:, kf * 512:(kf + 1) * 512],
                     start=(kf == 0), stop=(kf == 23),
                     inc=(cx.sT if kf == 23 else None))
            wdbuf_free[db] = cx.sT.n
            p.wait("v", cx.sT)
            if oc == 0:
                p.wait("v", cx.stq, ro_free)
            csl = slice(oc * 512, (oc + 1) * 512)
            p.op("v", "scalar_tensor_tensor", out=Ro[:, csl],
                 in0=cx.bank(bk), scalar=0.5 * SCALE, in1=Rt[:, csl],
                 op0=ALU.mult, op1=ALU.add,
                 inc=cx.sV)
            down_evac[oc] = cx.sV.n
        rt_free = cx.sV.n
        p.wait("q", cx.sV, cx.sV.n)
        p.op("q", "dma_start", out=r_out_v[:, :, tsl],
             in_=Ro[:].rearrange("p (c t) -> p c t", c=NCH), inc=cx.stq)
        ro_free = cx.stq.n
    p.wait("q", cx.stq, cx.stq.n)
    del cx.sb
    esF.close()
    cx.barrier()


def prep_ffn0_weights(cx, wgT_i, wuT_i, wdT_i, wgT, wuT, wdT):
    with cx.scope() as esP:
        stage = cx.sbs(esP, "w_stage", [128, 3072], BF16, side="right")
        dram_to_bf16(cx, wgT_i, wgT, stage[:])
        dram_to_bf16(cx, wuT_i, wuT, stage[:])
        dram_to_bf16(cx, wdT_i, wdT, stage[:, 0:1024])


DI = 2048
NJC = DI // 128
DS = 16
DTR = 64


def emit_mamba(cx, r_in, r_out, W, scr):
    p = cx.p
    u = cx.uid()
    cx.barrier()
    r_in_v = r_in.rearrange("(c p) t -> p c t", p=128)
    r_out_v = r_out.rearrange("(c p) t -> p c t", p=128)

    fc1b = cx.sb(f"m{u}_fc1b", [128, 8], F32)
    fc2b = cx.sb(f"m{u}_fc2b", [128, 8], F32)
    es0 = cx.scope()
    stage = cx.sbs(es0, f"m{u}_stage", [128, 4096], BF16, side="right")
    dram_to_bf16(cx, W["inpT"], scr["inpT_bf"], stage[:])
    dram_to_bf16(cx, W["opT"], scr["opT_bf"], stage[:, 0:1024])

    esW = cx.scope()
    xpw = cx.sbs(esW, f"m{u}_xpw", [128, NJC * 96], BF16)
    dtpw = cx.sbs(esW, f"m{u}_dtpw", [64, DI], BF16)
    A_neg = cx.sbs(esW, f"m{u}_Aneg", [128, NJC * DS], F32)
    convw = cx.sbs(esW, f"m{u}_convw", [128, NJC * 4], F32)
    convb = cx.sbs(esW, f"m{u}_convb", [128, NJC], F32)
    dtpb = cx.sbs(esW, f"m{u}_dtpb", [128, NJC], F32)
    D_sb = cx.sbs(esW, f"m{u}_D", [128, NJC], F32)
    sel = [cx.sbs(esW, f"m{u}_sel{i}", [32, 128], BF16) for i in range(32)]
    ridx32 = cx.sbs(esW, f"m{u}_ridx32", [32, 1], F32)
    dtlow_bf = cx.sbs(esW, f"m{u}_dtlow", [64, T], BF16)
    BC_bf = cx.sbs(esW, f"m{u}_BC", [32, T], BF16)

    es_h = cx.scope()
    h_sb = cx.sbs(es_h, f"m{u}_h", [128, 8 * T], BF16)
    es1w = cx.scope()
    fc1w = cx.sbs(es1w, f"m{u}_fc1w", [128, 8 * 1024], BF16)
    with cx.scope() as esw0:
        wf = cx.sbs(esw0, f"m{u}_wf", [128, 8 * 1024], F32, side="right")
        p.op("q", "dma_start", out=wf[:].rearrange("p (c o) -> p c o", c=8),
             in_=W["fc1_wT"].rearrange("(c p) o -> p c o", p=128), inc=cx.ld)
        p.wait("v", cx.ld)
        p.op("v", "tensor_copy", fc1w[:], wf[:], inc=cx.sV)
        p.wait("q", cx.sV)
        p.op("q", "dma_start",
             out=wf[:, 0:NJC * 96].rearrange("p (c o) -> p c o", c=NJC),
             in_=W["xpT"].rearrange("(c p) o -> p c o", p=128), inc=cx.ld)
        p.wait("v", cx.ld)
        p.op("v", "tensor_copy", xpw[:], wf[:, 0:NJC * 96], inc=cx.sV)
        p.wait("q", cx.sV)
        p.op("q", "dma_start", out=wf[0:64, 0:DI], in_=W["dtpT"], inc=cx.ld)
        p.wait("v", cx.ld)
        p.op("v", "tensor_copy", dtpw[:], wf[0:64, 0:DI], inc=cx.sV)
        p.wait("q", cx.sV)
        p.op("q", "dma_start", out=wf[:, 0:NJC * DS], in_=W["A_log"],
             inc=cx.ld)
        p.wait("s", cx.ld)
        p.op("s", "activation", out=A_neg[:], in_=wf[:, 0:NJC * DS],
             func=AF.Exp, inc=cx.sS)
        p.wait("v", cx.sS)
        p.op("v", "tensor_scalar_mul", A_neg[:], A_neg[:], -1.0, inc=cx.sV)
        p.op("q", "dma_start", out=fc1b[:], in_=W["fc1_b"], inc=cx.ld)
        p.op("q", "dma_start", out=fc2b[:], in_=W["fc2_b"], inc=cx.ld)
        p.op("q", "dma_start", out=convw[:], in_=W["conv_w"], inc=cx.ld)
        p.op("q", "dma_start", out=convb[:], in_=W["conv_b"], inc=cx.ld)
        p.op("q", "dma_start", out=dtpb[:], in_=W["dtp_b"], inc=cx.ld)
        p.op("q", "dma_start", out=D_sb[:], in_=W["Dv"], inc=cx.ld)
        p.op("g", "iota", ridx32[:], pattern=[[1, 1]], base=0,
             channel_multiplier=1, allow_small_or_imprecise_dtypes=True,
             inc=cx.sG)
        p.wait("g", cx.sG)
        for i in range(32):
            p.op("g", "tensor_scalar", sel[i][:],
                 ridx32[:].broadcast_to([32, 128]), float(i), None,
                 ALU.is_equal, inc=(cx.sG if i == 31 else None))
        prep_v = cx.sV.n
        prep_ld = cx.ld.n
        prep_g = cx.sG.n
    es0.close()
    cx.barrier()

    es12 = cx.scope()
    xn_sb = cx.sbs(es12, f"m{u}_xn", [128, 8 * T], BF16)
    es1 = cx.scope()
    Rt = cx.sbs(es1, f"m{u}_Rt", [128, NCH * 512], F32)
    sq = [cx.sbs(es1, f"m{u}_sq{i}", [128, 512], BF16) for i in range(2)]
    scrf = cx.sbs(es1, f"m{u}_scrf", [128, 512], F32)
    mu_sb = cx.sbs(es1, f"m{u}_mu", [1, 512], F32)
    var_sb = cx.sbs(es1, f"m{u}_var", [1, 512], F32)
    mu2_sb = cx.sbs(es1, f"m{u}_mu2", [1, 512], F32)
    for tt in range(NTT):
        tsl = slice(tt * 512, (tt + 1) * 512)
        p.wait("q", cx.sV)
        p.op("q", "dma_start",
             out=Rt[:].rearrange("p (c t) -> p c t", c=NCH),
             in_=r_in_v[:, :, tsl], inc=cx.ld)
        p.wait("t", cx.ld)
        p.wait("s", cx.ld)
        emit_ln_tile(cx, Rt[:], sq, mu_sb[:], var_sb[:], mu2_sb[:])
        emit_xn_tile(cx, Rt[:], None, scrf,
                     dst_fn=lambda c, tsl=tsl: xn_sb[:, c * T + tsl.start:
                                                     c * T + tsl.stop])
    es1.close()

    p.wait("t", cx.sV)
    evac_done = {}
    for oc in range(NCH):
        for tt in range(NTT):
            tsl = slice(tt * 512, (tt + 1) * 512)
            bk = (oc * NTT + tt) % 2
            key = oc * NTT + tt
            if key >= 2:
                p.wait("t", cx.sV, evac_done[key - 2][1])
            for c in range(NCH):
                p.op("t", "matmul", cx.bank(bk),
                     fc1w[:, c * 1024 + oc * 128:c * 1024 + (oc + 1) * 128],
                     xn_sb[:, c * T + tsl.start:c * T + tsl.stop],
                     start=(c == 0), stop=(c == NCH - 1),
                     inc=(cx.sT if c == NCH - 1 else None))
            p.wait("v", cx.sT)
            p.op("v", "tensor_scalar",
                 h_sb[:, oc * T + tsl.start:oc * T + tsl.stop],
                 cx.bank(bk), fc1b[:, oc:oc + 1], None, ALU.add, inc=cx.sV)
            evac_done[key] = ("v", cx.sV.n)
    p2_evac = cx.sV.n
    es12.close()
    es1w.close()
    cx.barrier()

    es3 = cx.scope()
    wsb = [cx.sbs(es3, f"m{u}_w3{i}", [128, 8 * 128], BF16) for i in range(2)]
    xcz = cx.sbs(es3, f"m{u}_xcz", [128, 4099], F32)
    conv_acc = cx.sbs(es3, f"m{u}_cacc", [128, T], F32)
    u_f = cx.sbs(es3, f"m{u}_uf", [128, T], F32)
    u_bf = cx.sbs(es3, f"m{u}_ubf", [128, T], BF16)
    xdbl = cx.sbs(es3, f"m{u}_xdbl", [96, T], F32)
    inpT_v = scr["inpT_bf"].rearrange("(c p) o -> p c o", p=128)
    u_d_v = scr["u_d"].rearrange("(c p) t -> p c t", p=128)
    z_d_v = scr["z_d"].rearrange("(c p) t -> p c t", p=128)

    p.op("g", "memset", xcz[:, 0:3], 0.0, inc=cx.sG)
    xcz_zero = cx.sG.n
    wbuf_free = [0, 0]
    xcz_free = 0
    xcz_free_eng = "none"
    ubf_free = 0
    uf_free = 0
    uf_free_st = 0
    conv_free = 0
    for j in range(2 * NJC):
        jb = j % 2
        p.wait("g", cx.sT, wbuf_free[jb])
        p.op("g", "dma_start",
             out=wsb[jb][:].rearrange("p (c o) -> p c o", c=8),
             in_=inpT_v[:, :, j * 128:(j + 1) * 128], inc=cx.ldg)
        p.wait("t", cx.ldg)
        for tt in range(NTT):
            tsl = slice(tt * 512, (tt + 1) * 512)
            bk = tt % 2
            if tt >= 2:
                p.wait("t", cx.sS, evac_done[tt - 2])
            elif j > 0:
                p.wait("t", cx.sS, evac_done[6 + tt])
            else:
                p.wait("t", cx.sV, p2_evac)
            for c in range(NCH):
                p.op("t", "matmul", cx.bank(bk),
                     wsb[jb][:, c * 128:(c + 1) * 128],
                     h_sb[:, c * T + tsl.start:c * T + tsl.stop],
                     start=(c == 0), stop=(c == NCH - 1),
                     inc=(cx.sT if c == NCH - 1 else None))
            if tt == NTT - 1:
                wbuf_free[jb] = cx.sT.n
            p.wait("s", cx.sT)
            if tt == 0:
                if xcz_free_eng == "v":
                    p.wait("s", cx.sV, xcz_free)
                elif xcz_free_eng == "q":
                    p.wait("s", cx.stq, xcz_free)
            p.op("s", "activation", out=xcz[:, 3 + tsl.start:3 + tsl.stop],
                 in_=cx.bank(bk), func=AF.Copy, inc=cx.sS)
            evac_done[tt] = cx.sS.n
        if j < NJC:
            p.wait("v", cx.sS)
            p.wait("v", cx.sG, xcz_zero)
            p.wait("v", cx.ld, prep_ld)
            p.wait("v", cx.sS, conv_free)
            p.op("v", "tensor_scalar",
                 conv_acc[:], xcz[:, 0:T], convw[:, 4 * j:4 * j + 1], None,
                 ALU.mult)
            for w in range(1, 4):
                p.op("v", "scalar_tensor_tensor", out=conv_acc[:],
                     in0=xcz[:, w:w + T],
                     scalar=convw[:, 4 * j + w:4 * j + w + 1],
                     in1=conv_acc[:], op0=ALU.mult, op1=ALU.add,
                     inc=(cx.sV if w == 3 else None))
            xcz_free, xcz_free_eng = cx.sV.n, "v"
            p.wait("s", cx.sV)
            p.wait("s", cx.stq, uf_free_st)
            p.wait("s", cx.sV, uf_free)
            p.op("s", "activation", out=u_f[:], in_=conv_acc[:], func=AF.Silu,
                 bias=convb[:, j:j + 1], inc=cx.sS)
            conv_free = cx.sS.n
            p.wait("q", cx.sS)
            p.op("q", "dma_start", out=u_d_v[:, j, :], in_=u_f[:], inc=cx.stq)
            uf_free_st = cx.stq.n
            p.wait("v", cx.sS)
            p.wait("v", cx.sT, ubf_free)
            p.op("v", "tensor_copy", u_bf[:], u_f[:], inc=cx.sV)
            uf_free = cx.sV.n
            p.wait("t", cx.sV)
            for tt in range(NTT):
                tsl = slice(tt * 512, (tt + 1) * 512)
                p.op("t", "matmul", cx.bank(2, 96),
                     xpw[:, j * 96:(j + 1) * 96], u_bf[:, tsl],
                     start=True, stop=True, inc=cx.sT)
                if j == 0:
                    p.wait("s", cx.sT)
                    p.op("s", "activation", out=xdbl[:, tsl],
                         in_=cx.bank(2, 96), func=AF.Copy, inc=cx.sS)
                    p.wait("t", cx.sS)
                else:
                    p.wait("v", cx.sT)
                    p.op("v", "tensor_add", xdbl[:, tsl], xdbl[:, tsl],
                         cx.bank(2, 96), inc=cx.sV)
                    p.wait("t", cx.sV)
            ubf_free = cx.sT.n
        else:
            p.wait("q", cx.sS)
            p.op("q", "dma_start", out=z_d_v[:, j - NJC, :],
                 in_=xcz[:, 3:3 + T], inc=cx.stq)
            xcz_free, xcz_free_eng = cx.stq.n, "q"
    p.wait("v", cx.sV)
    p.op("v", "tensor_copy", dtlow_bf[:], xdbl[0:64, :])
    p.op("v", "tensor_copy", BC_bf[:], xdbl[64:96, :], inc=cx.sV)
    es3.close()
    es_h.close()
    cx.barrier()

    TH = T // 4
    NTH = TH // 512
    es4 = cx.scope()
    u2 = [cx.sbs(es4, f"m{u}_u2{i}", [128, TH], F32) for i in range(2)]
    dt2 = [cx.sbs(es4, f"m{u}_dt2{i}", [128, TH], F32) for i in range(2)]
    w2 = [cx.sbs(es4, f"m{u}_w2{i}", [128, TH], F32) for i in range(2)]
    y2 = [cx.sbs(es4, f"m{u}_y2{i}", [128, TH], F32) for i in range(2)]
    dA2 = [[cx.sbs(es4, f"m{u}_dA{i}_{j}", [128, TH], F32) for j in range(2)]
           for i in range(2)]
    b2_ = [[cx.sbs(es4, f"m{u}_b2{i}_{j}", [128, TH], F32) for j in range(2)]
           for i in range(2)]
    H2 = [cx.sbs(es4, f"m{u}_H{i}", [128, TH], F32) for i in range(2)]
    CH2 = [cx.sbs(es4, f"m{u}_CH{i}", [128, TH], F32) for i in range(2)]
    Brep = [cx.sbs(es4, f"m{u}_Brep{j}", [128, TH], F32) for j in range(3)]
    Crep = [cx.sbs(es4, f"m{u}_Crep{j}", [128, TH], F32) for j in range(3)]
    dtraw = cx.sbs(es4, f"m{u}_dtraw", [128, TH], F32)
    carry = cx.sbs(es4, f"m{u}_carry", [128, NJC * DS], F32)
    y_d_v = scr["y_d"].rearrange("(c p) t -> p c t", p=128)

    p.op("g", "memset", carry[:], 0.0, inc=cx.sG)
    carry_zero = cx.sG.n
    scan_dA = [[0, 0], [0, 0]]
    scan_b = [[0, 0], [0, 0]]
    rep_b_user = [0, 0, 0]
    rep_c_user_g = [0, 0, 0]
    rep_c_user_v = [0, 0, 0]
    ch_done = [0, 0]
    yadd_done = [("g", 0), ("v", 0)]
    dt_done = [0, 0]
    b_used = [0, 0]
    ydma_done = [0, 0]
    for half in range(4):
        hsl = slice(half * TH, (half + 1) * TH)
        for g in range(NJC // 2):
            for b2 in range(2):
                j = 2 * g + b2
                ye, yv = yadd_done[b2]
                p.wait("q", cx.sG if ye == "g" else cx.sV, yv)
                p.op("q", "dma_start", out=u2[b2][:],
                     in_=u_d_v[:, j, hsl], inc=cx.ld)
                p.wait("s", cx.sV, max(dt_done))
                for tt in range(NTH):
                    tsl = slice(half * TH + tt * 512, half * TH + (tt + 1) * 512)
                    p.wait("t", cx.sS)
                    p.wait("t", cx.sV)
                    p.op("t", "matmul", cx.bank(3),
                         dtpw[:, j * 128:(j + 1) * 128],
                         dtlow_bf[:, tsl], start=True, stop=True, inc=cx.sT)
                    p.wait("v", cx.sT)
                    p.op("v", "tensor_scalar",
                         dtraw[:, tt * 512:(tt + 1) * 512],
                         cx.bank(3), dtpb[:, j:j + 1], None, ALU.add,
                         inc=cx.sV)
                scr_sp = dA2[b2][0]
                p.wait("s", cx.sV, scan_dA[b2][0])
                p.wait("s", cx.sV)
                p.op("s", "activation", out=scr_sp[:], in_=dtraw[:],
                     func=AF.Abs)
                p.op("s", "activation", out=scr_sp[:], in_=scr_sp[:],
                     func=AF.Exp, scale=-1.0)
                p.op("s", "activation", out=scr_sp[:], in_=scr_sp[:],
                     func=AF.Ln, bias=1.0, inc=cx.sS)
                p.wait("v", cx.sS)
                p.wait("v", cx.sV, scan_dA[b2][0])
                p.op("v", "tensor_scalar_max", dt2[b2][:], dtraw[:], 0.0)
                p.op("v", "tensor_add", dt2[b2][:], dt2[b2][:], scr_sp[:],
                     inc=cx.sV)
                p.wait("v", cx.ld)
                p.wait("v", cx.sG, b_used[b2])
                p.op("v", "tensor_mul", w2[b2][:], dt2[b2][:], u2[b2][:],
                     inc=cx.sV)
                dt_done[b2] = cx.sV.n
                ye, yv = yadd_done[b2]
                if ye == "v":
                    p.wait("g", cx.sV, yv)
                p.wait("g", cx.stq, ydma_done[b2])
                p.op("g", "memset", y2[b2][:], 0.0, inc=cx.sG)
            grp_ready_v = cx.sV.n
            grp_ready_g = cx.sG.n
            for n in range(DS):
                pn = n % 2
                rn = n % 3
                p.wait("t", cx.sV, grp_ready_v)
                p.wait("t", cx.sG, rep_b_user[rn])
                p.wait("t", cx.sG, rep_c_user_g[rn])
                p.wait("t", cx.sV, rep_c_user_v[rn])
                for tt in range(NTH):
                    tsl = slice(half * TH + tt * 512,
                                half * TH + (tt + 1) * 512)
                    p.op("t", "matmul", cx.bank(4 * pn + tt), sel[n][:],
                         BC_bf[:, tsl], start=True, stop=True)
                    p.op("t", "matmul", cx.bank(4 * pn + 2 + tt),
                         sel[16 + n][:], BC_bf[:, tsl], start=True, stop=True,
                         inc=(cx.sT if tt == NTH - 1 else None))
                p.wait("s", cx.sT)
                p.op("s", "activation", out=Brep[rn][:],
                     in_=cx.psa[:, 4 * pn * 512:(4 * pn + 2) * 512],
                     func=AF.Copy)
                p.op("s", "activation", out=Crep[rn][:],
                     in_=cx.psa[:, (4 * pn + 2) * 512:(4 * pn + 4) * 512],
                     func=AF.Copy, inc=cx.sS)
                rep_done = cx.sS.n
                scan_v = [0, 0]
                for b2 in range(2):
                    j = 2 * g + b2
                    cidx = j * DS + n
                    p.wait("s", cx.sV, scan_dA[b2][pn])
                    p.op("s", "activation", out=dA2[b2][pn][:],
                         in_=dt2[b2][:], func=AF.Exp,
                         scale=A_neg[:, cidx:cidx + 1], inc=cx.sS)
                    dA_done = cx.sS.n
                    p.wait("g", cx.sS, rep_done)
                    p.wait("g", cx.sV, scan_b[b2][pn])
                    p.op("g", "tensor_mul", b2_[b2][pn][:], w2[b2][:],
                         Brep[rn][:], inc=cx.sG)
                    rep_b_user[rn] = cx.sG.n
                    b_used[b2] = cx.sG.n
                    b_ready = cx.sG.n
                    p.wait("v", cx.sS, dA_done)
                    p.wait("v", cx.sG, b_ready)
                    if b2 == 0:
                        p.wait("v", cx.sG, ch_done[0])
                    p.wait("v", cx.sG, carry_zero)
                    p.op("v", "tensor_tensor_scan", H2[b2][:],
                         dA2[b2][pn][:], b2_[b2][pn][:],
                         carry[:, cidx:cidx + 1], ALU.mult, ALU.add,
                         inc=cx.sV)
                    scan_dA[b2][pn] = cx.sV.n
                    scan_b[b2][pn] = cx.sV.n
                    scan_v[b2] = cx.sV.n
                p.wait("v", cx.sV, scan_v[1])
                carry_v = [0, 0]
                for b2 in range(2):
                    j = 2 * g + b2
                    cidx = j * DS + n
                    p.op("v", "tensor_copy", carry[:, cidx:cidx + 1],
                         H2[b2][:, TH - 1:TH], inc=cx.sV)
                    carry_v[b2] = cx.sV.n
                p.wait("g", cx.sS, rep_done)
                p.wait("g", cx.sV, carry_v[0])
                p.op("g", "tensor_mul", CH2[0][:], H2[0][:],
                     Crep[rn][:], inc=cx.sG)
                ch_done[0] = cx.sG.n
                rep_c_user_g[rn] = cx.sG.n
                p.op("g", "tensor_add", y2[0][:], y2[0][:],
                     CH2[0][:], inc=cx.sG)
                yadd_done[0] = ("g", cx.sG.n)
                p.op("v", "tensor_mul", CH2[1][:], H2[1][:],
                     Crep[rn][:], inc=cx.sV)
                rep_c_user_v[rn] = cx.sV.n
                p.op("v", "tensor_add", y2[1][:], y2[1][:],
                     CH2[1][:], inc=cx.sV)
                yadd_done[1] = ("v", cx.sV.n)
                if n == 0:
                    p.wait("v", cx.sG, grp_ready_g)
            for b2 in range(2):
                j = 2 * g + b2
                p.wait("v", cx.stq, ydma_done[b2])
                if b2 == 0:
                    p.wait("v", cx.sG, ch_done[0])
                    p.wait("v", cx.sG)
                p.op("v", "scalar_tensor_tensor", out=y2[b2][:],
                     in0=u2[b2][:], scalar=D_sb[:, j:j + 1], in1=y2[b2][:],
                     op0=ALU.mult, op1=ALU.add, inc=cx.sV)
                yadd_done[b2] = ("v", cx.sV.n)
                p.wait("q", cx.sV, cx.sV.n)
                p.op("q", "dma_start", out=y_d_v[:, j, hsl], in_=y2[b2][:],
                     inc=cx.stq)
                ydma_done[b2] = cx.stq.n
    p.wait("q", cx.stq, cx.stq.n)
    es4.close()
    esW.close()
    cx.barrier()

    es5 = cx.scope()
    opw = cx.sbs(es5, f"m{u}_opw", [128, NJC * 1024], BF16)
    fc2w = cx.sbs(es5, f"m{u}_fc2w", [128, 8 * 1024], BF16)
    p.op("q", "dma_start",
         out=opw[:].rearrange("p (c o) -> p c o", c=NJC),
         in_=scr["opT_bf"].rearrange("(c p) o -> p c o", p=128), inc=cx.ld)
    with cx.scope() as esw1:
        wf2 = cx.sbs(esw1, f"m{u}_wf2", [128, 4 * 1024], F32, side="right")
        wv = W["fc2_wT"].rearrange("(c p) o -> p c o", p=128)
        for hf in range(2):
            p.wait("q", cx.sV)
            p.op("q", "dma_start",
                 out=wf2[:].rearrange("p (c o) -> p c o", c=4),
                 in_=wv[:, hf * 4:(hf + 1) * 4, :], inc=cx.ld)
            p.wait("v", cx.ld)
            p.op("v", "tensor_copy", fc2w[:, hf * 4096:(hf + 1) * 4096],
                 wf2[:], inc=cx.sV)
    opw_ld = cx.ld.n
    ybf_t = cx.sbs(es5, f"m{u}_ybft", [128, NJC * 512], BF16)
    zf = [cx.sbs(es5, f"m{u}_zf{i}", [128, 512], F32) for i in range(2)]
    yf = [cx.sbs(es5, f"m{u}_yf{i}", [128, 512], F32) for i in range(2)]
    zs = [cx.sbs(es5, f"m{u}_zs{i}", [128, 512], F32) for i in range(2)]
    o1t = cx.sbs(es5, f"m{u}_o1t", [128, 8 * 512], BF16)
    Rt5 = cx.sbs(es5, f"m{u}_Rt5", [128, 8 * 512], F32)
    Ro5 = cx.sbs(es5, f"m{u}_Ro5", [128, 8 * 512], F32)
    rt5_free = 0
    ro5_st = 0
    zy_free = [0, 0]
    for tt in range(NTT):
        tsl = slice(tt * 512, (tt + 1) * 512)
        p.wait("q", cx.sV, rt5_free)
        p.op("q", "dma_start",
             out=Rt5[:].rearrange("p (c t) -> p c t", c=NCH),
             in_=r_in_v[:, :, tsl], inc=cx.ld)
        rt5_ld = cx.ld.n
        ybf_mm = cx.sT.n
        for j in range(NJC):
            jb = j % 2
            p.wait("q", cx.sV, zy_free[jb])
            p.op("q", "dma_start", out=zf[jb][:], in_=z_d_v[:, j, tsl],
                 inc=cx.ld)
            p.op("q", "dma_start", out=yf[jb][:], in_=y_d_v[:, j, tsl],
                 inc=cx.ld)
            p.wait("s", cx.ld)
            p.wait("s", cx.sV, zy_free[jb])
            p.op("s", "activation", out=zs[jb][:], in_=zf[jb][:],
                 func=AF.Silu, inc=cx.sS)
            p.wait("v", cx.sS)
            if j < 2:
                p.wait("v", cx.sT, ybf_mm)
            p.op("v", "tensor_mul", ybf_t[:, j * 512:(j + 1) * 512],
                 yf[jb][:], zs[jb][:], inc=cx.sV)
            zy_free[jb] = cx.sV.n
        ybf_done = cx.sV.n
        p.wait("t", cx.sV, ybf_done)
        p.wait("t", cx.ld, opw_ld)
        if tt > 0:
            p.wait("t", cx.sS)
            p.wait("t", cx.sV)
        for oc in range(NCH):
            for c in range(NJC):
                p.op("t", "matmul", cx.bank(oc),
                     opw[:, c * 1024 + oc * 128:c * 1024 + (oc + 1) * 128],
                     ybf_t[:, c * 512:(c + 1) * 512],
                     start=(c == 0), stop=(c == NJC - 1),
                     inc=(cx.sT if c == NJC - 1 else None))
        p.wait("s", cx.sT)
        for oc in range(NCH):
            p.op("s", "activation", out=o1t[:, oc * 512:(oc + 1) * 512],
                 in_=cx.bank(oc), func=AF.Copy,
                 inc=(cx.sS if oc == NCH - 1 else None))
        if "o1_d" in scr:
            p.wait("q", cx.sS)
            p.op("q", "dma_start",
                 out=scr["o1_d"].rearrange("(c p) t -> p c t", p=128)[:, :, tsl],
                 in_=o1t[:].rearrange("p (c t) -> p c t", c=NCH), inc=cx.stq)
        p.wait("t", cx.sS)
        fc2_evac = {}
        for oc2 in range(NCH):
            bk = oc2 % 2
            if oc2 >= 2:
                p.wait("t", cx.sV, fc2_evac[oc2 - 2])
            for c in range(NCH):
                p.op("t", "matmul", cx.bank(bk),
                     fc2w[:, c * 1024 + oc2 * 128:c * 1024 + (oc2 + 1) * 128],
                     o1t[:, c * 512:(c + 1) * 512],
                     start=(c == 0), stop=(c == NCH - 1),
                     inc=(cx.sT if c == NCH - 1 else None))
            p.wait("v", cx.sT)
            if oc2 == 0:
                p.wait("v", cx.ld, rt5_ld)
                p.wait("v", cx.stq, ro5_st)
            csl = slice(oc2 * 512, (oc2 + 1) * 512)
            p.op("v", "scalar_tensor_tensor", out=Ro5[:, csl],
                 in0=cx.bank(bk), scalar=fc2b[:, oc2:oc2 + 1],
                 in1=Rt5[:, csl], op0=ALU.add, op1=ALU.add, inc=cx.sV)
            fc2_evac[oc2] = cx.sV.n
        rt5_free = cx.sV.n
        p.wait("q", cx.sV, rt5_free)
        p.op("q", "dma_start", out=r_out_v[:, :, tsl],
             in_=Ro5[:].rearrange("p (c t) -> p c t", c=NCH), inc=cx.stq)
        ro5_st = cx.stq.n
    p.wait("q", cx.stq, cx.stq.n)
    es5.close()


def to_pcol(v):
    import numpy as np
    v = np.asarray(v)
    if v.ndim == 1:
        c = v.shape[0] // 128
        return np.ascontiguousarray(v.reshape(c, 128).T)
    c = v.shape[0] // 128
    k = v.shape[1]
    return np.ascontiguousarray(
        v.reshape(c, 128, k).transpose(1, 0, 2).reshape(128, c * k))


NEXP = 8
DFF = 4096
NFC2 = DFF // 128


def emit_moe(cx, r_in, r_out, W, scr):
    p = cx.p
    u = cx.uid()
    DBG = scr.get("dbg_stage", 5)
    cx.barrier()
    r_in_v = r_in.rearrange("(c p) t -> p c t", p=128)
    r_out_v = r_out.rearrange("(c p) t -> p c t", p=128)

    if not scr.get("moe_prepped"):
        with cx.scope() as esP:
            stage = cx.sbs(esP, f"o{u}_stage", [128, 4096], BF16, side="right")
            for e in range(NEXP):
                dram_to_bf16(cx, W["w1"][e], scr["w1_bf"][e], stage[:])
                dram_to_bf16(cx, W["w2"][e], scr["w2_bf"][e],
                             stage[:, 0:1024])
        scr["moe_prepped"] = True

    esM = cx.scope()
    if "cpack" not in scr:
        gw = cx.sb(f"o{u}_gw", [128, 8 * 8], F32)
        gb = cx.sb(f"o{u}_gb", [8, 2], F32)
        gbias = cx.sb(f"o{u}_gbias", [8, 1], F32)
        b1_sb = cx.sb(f"o{u}_b1", [128, NEXP * 32], F32)
        b2_sb = cx.sb(f"o{u}_b2", [128, NEXP * 8], F32)
        sel8 = [cx.sb(f"o{u}_sel8_{e}", [8, 128], BF16) for e in range(NEXP)]
        ident_f = cx.sb(f"o{u}_identf", [128, 128], F32)
        revidx = cx.sb(f"o{u}_rev", [128, 8], F32)
        ridx8 = cx.sb(f"o{u}_ridx8", [8, 1], F32)
        p.op("q", "dma_start",
             out=gw[:].rearrange("p (c e) -> p c e", c=NCH),
             in_=W["gate_wT"].rearrange("(c p) e -> p c e", p=128), inc=cx.ld)
        p.op("q", "dma_start", out=gb[:], in_=W["gbias"], inc=cx.ld)
        p.op("q", "dma_start", out=b1_sb[:], in_=W["b1"], inc=cx.ld)
        p.op("q", "dma_start", out=b2_sb[:], in_=W["b2"], inc=cx.ld)
        p.wait("v", cx.ld)
        p.op("v", "tensor_add", gbias[:], gb[:, 0:1], gb[:, 1:2], inc=cx.sV)
        p.op("g", "iota", ridx8[:], pattern=[[1, 1]], base=0,
             channel_multiplier=1, allow_small_or_imprecise_dtypes=True,
             inc=cx.sG)
        p.wait("g", cx.sG)
        for e in range(NEXP):
            p.op("g", "tensor_scalar", sel8[e][:],
                 ridx8[:].broadcast_to([8, 128]), float(e), None,
                 ALU.is_equal)
        p.op("g", "memset", ident_f[:], 0.0)
        p.op("g", "affine_select", out=ident_f[:], in_=ident_f[:],
             compare_op=ALU.not_equal, fill=1.0, base=0,
             pattern=[[-1, 128]], channel_multiplier=1)
        p.op("g", "iota", revidx[:], pattern=[[1, 8]], base=0,
             channel_multiplier=0, allow_small_or_imprecise_dtypes=True,
             inc=cx.sG)
        p.wait("v", cx.sG)
        p.op("v", "tensor_scalar", revidx[:], revidx[:], -1.0, 8.0, ALU.mult,
             ALU.add, inc=cx.sV)
        scr["cpack"] = dict(gw=gw, gbias=gbias, b1_sb=b1_sb, b2_sb=b2_sb,
                            sel8=sel8, ident_f=ident_f, revidx=revidx)
    cp = scr["cpack"]
    gw = cp["gw"]; gbias = cp["gbias"]; b1_sb = cp["b1_sb"]
    b2_sb = cp["b2_sb"]; sel8 = cp["sel8"]; ident_f = cp["ident_f"]
    revidx = cp["revidx"]
    prep_g = cx.sG.n
    prep_v = cx.sV.n

    Rt = cx.sbs(esM, f"o{u}_Rt", [128, NCH * 512], F32)
    Ro = cx.sbs(esM, f"o{u}_Ro", [128, NCH * 512], F32)
    xnb = cx.sbs(esM, f"o{u}_xnb", [128, NCH * 512], BF16)
    xnf = cx.sbs(esM, f"o{u}_xnf", [128, NCH * 512], F32)
    sq = [cx.sbs(esM, f"o{u}_sq{i}", [128, 512], BF16) for i in range(2)]
    scrf = cx.sbs(esM, f"o{u}_scrf", [128, 512], F32)
    mu_sb = cx.sbs(esM, f"o{u}_mu", [1, 512], F32)
    var_sb = cx.sbs(esM, f"o{u}_var", [1, 512], F32)
    mu2_sb = cx.sbs(esM, f"o{u}_mu2", [1, 512], F32)
    g_sb = cx.sbs(esM, f"o{u}_gsb", [8, 512], F32)
    gts = cx.sbs(esM, f"o{u}_gts", [128, 4 * 8], F32)
    mx = cx.sbs(esM, f"o{u}_mx", [128, 1], F32)
    eq = cx.sbs(esM, f"o{u}_eq", [128, 8], F32)
    msk = cx.sbs(esM, f"o{u}_msk", [128, 4 * 8], BF16)
    mskT = cx.sbs(esM, f"o{u}_mskT", [8, 512], BF16)
    mrep = cx.sbs(esM, f"o{u}_mrep", [128, NEXP * 512], F32)
    he = cx.sbs(esM, f"o{u}_he", [128, NFC2 * 512], BF16)
    acc = cx.sbs(esM, f"o{u}_acc", [128, NCH * 512], F32)
    w1s = [cx.sbs(esM, f"o{u}_w1s{i}", [128, 8 * 512], BF16) for i in range(2)]
    w2s = [cx.sbs(esM, f"o{u}_w2s{i}", [128, 4 * 1024], BF16) for i in range(2)]

    w1v = [scr["w1_bf"][e].rearrange("(c p) f -> p c f", p=128)
           for e in range(NEXP)]
    w2v = [scr["w2_bf"][e].rearrange("(g p) d -> p g d", p=128)
           for e in range(NEXP)]

    rt_free = 0
    ro_st = 0
    w1_free = [0, 0]
    w2_free = [0, 0]
    for tt in range(NTT):
        tsl = slice(tt * 512, (tt + 1) * 512)
        p.wait("q", cx.sV, rt_free)
        p.op("q", "dma_start",
             out=Rt[:].rearrange("p (c t) -> p c t", c=NCH),
             in_=r_in_v[:, :, tsl], inc=cx.ld)
        p.wait("t", cx.ld)
        p.wait("s", cx.ld)
        emit_ln_tile(cx, Rt[:], sq, mu_sb[:], var_sb[:], mu2_sb[:])
        emit_xn_tile(cx, Rt[:], xnf[:], scrf)
        p.wait("v", cx.sV)
        p.op("v", "tensor_copy", xnb[:], xnf[:], inc=cx.sV)
        xn_done = cx.sV.n
        if DBG >= 1:
            p.wait("t", cx.sV, xn_done)
            for c in range(NCH):
                p.op("t", "matmul", cx.bank(4, 8), gw[:, c * 8:(c + 1) * 8],
                     xnf[:, c * 512:(c + 1) * 512], start=(c == 0),
                     stop=(c == NCH - 1), inc=(cx.sT if c == NCH - 1 else None))
            p.wait("v", cx.sT)
            p.wait("v", cx.sV, prep_v)
            p.op("v", "tensor_scalar", g_sb[:], cx.bank(4, 8), gbias[0:8, 0:1],
                 None, ALU.add, inc=cx.sV)
            p.wait("t", cx.sV)
            p.wait("t", cx.sG, prep_g)
            for st in range(4):
                p.op("t", "transpose",
                     cx.psa[0:128, 5 * 512 + st * 8:5 * 512 + (st + 1) * 8],
                     g_sb[:, st * 128:(st + 1) * 128], ident_f[0:8, 0:8],
                     inc=(cx.sT if st == 3 else None))
            p.wait("v", cx.sT)
            for st in range(4):
                p.op("v", "tensor_copy", gts[:, st * 8:(st + 1) * 8],
                     cx.psa[0:128, 5 * 512 + st * 8:5 * 512 + (st + 1) * 8],
                     inc=(cx.sV if st == 3 else None))
            gts_done = cx.sV.n
            if DBG >= 2:
                for st in range(4):
                    ss = slice(st * 8, (st + 1) * 8)
                    p.op("v", "reduce_max", mx[:], gts[:, ss], axis=AX.X, inc=cx.sV)
                    p.wait("v", cx.sV)
                    p.op("v", "tensor_scalar", eq[:], gts[:, ss], mx[:, 0:1], None,
                         ALU.is_ge, inc=cx.sV)
                    p.wait("v", cx.sV)
                    p.op("v", "tensor_mul", eq[:], eq[:], revidx[:], inc=cx.sV)
                    p.wait("v", cx.sV)
                    p.op("v", "reduce_max", mx[:], eq[:], axis=AX.X, inc=cx.sV)
                    p.wait("v", cx.sV)
                    p.op("v", "tensor_scalar", msk[:, ss], eq[:], mx[:, 0:1], None,
                         ALU.is_ge, inc=cx.sV)
                    p.wait("v", cx.sV)
                msk_done = cx.sV.n
            if DBG >= 3:
                p.wait("t", cx.sV, msk_done)
                for st in range(4):
                    p.op("t", "transpose",
                         cx.psa[0:8, 5 * 512:6 * 512].bitcast(BF16)[:, st * 128:
                                                                    (st + 1) * 128],
                         msk[:, st * 8:(st + 1) * 8], cx.ident[:],
                         inc=(cx.sT if st == 3 else None))
                p.wait("s", cx.sT)
                p.op("s", "activation", out=mskT[:],
                     in_=cx.psa[0:8, 5 * 512:6 * 512].bitcast(BF16)[:, 0:512],
                     func=AF.Copy, inc=cx.sS)
                mrep_mm = {}
                for e in range(NEXP):
                    bk = 6 + e % 2
                    p.wait("t", cx.sS)
                    if e >= 2:
                        p.wait("t", cx.sS, mrep_mm[e - 2])
                    p.op("t", "matmul", cx.bank(bk), sel8[e][:], mskT[:],
                         start=True, stop=True, inc=cx.sT)
                    p.wait("s", cx.sT)
                    p.op("s", "activation", out=mrep[:, e * 512:(e + 1) * 512],
                         in_=cx.bank(bk), func=AF.Copy, inc=cx.sS)
                    mrep_mm[e] = cx.sS.n
                mrep_done = cx.sS.n
            if DBG >= 4:
                acc_done = 0
                for e in range(NEXP):
                    he_evac = {}
                    for fg in range(NFC2 // 4):
                        wb = fg % 2
                        p.wait("g", cx.sT, w1_free[wb])
                        p.op("g", "dma_start",
                             out=w1s[wb][:].rearrange("p (c f) -> p c f", c=NCH),
                             in_=w1v[e][:, :, fg * 512:(fg + 1) * 512], inc=cx.ldg)
                        p.wait("t", cx.ldg)
                        for fl in range(4):
                            fc = fg * 4 + fl
                            bk = fc % 2
                            if fc >= 2:
                                p.wait("t", cx.sS, he_evac[fc - 2])
                            else:
                                p.wait("t", cx.sV, acc_done)
                                if not (e == 0 and tt == 0):
                                    p.wait("t", cx.sS)
                            for c in range(NCH):
                                p.op("t", "matmul", cx.bank(bk),
                                     w1s[wb][:, c * 512 + fl * 128:
                                             c * 512 + (fl + 1) * 128],
                                     xnb[:, c * 512:(c + 1) * 512],
                                     start=(c == 0), stop=(c == NCH - 1),
                                     inc=(cx.sT if c == NCH - 1 else None))
                            if fl == 3:
                                w1_free[wb] = cx.sT.n
                            p.wait("s", cx.sT)
                            p.op("s", "activation",
                                 out=he[:, fc * 512:(fc + 1) * 512],
                                 in_=cx.bank(bk), func=AF.Relu,
                                 bias=b1_sb[:, e * 32 + fc:e * 32 + fc + 1],
                                 inc=cx.sS)
                            he_evac[fc] = cx.sS.n
                    all_he = cx.sS.n
                    p.wait("t", cx.sS, all_he)
                    if e > 0:
                        p.wait("t", cx.sV, acc_done)
                    else:
                        p.wait("t", cx.sS, mrep_done)
                    for kg in range(8):
                        wb = kg % 2
                        p.wait("g", cx.sT, w2_free[wb])
                        p.op("g", "dma_start",
                             out=w2s[wb][:].rearrange("p (g o) -> p g o", g=4),
                             in_=w2v[e][:, kg * 4:(kg + 1) * 4, :], inc=cx.ldg)
                        p.wait("t", cx.ldg)
                        for kl in range(4):
                            kf = kg * 4 + kl
                            for oc in range(NCH):
                                p.op("t", "matmul", cx.bank(oc),
                                     w2s[wb][:, kl * 1024 + oc * 128:
                                             kl * 1024 + (oc + 1) * 128],
                                     he[:, kf * 512:(kf + 1) * 512],
                                     start=(kf == 0), stop=(kf == 31),
                                     inc=(cx.sT if kl == 3 and oc == NCH - 1
                                          else None))
                        w2_free[wb] = cx.sT.n
                    p.wait("v", cx.sT)
                    if e == 0:
                        p.wait("v", cx.sS, mrep_done)
                    for oc in range(NCH):
                        csl = slice(oc * 512, (oc + 1) * 512)
                        p.op("v", "scalar_tensor_tensor", out=scrf[:],
                             in0=cx.bank(oc), scalar=b2_sb[:, e * 8 + oc:e * 8 + oc + 1],
                             in1=mrep[:, e * 512:(e + 1) * 512], op0=ALU.add,
                             op1=ALU.mult)
                        if e == 0:
                            p.op("v", "tensor_copy", acc[:, csl], scrf[:],
                                 inc=(cx.sV if oc == NCH - 1 else None))
                        else:
                            p.op("v", "tensor_add", acc[:, csl], acc[:, csl],
                                 scrf[:], inc=(cx.sV if oc == NCH - 1 else None))
                    acc_done = cx.sV.n
        p.wait("v", cx.stq, ro_st)
        if DBG >= 4:
            for oc in range(NCH):
                csl = slice(oc * 512, (oc + 1) * 512)
                p.op("v", "scalar_tensor_tensor", out=Ro[:, csl],
                     in0=acc[:, csl], scalar=0.5, in1=Rt[:, csl],
                     op0=ALU.mult, op1=ALU.add,
                     inc=(cx.sV if oc == NCH - 1 else None))
        else:
            p.op("v", "tensor_copy", Ro[:], Rt[:], inc=cx.sV)
        rt_free = cx.sV.n
        p.wait("q", cx.sV, rt_free)
        p.op("q", "dma_start", out=r_out_v[:, :, tsl],
             in_=Ro[:].rearrange("p (c t) -> p c t", c=NCH), inc=cx.stq)
        ro_st = cx.stq.n
    p.wait("q", cx.stq, cx.stq.n)
    es_close = esM.close()
    cx.barrier()


NH = 16
HD = 64


def emit_attn(cx, r_in, r_out, W, scr):
    p = cx.p
    u = cx.uid()
    cx.barrier()
    r_in_v = r_in.rearrange("(c p) t -> p c t", p=128)
    r_out_v = r_out.rearrange("(c p) t -> p c t", p=128)
    xn_d_v = scr["xn_d"].rearrange("(c p) t -> p c t", p=128)
    q_d_v = scr["q_d"].rearrange("(c p) t -> p c t", p=128)
    k_d_v = scr["k_d"].rearrange("(c p) t -> p c t", p=128)
    st_d_v = scr["st_d"].rearrange("(c p) t -> p c t", p=128)
    st2_d_v = scr["st2_d"].rearrange("(c p) t -> p c t", p=128)
    v_d_v = scr["v_d"].rearrange("(kt p) (h d) -> p kt h d", p=128, h=NH)

    esA = cx.scope()
    wq = cx.sbs(esA, f"a{u}_wq", [128, 8 * 1024], BF16)
    wk = cx.sbs(esA, f"a{u}_wk", [128, 8 * 1024], BF16)
    wv = cx.sbs(esA, f"a{u}_wv", [128, 8 * 1024], BF16)
    wo = cx.sbs(esA, f"a{u}_wo", [128, 8 * 1024], BF16)
    bq = cx.sbs(esA, f"a{u}_bq", [128, 8], F32)
    bk = cx.sbs(esA, f"a{u}_bk", [128, 8], F32)
    bo = cx.sbs(esA, f"a{u}_bo", [128, 8], F32)
    lnw = cx.sbs(esA, f"a{u}_lnw", [128, 8], F32)
    lnb = cx.sbs(esA, f"a{u}_lnb", [128, 8], F32)
    bv_rep = cx.sbs(esA, f"a{u}_bvrep", [128, 1024], F32)
    bv_sb = cx.sbs(esA, f"a{u}_bv", [1, 1024], F32)
    hsel = cx.sbs(esA, f"a{u}_hsel", [128, 2], BF16)
    hrep_f = cx.sbs(esA, f"a{u}_hrepf", [2, 128], F32)
    idf2 = cx.sbs(esA, f"a{u}_idf2", [2, 2], F32)
    for nm, dst in (("bq", bq), ("bk", bk), ("bo", bo), ("lnw", lnw),
                    ("lnb", lnb)):
        p.op("q", "dma_start", out=dst[:], in_=W[nm], inc=cx.ld)
    p.op("q", "dma_start", out=bv_sb[:], in_=W["bv"], inc=cx.ld)
    with cx.scope() as esw:
        wf = cx.sbs(esw, f"a{u}_wf", [128, 8 * 1024], F32, side="right")
        for nm, dst in (("Wq", wq), ("Wk", wk), ("Wv", wv), ("Wo", wo)):
            p.wait("q", cx.sV)
            p.op("q", "dma_start",
                 out=wf[:].rearrange("p (c o) -> p c o", c=8),
                 in_=W[nm].rearrange("(c p) o -> p c o", p=128), inc=cx.ld)
            p.wait("v", cx.ld)
            p.op("v", "tensor_copy", dst[:], wf[:], inc=cx.sV)
    scr_pidx = cx.sbs(esA, f"a{u}_pidx", [128, 2], F32)
    scr_h2 = cx.sbs(esA, f"a{u}_h2", [2, 3 * 128], F32)
    hselF = cx.sbs(esA, f"a{u}_hselF", [128, 2], F32)
    p.op("g", "iota", scr_pidx[:, 0:1], pattern=[[1, 1]], base=0,
         channel_multiplier=1, allow_small_or_imprecise_dtypes=True,
         inc=cx.sG)
    p.wait("g", cx.sG)
    p.op("g", "tensor_scalar", hselF[:, 0:1], scr_pidx[:, 0:1], 64.0, None,
         ALU.is_lt)
    p.op("g", "tensor_scalar", hselF[:, 1:2], scr_pidx[:, 0:1], 64.0, None,
         ALU.is_ge, inc=cx.sG)
    p.wait("g", cx.sG)
    p.op("g", "tensor_copy", hsel[:], hselF[:], inc=cx.sG)
    p.op("g", "iota", scr_h2[:, 0:128], pattern=[[1, 128]], base=0,
         channel_multiplier=0, allow_small_or_imprecise_dtypes=True)
    p.op("g", "iota", scr_h2[:, 128:129], pattern=[[1, 1]], base=0,
         channel_multiplier=64, allow_small_or_imprecise_dtypes=True,
         inc=cx.sG)
    p.wait("g", cx.sG)
    p.op("g", "tensor_scalar", scr_h2[:, 0:128], scr_h2[:, 0:128],
         scr_h2[:, 128:129], None, ALU.subtract, inc=cx.sG)
    p.wait("g", cx.sG)
    p.op("g", "tensor_scalar", scr_h2[:, 128:256], scr_h2[:, 0:128], 0.0,
         None, ALU.is_ge)
    p.op("g", "tensor_scalar", scr_h2[:, 256:384], scr_h2[:, 0:128], 64.0,
         None, ALU.is_lt, inc=cx.sG)
    p.wait("g", cx.sG)
    p.op("g", "tensor_mul", hrep_f[:], scr_h2[:, 128:256],
         scr_h2[:, 256:384], inc=cx.sG)
    p.op("g", "memset", idf2[:], 0.0)
    p.op("g", "affine_select", out=idf2[:], in_=idf2[:],
         compare_op=ALU.not_equal, fill=1.0, base=0, pattern=[[-1, 2]],
         channel_multiplier=1, inc=cx.sG)
    prep_g = cx.sG.n
    p.wait("t", cx.ld)
    p.wait("t", cx.sV)
    for nt in range(2):
        p.op("t", "matmul", cx.bank(nt), cx.ones_fT[:],
             bv_sb[:, nt * 512:(nt + 1) * 512], start=True, stop=True,
             inc=(cx.sT if nt == 1 else None))
    p.wait("s", cx.sT)
    p.op("s", "activation", out=bv_rep[:], in_=cx.psa[:, 0:1024],
         func=AF.Copy, inc=cx.sS)
    bv_done = cx.sS.n

    esQ = cx.scope()
    Rt = cx.sbs(esQ, f"a{u}_Rt", [128, NCH * 512], F32)
    xn = cx.sbs(esQ, f"a{u}_xn", [128, NCH * 512], F32)
    zq = cx.sbs(esQ, f"a{u}_zq", [128, NCH * 512], BF16)
    sq = [cx.sbs(esQ, f"a{u}_sq{i}", [128, 512], BF16) for i in range(2)]
    scrf = cx.sbs(esQ, f"a{u}_scrf", [128, 512], F32)
    scrf2 = cx.sbs(esQ, f"a{u}_scrf2", [128, 512], F32)
    mu_sb = cx.sbs(esQ, f"a{u}_mu", [1, 512], F32)
    var_sb = cx.sbs(esQ, f"a{u}_var", [1, 512], F32)
    mu2_sb = cx.sbs(esQ, f"a{u}_mu2", [1, 512], F32)
    qkob = cx.sbs(esQ, f"a{u}_qkob", [128, 512], BF16)
    s2n = cx.sbs(esQ, f"a{u}_s2n", [2, 512], F32)
    qn_bf = cx.sbs(esQ, f"a{u}_qnbf", [128, 512], BF16)
    vout = cx.sbs(esQ, f"a{u}_vout", [128, 1024], BF16)

    for tt in range(NTT):
        tsl = slice(tt * 512, (tt + 1) * 512)
        p.wait("q", cx.sV)
        p.wait("q", cx.stq, cx.stq.n)
        p.op("q", "dma_start",
             out=Rt[:].rearrange("p (c t) -> p c t", c=NCH),
             in_=r_in_v[:, :, tsl], inc=cx.ld)
        p.wait("t", cx.ld)
        p.wait("s", cx.ld)
        emit_ln_tile(cx, Rt[:], sq, mu_sb[:], var_sb[:], mu2_sb[:])
        emit_xn_tile(cx, Rt[:], xn[:], scrf)
        p.wait("q", cx.sV)
        p.op("q", "dma_start", out=xn_d_v[:, :, tsl],
             in_=xn[:].rearrange("p (c t) -> p c t", c=NCH), inc=cx.stq)
        p.wait("t", cx.sV)
        emit_ln_tile(cx, xn[:], sq, mu_sb[:], var_sb[:], mu2_sb[:])
        p.wait("v", cx.ld)
        for c in range(NCH):
            csl = slice(c * 512, (c + 1) * 512)
            p.op("v", "tensor_sub", scrf[:], xn[:, csl], cx.bank(2))
            p.op("v", "tensor_mul", scrf2[:], scrf[:], cx.bank(3))
            p.op("v", "tensor_scalar", zq[:, csl], scrf2[:],
                 lnw[:, c:c + 1], lnb[:, c:c + 1], ALU.mult, ALU.add,
                 inc=(cx.sV if c == NCH - 1 else None))
        zq_done = cx.sV.n
        for which, wmat, bias, dst_v in (("q", wq, bq, q_d_v),
                                         ("k", wk, bk, k_d_v)):
            for oc in range(NCH):
                p.wait("t", cx.sV, zq_done)
                p.wait("t", cx.stq, cx.stq.n)
                for c in range(NCH):
                    p.op("t", "matmul", cx.bank(4),
                         wmat[:, c * 1024 + oc * 128:c * 1024 + (oc + 1) * 128],
                         zq[:, c * 512:(c + 1) * 512], start=(c == 0),
                         stop=(c == NCH - 1),
                         inc=(cx.sT if c == NCH - 1 else None))
                p.wait("v", cx.sT)
                p.op("v", "tensor_scalar", scrf[:], cx.bank(4),
                     bias[:, oc:oc + 1], None, ALU.add, inc=cx.sV)
                p.wait("s", cx.sV)
                p.op("s", "activation", out=qkob[:], in_=scrf[:],
                     func=AF.Square, inc=cx.sS)
                p.wait("t", cx.sS)
                p.op("t", "matmul", cx.banksl(5, slice(0, 512), 2),
                     hsel[:], qkob[:], start=True, stop=True, inc=cx.sT)
                p.wait("v", cx.sT)
                p.op("v", "tensor_copy", s2n[:], cx.banksl(5, slice(0, 512), 2),
                     inc=cx.sV)
                p.wait("s", cx.sV)
                p.op("s", "activation", out=s2n[:], in_=s2n[:], func=AF.Ln)
                p.op("s", "activation", out=s2n[:], in_=s2n[:], func=AF.Exp,
                     scale=cx.constf(0.5), inc=cx.sS)
                p.wait("v", cx.sS)
                p.op("v", "tensor_scalar_max", s2n[:], s2n[:], 1e-12,
                     inc=cx.sV)
                p.wait("v", cx.sV)
                p.op("v", "reciprocal", s2n[:], s2n[:], inc=cx.sV)
                p.wait("t", cx.sV)
                p.op("t", "matmul", cx.banksl(5, slice(0, 512), 128),
                     hrep_f[:], s2n[:], start=True, stop=True, inc=cx.sT)
                p.wait("v", cx.sT)
                p.op("v", "tensor_mul", qn_bf[:], scrf[:],
                     cx.banksl(5, slice(0, 512), 128), inc=cx.sV)
                p.wait("q", cx.sV)
                p.op("q", "dma_start", out=dst_v[:, oc, tsl], in_=qn_bf[:],
                     inc=cx.stq)
        p.wait("t", cx.sS, bv_done)
        for stile in range(4):
            p.wait("t", cx.stq, cx.stq.n)
            p.wait("t", cx.sV)
            for half in range(2):
                for c in range(NCH):
                    p.op("t", "matmul", cx.bank(6 + half),
                         zq[:, c * 512 + stile * 128:
                            c * 512 + (stile + 1) * 128],
                         wv[:, c * 1024 + half * 512:
                            c * 1024 + (half + 1) * 512],
                         start=(c == 0), stop=(c == NCH - 1),
                         inc=(cx.sT if c == NCH - 1 else None))
            p.wait("v", cx.sT)
            for half in range(2):
                p.op("v", "tensor_add", vout[:, half * 512:(half + 1) * 512],
                     cx.bank(6 + half), bv_rep[:, half * 512:(half + 1) * 512],
                     inc=(cx.sV if half == 1 else None))
            p.wait("q", cx.sV)
            p.op("q", "dma_start",
                 out=scr["v_d"][(tt * 4 + stile) * 128:
                                (tt * 4 + stile + 1) * 128, :],
                 in_=vout[:], inc=cx.stq)
    p.wait("q", cx.stq, cx.stq.n)
    esQ.close()
    cx.barrier()

    esB = cx.scope()
    kT = cx.sbs(esB, f"a{u}_kT", [128, T], BF16)
    stT = cx.sbs(esB, f"a{u}_stT", [128, T], BF16)
    stN = cx.sbs(esB, f"a{u}_stN", [128, T], BF16)
    vwk = [cx.sbs(esB, f"a{u}_vwk{i}", [128, 32 * 65], BF16) for i in range(2)]
    pT = [cx.sbs(esB, f"a{u}_pT0", [128, 32 * 1024], BF16)]
    sqc = cx.sbs(esB, f"a{u}_sqc", [128, T], BF16)
    cbs = [cx.sbs(esB, f"a{u}_cbs{i}", [128, 1], F32) for i in range(2)]
    negc = cx.sbs(esB, f"a{u}_negc", [2, 1], F32)
    negcT = cx.sbs(esB, f"a{u}_negcT", [1, 2], F32)
    rq = cx.sbs(esB, f"a{u}_rq", [1, 512], F32)
    avsb = [cx.sbs(esB, f"a{u}_avsb{i}", [65, 512], F32) for i in range(2)]
    p.op("g", "memset", vwk[0][:].rearrange("p (kt d) -> p kt d", kt=32)[:, :, 64:65], 1.0)
    p.op("g", "memset", vwk[1][:].rearrange("p (kt d) -> p kt d", kt=32)[:, :, 64:65], 1.0, inc=cx.sG)
    vones = cx.sG.n

    for s_ in range(2):
        src_v = q_d_v if s_ == 0 else st_d_v
        out_v = st_d_v if s_ == 0 else st2_d_v
        for hp in range(NH // 2):
            p.wait("q", cx.sT)
            p.wait("q", cx.stq, cx.stq.n)
            p.op("q", "dma_start", out=kT[:],
                 in_=k_d_v[:, hp, :], inc=cx.ld)
            p.op("q", "dma_start", out=stT[:],
                 in_=src_v[:, hp, :], inc=cx.ld)
            p.wait("g", cx.sT)
            p.wait("g", cx.sG, vones)
            for b in range(2):
                p.op("g", "dma_start",
                     out=vwk[b][:].rearrange("p (kt d) -> p kt d",
                                             kt=32)[:, :, 0:64],
                     in_=v_d_v[:, :, 2 * hp + b, :], inc=cx.ldg)
            ld_hp = cx.ld.n
            ldg_hp = cx.ldg.n
            p.wait("s", cx.ld, ld_hp)
            p.op("s", "activation", out=sqc[:], in_=stT[:], func=AF.Square,
                 inc=cx.sS)
            p.wait("t", cx.sS)
            p.wait("t", cx.sV)
            for nt in range(NTT):
                p.op("t", "matmul", cx.psa[0:2, nt * 512:(nt + 1) * 512],
                     hsel[:], sqc[:, nt * 512:(nt + 1) * 512],
                     start=True, stop=True,
                     inc=(cx.sT if nt == NTT - 1 else None))
            p.wait("v", cx.sT)
            p.op("v", "reduce_max", negc[:], cx.psa[0:2, 0:4096], axis=AX.X,
                 inc=cx.sV)
            p.wait("t", cx.sV)
            p.op("t", "matmul", cx.banksl(1, slice(0, 2), 1), negc[:],
                 idf2[:], start=True, stop=True, inc=cx.sT)
            p.wait("v", cx.sT)
            p.op("v", "tensor_copy", negcT[:], cx.banksl(1, slice(0, 2), 1),
                 inc=cx.sV)
            p.wait("v", cx.sV)
            p.op("v", "reduce_max", negcT[:, 0:1], negcT[:], axis=AX.X,
                 inc=cx.sV)
            p.wait("s", cx.sV)
            p.op("s", "activation", out=negcT[:, 0:1], in_=negcT[:, 0:1],
                 func=AF.Ln)
            p.op("s", "activation", out=negcT[:, 0:1], in_=negcT[:, 0:1],
                 func=AF.Exp, scale=cx.constf(0.5), inc=cx.sS)
            p.wait("v", cx.sS)
            p.op("v", "tensor_scalar_mul", negcT[:, 0:1], negcT[:, 0:1],
                 -1.0, inc=cx.sV)
            p.wait("t", cx.sV)
            p.op("t", "matmul", cx.banksl(1, slice(0, 1), 128),
                 cx.ones_fT[:], negcT[:, 0:1], start=True, stop=True,
                 inc=cx.sT)
            p.wait("s", cx.sT)
            p.op("s", "activation", out=cbs[0][:],
                 in_=cx.banksl(1, slice(0, 1), 128), func=AF.Copy,
                 inc=cx.sS)
            cb_done = cx.sS.n
            exp_done = {}
            avb_free = {0: 0, 1: 0}
            avb_free_s = {0: 0, 1: 0}
            for qc in range(NTT):
                qsl = slice(qc * 512, (qc + 1) * 512)
                av0 = 4 if (qc % 2 == 0) else 6
                for kt in range(33):
                    if kt < 32:
                        bb = 2 * (kt % 2)
                        if kt >= 2:
                            p.wait("t", cx.sS, exp_done[kt - 2])
                        elif qc == 0:
                            p.wait("t", cx.sS, cb_done)
                        else:
                            p.wait("t", cx.sS, prev_exp[kt - 2])
                        if kt == 0:
                            p.wait("t", cx.sV, avb_free[qc % 2])
                            p.wait("t", cx.sS, avb_free_s[qc % 2])
                        for b in range(2):
                            p.op("t", "matmul", cx.bank(bb + b),
                                 kT[b * 64:(b + 1) * 64,
                                    kt * 128:(kt + 1) * 128],
                                 stT[b * 64:(b + 1) * 64, qsl],
                                 start=True, stop=True,
                                 tile_position=(64 * b, 0),
                                 inc=(cx.sT if b == 1 else None))
                        mmd = cx.sT.n
                        p.wait("s", cx.sT, mmd)
                        p.op("s", "activation",
                             out=pT[0][:, kt * 1024:(kt + 1) * 1024],
                             in_=cx.psa[:, bb * 512:(bb + 2) * 512],
                             func=AF.Exp, bias=cbs[0][:, 0:1], inc=cx.sS)
                        exp_done[kt] = cx.sS.n
                    if kt >= 1:
                        ka = kt - 1
                        p.wait("t", cx.sS, exp_done[ka])
                        for b in range(2):
                            p.op("t", "matmul", cx.bank(av0 + b, 65),
                                 vwk[b][:, ka * 65:(ka + 1) * 65],
                                 pT[0][:, ka * 1024 + b * 512:
                                        ka * 1024 + (b + 1) * 512],
                                 start=(ka == 0), stop=(ka == 31),
                                 inc=(cx.sT if ka == 31 and b == 1 else None))
                prev_exp = {-2: exp_done.get(30, 0), -1: exp_done.get(31, 0)}
                for b in range(2):
                    p.wait("s", cx.sT)
                    p.op("s", "activation", out=avsb[b][:],
                         in_=cx.psa[0:65, (av0 + b) * 512:(av0 + b) * 512 + 512],
                         func=AF.Copy, inc=cx.sS)
                    p.wait("v", cx.sS)
                    p.op("v", "reciprocal", rq[:], avsb[b][64:65, :],
                         inc=cx.sV)
                    p.wait("t", cx.sV)
                    p.op("t", "matmul",
                         cx.banksl(av0 + b, slice(0, 512), 64),
                         cx.ones_fT[:, 0:64], rq[:], start=True, stop=True,
                         inc=cx.sT)
                    p.wait("v", cx.sT)
                    if qc <= 1:
                        p.wait("v", cx.stq, cx.stq.n)
                    p.op("v", "tensor_mul", stN[b * 64:(b + 1) * 64, qsl],
                         avsb[b][0:64, :],
                         cx.banksl(av0 + b, slice(0, 512), 64), inc=cx.sV)
                avb_free[qc % 2] = cx.sV.n
                avb_free_s[qc % 2] = cx.sS.n
            p.wait("q", cx.sV, max(avb_free.values()))
            p.op("q", "dma_start", out=out_v[:, hp, :], in_=stN[:],
                 inc=cx.stq)
    p.wait("q", cx.stq, cx.stq.n)
    esB.close()
    cx.barrier()

    esC = cx.scope()
    Rt2 = cx.sbs(esC, f"a{u}_Rt2", [128, NCH * 512], F32)
    xn2 = cx.sbs(esC, f"a{u}_xn2", [128, NCH * 512], F32)
    st2 = cx.sbs(esC, f"a{u}_st2", [128, NCH * 512], BF16)
    Ro = cx.sbs(esC, f"a{u}_Ro", [128, NCH * 512], F32)
    scrg = cx.sbs(esC, f"a{u}_scrg", [128, 512], F32)
    ro_st = 0
    for tt in range(NTT):
        tsl = slice(tt * 512, (tt + 1) * 512)
        p.wait("q", cx.sV)
        p.op("q", "dma_start",
             out=Rt2[:].rearrange("p (c t) -> p c t", c=NCH),
             in_=r_in_v[:, :, tsl], inc=cx.ld)
        p.op("q", "dma_start",
             out=xn2[:].rearrange("p (c t) -> p c t", c=NCH),
             in_=xn_d_v[:, :, tsl], inc=cx.ld)
        p.op("q", "dma_start",
             out=st2[:].rearrange("p (c t) -> p c t", c=NCH),
             in_=st2_d_v[:, :, tsl], inc=cx.ld)
        ld_t = cx.ld.n
        p.wait("t", cx.ld, ld_t)
        fc_evac = {}
        for oc in range(NCH):
            bk = oc % 2
            if oc >= 2:
                p.wait("t", cx.sV, fc_evac[oc - 2])
            elif tt > 0:
                p.wait("t", cx.sV, fc_evac_prev[NCH - 2 + oc])
            for c in range(NCH):
                p.op("t", "matmul", cx.bank(bk),
                     wo[:, c * 1024 + oc * 128:c * 1024 + (oc + 1) * 128],
                     st2[:, c * 512:(c + 1) * 512],
                     start=(c == 0), stop=(c == NCH - 1),
                     inc=(cx.sT if c == NCH - 1 else None))
            p.wait("v", cx.sT)
            if oc == 0:
                p.wait("v", cx.stq, ro_st)
            csl = slice(oc * 512, (oc + 1) * 512)
            p.op("v", "scalar_tensor_tensor", out=scrg[:], in0=cx.bank(bk),
                 scalar=bo[:, oc:oc + 1], in1=xn2[:, csl], op0=ALU.add,
                 op1=ALU.add, inc=cx.sV)
            p.wait("v", cx.sV)
            p.op("v", "tensor_add", Ro[:, csl], scrg[:], Rt2[:, csl],
                 inc=cx.sV)
            fc_evac[oc] = cx.sV.n
        fc_evac_prev = fc_evac
        p.wait("q", cx.sV)
        p.op("q", "dma_start", out=r_out_v[:, :, tsl],
             in_=Ro[:].rearrange("p (c t) -> p c t", c=NCH), inc=cx.stq)
        ro_st = cx.stq.n
    p.wait("q", cx.stq, cx.stq.n)
    esC.close()
    esA.close()
    cx.barrier()


from contextlib import ExitStack
from concourse.bass_utils import run_bass_kernel_spmd

NCORES = 8


def _build_ffn0():
    nc = bass.Bass("TRN2", target_bir_lowering=False)
    r_in = nc.dram_tensor("r_in", [1024, 4096], F32, kind="ExternalInput").ap()
    wgT_i = nc.dram_tensor("wgT_i", [1024, 3072], I32, kind="ExternalInput").ap()
    wuT_i = nc.dram_tensor("wuT_i", [1024, 3072], I32, kind="ExternalInput").ap()
    wdT_i = nc.dram_tensor("wdT_i", [3072, 1024], I32, kind="ExternalInput").ap()
    r_out = nc.dram_tensor("r_out", [1024, 4096], F32, kind="ExternalOutput").ap()
    wgT = nc.dram_tensor("wgT", [1024, 3072], BF16, kind="Internal").ap()
    wuT = nc.dram_tensor("wuT", [1024, 3072], BF16, kind="Internal").ap()
    wdT = nc.dram_tensor("wdT", [3072, 1024], BF16, kind="Internal").ap()
    es = ExitStack()
    cx = Ctx(nc, es)
    build_consts(cx)
    prep_ffn0_weights(cx, wgT_i, wuT_i, wdT_i, wgT, wuT, wdT)
    emit_ffn0(cx, r_in, r_out, wgT, wuT, wdT)
    cx.finalize()
    with nc.Block() as block:
        cx.p.replay(block)
    es.close()
    return nc


def _build_mamba():
    nc = bass.Bass("TRN2", target_bir_lowering=False)
    r_in = nc.dram_tensor("r_in", [1024, 4096], F32, kind="ExternalInput").ap()
    r_out = nc.dram_tensor("r_out", [1024, 4096], F32, kind="ExternalOutput").ap()
    Wsh = dict(fc1_wT=[1024, 1024], fc1_b=[128, 8], inpT=[1024, 4096],
               conv_w=[128, 64], conv_b=[128, 16], xpT=[2048, 96],
               dtpT=[64, 2048], dtp_b=[128, 16], A_log=[128, 256],
               Dv=[128, 16], opT=[2048, 1024], fc2_wT=[1024, 1024],
               fc2_b=[128, 8])
    W = {k: nc.dram_tensor(k, s, F32, kind="ExternalInput").ap()
         for k, s in Wsh.items()}
    scr = {
        "inpT_bf": nc.dram_tensor("inpT_bf", [1024, 4096], BF16, kind="Internal").ap(),
        "opT_bf": nc.dram_tensor("opT_bf", [2048, 1024], BF16, kind="Internal").ap(),
        "u_d": nc.dram_tensor("u_d", [2048, 4096], F32, kind="Internal").ap(),
        "z_d": nc.dram_tensor("z_d", [2048, 4096], F32, kind="Internal").ap(),
        "y_d": nc.dram_tensor("y_d", [2048, 4096], F32, kind="Internal").ap(),
    }
    es = ExitStack()
    cx = Ctx(nc, es)
    build_consts(cx)
    emit_mamba(cx, r_in, r_out, W, scr)
    cx.finalize()
    with nc.Block() as block:
        cx.p.replay(block)
    es.close()
    return nc


def _build_moe():
    nc = bass.Bass("TRN2", target_bir_lowering=False)
    r_in = nc.dram_tensor("r_in", [1024, 4096], F32, kind="ExternalInput").ap()
    r_out = nc.dram_tensor("r_out", [1024, 4096], F32, kind="ExternalOutput").ap()
    W = {
        "gate_wT": nc.dram_tensor("gate_wT", [1024, 8], F32, kind="ExternalInput").ap(),
        "gbias": nc.dram_tensor("gbias", [8, 2], F32, kind="ExternalInput").ap(),
        "b1": nc.dram_tensor("b1", [128, 256], F32, kind="ExternalInput").ap(),
        "b2": nc.dram_tensor("b2", [128, 64], F32, kind="ExternalInput").ap(),
        "w1": nc.dram_tensor("w1", [8, 1024, 4096], F32, kind="ExternalInput").ap(),
        "w2": nc.dram_tensor("w2", [8, 4096, 1024], F32, kind="ExternalInput").ap(),
    }
    scr = {
        "w1_bf": nc.dram_tensor("w1_bf", [8, 1024, 4096], BF16, kind="Internal").ap(),
        "w2_bf": nc.dram_tensor("w2_bf", [8, 4096, 1024], BF16, kind="Internal").ap(),
    }
    es = ExitStack()
    cx = Ctx(nc, es)
    build_consts(cx)
    emit_moe(cx, r_in, r_out, W, scr)
    cx.finalize()
    with nc.Block() as block:
        cx.p.replay(block)
    es.close()
    return nc


def _build_attn():
    nc = bass.Bass("TRN2", target_bir_lowering=False)
    r_in = nc.dram_tensor("r_in", [1024, 4096], F32, kind="ExternalInput").ap()
    r_out = nc.dram_tensor("r_out", [1024, 4096], F32, kind="ExternalOutput").ap()
    W = {}
    for nm in ("Wq", "Wk", "Wv", "Wo"):
        W[nm] = nc.dram_tensor(nm, [1024, 1024], F32, kind="ExternalInput").ap()
    for nm in ("bq", "bk", "bo", "lnw", "lnb"):
        W[nm] = nc.dram_tensor(nm, [128, 8], F32, kind="ExternalInput").ap()
    W["bv"] = nc.dram_tensor("bv", [1, 1024], F32, kind="ExternalInput").ap()
    scr = {
        "xn_d": nc.dram_tensor("xn_d", [1024, 4096], F32, kind="Internal").ap(),
        "q_d": nc.dram_tensor("q_d", [1024, 4096], BF16, kind="Internal").ap(),
        "k_d": nc.dram_tensor("k_d", [1024, 4096], BF16, kind="Internal").ap(),
        "st_d": nc.dram_tensor("st_d", [1024, 4096], BF16, kind="Internal").ap(),
        "st2_d": nc.dram_tensor("st2_d", [1024, 4096], BF16, kind="Internal").ap(),
        "v_d": nc.dram_tensor("v_d", [4096, 1024], BF16, kind="Internal").ap(),
    }
    es = ExitStack()
    cx = Ctx(nc, es)
    build_consts(cx)
    emit_attn(cx, r_in, r_out, W, scr)
    cx.finalize()
    with nc.Block() as block:
        cx.p.replay(block)
    es.close()
    return nc


def kernel_unfused(**inputs):
    g = inputs
    ct = np.ascontiguousarray
    r = [ct(g["L0_attn_x"][i].T) for i in range(NCORES)]

    ffn0_w = {
        "wgT_i": ct(g["L0_ffn_w_gate"].T),
        "wuT_i": ct(g["L0_ffn_w_up"].T),
        "wdT_i": ct(g["L0_ffn_w_down"].T),
    }
    mamba_w = {
        "fc1_wT": ct(np.asarray(g["L0_attn_fc1_w"]).T),
        "fc1_b": to_pcol(g["L0_attn_fc1_b"]),
        "inpT": ct(np.asarray(g["L0_attn_in_proj_w"]).T),
        "conv_w": to_pcol(np.asarray(g["L0_attn_conv_w"])),
        "conv_b": to_pcol(g["L0_attn_conv_b"]),
        "xpT": ct(np.asarray(g["L0_attn_x_proj_w"]).T),
        "dtpT": ct(np.asarray(g["L0_attn_dt_proj_w"]).T),
        "dtp_b": to_pcol(g["L0_attn_dt_proj_b"]),
        "A_log": to_pcol(np.asarray(g["L0_attn_A_log"])),
        "Dv": to_pcol(g["L0_attn_D"]),
        "opT": ct(np.asarray(g["L0_attn_out_proj_w"]).T),
        "fc2_wT": ct(np.asarray(g["L0_attn_fc2_w"]).T),
        "fc2_b": to_pcol(g["L0_attn_fc2_b"]),
    }
    moe_w = {
        "gate_wT": ct(np.asarray(g["L1_ffn_gate_w"]).T),
        "gbias": ct(np.stack([np.asarray(g["L1_ffn_gate_b"]),
                              np.asarray(g["L1_ffn_expert_bias"])], axis=1)),
        "b1": np.concatenate([to_pcol(np.asarray(g["L1_ffn_b1"])[e])
                              for e in range(8)], axis=1),
        "b2": np.concatenate([to_pcol(np.asarray(g["L1_ffn_b2"])[e])
                              for e in range(8)], axis=1),
        "w1": ct(np.asarray(g["L1_ffn_w1"])),
        "w2": ct(np.asarray(g["L1_ffn_w2"])),
    }
    attn_w = {
        "Wq": ct(np.asarray(g["L1_attn_Wq"])),
        "Wk": ct(np.asarray(g["L1_attn_Wk"])),
        "Wv": ct(np.asarray(g["L1_attn_Wv"])),
        "Wo": ct(np.asarray(g["L1_attn_Wo"])),
        "bq": to_pcol(g["L1_attn_bq"]),
        "bk": to_pcol(g["L1_attn_bk"]),
        "bo": to_pcol(g["L1_attn_bo"]),
        "lnw": to_pcol(g["L1_attn_ln_w"]),
        "lnb": to_pcol(g["L1_attn_ln_b"]),
        "bv": ct(np.asarray(g["L1_attn_bv"]).reshape(1, 1024)),
    }

    nc_ffn = _build_ffn0()
    nc_mamba = _build_mamba()
    nc_moe = _build_moe()
    nc_attn = _build_attn()

    def launch(nc, wdict):
        nonlocal r
        in_maps = [dict(wdict, r_in=r[i]) for i in range(NCORES)]
        res = run_bass_kernel_spmd(nc, in_maps, core_ids=list(range(NCORES)))
        r = [res.results[i]["r_out"] for i in range(NCORES)]

    launch(nc_ffn, ffn0_w)
    launch(nc_mamba, mamba_w)
    launch(nc_ffn, ffn0_w)
    launch(nc_moe, moe_w)
    launch(nc_attn, attn_w)
    launch(nc_moe, moe_w)

    out = np.stack([r[i].T for i in range(NCORES)], axis=0)
    return out.astype(np.float32)


def _build_fused():
    nc = bass.Bass("TRN2", target_bir_lowering=False)
    x_in = nc.dram_tensor("x_in", [1024, 4096], F32, kind="ExternalInput").ap()
    r_fin = nc.dram_tensor("r_fin", [1024, 4096], F32,
                           kind="ExternalOutput").ap()
    rbuf = [nc.dram_tensor(f"rb{i}", [1024, 4096], F32, kind="Internal").ap()
            for i in range(5)]
    wgT_i = nc.dram_tensor("wgT_i", [1024, 3072], I32, kind="ExternalInput").ap()
    wuT_i = nc.dram_tensor("wuT_i", [1024, 3072], I32, kind="ExternalInput").ap()
    wdT_i = nc.dram_tensor("wdT_i", [3072, 1024], I32, kind="ExternalInput").ap()
    wgT = nc.dram_tensor("wgT", [1024, 3072], BF16, kind="Internal").ap()
    wuT = nc.dram_tensor("wuT", [1024, 3072], BF16, kind="Internal").ap()
    wdT = nc.dram_tensor("wdT", [3072, 1024], BF16, kind="Internal").ap()
    Wm_sh = dict(fc1_wT=[1024, 1024], fc1_b=[128, 8], inpT=[1024, 4096],
                 conv_w=[128, 64], conv_b=[128, 16], xpT=[2048, 96],
                 dtpT=[64, 2048], dtp_b=[128, 16], A_log=[128, 256],
                 Dv=[128, 16], opT=[2048, 1024], fc2_wT=[1024, 1024],
                 fc2_b=[128, 8])
    Wm = {k: nc.dram_tensor("m_" + k, s, F32, kind="ExternalInput").ap()
          for k, s in Wm_sh.items()}
    scrm = {
        "inpT_bf": nc.dram_tensor("inpT_bf", [1024, 4096], BF16, kind="Internal").ap(),
        "opT_bf": nc.dram_tensor("opT_bf", [2048, 1024], BF16, kind="Internal").ap(),
        "u_d": nc.dram_tensor("u_d", [2048, 4096], F32, kind="Internal").ap(),
        "z_d": nc.dram_tensor("z_d", [2048, 4096], F32, kind="Internal").ap(),
        "y_d": nc.dram_tensor("y_d", [2048, 4096], F32, kind="Internal").ap(),
    }
    Wo_ = {
        "gate_wT": nc.dram_tensor("gate_wT", [1024, 8], F32, kind="ExternalInput").ap(),
        "gbias": nc.dram_tensor("gbias", [8, 2], F32, kind="ExternalInput").ap(),
        "b1": nc.dram_tensor("b1", [128, 256], F32, kind="ExternalInput").ap(),
        "b2": nc.dram_tensor("b2", [128, 64], F32, kind="ExternalInput").ap(),
        "w1": nc.dram_tensor("w1", [8, 1024, 4096], F32, kind="ExternalInput").ap(),
        "w2": nc.dram_tensor("w2", [8, 4096, 1024], F32, kind="ExternalInput").ap(),
    }
    scro = {
        "w1_bf": nc.dram_tensor("w1_bf", [8, 1024, 4096], BF16, kind="Internal").ap(),
        "w2_bf": nc.dram_tensor("w2_bf", [8, 4096, 1024], BF16, kind="Internal").ap(),
    }
    Wa = {}
    for nm in ("Wq", "Wk", "Wv", "Wo"):
        Wa[nm] = nc.dram_tensor("a_" + nm, [1024, 1024], F32, kind="ExternalInput").ap()
    for nm in ("bq", "bk", "bo", "lnw", "lnb"):
        Wa[nm] = nc.dram_tensor("a_" + nm, [128, 8], F32, kind="ExternalInput").ap()
    Wa["bv"] = nc.dram_tensor("a_bv", [1, 1024], F32, kind="ExternalInput").ap()
    scra = {
        "xn_d": nc.dram_tensor("xn_d", [1024, 4096], F32, kind="Internal").ap(),
        "q_d": nc.dram_tensor("q_d", [1024, 4096], BF16, kind="Internal").ap(),
        "k_d": nc.dram_tensor("k_d", [1024, 4096], BF16, kind="Internal").ap(),
        "st_d": nc.dram_tensor("st_d", [1024, 4096], BF16, kind="Internal").ap(),
        "st2_d": nc.dram_tensor("st2_d", [1024, 4096], BF16, kind="Internal").ap(),
        "v_d": nc.dram_tensor("v_d", [4096, 1024], BF16, kind="Internal").ap(),
    }

    es = ExitStack()
    cx = Ctx(nc, es)
    build_consts(cx)
    prep_ffn0_weights(cx, wgT_i, wuT_i, wdT_i, wgT, wuT, wdT)
    cx.phase_boundary()
    emit_ffn0(cx, x_in, rbuf[0], wgT, wuT, wdT)
    cx.phase_boundary()
    emit_mamba(cx, rbuf[0], rbuf[1], Wm, scrm)
    cx.phase_boundary()
    emit_ffn0(cx, rbuf[1], rbuf[2], wgT, wuT, wdT)
    cx.phase_boundary()
    emit_moe(cx, rbuf[2], rbuf[3], Wo_, scro)
    cx.phase_boundary()
    emit_attn(cx, rbuf[3], rbuf[4], Wa, scra)
    cx.phase_boundary()
    emit_moe(cx, rbuf[4], r_fin, Wo_, scro)
    cx.finalize()
    with nc.Block() as block:
        cx.p.replay(block)
    es.close()
    return nc, cx.p.n_inst


def kernel(**inputs):
    g = inputs
    ct = np.ascontiguousarray
    wmap = {
        "wgT_i": ct(np.asarray(g["L0_ffn_w_gate"]).T),
        "wuT_i": ct(np.asarray(g["L0_ffn_w_up"]).T),
        "wdT_i": ct(np.asarray(g["L0_ffn_w_down"]).T),
        "m_fc1_wT": ct(np.asarray(g["L0_attn_fc1_w"]).T),
        "m_fc1_b": to_pcol(g["L0_attn_fc1_b"]),
        "m_inpT": ct(np.asarray(g["L0_attn_in_proj_w"]).T),
        "m_conv_w": to_pcol(np.asarray(g["L0_attn_conv_w"])),
        "m_conv_b": to_pcol(g["L0_attn_conv_b"]),
        "m_xpT": ct(np.asarray(g["L0_attn_x_proj_w"]).T),
        "m_dtpT": ct(np.asarray(g["L0_attn_dt_proj_w"]).T),
        "m_dtp_b": to_pcol(g["L0_attn_dt_proj_b"]),
        "m_A_log": to_pcol(np.asarray(g["L0_attn_A_log"])),
        "m_Dv": to_pcol(g["L0_attn_D"]),
        "m_opT": ct(np.asarray(g["L0_attn_out_proj_w"]).T),
        "m_fc2_wT": ct(np.asarray(g["L0_attn_fc2_w"]).T),
        "m_fc2_b": to_pcol(g["L0_attn_fc2_b"]),
        "gate_wT": ct(np.asarray(g["L1_ffn_gate_w"]).T),
        "gbias": ct(np.stack([np.asarray(g["L1_ffn_gate_b"]),
                              np.asarray(g["L1_ffn_expert_bias"])], axis=1)),
        "b1": np.concatenate([to_pcol(np.asarray(g["L1_ffn_b1"])[e])
                              for e in range(8)], axis=1),
        "b2": np.concatenate([to_pcol(np.asarray(g["L1_ffn_b2"])[e])
                              for e in range(8)], axis=1),
        "w1": ct(np.asarray(g["L1_ffn_w1"])),
        "w2": ct(np.asarray(g["L1_ffn_w2"])),
        "a_Wq": ct(np.asarray(g["L1_attn_Wq"])),
        "a_Wk": ct(np.asarray(g["L1_attn_Wk"])),
        "a_Wv": ct(np.asarray(g["L1_attn_Wv"])),
        "a_Wo": ct(np.asarray(g["L1_attn_Wo"])),
        "a_bq": to_pcol(g["L1_attn_bq"]),
        "a_bk": to_pcol(g["L1_attn_bk"]),
        "a_bo": to_pcol(g["L1_attn_bo"]),
        "a_lnw": to_pcol(g["L1_attn_ln_w"]),
        "a_lnb": to_pcol(g["L1_attn_ln_b"]),
        "a_bv": ct(np.asarray(g["L1_attn_bv"]).reshape(1, 1024)),
    }
    nc, _ = _build_fused()
    in_maps = [dict(wmap, x_in=ct(np.asarray(g["L0_attn_x"])[i].T))
               for i in range(NCORES)]
    res = run_bass_kernel_spmd(nc, in_maps, core_ids=list(range(NCORES)))
    out = np.stack([res.results[i]["r_fin"].T for i in range(NCORES)], axis=0)
    return out.astype(np.float32)


# version 09
# speedup vs baseline: 1.1707x; 1.0100x over previous
import numpy as np
import concourse.bass as bass
import concourse.mybir as mybir

F32 = mybir.dt.float32
BF16 = mybir.dt.bfloat16
I32 = mybir.dt.int32
AF = mybir.ActivationFunctionType
ALU = mybir.AluOpType
AX = mybir.AxisListType

T = 4096
D = 1024
NCH = D // 128
NTT = T // 512


class Sem:
    def __init__(self, h):
        self.h = h
        self.n = 0


class Prog:
    ENGS = "qgsvt"

    def __init__(self, nc):
        self.nc = nc
        self.ops = {e: [] for e in self.ENGS}
        self.n_inst = 0

    def wait(self, eng, sem, val=None):
        v = sem.n if val is None else val
        if v <= 0:
            return
        self.ops[eng].append(lambda E, s=sem.h, vv=v: E.wait_ge(s, vv))

    def op(self, eng, name, *args, inc=None, **kwargs):
        amt = 0
        if inc is not None:
            amt = 16 if "dma" in name else 1
            inc.n += amt

        def fn(E, name=name, args=args, kwargs=kwargs, inc=inc, amt=amt):
            ins = getattr(E, name)(*args, **kwargs)
            if inc is not None:
                ins.then_inc(inc.h, amt)
        self.ops[eng].append(fn)
        self.n_inst += 1

    def replay(self, block):
        mapping = {"q": block.sync, "g": block.gpsimd, "s": block.scalar,
                   "v": block.vector, "t": block.tensor}
        for e in self.ENGS:
            ops = self.ops[e]

            def body(E, ops=ops):
                for fn in ops:
                    fn(E)
            mapping[e](body)


class Ctx:
    def __init__(self, nc, es):
        self.nc = nc
        self.es = es
        self.p = Prog(nc)
        self._semgen = 0
        self.new_sems()
        self.psa = es.enter_context(nc.psum_tensor("psarena", [128, 4096], F32))
        self._uid = 0
        self.sC = Sem(es.enter_context(nc.semaphore("sC")))
        self._consts = {}
        self._const_t = es.enter_context(nc.sbuf_tensor("constp", [128, 32], F32))
        self._const_n = 0

    def sb(self, name, shape, dt=F32, side=None):
        return self.es.enter_context(
            self.nc.sbuf_tensor(name, shape, dt, side=side))

    def bank(self, b, rows=128):
        return self.psa[0:rows, b * 512:(b + 1) * 512]

    def banksl(self, b, sl, rows=128):
        return self.psa[0:rows, b * 512 + sl.start: b * 512 + sl.stop]

    def uid(self):
        self._uid += 1
        return self._uid

    def new_sems(self):
        g = self._semgen
        self._semgen += 1
        for nm in ("ld", "ldg", "stq", "sT", "sV", "sS", "sG"):
            h = self.es.enter_context(self.nc.semaphore(f"{nm}{g}"))
            setattr(self, nm, Sem(h))

    def phase_boundary(self):
        self.barrier()
        self.new_sems()

    def scope(self):
        from contextlib import ExitStack
        return ExitStack()

    def barrier(self):
        sems = [self.ld, self.ldg, self.stq, self.sT, self.sV, self.sS,
                self.sG]
        for e in "qgsvt":
            for s in sems:
                self.p.wait(e, s, s.n)

    def sbs(self, es, name, shape, dt=F32, side=None):
        return es.enter_context(
            self.nc.sbuf_tensor(name, shape, dt, side=side))

    def constf(self, val):
        val = float(val)
        if val not in self._consts and val not in (0.0, 1.0, -1.0):
            i = self._const_n
            assert i < 32
            self._const_n += 1
            ap = self._const_t[:, i:i + 1]
            self.nc.const_aps.register(F32, val, ap)
            self._consts[val] = ap
        return val

    def finalize(self):
        pre = []
        items = list(self._consts.items())
        for i, (val, ap) in enumerate(items):
            def fn(E, ap=ap, val=val, last=(i == len(items) - 1)):
                ins = E.memset(ap, val)
                if last:
                    ins.then_inc(self.sC.h, 1)
            pre.append(fn)
        if not items:
            def fn0(E):
                E.memset(self._const_t[:, 0:1], 0.0).then_inc(self.sC.h, 1)
            pre.append(fn0)
        self.p.ops["g"] = pre + self.p.ops["g"]
        for e in "qsvt":
            def w(E):
                E.wait_ge(self.sC.h, 1)
            self.p.ops[e] = [w] + self.p.ops[e]


def build_consts(cx):
    p = cx.p
    cx.ones_b = cx.sb("ones_b", [128, 1], BF16)
    cx.ones_f = cx.sb("ones_f", [128, 1], F32)
    cx.ones_fT = cx.sb("ones_fT", [1, 128], F32)
    cx.ones_bT = cx.sb("ones_bT", [1, 128], BF16)
    cx.ident = cx.sb("ident", [128, 128], BF16)
    p.op("g", "memset", cx.ones_b[:], 1.0)
    p.op("g", "memset", cx.ones_f[:], 1.0)
    p.op("g", "memset", cx.ones_fT[:], 1.0)
    p.op("g", "memset", cx.ones_bT[:], 1.0)
    p.op("g", "memset", cx.ident[:], 0.0)
    p.op("g", "affine_select", out=cx.ident[:], in_=cx.ident[:],
         compare_op=ALU.not_equal, fill=1.0, base=0,
         pattern=[[-1, 128]], channel_multiplier=1, inc=cx.sG)


def dram_to_bf16(cx, src_ap, dst_ap, stage_sb):
    p = cx.p
    R, C = src_ap.shape
    n = R // 128
    src_v = src_ap.rearrange("(n p) f -> n p f", p=128)
    dst_v = dst_ap.rearrange("(n p) f -> n p f", p=128)
    for i in range(n):
        p.wait("g", cx.stq, cx.stq.n)
        p.op("g", "dma_start", out=stage_sb, in_=src_v[i], inc=cx.ldg)
        p.wait("q", cx.ldg, cx.ldg.n)
        p.op("q", "dma_start", out=dst_v[i], in_=stage_sb, inc=cx.stq)
    p.wait("q", cx.stq, cx.stq.n)


def emit_ln_tile(cx, Rt, sq, mu_sb, var_sb, mu2_sb, eps=1e-5):
    p = cx.p
    sT, sV, sS = cx.sT, cx.sV, cx.sS
    sq_done = []
    for c in range(NCH):
        csl = slice(c * 512, (c + 1) * 512)
        p.op("t", "matmul", cx.bank(0, 1), cx.ones_f[:], Rt[:, csl],
             start=(c == 0), stop=(c == NCH - 1))
        if c >= 2:
            p.wait("s", sT, sq_done[c - 2])
        p.op("s", "activation", out=sq[c % 2][:], in_=Rt[:, csl],
             func=AF.Square, inc=sS)
        p.wait("t", sS)
        p.op("t", "matmul", cx.bank(1, 1), cx.ones_b[:], sq[c % 2][:],
             start=(c == 0), stop=(c == NCH - 1), inc=sT)
        sq_done.append(sT.n)
    p.wait("v", sT)
    p.op("v", "tensor_scalar_mul", mu_sb, cx.bank(0, 1), 1.0 / D)
    p.op("v", "tensor_scalar_mul", var_sb, cx.bank(1, 1), 1.0 / D, inc=sV)
    p.wait("s", sV)
    p.op("s", "activation", out=mu2_sb, in_=mu_sb, func=AF.Square, inc=sS)
    p.wait("v", sS)
    p.op("v", "tensor_sub", var_sb, var_sb, mu2_sb, inc=sV)
    p.wait("s", sV)
    p.op("s", "activation", out=var_sb, in_=var_sb, func=AF.Ln, bias=cx.constf(eps))
    p.op("s", "activation", out=var_sb, in_=var_sb, func=AF.Exp, scale=cx.constf(-0.5),
         inc=sS)
    p.wait("t", sS)
    p.op("t", "matmul", cx.bank(2), cx.ones_fT[:], mu_sb, start=True,
         stop=True)
    p.op("t", "matmul", cx.bank(3), cx.ones_fT[:], var_sb, start=True,
         stop=True, inc=sT)
    p.wait("v", sT)


def emit_xn_tile(cx, Rt, xn, scrf, dst_fn=None):
    p = cx.p
    for c in range(NCH):
        csl = slice(c * 512, (c + 1) * 512)
        dst = xn[:, csl] if dst_fn is None else dst_fn(c)
        p.op("v", "tensor_sub", scrf[:], Rt[:, csl], cx.bank(2))
        p.op("v", "tensor_mul", dst, scrf[:], cx.bank(3),
             inc=(cx.sV if c == NCH - 1 else None))


FF = 3072
NFC = FF // 128
SCALE = 1.0 / 127.0


def emit_ffn0(cx, r_in, r_out, wgT, wuT, wdT):
    p = cx.p
    u = cx.uid()
    cx.barrier()
    esF = cx.scope()
    cx.sb = lambda name, shape, dt=F32, side=None, _sbs=cx.sbs, _es=esF: _sbs(_es, name, shape, dt, side)
    Rt = cx.sb(f"f{u}_Rt", [128, NCH * 512], F32)
    Ro = cx.sb(f"f{u}_Ro", [128, NCH * 512], F32)
    xn = cx.sb(f"f{u}_xn", [128, NCH * 512], BF16)
    sq = [cx.sb(f"f{u}_sq{i}", [128, 512], BF16) for i in range(2)]
    scrf = cx.sb(f"f{u}_scrf", [128, 512], F32)
    mu_sb = cx.sb(f"f{u}_mu", [1, 512], F32)
    var_sb = cx.sb(f"f{u}_var", [1, 512], F32)
    mu2_sb = cx.sb(f"f{u}_mu2", [1, 512], F32)
    gu = cx.sb(f"f{u}_gu", [128, NFC * 512], BF16)
    gsil = [cx.sb(f"f{u}_gsil{i}", [128, 512], BF16) for i in range(2)]
    usil = [cx.sb(f"f{u}_usil{i}", [128, 512], BF16) for i in range(2)]
    wg_sb = cx.sb(f"f{u}_wgR", [128, NCH * 3072], BF16)
    wu_sb = cx.sb(f"f{u}_wuR", [128, NCH * 3072], BF16)
    wd_sb = [cx.sb(f"f{u}_wd{i}", [128, 24 * 128], BF16) for i in range(2)]

    r_in_v = r_in.rearrange("(c p) t -> p c t", p=128)
    r_out_v = r_out.rearrange("(c p) t -> p c t", p=128)
    wgT_v = wgT.rearrange("(c p) f -> p c f", p=128)
    wuT_v = wuT.rearrange("(c p) f -> p c f", p=128)
    wdT_v = wdT.rearrange("(k p) o -> p k o", p=128)

    rt_free = 0
    ro_free = 0
    wdbuf_free = [0, 0]
    gu_evac = {}
    p.op("g", "dma_start",
         out=wg_sb[:].rearrange("p (c f) -> p c f", c=NCH),
         in_=wgT_v, inc=cx.ldg)
    p.op("g", "dma_start",
         out=wu_sb[:].rearrange("p (c f) -> p c f", c=NCH),
         in_=wuT_v, inc=cx.ldg)
    gw_ld = cx.ldg.n

    for tt in range(NTT):
        tsl = slice(tt * 512, (tt + 1) * 512)
        p.wait("q", cx.sV, rt_free)
        p.op("q", "dma_start",
             out=Rt[:].rearrange("p (c t) -> p c t", c=NCH),
             in_=r_in_v[:, :, tsl], inc=cx.ld)
        ld_rt = cx.ld.n
        p.wait("t", cx.ld, ld_rt)
        p.wait("s", cx.ld, ld_rt)
        emit_ln_tile(cx, Rt[:], sq, mu_sb[:], var_sb[:], mu2_sb[:])
        emit_xn_tile(cx, Rt[:], xn[:], scrf)
        xn_done = cx.sV.n
        p.wait("t", cx.ldg, gw_ld)
        if True:
            p.wait("t", cx.sV, xn_done)
            for fc in range(24):
                bk_g = 4 + (fc % 2) * 2
                bk_u = 5 + (fc % 2) * 2
                prev_fc = fc - 2 if fc >= 2 else 22 + fc
                p.wait("t", cx.sV, gu_evac.get(prev_fc, 0))
                for c in range(NCH):
                    lhs_g = wg_sb[:, c * 3072 + fc * 128:
                                  c * 3072 + (fc + 1) * 128]
                    lhs_u = wu_sb[:, c * 3072 + fc * 128:
                                  c * 3072 + (fc + 1) * 128]
                    rhs = xn[:, c * 512:(c + 1) * 512]
                    p.op("t", "matmul", cx.bank(bk_g), lhs_g, rhs,
                         start=(c == 0), stop=(c == NCH - 1))
                    p.op("t", "matmul", cx.bank(bk_u), lhs_u, rhs,
                         start=(c == 0), stop=(c == NCH - 1),
                         inc=(cx.sT if c == NCH - 1 else None))
                mm_done = cx.sT.n
                p.wait("s", cx.sT, mm_done)
                p.wait("s", cx.sV, gu_evac.get(prev_fc + 1 if False else (fc - 2 if fc >= 2 else 22 + fc), 0))
                p.op("s", "activation", out=gsil[fc % 2][:], in_=cx.bank(bk_g),
                     func=AF.Silu, scale=cx.constf(SCALE))
                p.op("s", "activation", out=usil[fc % 2][:], in_=cx.bank(bk_u),
                     func=AF.Copy, scale=cx.constf(SCALE), inc=cx.sS)
                p.wait("v", cx.sS)
                p.op("v", "tensor_mul", gu[:, fc * 512:(fc + 1) * 512],
                     gsil[fc % 2][:], usil[fc % 2][:], inc=cx.sV)
                gu_evac[fc] = cx.sV.n
        all_gu = cx.sV.n
        down_evac = {}
        for oc in range(NCH):
            db = oc % 2
            p.wait("g", cx.sT, wdbuf_free[db])
            p.op("g", "dma_start",
                 out=wd_sb[db][:].rearrange("p (k o) -> p k o", k=24),
                 in_=wdT_v[:, :, oc * 128:(oc + 1) * 128], inc=cx.ldg)
            ld_wd = cx.ldg.n
            p.wait("t", cx.ldg, ld_wd)
            p.wait("t", cx.sV, all_gu)
            bk = oc % 2
            if oc >= 2:
                p.wait("t", cx.sV, down_evac[oc - 2])
            for kf in range(24):
                p.op("t", "matmul", cx.bank(bk),
                     wd_sb[db][:, kf * 128:(kf + 1) * 128],
                     gu[:, kf * 512:(kf + 1) * 512],
                     start=(kf == 0), stop=(kf == 23),
                     inc=(cx.sT if kf == 23 else None))
            wdbuf_free[db] = cx.sT.n
            p.wait("v", cx.sT)
            if oc == 0:
                p.wait("v", cx.stq, ro_free)
            csl = slice(oc * 512, (oc + 1) * 512)
            p.op("v", "scalar_tensor_tensor", out=Ro[:, csl],
                 in0=cx.bank(bk), scalar=0.5 * SCALE, in1=Rt[:, csl],
                 op0=ALU.mult, op1=ALU.add,
                 inc=cx.sV)
            down_evac[oc] = cx.sV.n
        rt_free = cx.sV.n
        p.wait("q", cx.sV, cx.sV.n)
        p.op("q", "dma_start", out=r_out_v[:, :, tsl],
             in_=Ro[:].rearrange("p (c t) -> p c t", c=NCH), inc=cx.stq)
        ro_free = cx.stq.n
    p.wait("q", cx.stq, cx.stq.n)
    del cx.sb
    esF.close()
    cx.barrier()


def prep_ffn0_weights(cx, wgT_i, wuT_i, wdT_i, wgT, wuT, wdT):
    with cx.scope() as esP:
        stage = cx.sbs(esP, "w_stage", [128, 3072], BF16, side="right")
        dram_to_bf16(cx, wgT_i, wgT, stage[:])
        dram_to_bf16(cx, wuT_i, wuT, stage[:])
        dram_to_bf16(cx, wdT_i, wdT, stage[:, 0:1024])


DI = 2048
NJC = DI // 128
DS = 16
DTR = 64


def emit_mamba(cx, r_in, r_out, W, scr):
    p = cx.p
    u = cx.uid()
    cx.barrier()
    r_in_v = r_in.rearrange("(c p) t -> p c t", p=128)
    r_out_v = r_out.rearrange("(c p) t -> p c t", p=128)

    fc1b = cx.sb(f"m{u}_fc1b", [128, 8], F32)
    fc2b = cx.sb(f"m{u}_fc2b", [128, 8], F32)
    es0 = cx.scope()
    stage = cx.sbs(es0, f"m{u}_stage", [128, 4096], BF16, side="right")
    dram_to_bf16(cx, W["inpT"], scr["inpT_bf"], stage[:])
    dram_to_bf16(cx, W["opT"], scr["opT_bf"], stage[:, 0:1024])

    esW = cx.scope()
    xpw = cx.sbs(esW, f"m{u}_xpw", [128, NJC * 96], BF16)
    dtpw = cx.sbs(esW, f"m{u}_dtpw", [64, DI], BF16)
    A_neg = cx.sbs(esW, f"m{u}_Aneg", [128, NJC * DS], F32)
    convw = cx.sbs(esW, f"m{u}_convw", [128, NJC * 4], F32)
    convb = cx.sbs(esW, f"m{u}_convb", [128, NJC], F32)
    dtpb = cx.sbs(esW, f"m{u}_dtpb", [128, NJC], F32)
    D_sb = cx.sbs(esW, f"m{u}_D", [128, NJC], F32)
    sel = [cx.sbs(esW, f"m{u}_sel{i}", [32, 128], BF16) for i in range(32)]
    ridx32 = cx.sbs(esW, f"m{u}_ridx32", [32, 1], F32)
    dtlow_bf = cx.sbs(esW, f"m{u}_dtlow", [64, T], BF16)
    BC_bf = cx.sbs(esW, f"m{u}_BC", [32, T], BF16)

    es_h = cx.scope()
    h_sb = cx.sbs(es_h, f"m{u}_h", [128, 8 * T], BF16)
    es1w = cx.scope()
    fc1w = cx.sbs(es1w, f"m{u}_fc1w", [128, 8 * 1024], BF16)
    with cx.scope() as esw0:
        wf = cx.sbs(esw0, f"m{u}_wf", [128, 8 * 1024], F32, side="right")
        p.op("q", "dma_start", out=wf[:].rearrange("p (c o) -> p c o", c=8),
             in_=W["fc1_wT"].rearrange("(c p) o -> p c o", p=128), inc=cx.ld)
        p.wait("v", cx.ld)
        p.op("v", "tensor_copy", fc1w[:], wf[:], inc=cx.sV)
        p.wait("q", cx.sV)
        p.op("q", "dma_start",
             out=wf[:, 0:NJC * 96].rearrange("p (c o) -> p c o", c=NJC),
             in_=W["xpT"].rearrange("(c p) o -> p c o", p=128), inc=cx.ld)
        p.wait("v", cx.ld)
        p.op("v", "tensor_copy", xpw[:], wf[:, 0:NJC * 96], inc=cx.sV)
        p.wait("q", cx.sV)
        p.op("q", "dma_start", out=wf[0:64, 0:DI], in_=W["dtpT"], inc=cx.ld)
        p.wait("v", cx.ld)
        p.op("v", "tensor_copy", dtpw[:], wf[0:64, 0:DI], inc=cx.sV)
        p.wait("q", cx.sV)
        p.op("q", "dma_start", out=wf[:, 0:NJC * DS], in_=W["A_log"],
             inc=cx.ld)
        p.wait("s", cx.ld)
        p.op("s", "activation", out=A_neg[:], in_=wf[:, 0:NJC * DS],
             func=AF.Exp, inc=cx.sS)
        p.wait("v", cx.sS)
        p.op("v", "tensor_scalar_mul", A_neg[:], A_neg[:], -1.0, inc=cx.sV)
        p.op("q", "dma_start", out=fc1b[:], in_=W["fc1_b"], inc=cx.ld)
        p.op("q", "dma_start", out=fc2b[:], in_=W["fc2_b"], inc=cx.ld)
        p.op("q", "dma_start", out=convw[:], in_=W["conv_w"], inc=cx.ld)
        p.op("q", "dma_start", out=convb[:], in_=W["conv_b"], inc=cx.ld)
        p.op("q", "dma_start", out=dtpb[:], in_=W["dtp_b"], inc=cx.ld)
        p.op("q", "dma_start", out=D_sb[:], in_=W["Dv"], inc=cx.ld)
        p.op("g", "iota", ridx32[:], pattern=[[1, 1]], base=0,
             channel_multiplier=1, allow_small_or_imprecise_dtypes=True,
             inc=cx.sG)
        p.wait("g", cx.sG)
        for i in range(32):
            p.op("g", "tensor_scalar", sel[i][:],
                 ridx32[:].broadcast_to([32, 128]), float(i), None,
                 ALU.is_equal, inc=(cx.sG if i == 31 else None))
        prep_v = cx.sV.n
        prep_ld = cx.ld.n
        prep_g = cx.sG.n
    es0.close()
    cx.barrier()

    es12 = cx.scope()
    xn_sb = cx.sbs(es12, f"m{u}_xn", [128, 8 * T], BF16)
    es1 = cx.scope()
    Rt = cx.sbs(es1, f"m{u}_Rt", [128, NCH * 512], F32)
    sq = [cx.sbs(es1, f"m{u}_sq{i}", [128, 512], BF16) for i in range(2)]
    scrf = cx.sbs(es1, f"m{u}_scrf", [128, 512], F32)
    mu_sb = cx.sbs(es1, f"m{u}_mu", [1, 512], F32)
    var_sb = cx.sbs(es1, f"m{u}_var", [1, 512], F32)
    mu2_sb = cx.sbs(es1, f"m{u}_mu2", [1, 512], F32)
    for tt in range(NTT):
        tsl = slice(tt * 512, (tt + 1) * 512)
        p.wait("q", cx.sV)
        p.op("q", "dma_start",
             out=Rt[:].rearrange("p (c t) -> p c t", c=NCH),
             in_=r_in_v[:, :, tsl], inc=cx.ld)
        p.wait("t", cx.ld)
        p.wait("s", cx.ld)
        emit_ln_tile(cx, Rt[:], sq, mu_sb[:], var_sb[:], mu2_sb[:])
        emit_xn_tile(cx, Rt[:], None, scrf,
                     dst_fn=lambda c, tsl=tsl: xn_sb[:, c * T + tsl.start:
                                                     c * T + tsl.stop])
    es1.close()

    p.wait("t", cx.sV)
    evac_done = {}
    for oc in range(NCH):
        for tt in range(NTT):
            tsl = slice(tt * 512, (tt + 1) * 512)
            bk = (oc * NTT + tt) % 2
            key = oc * NTT + tt
            if key >= 2:
                p.wait("t", cx.sV, evac_done[key - 2][1])
            for c in range(NCH):
                p.op("t", "matmul", cx.bank(bk),
                     fc1w[:, c * 1024 + oc * 128:c * 1024 + (oc + 1) * 128],
                     xn_sb[:, c * T + tsl.start:c * T + tsl.stop],
                     start=(c == 0), stop=(c == NCH - 1),
                     inc=(cx.sT if c == NCH - 1 else None))
            p.wait("v", cx.sT)
            p.op("v", "tensor_scalar",
                 h_sb[:, oc * T + tsl.start:oc * T + tsl.stop],
                 cx.bank(bk), fc1b[:, oc:oc + 1], None, ALU.add, inc=cx.sV)
            evac_done[key] = ("v", cx.sV.n)
    p2_evac = cx.sV.n
    es12.close()
    es1w.close()
    cx.barrier()

    es3 = cx.scope()
    wsb = [cx.sbs(es3, f"m{u}_w3{i}", [128, 8 * 128], BF16) for i in range(2)]
    xcz = cx.sbs(es3, f"m{u}_xcz", [128, 4099], F32)
    conv_acc = cx.sbs(es3, f"m{u}_cacc", [128, T], F32)
    u_f = cx.sbs(es3, f"m{u}_uf", [128, T], F32)
    u_bf = cx.sbs(es3, f"m{u}_ubf", [128, T], BF16)
    xdbl = cx.sbs(es3, f"m{u}_xdbl", [96, T], F32)
    inpT_v = scr["inpT_bf"].rearrange("(c p) o -> p c o", p=128)
    u_d_v = scr["u_d"].rearrange("(c p) t -> p c t", p=128)
    z_d_v = scr["z_d"].rearrange("(c p) t -> p c t", p=128)

    p.op("g", "memset", xcz[:, 0:3], 0.0, inc=cx.sG)
    xcz_zero = cx.sG.n
    wbuf_free = [0, 0]
    xcz_free = 0
    xcz_free_eng = "none"
    ubf_free = 0
    uf_free = 0
    uf_free_st = 0
    conv_free = 0
    for j in range(2 * NJC):
        jb = j % 2
        p.wait("g", cx.sT, wbuf_free[jb])
        p.op("g", "dma_start",
             out=wsb[jb][:].rearrange("p (c o) -> p c o", c=8),
             in_=inpT_v[:, :, j * 128:(j + 1) * 128], inc=cx.ldg)
        p.wait("t", cx.ldg)
        for tt in range(NTT):
            tsl = slice(tt * 512, (tt + 1) * 512)
            bk = tt % 2
            if tt >= 2:
                p.wait("t", cx.sS, evac_done[tt - 2])
            elif j > 0:
                p.wait("t", cx.sS, evac_done[6 + tt])
            else:
                p.wait("t", cx.sV, p2_evac)
            for c in range(NCH):
                p.op("t", "matmul", cx.bank(bk),
                     wsb[jb][:, c * 128:(c + 1) * 128],
                     h_sb[:, c * T + tsl.start:c * T + tsl.stop],
                     start=(c == 0), stop=(c == NCH - 1),
                     inc=(cx.sT if c == NCH - 1 else None))
            if tt == NTT - 1:
                wbuf_free[jb] = cx.sT.n
            p.wait("s", cx.sT)
            if tt == 0:
                if xcz_free_eng == "v":
                    p.wait("s", cx.sV, xcz_free)
                elif xcz_free_eng == "q":
                    p.wait("s", cx.stq, xcz_free)
            p.op("s", "activation", out=xcz[:, 3 + tsl.start:3 + tsl.stop],
                 in_=cx.bank(bk), func=AF.Copy, inc=cx.sS)
            evac_done[tt] = cx.sS.n
        if j < NJC:
            p.wait("v", cx.sS)
            p.wait("v", cx.sG, xcz_zero)
            p.wait("v", cx.ld, prep_ld)
            p.wait("v", cx.sS, conv_free)
            p.op("v", "tensor_scalar",
                 conv_acc[:], xcz[:, 0:T], convw[:, 4 * j:4 * j + 1], None,
                 ALU.mult)
            for w in range(1, 4):
                p.op("v", "scalar_tensor_tensor", out=conv_acc[:],
                     in0=xcz[:, w:w + T],
                     scalar=convw[:, 4 * j + w:4 * j + w + 1],
                     in1=conv_acc[:], op0=ALU.mult, op1=ALU.add,
                     inc=(cx.sV if w == 3 else None))
            xcz_free, xcz_free_eng = cx.sV.n, "v"
            p.wait("s", cx.sV)
            p.wait("s", cx.stq, uf_free_st)
            p.wait("s", cx.sV, uf_free)
            p.op("s", "activation", out=u_f[:], in_=conv_acc[:], func=AF.Silu,
                 bias=convb[:, j:j + 1], inc=cx.sS)
            conv_free = cx.sS.n
            p.wait("q", cx.sS)
            p.op("q", "dma_start", out=u_d_v[:, j, :], in_=u_f[:], inc=cx.stq)
            uf_free_st = cx.stq.n
            p.wait("v", cx.sS)
            p.wait("v", cx.sT, ubf_free)
            p.op("v", "tensor_copy", u_bf[:], u_f[:], inc=cx.sV)
            uf_free = cx.sV.n
            p.wait("t", cx.sV)
            for tt in range(NTT):
                tsl = slice(tt * 512, (tt + 1) * 512)
                p.op("t", "matmul", cx.bank(2, 96),
                     xpw[:, j * 96:(j + 1) * 96], u_bf[:, tsl],
                     start=True, stop=True, inc=cx.sT)
                if j == 0:
                    p.wait("s", cx.sT)
                    p.op("s", "activation", out=xdbl[:, tsl],
                         in_=cx.bank(2, 96), func=AF.Copy, inc=cx.sS)
                    p.wait("t", cx.sS)
                else:
                    p.wait("v", cx.sT)
                    p.op("v", "tensor_add", xdbl[:, tsl], xdbl[:, tsl],
                         cx.bank(2, 96), inc=cx.sV)
                    p.wait("t", cx.sV)
            ubf_free = cx.sT.n
        else:
            p.wait("q", cx.sS)
            p.op("q", "dma_start", out=z_d_v[:, j - NJC, :],
                 in_=xcz[:, 3:3 + T], inc=cx.stq)
            xcz_free, xcz_free_eng = cx.stq.n, "q"
    p.wait("v", cx.sV)
    p.op("v", "tensor_copy", dtlow_bf[:], xdbl[0:64, :])
    p.op("v", "tensor_copy", BC_bf[:], xdbl[64:96, :], inc=cx.sV)
    es3.close()
    es_h.close()
    cx.barrier()

    TH = T // 4
    NTH = TH // 512
    es4 = cx.scope()
    u2 = [cx.sbs(es4, f"m{u}_u2{i}", [128, TH], F32) for i in range(2)]
    dt2 = [cx.sbs(es4, f"m{u}_dt2{i}", [128, TH], F32) for i in range(2)]
    w2 = [cx.sbs(es4, f"m{u}_w2{i}", [128, TH], F32) for i in range(2)]
    y2 = [cx.sbs(es4, f"m{u}_y2{i}", [128, TH], F32) for i in range(2)]
    dA2 = [[cx.sbs(es4, f"m{u}_dA{i}_{j}", [128, TH], F32) for j in range(2)]
           for i in range(2)]
    b2_ = [[cx.sbs(es4, f"m{u}_b2{i}_{j}", [128, TH], F32) for j in range(2)]
           for i in range(2)]
    H2 = [cx.sbs(es4, f"m{u}_H{i}", [128, TH], F32) for i in range(2)]
    CH2 = [cx.sbs(es4, f"m{u}_CH{i}", [128, TH], F32) for i in range(2)]
    Brep = [cx.sbs(es4, f"m{u}_Brep{j}", [128, TH], F32) for j in range(3)]
    Crep = [cx.sbs(es4, f"m{u}_Crep{j}", [128, TH], F32) for j in range(3)]
    dtraw = cx.sbs(es4, f"m{u}_dtraw", [128, TH], F32)
    carry = cx.sbs(es4, f"m{u}_carry", [128, NJC * DS], F32)
    y_d_v = scr["y_d"].rearrange("(c p) t -> p c t", p=128)

    p.op("g", "memset", carry[:], 0.0, inc=cx.sG)
    carry_zero = cx.sG.n
    scan_dA = [[0, 0], [0, 0]]
    scan_b = [[0, 0], [0, 0]]
    rep_b_user = [0, 0, 0]
    rep_c_user_g = [0, 0, 0]
    rep_c_user_v = [0, 0, 0]
    ch_done = [0, 0]
    yadd_done = [("g", 0), ("v", 0)]
    dt_done = [0, 0]
    b_used = [0, 0]
    ydma_done = [0, 0]
    for half in range(4):
        hsl = slice(half * TH, (half + 1) * TH)
        for g in range(NJC // 2):
            for b2 in range(2):
                j = 2 * g + b2
                ye, yv = yadd_done[b2]
                p.wait("q", cx.sG if ye == "g" else cx.sV, yv)
                p.op("q", "dma_start", out=u2[b2][:],
                     in_=u_d_v[:, j, hsl], inc=cx.ld)
                p.wait("s", cx.sV, max(dt_done))
                for tt in range(NTH):
                    tsl = slice(half * TH + tt * 512, half * TH + (tt + 1) * 512)
                    p.wait("t", cx.sS)
                    p.wait("t", cx.sV)
                    p.op("t", "matmul", cx.bank(3),
                         dtpw[:, j * 128:(j + 1) * 128],
                         dtlow_bf[:, tsl], start=True, stop=True, inc=cx.sT)
                    p.wait("v", cx.sT)
                    p.op("v", "tensor_scalar",
                         dtraw[:, tt * 512:(tt + 1) * 512],
                         cx.bank(3), dtpb[:, j:j + 1], None, ALU.add,
                         inc=cx.sV)
                scr_sp = dA2[b2][0]
                p.wait("s", cx.sV, scan_dA[b2][0])
                p.wait("s", cx.sV)
                p.op("s", "activation", out=scr_sp[:], in_=dtraw[:],
                     func=AF.Abs)
                p.op("s", "activation", out=scr_sp[:], in_=scr_sp[:],
                     func=AF.Exp, scale=-1.0)
                p.op("s", "activation", out=scr_sp[:], in_=scr_sp[:],
                     func=AF.Ln, bias=1.0, inc=cx.sS)
                p.wait("v", cx.sS)
                p.wait("v", cx.sV, scan_dA[b2][0])
                p.op("v", "tensor_scalar_max", dt2[b2][:], dtraw[:], 0.0)
                p.op("v", "tensor_add", dt2[b2][:], dt2[b2][:], scr_sp[:],
                     inc=cx.sV)
                p.wait("v", cx.ld)
                p.wait("v", cx.sG, b_used[b2])
                p.op("v", "tensor_mul", w2[b2][:], dt2[b2][:], u2[b2][:],
                     inc=cx.sV)
                dt_done[b2] = cx.sV.n
                ye, yv = yadd_done[b2]
                if ye == "v":
                    p.wait("g", cx.sV, yv)
                p.wait("g", cx.stq, ydma_done[b2])
                p.op("g", "memset", y2[b2][:], 0.0, inc=cx.sG)
            grp_ready_v = cx.sV.n
            grp_ready_g = cx.sG.n
            def emit_chy(nn):
                rnn = nn % 3
                p.wait("g", cx.sV, carry_hist[0])
                p.op("g", "tensor_mul", CH2[0][:], H2[0][:],
                     Crep[rnn][:], inc=cx.sG)
                ch_done[0] = cx.sG.n
                rep_c_user_g[rnn] = cx.sG.n
                p.op("g", "tensor_add", y2[0][:], y2[0][:],
                     CH2[0][:], inc=cx.sG)
                yadd_done[0] = ("g", cx.sG.n)
                p.op("v", "tensor_mul", CH2[1][:], H2[1][:],
                     Crep[rnn][:], inc=cx.sV)
                rep_c_user_v[rnn] = cx.sV.n
                p.op("v", "tensor_add", y2[1][:], y2[1][:],
                     CH2[1][:], inc=cx.sV)
                yadd_done[1] = ("v", cx.sV.n)
                ch_flush[0] = cx.sG.n
                ch_flush[1] = cx.sV.n
            carry_hist = [0, 0]
            ch_flush = [0, 0]
            pend = None
            for n in range(DS):
                pn = n % 2
                rn = n % 3
                p.wait("t", cx.sV, grp_ready_v)
                p.wait("t", cx.sG, rep_b_user[rn])
                p.wait("t", cx.sG, rep_c_user_g[rn])
                p.wait("t", cx.sV, rep_c_user_v[rn])
                for tt in range(NTH):
                    tsl = slice(half * TH + tt * 512,
                                half * TH + (tt + 1) * 512)
                    p.op("t", "matmul", cx.bank(4 * pn + tt), sel[n][:],
                         BC_bf[:, tsl], start=True, stop=True)
                    p.op("t", "matmul", cx.bank(4 * pn + 2 + tt),
                         sel[16 + n][:], BC_bf[:, tsl], start=True, stop=True,
                         inc=(cx.sT if tt == NTH - 1 else None))
                p.wait("s", cx.sT)
                p.op("s", "activation", out=Brep[rn][:],
                     in_=cx.psa[:, 4 * pn * 512:(4 * pn + 2) * 512],
                     func=AF.Copy)
                p.op("s", "activation", out=Crep[rn][:],
                     in_=cx.psa[:, (4 * pn + 2) * 512:(4 * pn + 4) * 512],
                     func=AF.Copy, inc=cx.sS)
                rep_done = cx.sS.n
                dA_dn = [0, 0]
                b_dn = [0, 0]
                for b2 in range(2):
                    j = 2 * g + b2
                    cidx = j * DS + n
                    p.wait("s", cx.sV, scan_dA[b2][pn])
                    p.op("s", "activation", out=dA2[b2][pn][:],
                         in_=dt2[b2][:], func=AF.Exp,
                         scale=A_neg[:, cidx:cidx + 1], inc=cx.sS)
                    dA_dn[b2] = cx.sS.n
                    p.wait("g", cx.sS, rep_done)
                    p.wait("g", cx.sV, scan_b[b2][pn])
                    p.op("g", "tensor_mul", b2_[b2][pn][:], w2[b2][:],
                         Brep[rn][:], inc=cx.sG)
                    rep_b_user[rn] = cx.sG.n
                    b_used[b2] = cx.sG.n
                    b_dn[b2] = cx.sG.n
                if pend is not None:
                    emit_chy(pend)
                scan_v = [0, 0]
                for b2 in range(2):
                    j = 2 * g + b2
                    cidx = j * DS + n
                    p.wait("v", cx.sS, dA_dn[b2])
                    p.wait("v", cx.sG, b_dn[b2])
                    if b2 == 0:
                        p.wait("v", cx.sG, ch_flush[0])
                    p.wait("v", cx.sG, carry_zero)
                    p.op("v", "tensor_tensor_scan", H2[b2][:],
                         dA2[b2][pn][:], b2_[b2][pn][:],
                         carry[:, cidx:cidx + 1], ALU.mult, ALU.add,
                         inc=cx.sV)
                    scan_dA[b2][pn] = cx.sV.n
                    scan_b[b2][pn] = cx.sV.n
                    scan_v[b2] = cx.sV.n
                p.wait("v", cx.sV, scan_v[1])
                for b2 in range(2):
                    j = 2 * g + b2
                    cidx = j * DS + n
                    p.op("v", "tensor_copy", carry[:, cidx:cidx + 1],
                         H2[b2][:, TH - 1:TH], inc=cx.sV)
                    carry_hist[b2] = cx.sV.n
                pend = n
                if n == 0:
                    p.wait("v", cx.sG, grp_ready_g)
            emit_chy(pend)
            for b2 in range(2):
                j = 2 * g + b2
                p.wait("v", cx.stq, ydma_done[b2])
                if b2 == 0:
                    p.wait("v", cx.sG, ch_done[0])
                    p.wait("v", cx.sG)
                p.op("v", "scalar_tensor_tensor", out=y2[b2][:],
                     in0=u2[b2][:], scalar=D_sb[:, j:j + 1], in1=y2[b2][:],
                     op0=ALU.mult, op1=ALU.add, inc=cx.sV)
                yadd_done[b2] = ("v", cx.sV.n)
                p.wait("q", cx.sV, cx.sV.n)
                p.op("q", "dma_start", out=y_d_v[:, j, hsl], in_=y2[b2][:],
                     inc=cx.stq)
                ydma_done[b2] = cx.stq.n
    p.wait("q", cx.stq, cx.stq.n)
    es4.close()
    esW.close()
    cx.barrier()

    es5 = cx.scope()
    opw = cx.sbs(es5, f"m{u}_opw", [128, NJC * 1024], BF16)
    fc2w = cx.sbs(es5, f"m{u}_fc2w", [128, 8 * 1024], BF16)
    p.op("q", "dma_start",
         out=opw[:].rearrange("p (c o) -> p c o", c=NJC),
         in_=scr["opT_bf"].rearrange("(c p) o -> p c o", p=128), inc=cx.ld)
    with cx.scope() as esw1:
        wf2 = cx.sbs(esw1, f"m{u}_wf2", [128, 4 * 1024], F32, side="right")
        wv = W["fc2_wT"].rearrange("(c p) o -> p c o", p=128)
        for hf in range(2):
            p.wait("q", cx.sV)
            p.op("q", "dma_start",
                 out=wf2[:].rearrange("p (c o) -> p c o", c=4),
                 in_=wv[:, hf * 4:(hf + 1) * 4, :], inc=cx.ld)
            p.wait("v", cx.ld)
            p.op("v", "tensor_copy", fc2w[:, hf * 4096:(hf + 1) * 4096],
                 wf2[:], inc=cx.sV)
    opw_ld = cx.ld.n
    ybf_t = cx.sbs(es5, f"m{u}_ybft", [128, NJC * 512], BF16)
    zf = [cx.sbs(es5, f"m{u}_zf{i}", [128, 512], F32) for i in range(2)]
    yf = [cx.sbs(es5, f"m{u}_yf{i}", [128, 512], F32) for i in range(2)]
    zs = [cx.sbs(es5, f"m{u}_zs{i}", [128, 512], F32) for i in range(2)]
    o1t = cx.sbs(es5, f"m{u}_o1t", [128, 8 * 512], BF16)
    Rt5 = cx.sbs(es5, f"m{u}_Rt5", [128, 8 * 512], F32)
    Ro5 = cx.sbs(es5, f"m{u}_Ro5", [128, 8 * 512], F32)
    rt5_free = 0
    ro5_st = 0
    zy_free = [0, 0]
    for tt in range(NTT):
        tsl = slice(tt * 512, (tt + 1) * 512)
        p.wait("q", cx.sV, rt5_free)
        p.op("q", "dma_start",
             out=Rt5[:].rearrange("p (c t) -> p c t", c=NCH),
             in_=r_in_v[:, :, tsl], inc=cx.ld)
        rt5_ld = cx.ld.n
        ybf_mm = cx.sT.n
        for j in range(NJC):
            jb = j % 2
            p.wait("q", cx.sV, zy_free[jb])
            p.op("q", "dma_start", out=zf[jb][:], in_=z_d_v[:, j, tsl],
                 inc=cx.ld)
            p.op("q", "dma_start", out=yf[jb][:], in_=y_d_v[:, j, tsl],
                 inc=cx.ld)
            p.wait("s", cx.ld)
            p.wait("s", cx.sV, zy_free[jb])
            p.op("s", "activation", out=zs[jb][:], in_=zf[jb][:],
                 func=AF.Silu, inc=cx.sS)
            p.wait("v", cx.sS)
            if j < 2:
                p.wait("v", cx.sT, ybf_mm)
            p.op("v", "tensor_mul", ybf_t[:, j * 512:(j + 1) * 512],
                 yf[jb][:], zs[jb][:], inc=cx.sV)
            zy_free[jb] = cx.sV.n
        ybf_done = cx.sV.n
        p.wait("t", cx.sV, ybf_done)
        p.wait("t", cx.ld, opw_ld)
        if tt > 0:
            p.wait("t", cx.sS)
            p.wait("t", cx.sV)
        for oc in range(NCH):
            for c in range(NJC):
                p.op("t", "matmul", cx.bank(oc),
                     opw[:, c * 1024 + oc * 128:c * 1024 + (oc + 1) * 128],
                     ybf_t[:, c * 512:(c + 1) * 512],
                     start=(c == 0), stop=(c == NJC - 1),
                     inc=(cx.sT if c == NJC - 1 else None))
        p.wait("s", cx.sT)
        for oc in range(NCH):
            p.op("s", "activation", out=o1t[:, oc * 512:(oc + 1) * 512],
                 in_=cx.bank(oc), func=AF.Copy,
                 inc=(cx.sS if oc == NCH - 1 else None))
        if "o1_d" in scr:
            p.wait("q", cx.sS)
            p.op("q", "dma_start",
                 out=scr["o1_d"].rearrange("(c p) t -> p c t", p=128)[:, :, tsl],
                 in_=o1t[:].rearrange("p (c t) -> p c t", c=NCH), inc=cx.stq)
        p.wait("t", cx.sS)
        fc2_evac = {}
        for oc2 in range(NCH):
            bk = oc2 % 2
            if oc2 >= 2:
                p.wait("t", cx.sV, fc2_evac[oc2 - 2])
            for c in range(NCH):
                p.op("t", "matmul", cx.bank(bk),
                     fc2w[:, c * 1024 + oc2 * 128:c * 1024 + (oc2 + 1) * 128],
                     o1t[:, c * 512:(c + 1) * 512],
                     start=(c == 0), stop=(c == NCH - 1),
                     inc=(cx.sT if c == NCH - 1 else None))
            p.wait("v", cx.sT)
            if oc2 == 0:
                p.wait("v", cx.ld, rt5_ld)
                p.wait("v", cx.stq, ro5_st)
            csl = slice(oc2 * 512, (oc2 + 1) * 512)
            p.op("v", "scalar_tensor_tensor", out=Ro5[:, csl],
                 in0=cx.bank(bk), scalar=fc2b[:, oc2:oc2 + 1],
                 in1=Rt5[:, csl], op0=ALU.add, op1=ALU.add, inc=cx.sV)
            fc2_evac[oc2] = cx.sV.n
        rt5_free = cx.sV.n
        p.wait("q", cx.sV, rt5_free)
        p.op("q", "dma_start", out=r_out_v[:, :, tsl],
             in_=Ro5[:].rearrange("p (c t) -> p c t", c=NCH), inc=cx.stq)
        ro5_st = cx.stq.n
    p.wait("q", cx.stq, cx.stq.n)
    es5.close()


def to_pcol(v):
    import numpy as np
    v = np.asarray(v)
    if v.ndim == 1:
        c = v.shape[0] // 128
        return np.ascontiguousarray(v.reshape(c, 128).T)
    c = v.shape[0] // 128
    k = v.shape[1]
    return np.ascontiguousarray(
        v.reshape(c, 128, k).transpose(1, 0, 2).reshape(128, c * k))


NEXP = 8
DFF = 4096
NFC2 = DFF // 128


def emit_moe(cx, r_in, r_out, W, scr):
    p = cx.p
    u = cx.uid()
    DBG = scr.get("dbg_stage", 5)
    cx.barrier()
    r_in_v = r_in.rearrange("(c p) t -> p c t", p=128)
    r_out_v = r_out.rearrange("(c p) t -> p c t", p=128)

    if not scr.get("moe_prepped"):
        with cx.scope() as esP:
            stage = cx.sbs(esP, f"o{u}_stage", [128, 4096], BF16, side="right")
            for e in range(NEXP):
                dram_to_bf16(cx, W["w1"][e], scr["w1_bf"][e], stage[:])
                dram_to_bf16(cx, W["w2"][e], scr["w2_bf"][e],
                             stage[:, 0:1024])
        scr["moe_prepped"] = True

    esM = cx.scope()
    if "cpack" not in scr:
        gw = cx.sb(f"o{u}_gw", [128, 8 * 8], F32)
        gb = cx.sb(f"o{u}_gb", [8, 2], F32)
        gbias = cx.sb(f"o{u}_gbias", [8, 1], F32)
        b1_sb = cx.sb(f"o{u}_b1", [128, NEXP * 32], F32)
        b2_sb = cx.sb(f"o{u}_b2", [128, NEXP * 8], F32)
        sel8 = [cx.sb(f"o{u}_sel8_{e}", [8, 128], BF16) for e in range(NEXP)]
        ident_f = cx.sb(f"o{u}_identf", [128, 128], F32)
        revidx = cx.sb(f"o{u}_rev", [128, 8], F32)
        ridx8 = cx.sb(f"o{u}_ridx8", [8, 1], F32)
        p.op("q", "dma_start",
             out=gw[:].rearrange("p (c e) -> p c e", c=NCH),
             in_=W["gate_wT"].rearrange("(c p) e -> p c e", p=128), inc=cx.ld)
        p.op("q", "dma_start", out=gb[:], in_=W["gbias"], inc=cx.ld)
        p.op("q", "dma_start", out=b1_sb[:], in_=W["b1"], inc=cx.ld)
        p.op("q", "dma_start", out=b2_sb[:], in_=W["b2"], inc=cx.ld)
        p.wait("v", cx.ld)
        p.op("v", "tensor_add", gbias[:], gb[:, 0:1], gb[:, 1:2], inc=cx.sV)
        p.op("g", "iota", ridx8[:], pattern=[[1, 1]], base=0,
             channel_multiplier=1, allow_small_or_imprecise_dtypes=True,
             inc=cx.sG)
        p.wait("g", cx.sG)
        for e in range(NEXP):
            p.op("g", "tensor_scalar", sel8[e][:],
                 ridx8[:].broadcast_to([8, 128]), float(e), None,
                 ALU.is_equal)
        p.op("g", "memset", ident_f[:], 0.0)
        p.op("g", "affine_select", out=ident_f[:], in_=ident_f[:],
             compare_op=ALU.not_equal, fill=1.0, base=0,
             pattern=[[-1, 128]], channel_multiplier=1)
        p.op("g", "iota", revidx[:], pattern=[[1, 8]], base=0,
             channel_multiplier=0, allow_small_or_imprecise_dtypes=True,
             inc=cx.sG)
        p.wait("v", cx.sG)
        p.op("v", "tensor_scalar", revidx[:], revidx[:], -1.0, 8.0, ALU.mult,
             ALU.add, inc=cx.sV)
        scr["cpack"] = dict(gw=gw, gbias=gbias, b1_sb=b1_sb, b2_sb=b2_sb,
                            sel8=sel8, ident_f=ident_f, revidx=revidx)
    cp = scr["cpack"]
    gw = cp["gw"]; gbias = cp["gbias"]; b1_sb = cp["b1_sb"]
    b2_sb = cp["b2_sb"]; sel8 = cp["sel8"]; ident_f = cp["ident_f"]
    revidx = cp["revidx"]
    prep_g = cx.sG.n
    prep_v = cx.sV.n

    Rt = cx.sbs(esM, f"o{u}_Rt", [128, NCH * 512], F32)
    Ro = cx.sbs(esM, f"o{u}_Ro", [128, NCH * 512], F32)
    xnb = cx.sbs(esM, f"o{u}_xnb", [128, NCH * 512], BF16)
    xnf = cx.sbs(esM, f"o{u}_xnf", [128, NCH * 512], F32)
    sq = [cx.sbs(esM, f"o{u}_sq{i}", [128, 512], BF16) for i in range(2)]
    scrf = cx.sbs(esM, f"o{u}_scrf", [128, 512], F32)
    mu_sb = cx.sbs(esM, f"o{u}_mu", [1, 512], F32)
    var_sb = cx.sbs(esM, f"o{u}_var", [1, 512], F32)
    mu2_sb = cx.sbs(esM, f"o{u}_mu2", [1, 512], F32)
    g_sb = cx.sbs(esM, f"o{u}_gsb", [8, 512], F32)
    gts = cx.sbs(esM, f"o{u}_gts", [128, 4 * 8], F32)
    mx = cx.sbs(esM, f"o{u}_mx", [128, 1], F32)
    eq = cx.sbs(esM, f"o{u}_eq", [128, 8], F32)
    msk = cx.sbs(esM, f"o{u}_msk", [128, 4 * 8], BF16)
    mskT = cx.sbs(esM, f"o{u}_mskT", [8, 512], BF16)
    mrep = cx.sbs(esM, f"o{u}_mrep", [128, NEXP * 512], F32)
    he = cx.sbs(esM, f"o{u}_he", [128, NFC2 * 512], BF16)
    acc = cx.sbs(esM, f"o{u}_acc", [128, NCH * 512], F32)
    w1s = [cx.sbs(esM, f"o{u}_w1s{i}", [128, 8 * 512], BF16) for i in range(2)]
    w2s = [cx.sbs(esM, f"o{u}_w2s{i}", [128, 4 * 1024], BF16) for i in range(2)]

    w1v = [scr["w1_bf"][e].rearrange("(c p) f -> p c f", p=128)
           for e in range(NEXP)]
    w2v = [scr["w2_bf"][e].rearrange("(g p) d -> p g d", p=128)
           for e in range(NEXP)]

    rt_free = 0
    ro_st = 0
    w1_free = [0, 0]
    w2_free = [0, 0]
    for tt in range(NTT):
        tsl = slice(tt * 512, (tt + 1) * 512)
        p.wait("q", cx.sV, rt_free)
        p.op("q", "dma_start",
             out=Rt[:].rearrange("p (c t) -> p c t", c=NCH),
             in_=r_in_v[:, :, tsl], inc=cx.ld)
        p.wait("t", cx.ld)
        p.wait("s", cx.ld)
        emit_ln_tile(cx, Rt[:], sq, mu_sb[:], var_sb[:], mu2_sb[:])
        emit_xn_tile(cx, Rt[:], xnf[:], scrf)
        p.wait("v", cx.sV)
        p.op("v", "tensor_copy", xnb[:], xnf[:], inc=cx.sV)
        xn_done = cx.sV.n
        if DBG >= 1:
            p.wait("t", cx.sV, xn_done)
            for c in range(NCH):
                p.op("t", "matmul", cx.bank(4, 8), gw[:, c * 8:(c + 1) * 8],
                     xnf[:, c * 512:(c + 1) * 512], start=(c == 0),
                     stop=(c == NCH - 1), inc=(cx.sT if c == NCH - 1 else None))
            p.wait("v", cx.sT)
            p.wait("v", cx.sV, prep_v)
            p.op("v", "tensor_scalar", g_sb[:], cx.bank(4, 8), gbias[0:8, 0:1],
                 None, ALU.add, inc=cx.sV)
            p.wait("t", cx.sV)
            p.wait("t", cx.sG, prep_g)
            for st in range(4):
                p.op("t", "transpose",
                     cx.psa[0:128, 5 * 512 + st * 8:5 * 512 + (st + 1) * 8],
                     g_sb[:, st * 128:(st + 1) * 128], ident_f[0:8, 0:8],
                     inc=(cx.sT if st == 3 else None))
            p.wait("v", cx.sT)
            for st in range(4):
                p.op("v", "tensor_copy", gts[:, st * 8:(st + 1) * 8],
                     cx.psa[0:128, 5 * 512 + st * 8:5 * 512 + (st + 1) * 8],
                     inc=(cx.sV if st == 3 else None))
            gts_done = cx.sV.n
            if DBG >= 2:
                for st in range(4):
                    ss = slice(st * 8, (st + 1) * 8)
                    p.op("v", "reduce_max", mx[:], gts[:, ss], axis=AX.X, inc=cx.sV)
                    p.wait("v", cx.sV)
                    p.op("v", "tensor_scalar", eq[:], gts[:, ss], mx[:, 0:1], None,
                         ALU.is_ge, inc=cx.sV)
                    p.wait("v", cx.sV)
                    p.op("v", "tensor_mul", eq[:], eq[:], revidx[:], inc=cx.sV)
                    p.wait("v", cx.sV)
                    p.op("v", "reduce_max", mx[:], eq[:], axis=AX.X, inc=cx.sV)
                    p.wait("v", cx.sV)
                    p.op("v", "tensor_scalar", msk[:, ss], eq[:], mx[:, 0:1], None,
                         ALU.is_ge, inc=cx.sV)
                    p.wait("v", cx.sV)
                msk_done = cx.sV.n
            if DBG >= 3:
                p.wait("t", cx.sV, msk_done)
                for st in range(4):
                    p.op("t", "transpose",
                         cx.psa[0:8, 5 * 512:6 * 512].bitcast(BF16)[:, st * 128:
                                                                    (st + 1) * 128],
                         msk[:, st * 8:(st + 1) * 8], cx.ident[:],
                         inc=(cx.sT if st == 3 else None))
                p.wait("s", cx.sT)
                p.op("s", "activation", out=mskT[:],
                     in_=cx.psa[0:8, 5 * 512:6 * 512].bitcast(BF16)[:, 0:512],
                     func=AF.Copy, inc=cx.sS)
                mrep_mm = {}
                for e in range(NEXP):
                    bk = 6 + e % 2
                    p.wait("t", cx.sS)
                    if e >= 2:
                        p.wait("t", cx.sS, mrep_mm[e - 2])
                    p.op("t", "matmul", cx.bank(bk), sel8[e][:], mskT[:],
                         start=True, stop=True, inc=cx.sT)
                    p.wait("s", cx.sT)
                    p.op("s", "activation", out=mrep[:, e * 512:(e + 1) * 512],
                         in_=cx.bank(bk), func=AF.Copy, inc=cx.sS)
                    mrep_mm[e] = cx.sS.n
                mrep_done = cx.sS.n
            if DBG >= 4:
                acc_done = 0
                for e in range(NEXP):
                    he_evac = {}
                    for fg in range(NFC2 // 4):
                        wb = fg % 2
                        p.wait("g", cx.sT, w1_free[wb])
                        p.op("g", "dma_start",
                             out=w1s[wb][:].rearrange("p (c f) -> p c f", c=NCH),
                             in_=w1v[e][:, :, fg * 512:(fg + 1) * 512], inc=cx.ldg)
                        p.wait("t", cx.ldg)
                        for fl in range(4):
                            fc = fg * 4 + fl
                            bk = fc % 2
                            if fc >= 2:
                                p.wait("t", cx.sS, he_evac[fc - 2])
                            else:
                                p.wait("t", cx.sV, acc_done)
                                if not (e == 0 and tt == 0):
                                    p.wait("t", cx.sS)
                            for c in range(NCH):
                                p.op("t", "matmul", cx.bank(bk),
                                     w1s[wb][:, c * 512 + fl * 128:
                                             c * 512 + (fl + 1) * 128],
                                     xnb[:, c * 512:(c + 1) * 512],
                                     start=(c == 0), stop=(c == NCH - 1),
                                     inc=(cx.sT if c == NCH - 1 else None))
                            if fl == 3:
                                w1_free[wb] = cx.sT.n
                            p.wait("s", cx.sT)
                            p.op("s", "activation",
                                 out=he[:, fc * 512:(fc + 1) * 512],
                                 in_=cx.bank(bk), func=AF.Relu,
                                 bias=b1_sb[:, e * 32 + fc:e * 32 + fc + 1],
                                 inc=cx.sS)
                            he_evac[fc] = cx.sS.n
                    all_he = cx.sS.n
                    p.wait("t", cx.sS, all_he)
                    if e > 0:
                        p.wait("t", cx.sV, acc_done)
                    else:
                        p.wait("t", cx.sS, mrep_done)
                    for kg in range(8):
                        wb = kg % 2
                        p.wait("g", cx.sT, w2_free[wb])
                        p.op("g", "dma_start",
                             out=w2s[wb][:].rearrange("p (g o) -> p g o", g=4),
                             in_=w2v[e][:, kg * 4:(kg + 1) * 4, :], inc=cx.ldg)
                        p.wait("t", cx.ldg)
                        for kl in range(4):
                            kf = kg * 4 + kl
                            for oc in range(NCH):
                                p.op("t", "matmul", cx.bank(oc),
                                     w2s[wb][:, kl * 1024 + oc * 128:
                                             kl * 1024 + (oc + 1) * 128],
                                     he[:, kf * 512:(kf + 1) * 512],
                                     start=(kf == 0), stop=(kf == 31),
                                     inc=(cx.sT if kl == 3 and oc == NCH - 1
                                          else None))
                        w2_free[wb] = cx.sT.n
                    p.wait("v", cx.sT)
                    if e == 0:
                        p.wait("v", cx.sS, mrep_done)
                    for oc in range(NCH):
                        csl = slice(oc * 512, (oc + 1) * 512)
                        p.op("v", "scalar_tensor_tensor", out=scrf[:],
                             in0=cx.bank(oc), scalar=b2_sb[:, e * 8 + oc:e * 8 + oc + 1],
                             in1=mrep[:, e * 512:(e + 1) * 512], op0=ALU.add,
                             op1=ALU.mult)
                        if e == 0:
                            p.op("v", "tensor_copy", acc[:, csl], scrf[:],
                                 inc=(cx.sV if oc == NCH - 1 else None))
                        else:
                            p.op("v", "tensor_add", acc[:, csl], acc[:, csl],
                                 scrf[:], inc=(cx.sV if oc == NCH - 1 else None))
                    acc_done = cx.sV.n
        p.wait("v", cx.stq, ro_st)
        if DBG >= 4:
            for oc in range(NCH):
                csl = slice(oc * 512, (oc + 1) * 512)
                p.op("v", "scalar_tensor_tensor", out=Ro[:, csl],
                     in0=acc[:, csl], scalar=0.5, in1=Rt[:, csl],
                     op0=ALU.mult, op1=ALU.add,
                     inc=(cx.sV if oc == NCH - 1 else None))
        else:
            p.op("v", "tensor_copy", Ro[:], Rt[:], inc=cx.sV)
        rt_free = cx.sV.n
        p.wait("q", cx.sV, rt_free)
        p.op("q", "dma_start", out=r_out_v[:, :, tsl],
             in_=Ro[:].rearrange("p (c t) -> p c t", c=NCH), inc=cx.stq)
        ro_st = cx.stq.n
    p.wait("q", cx.stq, cx.stq.n)
    es_close = esM.close()
    cx.barrier()


NH = 16
HD = 64


def emit_attn(cx, r_in, r_out, W, scr):
    p = cx.p
    u = cx.uid()
    cx.barrier()
    r_in_v = r_in.rearrange("(c p) t -> p c t", p=128)
    r_out_v = r_out.rearrange("(c p) t -> p c t", p=128)
    xn_d_v = scr["xn_d"].rearrange("(c p) t -> p c t", p=128)
    q_d_v = scr["q_d"].rearrange("(c p) t -> p c t", p=128)
    k_d_v = scr["k_d"].rearrange("(c p) t -> p c t", p=128)
    st_d_v = scr["st_d"].rearrange("(c p) t -> p c t", p=128)
    st2_d_v = scr["st2_d"].rearrange("(c p) t -> p c t", p=128)
    v_d_v = scr["v_d"].rearrange("(kt p) (h d) -> p kt h d", p=128, h=NH)

    esA = cx.scope()
    wq = cx.sbs(esA, f"a{u}_wq", [128, 8 * 1024], BF16)
    wk = cx.sbs(esA, f"a{u}_wk", [128, 8 * 1024], BF16)
    wv = cx.sbs(esA, f"a{u}_wv", [128, 8 * 1024], BF16)
    wo = cx.sbs(esA, f"a{u}_wo", [128, 8 * 1024], BF16)
    bq = cx.sbs(esA, f"a{u}_bq", [128, 8], F32)
    bk = cx.sbs(esA, f"a{u}_bk", [128, 8], F32)
    bo = cx.sbs(esA, f"a{u}_bo", [128, 8], F32)
    lnw = cx.sbs(esA, f"a{u}_lnw", [128, 8], F32)
    lnb = cx.sbs(esA, f"a{u}_lnb", [128, 8], F32)
    bv_rep = cx.sbs(esA, f"a{u}_bvrep", [128, 1024], F32)
    bv_sb = cx.sbs(esA, f"a{u}_bv", [1, 1024], F32)
    hsel = cx.sbs(esA, f"a{u}_hsel", [128, 2], BF16)
    hrep_f = cx.sbs(esA, f"a{u}_hrepf", [2, 128], F32)
    idf2 = cx.sbs(esA, f"a{u}_idf2", [2, 2], F32)
    for nm, dst in (("bq", bq), ("bk", bk), ("bo", bo), ("lnw", lnw),
                    ("lnb", lnb)):
        p.op("q", "dma_start", out=dst[:], in_=W[nm], inc=cx.ld)
    p.op("q", "dma_start", out=bv_sb[:], in_=W["bv"], inc=cx.ld)
    with cx.scope() as esw:
        wf = cx.sbs(esw, f"a{u}_wf", [128, 8 * 1024], F32, side="right")
        for nm, dst in (("Wq", wq), ("Wk", wk), ("Wv", wv), ("Wo", wo)):
            p.wait("q", cx.sV)
            p.op("q", "dma_start",
                 out=wf[:].rearrange("p (c o) -> p c o", c=8),
                 in_=W[nm].rearrange("(c p) o -> p c o", p=128), inc=cx.ld)
            p.wait("v", cx.ld)
            p.op("v", "tensor_copy", dst[:], wf[:], inc=cx.sV)
    scr_pidx = cx.sbs(esA, f"a{u}_pidx", [128, 2], F32)
    scr_h2 = cx.sbs(esA, f"a{u}_h2", [2, 3 * 128], F32)
    hselF = cx.sbs(esA, f"a{u}_hselF", [128, 2], F32)
    p.op("g", "iota", scr_pidx[:, 0:1], pattern=[[1, 1]], base=0,
         channel_multiplier=1, allow_small_or_imprecise_dtypes=True,
         inc=cx.sG)
    p.wait("g", cx.sG)
    p.op("g", "tensor_scalar", hselF[:, 0:1], scr_pidx[:, 0:1], 64.0, None,
         ALU.is_lt)
    p.op("g", "tensor_scalar", hselF[:, 1:2], scr_pidx[:, 0:1], 64.0, None,
         ALU.is_ge, inc=cx.sG)
    p.wait("g", cx.sG)
    p.op("g", "tensor_copy", hsel[:], hselF[:], inc=cx.sG)
    p.op("g", "iota", scr_h2[:, 0:128], pattern=[[1, 128]], base=0,
         channel_multiplier=0, allow_small_or_imprecise_dtypes=True)
    p.op("g", "iota", scr_h2[:, 128:129], pattern=[[1, 1]], base=0,
         channel_multiplier=64, allow_small_or_imprecise_dtypes=True,
         inc=cx.sG)
    p.wait("g", cx.sG)
    p.op("g", "tensor_scalar", scr_h2[:, 0:128], scr_h2[:, 0:128],
         scr_h2[:, 128:129], None, ALU.subtract, inc=cx.sG)
    p.wait("g", cx.sG)
    p.op("g", "tensor_scalar", scr_h2[:, 128:256], scr_h2[:, 0:128], 0.0,
         None, ALU.is_ge)
    p.op("g", "tensor_scalar", scr_h2[:, 256:384], scr_h2[:, 0:128], 64.0,
         None, ALU.is_lt, inc=cx.sG)
    p.wait("g", cx.sG)
    p.op("g", "tensor_mul", hrep_f[:], scr_h2[:, 128:256],
         scr_h2[:, 256:384], inc=cx.sG)
    p.op("g", "memset", idf2[:], 0.0)
    p.op("g", "affine_select", out=idf2[:], in_=idf2[:],
         compare_op=ALU.not_equal, fill=1.0, base=0, pattern=[[-1, 2]],
         channel_multiplier=1, inc=cx.sG)
    prep_g = cx.sG.n
    p.wait("t", cx.ld)
    p.wait("t", cx.sV)
    for nt in range(2):
        p.op("t", "matmul", cx.bank(nt), cx.ones_fT[:],
             bv_sb[:, nt * 512:(nt + 1) * 512], start=True, stop=True,
             inc=(cx.sT if nt == 1 else None))
    p.wait("s", cx.sT)
    p.op("s", "activation", out=bv_rep[:], in_=cx.psa[:, 0:1024],
         func=AF.Copy, inc=cx.sS)
    bv_done = cx.sS.n

    esQ = cx.scope()
    Rt = cx.sbs(esQ, f"a{u}_Rt", [128, NCH * 512], F32)
    xn = cx.sbs(esQ, f"a{u}_xn", [128, NCH * 512], F32)
    zq = cx.sbs(esQ, f"a{u}_zq", [128, NCH * 512], BF16)
    sq = [cx.sbs(esQ, f"a{u}_sq{i}", [128, 512], BF16) for i in range(2)]
    scrf = cx.sbs(esQ, f"a{u}_scrf", [128, 512], F32)
    scrf2 = cx.sbs(esQ, f"a{u}_scrf2", [128, 512], F32)
    mu_sb = cx.sbs(esQ, f"a{u}_mu", [1, 512], F32)
    var_sb = cx.sbs(esQ, f"a{u}_var", [1, 512], F32)
    mu2_sb = cx.sbs(esQ, f"a{u}_mu2", [1, 512], F32)
    qkob = cx.sbs(esQ, f"a{u}_qkob", [128, 512], BF16)
    s2n = cx.sbs(esQ, f"a{u}_s2n", [2, 512], F32)
    qn_bf = cx.sbs(esQ, f"a{u}_qnbf", [128, 512], BF16)
    vout = cx.sbs(esQ, f"a{u}_vout", [128, 1024], BF16)

    for tt in range(NTT):
        tsl = slice(tt * 512, (tt + 1) * 512)
        p.wait("q", cx.sV)
        p.wait("q", cx.stq, cx.stq.n)
        p.op("q", "dma_start",
             out=Rt[:].rearrange("p (c t) -> p c t", c=NCH),
             in_=r_in_v[:, :, tsl], inc=cx.ld)
        p.wait("t", cx.ld)
        p.wait("s", cx.ld)
        emit_ln_tile(cx, Rt[:], sq, mu_sb[:], var_sb[:], mu2_sb[:])
        emit_xn_tile(cx, Rt[:], xn[:], scrf)
        p.wait("q", cx.sV)
        p.op("q", "dma_start", out=xn_d_v[:, :, tsl],
             in_=xn[:].rearrange("p (c t) -> p c t", c=NCH), inc=cx.stq)
        p.wait("t", cx.sV)
        emit_ln_tile(cx, xn[:], sq, mu_sb[:], var_sb[:], mu2_sb[:])
        p.wait("v", cx.ld)
        for c in range(NCH):
            csl = slice(c * 512, (c + 1) * 512)
            p.op("v", "tensor_sub", scrf[:], xn[:, csl], cx.bank(2))
            p.op("v", "tensor_mul", scrf2[:], scrf[:], cx.bank(3))
            p.op("v", "tensor_scalar", zq[:, csl], scrf2[:],
                 lnw[:, c:c + 1], lnb[:, c:c + 1], ALU.mult, ALU.add,
                 inc=(cx.sV if c == NCH - 1 else None))
        zq_done = cx.sV.n
        for which, wmat, bias, dst_v in (("q", wq, bq, q_d_v),
                                         ("k", wk, bk, k_d_v)):
            for oc in range(NCH):
                p.wait("t", cx.sV, zq_done)
                p.wait("t", cx.stq, cx.stq.n)
                for c in range(NCH):
                    p.op("t", "matmul", cx.bank(4),
                         wmat[:, c * 1024 + oc * 128:c * 1024 + (oc + 1) * 128],
                         zq[:, c * 512:(c + 1) * 512], start=(c == 0),
                         stop=(c == NCH - 1),
                         inc=(cx.sT if c == NCH - 1 else None))
                p.wait("v", cx.sT)
                p.op("v", "tensor_scalar", scrf[:], cx.bank(4),
                     bias[:, oc:oc + 1], None, ALU.add, inc=cx.sV)
                p.wait("s", cx.sV)
                p.op("s", "activation", out=qkob[:], in_=scrf[:],
                     func=AF.Square, inc=cx.sS)
                p.wait("t", cx.sS)
                p.op("t", "matmul", cx.banksl(5, slice(0, 512), 2),
                     hsel[:], qkob[:], start=True, stop=True, inc=cx.sT)
                p.wait("v", cx.sT)
                p.op("v", "tensor_copy", s2n[:], cx.banksl(5, slice(0, 512), 2),
                     inc=cx.sV)
                p.wait("s", cx.sV)
                p.op("s", "activation", out=s2n[:], in_=s2n[:], func=AF.Ln)
                p.op("s", "activation", out=s2n[:], in_=s2n[:], func=AF.Exp,
                     scale=cx.constf(0.5), inc=cx.sS)
                p.wait("v", cx.sS)
                p.op("v", "tensor_scalar_max", s2n[:], s2n[:], 1e-12,
                     inc=cx.sV)
                p.wait("v", cx.sV)
                p.op("v", "reciprocal", s2n[:], s2n[:], inc=cx.sV)
                p.wait("t", cx.sV)
                p.op("t", "matmul", cx.banksl(5, slice(0, 512), 128),
                     hrep_f[:], s2n[:], start=True, stop=True, inc=cx.sT)
                p.wait("v", cx.sT)
                p.op("v", "tensor_mul", qn_bf[:], scrf[:],
                     cx.banksl(5, slice(0, 512), 128), inc=cx.sV)
                p.wait("q", cx.sV)
                p.op("q", "dma_start", out=dst_v[:, oc, tsl], in_=qn_bf[:],
                     inc=cx.stq)
        p.wait("t", cx.sS, bv_done)
        for stile in range(4):
            p.wait("t", cx.stq, cx.stq.n)
            p.wait("t", cx.sV)
            for half in range(2):
                for c in range(NCH):
                    p.op("t", "matmul", cx.bank(6 + half),
                         zq[:, c * 512 + stile * 128:
                            c * 512 + (stile + 1) * 128],
                         wv[:, c * 1024 + half * 512:
                            c * 1024 + (half + 1) * 512],
                         start=(c == 0), stop=(c == NCH - 1),
                         inc=(cx.sT if c == NCH - 1 else None))
            p.wait("v", cx.sT)
            for half in range(2):
                p.op("v", "tensor_add", vout[:, half * 512:(half + 1) * 512],
                     cx.bank(6 + half), bv_rep[:, half * 512:(half + 1) * 512],
                     inc=(cx.sV if half == 1 else None))
            p.wait("q", cx.sV)
            p.op("q", "dma_start",
                 out=scr["v_d"][(tt * 4 + stile) * 128:
                                (tt * 4 + stile + 1) * 128, :],
                 in_=vout[:], inc=cx.stq)
    p.wait("q", cx.stq, cx.stq.n)
    esQ.close()
    cx.barrier()

    esB = cx.scope()
    kT = cx.sbs(esB, f"a{u}_kT", [128, T], BF16)
    stT = cx.sbs(esB, f"a{u}_stT", [128, T], BF16)
    stN = cx.sbs(esB, f"a{u}_stN", [128, T], BF16)
    vwk = [cx.sbs(esB, f"a{u}_vwk{i}", [128, 32 * 65], BF16) for i in range(2)]
    pT = [cx.sbs(esB, f"a{u}_pT0", [128, 32 * 1024], BF16)]
    sqc = cx.sbs(esB, f"a{u}_sqc", [128, T], BF16)
    cbs = [cx.sbs(esB, f"a{u}_cbs{i}", [128, 1], F32) for i in range(2)]
    negc = cx.sbs(esB, f"a{u}_negc", [2, 1], F32)
    negcT = cx.sbs(esB, f"a{u}_negcT", [1, 2], F32)
    rq = cx.sbs(esB, f"a{u}_rq", [1, 512], F32)
    avsb = [cx.sbs(esB, f"a{u}_avsb{i}", [65, 512], F32) for i in range(2)]
    p.op("g", "memset", vwk[0][:].rearrange("p (kt d) -> p kt d", kt=32)[:, :, 64:65], 1.0)
    p.op("g", "memset", vwk[1][:].rearrange("p (kt d) -> p kt d", kt=32)[:, :, 64:65], 1.0, inc=cx.sG)
    vones = cx.sG.n

    for s_ in range(2):
        src_v = q_d_v if s_ == 0 else st_d_v
        out_v = st_d_v if s_ == 0 else st2_d_v
        for hp in range(NH // 2):
            p.wait("q", cx.sT)
            p.wait("q", cx.stq, cx.stq.n)
            p.op("q", "dma_start", out=kT[:],
                 in_=k_d_v[:, hp, :], inc=cx.ld)
            p.op("q", "dma_start", out=stT[:],
                 in_=src_v[:, hp, :], inc=cx.ld)
            p.wait("g", cx.sT)
            p.wait("g", cx.sG, vones)
            for b in range(2):
                p.op("g", "dma_start",
                     out=vwk[b][:].rearrange("p (kt d) -> p kt d",
                                             kt=32)[:, :, 0:64],
                     in_=v_d_v[:, :, 2 * hp + b, :], inc=cx.ldg)
            ld_hp = cx.ld.n
            ldg_hp = cx.ldg.n
            p.wait("s", cx.ld, ld_hp)
            p.op("s", "activation", out=sqc[:], in_=stT[:], func=AF.Square,
                 inc=cx.sS)
            p.wait("t", cx.sS)
            p.wait("t", cx.sV)
            for nt in range(NTT):
                p.op("t", "matmul", cx.psa[0:2, nt * 512:(nt + 1) * 512],
                     hsel[:], sqc[:, nt * 512:(nt + 1) * 512],
                     start=True, stop=True,
                     inc=(cx.sT if nt == NTT - 1 else None))
            p.wait("v", cx.sT)
            p.op("v", "reduce_max", negc[:], cx.psa[0:2, 0:4096], axis=AX.X,
                 inc=cx.sV)
            p.wait("t", cx.sV)
            p.op("t", "matmul", cx.banksl(1, slice(0, 2), 1), negc[:],
                 idf2[:], start=True, stop=True, inc=cx.sT)
            p.wait("v", cx.sT)
            p.op("v", "tensor_copy", negcT[:], cx.banksl(1, slice(0, 2), 1),
                 inc=cx.sV)
            p.wait("v", cx.sV)
            p.op("v", "reduce_max", negcT[:, 0:1], negcT[:], axis=AX.X,
                 inc=cx.sV)
            p.wait("s", cx.sV)
            p.op("s", "activation", out=negcT[:, 0:1], in_=negcT[:, 0:1],
                 func=AF.Ln)
            p.op("s", "activation", out=negcT[:, 0:1], in_=negcT[:, 0:1],
                 func=AF.Exp, scale=cx.constf(0.5), inc=cx.sS)
            p.wait("v", cx.sS)
            p.op("v", "tensor_scalar_mul", negcT[:, 0:1], negcT[:, 0:1],
                 -1.0, inc=cx.sV)
            p.wait("t", cx.sV)
            p.op("t", "matmul", cx.banksl(1, slice(0, 1), 128),
                 cx.ones_fT[:], negcT[:, 0:1], start=True, stop=True,
                 inc=cx.sT)
            p.wait("s", cx.sT)
            p.op("s", "activation", out=cbs[0][:],
                 in_=cx.banksl(1, slice(0, 1), 128), func=AF.Copy,
                 inc=cx.sS)
            cb_done = cx.sS.n
            exp_done = {}
            avb_free = {0: 0, 1: 0}
            avb_free_s = {0: 0, 1: 0}
            for qc in range(NTT):
                qsl = slice(qc * 512, (qc + 1) * 512)
                av0 = 4 if (qc % 2 == 0) else 6
                for kt in range(33):
                    if kt < 32:
                        bb = 2 * (kt % 2)
                        if kt >= 2:
                            p.wait("t", cx.sS, exp_done[kt - 2])
                        elif qc == 0:
                            p.wait("t", cx.sS, cb_done)
                        else:
                            p.wait("t", cx.sS, prev_exp[kt - 2])
                        if kt == 0:
                            p.wait("t", cx.sV, avb_free[qc % 2])
                            p.wait("t", cx.sS, avb_free_s[qc % 2])
                        for b in range(2):
                            p.op("t", "matmul", cx.bank(bb + b),
                                 kT[b * 64:(b + 1) * 64,
                                    kt * 128:(kt + 1) * 128],
                                 stT[b * 64:(b + 1) * 64, qsl],
                                 start=True, stop=True,
                                 tile_position=(64 * b, 0),
                                 inc=(cx.sT if b == 1 else None))
                        mmd = cx.sT.n
                        p.wait("s", cx.sT, mmd)
                        p.op("s", "activation",
                             out=pT[0][:, kt * 1024:(kt + 1) * 1024],
                             in_=cx.psa[:, bb * 512:(bb + 2) * 512],
                             func=AF.Exp, bias=cbs[0][:, 0:1], inc=cx.sS)
                        exp_done[kt] = cx.sS.n
                    if kt >= 1:
                        ka = kt - 1
                        p.wait("t", cx.sS, exp_done[ka])
                        for b in range(2):
                            p.op("t", "matmul", cx.bank(av0 + b, 65),
                                 vwk[b][:, ka * 65:(ka + 1) * 65],
                                 pT[0][:, ka * 1024 + b * 512:
                                        ka * 1024 + (b + 1) * 512],
                                 start=(ka == 0), stop=(ka == 31),
                                 inc=(cx.sT if ka == 31 and b == 1 else None))
                prev_exp = {-2: exp_done.get(30, 0), -1: exp_done.get(31, 0)}
                for b in range(2):
                    p.wait("s", cx.sT)
                    p.op("s", "activation", out=avsb[b][:],
                         in_=cx.psa[0:65, (av0 + b) * 512:(av0 + b) * 512 + 512],
                         func=AF.Copy, inc=cx.sS)
                    p.wait("v", cx.sS)
                    p.op("v", "reciprocal", rq[:], avsb[b][64:65, :],
                         inc=cx.sV)
                    p.wait("t", cx.sV)
                    p.op("t", "matmul",
                         cx.banksl(av0 + b, slice(0, 512), 64),
                         cx.ones_fT[:, 0:64], rq[:], start=True, stop=True,
                         inc=cx.sT)
                    p.wait("v", cx.sT)
                    if qc <= 1:
                        p.wait("v", cx.stq, cx.stq.n)
                    p.op("v", "tensor_mul", stN[b * 64:(b + 1) * 64, qsl],
                         avsb[b][0:64, :],
                         cx.banksl(av0 + b, slice(0, 512), 64), inc=cx.sV)
                avb_free[qc % 2] = cx.sV.n
                avb_free_s[qc % 2] = cx.sS.n
            p.wait("q", cx.sV, max(avb_free.values()))
            p.op("q", "dma_start", out=out_v[:, hp, :], in_=stN[:],
                 inc=cx.stq)
    p.wait("q", cx.stq, cx.stq.n)
    esB.close()
    cx.barrier()

    esC = cx.scope()
    Rt2 = cx.sbs(esC, f"a{u}_Rt2", [128, NCH * 512], F32)
    xn2 = cx.sbs(esC, f"a{u}_xn2", [128, NCH * 512], F32)
    st2 = cx.sbs(esC, f"a{u}_st2", [128, NCH * 512], BF16)
    Ro = cx.sbs(esC, f"a{u}_Ro", [128, NCH * 512], F32)
    scrg = cx.sbs(esC, f"a{u}_scrg", [128, 512], F32)
    ro_st = 0
    for tt in range(NTT):
        tsl = slice(tt * 512, (tt + 1) * 512)
        p.wait("q", cx.sV)
        p.op("q", "dma_start",
             out=Rt2[:].rearrange("p (c t) -> p c t", c=NCH),
             in_=r_in_v[:, :, tsl], inc=cx.ld)
        p.op("q", "dma_start",
             out=xn2[:].rearrange("p (c t) -> p c t", c=NCH),
             in_=xn_d_v[:, :, tsl], inc=cx.ld)
        p.op("q", "dma_start",
             out=st2[:].rearrange("p (c t) -> p c t", c=NCH),
             in_=st2_d_v[:, :, tsl], inc=cx.ld)
        ld_t = cx.ld.n
        p.wait("t", cx.ld, ld_t)
        fc_evac = {}
        for oc in range(NCH):
            bk = oc % 2
            if oc >= 2:
                p.wait("t", cx.sV, fc_evac[oc - 2])
            elif tt > 0:
                p.wait("t", cx.sV, fc_evac_prev[NCH - 2 + oc])
            for c in range(NCH):
                p.op("t", "matmul", cx.bank(bk),
                     wo[:, c * 1024 + oc * 128:c * 1024 + (oc + 1) * 128],
                     st2[:, c * 512:(c + 1) * 512],
                     start=(c == 0), stop=(c == NCH - 1),
                     inc=(cx.sT if c == NCH - 1 else None))
            p.wait("v", cx.sT)
            if oc == 0:
                p.wait("v", cx.stq, ro_st)
            csl = slice(oc * 512, (oc + 1) * 512)
            p.op("v", "scalar_tensor_tensor", out=scrg[:], in0=cx.bank(bk),
                 scalar=bo[:, oc:oc + 1], in1=xn2[:, csl], op0=ALU.add,
                 op1=ALU.add, inc=cx.sV)
            p.wait("v", cx.sV)
            p.op("v", "tensor_add", Ro[:, csl], scrg[:], Rt2[:, csl],
                 inc=cx.sV)
            fc_evac[oc] = cx.sV.n
        fc_evac_prev = fc_evac
        p.wait("q", cx.sV)
        p.op("q", "dma_start", out=r_out_v[:, :, tsl],
             in_=Ro[:].rearrange("p (c t) -> p c t", c=NCH), inc=cx.stq)
        ro_st = cx.stq.n
    p.wait("q", cx.stq, cx.stq.n)
    esC.close()
    esA.close()
    cx.barrier()


from contextlib import ExitStack
from concourse.bass_utils import run_bass_kernel_spmd

NCORES = 8


def _build_ffn0():
    nc = bass.Bass("TRN2", target_bir_lowering=False)
    r_in = nc.dram_tensor("r_in", [1024, 4096], F32, kind="ExternalInput").ap()
    wgT_i = nc.dram_tensor("wgT_i", [1024, 3072], I32, kind="ExternalInput").ap()
    wuT_i = nc.dram_tensor("wuT_i", [1024, 3072], I32, kind="ExternalInput").ap()
    wdT_i = nc.dram_tensor("wdT_i", [3072, 1024], I32, kind="ExternalInput").ap()
    r_out = nc.dram_tensor("r_out", [1024, 4096], F32, kind="ExternalOutput").ap()
    wgT = nc.dram_tensor("wgT", [1024, 3072], BF16, kind="Internal").ap()
    wuT = nc.dram_tensor("wuT", [1024, 3072], BF16, kind="Internal").ap()
    wdT = nc.dram_tensor("wdT", [3072, 1024], BF16, kind="Internal").ap()
    es = ExitStack()
    cx = Ctx(nc, es)
    build_consts(cx)
    prep_ffn0_weights(cx, wgT_i, wuT_i, wdT_i, wgT, wuT, wdT)
    emit_ffn0(cx, r_in, r_out, wgT, wuT, wdT)
    cx.finalize()
    with nc.Block() as block:
        cx.p.replay(block)
    es.close()
    return nc


def _build_mamba():
    nc = bass.Bass("TRN2", target_bir_lowering=False)
    r_in = nc.dram_tensor("r_in", [1024, 4096], F32, kind="ExternalInput").ap()
    r_out = nc.dram_tensor("r_out", [1024, 4096], F32, kind="ExternalOutput").ap()
    Wsh = dict(fc1_wT=[1024, 1024], fc1_b=[128, 8], inpT=[1024, 4096],
               conv_w=[128, 64], conv_b=[128, 16], xpT=[2048, 96],
               dtpT=[64, 2048], dtp_b=[128, 16], A_log=[128, 256],
               Dv=[128, 16], opT=[2048, 1024], fc2_wT=[1024, 1024],
               fc2_b=[128, 8])
    W = {k: nc.dram_tensor(k, s, F32, kind="ExternalInput").ap()
         for k, s in Wsh.items()}
    scr = {
        "inpT_bf": nc.dram_tensor("inpT_bf", [1024, 4096], BF16, kind="Internal").ap(),
        "opT_bf": nc.dram_tensor("opT_bf", [2048, 1024], BF16, kind="Internal").ap(),
        "u_d": nc.dram_tensor("u_d", [2048, 4096], F32, kind="Internal").ap(),
        "z_d": nc.dram_tensor("z_d", [2048, 4096], F32, kind="Internal").ap(),
        "y_d": nc.dram_tensor("y_d", [2048, 4096], F32, kind="Internal").ap(),
    }
    es = ExitStack()
    cx = Ctx(nc, es)
    build_consts(cx)
    emit_mamba(cx, r_in, r_out, W, scr)
    cx.finalize()
    with nc.Block() as block:
        cx.p.replay(block)
    es.close()
    return nc


def _build_moe():
    nc = bass.Bass("TRN2", target_bir_lowering=False)
    r_in = nc.dram_tensor("r_in", [1024, 4096], F32, kind="ExternalInput").ap()
    r_out = nc.dram_tensor("r_out", [1024, 4096], F32, kind="ExternalOutput").ap()
    W = {
        "gate_wT": nc.dram_tensor("gate_wT", [1024, 8], F32, kind="ExternalInput").ap(),
        "gbias": nc.dram_tensor("gbias", [8, 2], F32, kind="ExternalInput").ap(),
        "b1": nc.dram_tensor("b1", [128, 256], F32, kind="ExternalInput").ap(),
        "b2": nc.dram_tensor("b2", [128, 64], F32, kind="ExternalInput").ap(),
        "w1": nc.dram_tensor("w1", [8, 1024, 4096], F32, kind="ExternalInput").ap(),
        "w2": nc.dram_tensor("w2", [8, 4096, 1024], F32, kind="ExternalInput").ap(),
    }
    scr = {
        "w1_bf": nc.dram_tensor("w1_bf", [8, 1024, 4096], BF16, kind="Internal").ap(),
        "w2_bf": nc.dram_tensor("w2_bf", [8, 4096, 1024], BF16, kind="Internal").ap(),
    }
    es = ExitStack()
    cx = Ctx(nc, es)
    build_consts(cx)
    emit_moe(cx, r_in, r_out, W, scr)
    cx.finalize()
    with nc.Block() as block:
        cx.p.replay(block)
    es.close()
    return nc


def _build_attn():
    nc = bass.Bass("TRN2", target_bir_lowering=False)
    r_in = nc.dram_tensor("r_in", [1024, 4096], F32, kind="ExternalInput").ap()
    r_out = nc.dram_tensor("r_out", [1024, 4096], F32, kind="ExternalOutput").ap()
    W = {}
    for nm in ("Wq", "Wk", "Wv", "Wo"):
        W[nm] = nc.dram_tensor(nm, [1024, 1024], F32, kind="ExternalInput").ap()
    for nm in ("bq", "bk", "bo", "lnw", "lnb"):
        W[nm] = nc.dram_tensor(nm, [128, 8], F32, kind="ExternalInput").ap()
    W["bv"] = nc.dram_tensor("bv", [1, 1024], F32, kind="ExternalInput").ap()
    scr = {
        "xn_d": nc.dram_tensor("xn_d", [1024, 4096], F32, kind="Internal").ap(),
        "q_d": nc.dram_tensor("q_d", [1024, 4096], BF16, kind="Internal").ap(),
        "k_d": nc.dram_tensor("k_d", [1024, 4096], BF16, kind="Internal").ap(),
        "st_d": nc.dram_tensor("st_d", [1024, 4096], BF16, kind="Internal").ap(),
        "st2_d": nc.dram_tensor("st2_d", [1024, 4096], BF16, kind="Internal").ap(),
        "v_d": nc.dram_tensor("v_d", [4096, 1024], BF16, kind="Internal").ap(),
    }
    es = ExitStack()
    cx = Ctx(nc, es)
    build_consts(cx)
    emit_attn(cx, r_in, r_out, W, scr)
    cx.finalize()
    with nc.Block() as block:
        cx.p.replay(block)
    es.close()
    return nc


def kernel_unfused(**inputs):
    g = inputs
    ct = np.ascontiguousarray
    r = [ct(g["L0_attn_x"][i].T) for i in range(NCORES)]

    ffn0_w = {
        "wgT_i": ct(g["L0_ffn_w_gate"].T),
        "wuT_i": ct(g["L0_ffn_w_up"].T),
        "wdT_i": ct(g["L0_ffn_w_down"].T),
    }
    mamba_w = {
        "fc1_wT": ct(np.asarray(g["L0_attn_fc1_w"]).T),
        "fc1_b": to_pcol(g["L0_attn_fc1_b"]),
        "inpT": ct(np.asarray(g["L0_attn_in_proj_w"]).T),
        "conv_w": to_pcol(np.asarray(g["L0_attn_conv_w"])),
        "conv_b": to_pcol(g["L0_attn_conv_b"]),
        "xpT": ct(np.asarray(g["L0_attn_x_proj_w"]).T),
        "dtpT": ct(np.asarray(g["L0_attn_dt_proj_w"]).T),
        "dtp_b": to_pcol(g["L0_attn_dt_proj_b"]),
        "A_log": to_pcol(np.asarray(g["L0_attn_A_log"])),
        "Dv": to_pcol(g["L0_attn_D"]),
        "opT": ct(np.asarray(g["L0_attn_out_proj_w"]).T),
        "fc2_wT": ct(np.asarray(g["L0_attn_fc2_w"]).T),
        "fc2_b": to_pcol(g["L0_attn_fc2_b"]),
    }
    moe_w = {
        "gate_wT": ct(np.asarray(g["L1_ffn_gate_w"]).T),
        "gbias": ct(np.stack([np.asarray(g["L1_ffn_gate_b"]),
                              np.asarray(g["L1_ffn_expert_bias"])], axis=1)),
        "b1": np.concatenate([to_pcol(np.asarray(g["L1_ffn_b1"])[e])
                              for e in range(8)], axis=1),
        "b2": np.concatenate([to_pcol(np.asarray(g["L1_ffn_b2"])[e])
                              for e in range(8)], axis=1),
        "w1": ct(np.asarray(g["L1_ffn_w1"])),
        "w2": ct(np.asarray(g["L1_ffn_w2"])),
    }
    attn_w = {
        "Wq": ct(np.asarray(g["L1_attn_Wq"])),
        "Wk": ct(np.asarray(g["L1_attn_Wk"])),
        "Wv": ct(np.asarray(g["L1_attn_Wv"])),
        "Wo": ct(np.asarray(g["L1_attn_Wo"])),
        "bq": to_pcol(g["L1_attn_bq"]),
        "bk": to_pcol(g["L1_attn_bk"]),
        "bo": to_pcol(g["L1_attn_bo"]),
        "lnw": to_pcol(g["L1_attn_ln_w"]),
        "lnb": to_pcol(g["L1_attn_ln_b"]),
        "bv": ct(np.asarray(g["L1_attn_bv"]).reshape(1, 1024)),
    }

    nc_ffn = _build_ffn0()
    nc_mamba = _build_mamba()
    nc_moe = _build_moe()
    nc_attn = _build_attn()

    def launch(nc, wdict):
        nonlocal r
        in_maps = [dict(wdict, r_in=r[i]) for i in range(NCORES)]
        res = run_bass_kernel_spmd(nc, in_maps, core_ids=list(range(NCORES)))
        r = [res.results[i]["r_out"] for i in range(NCORES)]

    launch(nc_ffn, ffn0_w)
    launch(nc_mamba, mamba_w)
    launch(nc_ffn, ffn0_w)
    launch(nc_moe, moe_w)
    launch(nc_attn, attn_w)
    launch(nc_moe, moe_w)

    out = np.stack([r[i].T for i in range(NCORES)], axis=0)
    return out.astype(np.float32)


def _build_fused():
    nc = bass.Bass("TRN2", target_bir_lowering=False)
    x_in = nc.dram_tensor("x_in", [1024, 4096], F32, kind="ExternalInput").ap()
    r_fin = nc.dram_tensor("r_fin", [1024, 4096], F32,
                           kind="ExternalOutput").ap()
    rbuf = [nc.dram_tensor(f"rb{i}", [1024, 4096], F32, kind="Internal").ap()
            for i in range(5)]
    wgT_i = nc.dram_tensor("wgT_i", [1024, 3072], I32, kind="ExternalInput").ap()
    wuT_i = nc.dram_tensor("wuT_i", [1024, 3072], I32, kind="ExternalInput").ap()
    wdT_i = nc.dram_tensor("wdT_i", [3072, 1024], I32, kind="ExternalInput").ap()
    wgT = nc.dram_tensor("wgT", [1024, 3072], BF16, kind="Internal").ap()
    wuT = nc.dram_tensor("wuT", [1024, 3072], BF16, kind="Internal").ap()
    wdT = nc.dram_tensor("wdT", [3072, 1024], BF16, kind="Internal").ap()
    Wm_sh = dict(fc1_wT=[1024, 1024], fc1_b=[128, 8], inpT=[1024, 4096],
                 conv_w=[128, 64], conv_b=[128, 16], xpT=[2048, 96],
                 dtpT=[64, 2048], dtp_b=[128, 16], A_log=[128, 256],
                 Dv=[128, 16], opT=[2048, 1024], fc2_wT=[1024, 1024],
                 fc2_b=[128, 8])
    Wm = {k: nc.dram_tensor("m_" + k, s, F32, kind="ExternalInput").ap()
          for k, s in Wm_sh.items()}
    scrm = {
        "inpT_bf": nc.dram_tensor("inpT_bf", [1024, 4096], BF16, kind="Internal").ap(),
        "opT_bf": nc.dram_tensor("opT_bf", [2048, 1024], BF16, kind="Internal").ap(),
        "u_d": nc.dram_tensor("u_d", [2048, 4096], F32, kind="Internal").ap(),
        "z_d": nc.dram_tensor("z_d", [2048, 4096], F32, kind="Internal").ap(),
        "y_d": nc.dram_tensor("y_d", [2048, 4096], F32, kind="Internal").ap(),
    }
    Wo_ = {
        "gate_wT": nc.dram_tensor("gate_wT", [1024, 8], F32, kind="ExternalInput").ap(),
        "gbias": nc.dram_tensor("gbias", [8, 2], F32, kind="ExternalInput").ap(),
        "b1": nc.dram_tensor("b1", [128, 256], F32, kind="ExternalInput").ap(),
        "b2": nc.dram_tensor("b2", [128, 64], F32, kind="ExternalInput").ap(),
        "w1": nc.dram_tensor("w1", [8, 1024, 4096], F32, kind="ExternalInput").ap(),
        "w2": nc.dram_tensor("w2", [8, 4096, 1024], F32, kind="ExternalInput").ap(),
    }
    scro = {
        "w1_bf": nc.dram_tensor("w1_bf", [8, 1024, 4096], BF16, kind="Internal").ap(),
        "w2_bf": nc.dram_tensor("w2_bf", [8, 4096, 1024], BF16, kind="Internal").ap(),
    }
    Wa = {}
    for nm in ("Wq", "Wk", "Wv", "Wo"):
        Wa[nm] = nc.dram_tensor("a_" + nm, [1024, 1024], F32, kind="ExternalInput").ap()
    for nm in ("bq", "bk", "bo", "lnw", "lnb"):
        Wa[nm] = nc.dram_tensor("a_" + nm, [128, 8], F32, kind="ExternalInput").ap()
    Wa["bv"] = nc.dram_tensor("a_bv", [1, 1024], F32, kind="ExternalInput").ap()
    scra = {
        "xn_d": nc.dram_tensor("xn_d", [1024, 4096], F32, kind="Internal").ap(),
        "q_d": nc.dram_tensor("q_d", [1024, 4096], BF16, kind="Internal").ap(),
        "k_d": nc.dram_tensor("k_d", [1024, 4096], BF16, kind="Internal").ap(),
        "st_d": nc.dram_tensor("st_d", [1024, 4096], BF16, kind="Internal").ap(),
        "st2_d": nc.dram_tensor("st2_d", [1024, 4096], BF16, kind="Internal").ap(),
        "v_d": nc.dram_tensor("v_d", [4096, 1024], BF16, kind="Internal").ap(),
    }

    es = ExitStack()
    cx = Ctx(nc, es)
    build_consts(cx)
    prep_ffn0_weights(cx, wgT_i, wuT_i, wdT_i, wgT, wuT, wdT)
    cx.phase_boundary()
    emit_ffn0(cx, x_in, rbuf[0], wgT, wuT, wdT)
    cx.phase_boundary()
    emit_mamba(cx, rbuf[0], rbuf[1], Wm, scrm)
    cx.phase_boundary()
    emit_ffn0(cx, rbuf[1], rbuf[2], wgT, wuT, wdT)
    cx.phase_boundary()
    emit_moe(cx, rbuf[2], rbuf[3], Wo_, scro)
    cx.phase_boundary()
    emit_attn(cx, rbuf[3], rbuf[4], Wa, scra)
    cx.phase_boundary()
    emit_moe(cx, rbuf[4], r_fin, Wo_, scro)
    cx.finalize()
    with nc.Block() as block:
        cx.p.replay(block)
    es.close()
    return nc, cx.p.n_inst


def kernel(**inputs):
    g = inputs
    ct = np.ascontiguousarray
    wmap = {
        "wgT_i": ct(np.asarray(g["L0_ffn_w_gate"]).T),
        "wuT_i": ct(np.asarray(g["L0_ffn_w_up"]).T),
        "wdT_i": ct(np.asarray(g["L0_ffn_w_down"]).T),
        "m_fc1_wT": ct(np.asarray(g["L0_attn_fc1_w"]).T),
        "m_fc1_b": to_pcol(g["L0_attn_fc1_b"]),
        "m_inpT": ct(np.asarray(g["L0_attn_in_proj_w"]).T),
        "m_conv_w": to_pcol(np.asarray(g["L0_attn_conv_w"])),
        "m_conv_b": to_pcol(g["L0_attn_conv_b"]),
        "m_xpT": ct(np.asarray(g["L0_attn_x_proj_w"]).T),
        "m_dtpT": ct(np.asarray(g["L0_attn_dt_proj_w"]).T),
        "m_dtp_b": to_pcol(g["L0_attn_dt_proj_b"]),
        "m_A_log": to_pcol(np.asarray(g["L0_attn_A_log"])),
        "m_Dv": to_pcol(g["L0_attn_D"]),
        "m_opT": ct(np.asarray(g["L0_attn_out_proj_w"]).T),
        "m_fc2_wT": ct(np.asarray(g["L0_attn_fc2_w"]).T),
        "m_fc2_b": to_pcol(g["L0_attn_fc2_b"]),
        "gate_wT": ct(np.asarray(g["L1_ffn_gate_w"]).T),
        "gbias": ct(np.stack([np.asarray(g["L1_ffn_gate_b"]),
                              np.asarray(g["L1_ffn_expert_bias"])], axis=1)),
        "b1": np.concatenate([to_pcol(np.asarray(g["L1_ffn_b1"])[e])
                              for e in range(8)], axis=1),
        "b2": np.concatenate([to_pcol(np.asarray(g["L1_ffn_b2"])[e])
                              for e in range(8)], axis=1),
        "w1": ct(np.asarray(g["L1_ffn_w1"])),
        "w2": ct(np.asarray(g["L1_ffn_w2"])),
        "a_Wq": ct(np.asarray(g["L1_attn_Wq"])),
        "a_Wk": ct(np.asarray(g["L1_attn_Wk"])),
        "a_Wv": ct(np.asarray(g["L1_attn_Wv"])),
        "a_Wo": ct(np.asarray(g["L1_attn_Wo"])),
        "a_bq": to_pcol(g["L1_attn_bq"]),
        "a_bk": to_pcol(g["L1_attn_bk"]),
        "a_bo": to_pcol(g["L1_attn_bo"]),
        "a_lnw": to_pcol(g["L1_attn_ln_w"]),
        "a_lnb": to_pcol(g["L1_attn_ln_b"]),
        "a_bv": ct(np.asarray(g["L1_attn_bv"]).reshape(1, 1024)),
    }
    nc, _ = _build_fused()
    in_maps = [dict(wmap, x_in=ct(np.asarray(g["L0_attn_x"])[i].T))
               for i in range(NCORES)]
    res = run_bass_kernel_spmd(nc, in_maps, core_ids=list(range(NCORES)))
    out = np.stack([res.results[i]["r_fin"].T for i in range(NCORES)], axis=0)
    return out.astype(np.float32)
